# version 03
# baseline (speedup 1.0000x reference)
import numpy as np
import ml_dtypes
import concourse.bass as bass
import concourse.mybir as mybir
from concourse.bass_core_rust import IndirectOffsetOnAxis
from concourse.bass_utils import run_bass_kernel_spmd
from contextlib import ExitStack

F32 = mybir.dt.float32
BF16 = mybir.dt.bfloat16
I32 = mybir.dt.int32
AF = mybir.ActivationFunctionType
ALU = mybir.AluOpType
AX = mybir.AxisListType

NC = 8
D = 1024
DI = 2048
NST = 16
DTR = 64
NZ = 8
DFF = 4096
RH = 256
BATCH = 2
PAD = 128
MM = 512
EPS = 1e-5
ENGINES = ("pe", "dve", "act", "sp", "pool")
JOBS = [("f", 0), ("b", 0), ("f", 1), ("b", 1)]


class Sched:
    def __init__(self, nc, chain=False):
        self.nc = nc
        self.chain = chain
        self.ops = {e: [] for e in ENGINES}
        self.cnt = {}
        self.sems = {}
        self._ctx = []
        self.last_inc = {e: None for e in ENGINES}
        self.marks = {}

    def mark(self, name):
        self.marks[name] = ({e: len(self.ops[e]) for e in ENGINES},
                            dict(self.cnt), dict(self.last_inc))

    def truncate(self, name):
        lens, cnt, last = self.marks[name]
        for e in ENGINES:
            self.ops[e] = self.ops[e][:lens[e]]
        self.cnt = dict(cnt)
        self.last_inc = dict(last)

    def sem(self, name):
        if name not in self.sems:
            c = self.nc.semaphore(name)
            self.sems[name] = c.__enter__()
            self._ctx.append(c)
            self.cnt[name] = 0
        return name

    def do(self, eng, fn, waits=(), inc=None, incn=1, cc=False):
        w = []
        for s, val in waits:
            self.sem(s)
            assert val <= self.cnt[s], f"wait {s}>={val} but only {self.cnt[s]} scheduled"
            if val > 0:
                w.append((s, val))
        if self.chain and eng in ("pe", "dve", "act") and self.last_inc[eng]:
            w.append(self.last_inc[eng])
        if inc is not None:
            self.sem(inc)
            self.cnt[inc] += incn
            self.last_inc[eng] = (inc, self.cnt[inc])
        self.ops[eng].append((tuple(w), fn, inc, incn))

    def v(self, name):
        self.sem(name)
        return self.cnt.get(name, 0)

    def emit(self, eng_name, eng):
        for w, fn, inc, incn in self.ops[eng_name]:
            seen = set()
            for s, val in w:
                if (s, val) in seen:
                    continue
                seen.add((s, val))
                eng.wait_ge(self.sems[s], val)
            inst = fn()
            if inc is not None and inst is not None:
                inst.then_inc(self.sems[inc], incn)


def build_program(L, debug=False, chain=False, phases=None):
    TOK = L // 4
    TOKP = TOK + 2 * PAD
    NTT = TOKP // 128
    CSL = DI // NC
    NCT = CSL // 128
    SC = min(1024, TOK)
    NSUB = TOK // SC
    NMC = TOK // MM
    NFC = NC * TOK // MM
    QT = TOK // 4

    nc = bass.Bass("TRN2", target_bir_lowering=False)
    S = Sched(nc, chain=chain)
    RG = [list(range(NC))]

    def inp(name, shape, dtype=F32):
        return nc.dram_tensor(name, list(shape), dtype, kind="ExternalInput").ap()

    x0_d = inp("x0", (TOKP, D))
    w_in_xi = {d: inp(f"w_in_xi_{d}", (D, DI)) for d in "fb"}
    w_in_z = {d: inp(f"w_in_z_{d}", (D, DI)) for d in "fb"}
    w_x = {d: inp(f"w_x_{d}", (128, 16 * 96)) for d in "fb"}
    w_dt = {d: inp(f"w_dt_{d}", (DTR, CSL)) for d in "fb"}
    w_out = {d: inp(f"w_out_{d}", (128, 16 * D)) for d in "fb"}
    w_comb = inp("w_comb", (128, 16 * D))
    conv_w = {d: inp(f"conv_w_{d}", (128, 16 * 4)) for d in "fb"}
    conv_b = {d: inp(f"conv_b_{d}", (128, 16)) for d in "fb"}
    dt_b = {d: inp(f"dt_b_{d}", (128, NCT)) for d in "fb"}
    A_in = {d: inp(f"A_{d}", (128, NCT * NST)) for d in "fb"}
    D_in = {d: inp(f"D_{d}", (128, NCT)) for d in "fb"}
    g_col = inp("g_col", (128, 8))
    b_col = inp("b_col", (128, 8))
    cb_col = inp("cb_col", (128, 8))
    wa_d = inp("wa", (D, DFF))
    wb_d = inp("wb", (128, 32 * D))
    w1_d = inp("w1", (128, 8 * RH))
    b1_d = inp("b1_col", (128, 2))
    w2_d = inp("w2", (128, 2 * NZ))
    b2_d = inp("b2_col", (NZ, 1))
    sel_d = inp("sel", (16, NST * 128))
    ones1_d = inp("ones1", (1, 128))
    onescol_d = inp("onescol", (128, 1))
    ident_d = inp("ident", (128, 128))
    edge_d = inp("edge", (128, 2))
    bmask_d = inp("bmask", (128, 2))
    zmask_d = inp("zmask", (128, 8))
    cst_d = inp("cst_in", (128, 4))
    idxu_d = inp("idxu", (128, 2 * 2 * 4 * NCT), I32)
    idxy_d = inp("idxy", (128, 2 * 16), I32)

    out_d = nc.dram_tensor("out", [TOK, D], F32, kind="ExternalOutput").ap()
    dbg = {}
    if debug:
        for nm, shp in [("xn2", (D, TOKP)), ("u_f", (DI, TOK)), ("dbc_f", (96, TOK)),
                        ("yf", (DI, TOK)), ("x1", (D, TOK)), ("xn3", (D, TOK)),
                        ("w", (1, 2 * NZ)), ("zs", (D, TOK))]:
            dbg[nm] = nc.dram_tensor("dbg_" + nm, list(shp), F32, kind="ExternalOutput").ap()

    dr = lambda name, shape, dtype=F32: nc.dram_tensor(name, list(shape), dtype)
    u_in = {d: dr(f"u_in_{d}", (DI, TOK), F32) for d in "fb"}
    u_all = {d: dr(f"u_all_{d}", (NC * DI, TOK), F32) for d in "fb"}
    dbc_in = {d: dr(f"dbc_in_{d}", (96, TOK), F32) for d in "fb"}
    dbc_all = {d: dr(f"dbc_all_{d}", (NC * 96, TOK), F32) for d in "fb"}
    y_in = [dr(f"y_in_{g}", (4 * CSL, TOK), F32) for g in range(4)]
    y_allb = dr("y_allb", (4 * NC * 4 * CSL, TOK), F32)
    of_sp = {d: dr(f"of_{d}", (D, TOK), BF16) for d in "fb"}
    xn3_in = dr("xn3_in", (D, TOK), BF16)
    xn3_all = dr("xn3_all", (NC * D, TOK), BF16)
    pooled_in = dr("pooled_in", (BATCH * 128, 8), F32)
    pooled_out = dr("pooled_out", (BATCH * 128, 8), F32)
    x1_sp = dr("x1_sp", (D, TOK), F32)
    rs_in = [dr(f"rs_in{q}", (NC * D, MM), F32) for q in range(NMC)]
    rs_out = [dr(f"rs_out{q}", (D, MM), F32) for q in range(NMC)]
    lg_bounce = dr("lg_bounce", (BATCH, NZ), F32)

    top = ExitStack()
    sbT = lambda n, s, dt=F32: top.enter_context(nc.sbuf_tensor(n, list(s), dt))

    convw_sb = {d: sbT(f"convw_{d}", (128, 16 * 4)) for d in "fb"}
    convb_sb = {d: sbT(f"convb_{d}", (128, 16)) for d in "fb"}
    dtb_sb = {d: sbT(f"dtb_{d}", (128, NCT)) for d in "fb"}
    A_sb = {d: sbT(f"Am_{d}", (128, NCT * NST)) for d in "fb"}
    D_sb = {d: sbT(f"Dp_{d}", (128, NCT)) for d in "fb"}
    g_sb = sbT("g_sb", (128, 8))
    bcol_sb = sbT("bcol_sb", (128, 8))
    cb_sb = sbT("cb_sb", (128, 8))
    sel_sb = sbT("sel_sb", (16, NST * 128))
    ones1_sb = sbT("ones1_sb", (1, 128))
    onescol_sb = sbT("onescol_sb", (128, 1))
    ident_sb = sbT("ident_sb", (128, 128))
    edge_sb = sbT("edge_sb", (128, 2))
    bmask_sb = sbT("bmask_sb", (128, 2))
    zmask_sb = sbT("zmask_sb", (128, 8))
    cst = sbT("cst", (128, 4))
    idxu_sb = sbT("idxu_sb", (128, 2 * 2 * 4 * NCT), I32)
    idxy_sb = sbT("idxy_sb", (128, 2 * 16), I32)
    wdt_sb = {d: sbT(f"wdt_{d}", (DTR, CSL), BF16) for d in "fb"}
    wx_sb = {d: sbT(f"wx_{d}", (128, 16 * 96), BF16) for d in "fb"}
    w1_sb = sbT("w1_sb", (128, 8 * RH), BF16)
    b1_sb = sbT("b1_sb", (128, 2))
    b2_sb = sbT("b2_sb", (NZ, 1))
    w2_sb = sbT("w2_sb", (128, 2 * NZ), BF16)
    stat = sbT("stat", (128, 16))
    state = sbT("state", (128, 4 * NST * NCT))
    pooled_sb = sbT("pooled_sb", (128, 2 * 8))
    h_sb = sbT("h_sb", (128, 4))
    lgsb = sbT("lgsb", (NZ, 2))
    lgr = sbT("lgr", (1, 2 * NZ))
    rscr = sbT("rscr", (1, 4 * NZ))
    wrow = sbT("wrow", (1, 2 * NZ))
    w_bc = sbT("w_bc", (128, 2 * NZ))
    wz_col = sbT("wz_col", (128, 2))
    zscr = sbT("zscr", (128, 8))
    colsum = sbT("colsum", (128, 8))
    pooled_bf = sbT("pooled_bf", (128, 2 * 8), BF16)
    h_bf = sbT("h_bf", (128, 4), BF16)
    mid = ExitStack()
    xn2_t = mid.enter_context(nc.sbuf_tensor("xn2", [128, 8 * TOKP], BF16))

    for dst, src in [
        (convw_sb["f"][:], conv_w["f"]), (convw_sb["b"][:], conv_w["b"]),
        (convb_sb["f"][:], conv_b["f"]), (convb_sb["b"][:], conv_b["b"]),
        (dtb_sb["f"][:], dt_b["f"]), (dtb_sb["b"][:], dt_b["b"]),
        (D_sb["f"][:], D_in["f"]), (D_sb["b"][:], D_in["b"]),
        (A_sb["f"][:], A_in["f"]), (A_sb["b"][:], A_in["b"]),
        (g_sb[:], g_col), (bcol_sb[:], b_col), (cb_sb[:], cb_col),
        (sel_sb[:], sel_d), (ones1_sb[:], ones1_d), (onescol_sb[:], onescol_d),
        (ident_sb[:], ident_d), (edge_sb[:], edge_d), (bmask_sb[:], bmask_d),
        (zmask_sb[:], zmask_d), (idxu_sb[:], idxu_d), (idxy_sb[:], idxy_d),
        (cst[:], cst_d),
        (b1_sb[:], b1_d), (b2_sb[:], b2_d),
    ]:
        S.do("sp", lambda d=dst, s=src: nc.sync.dma_start(out=d, in_=s), inc="ldc", incn=16)
    for d in "fb":
        S.do("pool", lambda dd=d: nc.gpsimd.dma_start(out=wdt_sb[dd][:], in_=w_dt[dd]), inc="ldc2", incn=16)
        S.do("pool", lambda dd=d: nc.gpsimd.dma_start(out=wx_sb[dd][:], in_=w_x[dd]), inc="ldc2", incn=16)
    S.do("pool", lambda: nc.gpsimd.dma_start(out=w1_sb[:], in_=w1_d), inc="ldc2", incn=16)
    S.do("pool", lambda: nc.gpsimd.dma_start(out=w2_sb[:], in_=w2_d), inc="ldc2", incn=16)
    LDC = S.v("ldc")
    LDC2 = S.v("ldc2")
    for d in "fb":
        S.do("act", lambda dd=d: nc.scalar.activation(A_sb[dd][:], A_sb[dd][:], AF.Exp),
             waits=[("ldc", LDC)], inc="p1a")
        S.do("dve", lambda dd=d: nc.vector.tensor_scalar(
            A_sb[dd][:], A_sb[dd][:], -1.0, None, ALU.mult),
            waits=[("p1a", S.v("p1a"))], inc="p1v")
    AEXP = S.v("p1v")

    INV_D = 1.0 / D
    with ExitStack() as c1:
        sb1 = lambda n, s, dt=F32: c1.enter_context(nc.sbuf_tensor(n, list(s), dt))
        x0_tm = [sb1(f"x0tm{i}", (128, D)) for i in range(2)]
        sq_tm = [sb1(f"sqtm{i}", (128, D)) for i in range(2)]
        xn1_tm = [sb1(f"xn1tm{i}", (128, D)) for i in range(2)]
        x1ev = [sb1(f"x1ev{i}", (128, 128)) for i in range(4)]
        with nc.psum_tensor("p1ps0", [128, MM], F32) as pA, \
             nc.psum_tensor("p1ps1", [128, MM], F32) as pB:
            psb = [pA, pB]
            slot_free = [("p1tp", 0), ("p1tp", 0)]
            x1ev_free = [(f"p1q{i}", 0) for i in range(4)]
            bank_free = [[("p1ev", 0)], [("p1ev", 0)]]
            nev = 0
            for ti in range(NTT):
                pp = ti % 2
                x0t, sqt, xn1t = x0_tm[pp], sq_tm[pp], xn1_tm[pp]
                own = 1 <= ti <= NTT - 2
                S.do("sp", lambda t=ti, b=x0t: nc.sync.dma_start(
                    out=b[:], in_=x0_d[t * 128:(t + 1) * 128, :]),
                    waits=[slot_free[pp]], inc=f"p1x{pp}", incn=16)
                ld = [(f"p1x{pp}", S.v(f"p1x{pp}"))]
                S.do("act", lambda b=x0t, o=sqt: nc.scalar.activation(o[:], b[:], AF.Square),
                     waits=ld + [("p1v", S.v("p1v"))], inc="p1a")
                S.do("dve", lambda b=x0t: nc.vector.reduce_sum(stat[:, 0:1], b[:], AX.X),
                     waits=ld + [("p1a", S.v("p1a") - 1)], inc="p1v")
                S.do("dve", lambda o=sqt: nc.vector.reduce_sum(stat[:, 1:2], o[:], AX.X),
                     waits=[("p1a", S.v("p1a"))], inc="p1v")
                S.do("dve", lambda: nc.vector.tensor_scalar(stat[:, 2:3], stat[:, 0:1], INV_D, None, ALU.mult), inc="p1v")
                S.do("dve", lambda: nc.vector.tensor_scalar(stat[:, 3:4], stat[:, 1:2], INV_D, None, ALU.mult), inc="p1v")
                S.do("dve", lambda: nc.vector.tensor_tensor(stat[:, 4:5], stat[:, 2:3], stat[:, 2:3], ALU.mult), inc="p1v")
                S.do("dve", lambda: nc.vector.tensor_tensor(stat[:, 5:6], stat[:, 3:4], stat[:, 4:5], ALU.subtract), inc="p1v")
                S.do("act", lambda: nc.scalar.activation(stat[:, 6:7], stat[:, 5:6], AF.Ln, bias=cst[:, 0:1]),
                     waits=[("p1v", S.v("p1v"))], inc="p1a")
                S.do("act", lambda: nc.scalar.activation(stat[:, 7:8], stat[:, 6:7], AF.Exp, scale=cst[:, 1:2]), inc="p1a")
                S.do("dve", lambda b=x0t, o=xn1t: nc.vector.tensor_scalar(
                    o[:], b[:], stat[:, 2:3], stat[:, 7:8], ALU.subtract, ALU.mult),
                    waits=[("p1a", S.v("p1a"))], inc="p1v")
                S.do("act", lambda b=xn1t, o=sqt: nc.scalar.activation(o[:], b[:], AF.Square),
                     waits=[("p1v", S.v("p1v"))], inc="p1a")
                S.do("dve", lambda b=xn1t: nc.vector.reduce_sum(stat[:, 8:9], b[:], AX.X), inc="p1v")
                S.do("dve", lambda o=sqt: nc.vector.reduce_sum(stat[:, 9:10], o[:], AX.X),
                     waits=[("p1a", S.v("p1a"))], inc="p1v")
                S.do("dve", lambda: nc.vector.tensor_scalar(stat[:, 10:11], stat[:, 8:9], INV_D, None, ALU.mult), inc="p1v")
                S.do("dve", lambda: nc.vector.tensor_scalar(stat[:, 11:12], stat[:, 9:10], INV_D, None, ALU.mult), inc="p1v")
                S.do("dve", lambda: nc.vector.tensor_tensor(stat[:, 12:13], stat[:, 10:11], stat[:, 10:11], ALU.mult), inc="p1v")
                S.do("dve", lambda: nc.vector.tensor_tensor(stat[:, 13:14], stat[:, 11:12], stat[:, 12:13], ALU.subtract), inc="p1v")
                S.do("act", lambda: nc.scalar.activation(stat[:, 14:15], stat[:, 13:14], AF.Ln, bias=cst[:, 0:1]),
                     waits=[("p1v", S.v("p1v"))], inc="p1a")
                S.do("act", lambda: nc.scalar.activation(stat[:, 15:16], stat[:, 14:15], AF.Exp, scale=cst[:, 1:2]), inc="p1a")
                a2 = [("p1a", S.v("p1a"))]
                if own:
                    S.do("dve", lambda b=x0t, a=xn1t: nc.vector.tensor_tensor(b[:], b[:], a[:], ALU.add),
                         waits=a2, inc="p1v")
                S.do("dve", lambda a=xn1t: nc.vector.tensor_scalar(
                    a[:], a[:], stat[:, 10:11], stat[:, 15:16], ALU.subtract, ALU.mult),
                    waits=a2, inc="p1v")
                vr = [("p1v", S.v("p1v"))]
                for dt_ in range(8):
                    bk = dt_ % 2
                    pb = psb[bk]
                    S.do("pe", lambda a=xn1t, p=pb, d=dt_: nc.tensor.transpose(
                        p[:, 0:128], a[:, d * 128:(d + 1) * 128], ident_sb[:]),
                        waits=[("ldc", LDC)] + vr + bank_free[bk], inc="p1tp")
                    S.do("dve", lambda p=pb, d=dt_, t=ti: nc.vector.tensor_scalar(
                        xn2_t[:, d * TOKP + t * 128: d * TOKP + (t + 1) * 128], p[:, 0:128],
                        g_sb[:, d:d + 1], bcol_sb[:, d:d + 1], ALU.mult, ALU.add),
                        waits=[("p1tp", S.v("p1tp"))], inc="p1ev")
                    bank_free[bk] = [("p1ev", S.v("p1ev"))]
                    if own:
                        ev = x1ev[nev % 4]
                        S.do("pe", lambda b=x0t, p=pb, d=dt_: nc.tensor.transpose(
                            p[:, 128:256], b[:, d * 128:(d + 1) * 128], ident_sb[:]),
                            waits=[("p1ev", S.v("p1ev"))] + bank_free[bk][1:], inc="p1tp")
                        S.do("dve", lambda p=pb, e=ev: nc.vector.tensor_copy(e[:], p[:, 128:256]),
                             waits=[("p1tp", S.v("p1tp")), x1ev_free[nev % 4]], inc="p1ev2")
                        bank_free[bk] = [("p1ev", S.v("p1ev")), ("p1ev2", S.v("p1ev2"))]
                        S.do("sp", lambda e=ev, d=dt_, t=ti: nc.sync.dma_start(
                            out=x1_sp.ap()[d * 128:(d + 1) * 128, (t - 1) * 128:t * 128], in_=e[:]),
                            waits=[("p1ev2", S.v("p1ev2"))], inc=f"p1q{nev % 4}", incn=16)
                        x1ev_free[nev % 4] = (f"p1q{nev % 4}", S.v(f"p1q{nev % 4}"))
                        nev += 1
                slot_free[pp] = ("p1tp", S.v("p1tp"))
    LNEV = S.v("p1ev")
    X1Q = [(f"p1q{i}", S.v(f"p1q{i}")) for i in range(4)]
    P1TPF = S.v("p1tp")
    P1GATE = [("p1tp", P1TPF), ("p1ev", LNEV), ("p1ev2", S.v("p1ev2"))] + X1Q

    if debug:
        for dt_ in range(8):
            S.do("pool", lambda d=dt_: nc.gpsimd.dma_start(
                out=dbg["xn2"][d * 128:(d + 1) * 128, :], in_=xn2_t[:, d * TOKP:(d + 1) * TOKP]),
                waits=[("p1ev", LNEV)], inc="dbgs", incn=16)

    S.mark("p1")
    with ExitStack() as c2:
        sb2 = lambda n, s, dt=F32: c2.enter_context(nc.sbuf_tensor(n, list(s), dt))
        w_ct = [sb2(f"wct{i}", (128, 8 * 128), BF16) for i in range(2)]
        xipre = [sb2(f"xipre{i}", (128, TOKP)) for i in range(2)]
        xi_f32 = [sb2(f"xif{i}", (128, TOK)) for i in range(2)]
        xi_bf = sb2("xibf", (128, 16 * TOK), BF16)
        dbc_ev = sb2("dbcev", (80, TOK))
        dbc_ev2 = sb2("dbcev2", (16, TOK))
        with nc.psum_tensor("feps0", [128, MM], F32) as f0, \
             nc.psum_tensor("feps1", [128, MM], F32) as f1, \
             nc.psum_tensor("feps2", [128, MM], F32) as f2, \
             nc.psum_tensor("feps3", [128, MM], F32) as f3:
            col_chunks = []
            c0 = 0
            while c0 < TOKP:
                cw = min(MM, TOKP - c0)
                col_chunks.append((c0, cw))
                c0 += cw
            xi_guard = ("fxp", 0)
            dbc_guard = ("fxd", 0)
            wslot = [("fei", 0), ("fei", 0)]
            xpre_slot = [("fec", 0), ("fec", 0)]
            xf_slot = [(("feu0", 0), ("feb", 0)), (("feu1", 0), ("feb", 0))]
            for dir_ in "fb":
                for ct in range(16):
                    pp = ct % 2
                    wt, xp, xf = w_ct[pp], xipre[pp], xi_f32[pp]
                    S.do("pool", lambda d=dir_, c=ct, w=wt: nc.gpsimd.dma_start(
                        out=w[:].rearrange("p (kt c) -> p kt c", kt=8),
                        in_=w_in_xi[d][:, c * 128:(c + 1) * 128]
                        .rearrange("(kt p) c -> p kt c", p=128)),
                        waits=[wslot[pp]] + (P1GATE if (dir_ == "f" and ct == 0) else []),
                        inc=f"few{pp}", incn=16)
                    S.mark(f"fw_{dir_}_{ct}")
                    wr = (f"few{pp}", S.v(f"few{pp}"))
                    for ci, (cc0, cw) in enumerate(col_chunks):
                        pb = [f0, f1][(ct * len(col_chunks) + ci) % 2]
                        for k in range(8):
                            S.do("pe", lambda kk=k, o=cc0, w=cw, p=pb, wt_=wt: nc.tensor.matmul(
                                p[:, 0:w], wt_[:, kk * 128:(kk + 1) * 128],
                                xn2_t[:, kk * TOKP + o: kk * TOKP + o + w],
                                start=(kk == 0), stop=(kk == 7)),
                                waits=([wr, ("p1ev", LNEV),
                                        ("fee", max(0, S.v("fee") - 1))] if k == 0 else []),
                                inc=("fei" if k == 7 else None))
                        S.mark(f"fmm_{dir_}_{ct}_{ci}")
                        S.do("act", lambda o=cc0, w=cw, p=pb, t=xp: nc.scalar.activation(
                            t[:, o:o + w], p[:, 0:w], AF.Copy),
                            waits=[("fei", S.v("fei")), xpre_slot[pp]], inc="fee")
                        S.mark(f"fm_{dir_}_{ct}_{ci}")
                    ir = [("fee", S.v("fee")), ("p1v", AEXP)]
                    if dir_ == "f":
                        S.do("dve", lambda t=xp: nc.vector.tensor_scalar(
                            t[:, PAD - 3:PAD], t[:, PAD - 3:PAD], edge_sb[:, 0:1], None, ALU.mult),
                            waits=ir, inc="fec")
                    else:
                        S.do("dve", lambda t=xp: nc.vector.tensor_scalar(
                            t[:, PAD + TOK:PAD + TOK + 3], t[:, PAD + TOK:PAD + TOK + 3],
                            edge_sb[:, 1:2], None, ALU.mult),
                            waits=ir, inc="fec")
                    offs = [PAD - 3, PAD - 2, PAD - 1, PAD] if dir_ == "f" else [PAD, PAD + 1, PAD + 2, PAD + 3]
                    wsel = [0, 1, 2, 3] if dir_ == "f" else [3, 2, 1, 0]
                    S.do("dve", lambda t=xp, o=xf, c=ct, d=dir_, oj=offs[0], wj=wsel[0]: nc.vector.tensor_scalar(
                        o[:], t[:, oj:oj + TOK], convw_sb[d][:, c * 4 + wj:c * 4 + wj + 1], None, ALU.mult),
                        waits=[xf_slot[pp][0]], inc="fec")
                    for j in range(1, 4):
                        S.do("dve", lambda t=xp, o=xf, c=ct, d=dir_, oj=offs[j], wj=wsel[j]: nc.vector.scalar_tensor_tensor(
                            o[:], t[:, oj:oj + TOK], convw_sb[d][:, c * 4 + wj:c * 4 + wj + 1], o[:],
                            ALU.mult, ALU.add), inc="fec")
                    xpre_slot[pp] = ("fec", S.v("fec"))
                    S.mark(f"fc_{dir_}_{ct}")
                    S.do("act", lambda o=xf, c=ct, d=dir_: nc.scalar.activation(
                        o[:], o[:], AF.Silu, bias=convb_sb[d][:, c:c + 1]),
                        waits=[("fec", S.v("fec"))], inc="fes")
                    sr = S.v("fes")
                    S.mark(f"fs_{dir_}_{ct}")
                    S.do("sp", lambda o=xf, c=ct, d=dir_: nc.sync.dma_start(
                        out=u_in[d].ap()[c * 128:(c + 1) * 128, :], in_=o[:]),
                        waits=[("fes", sr)], inc=f"feu{pp}", incn=16)
                    S.do("dve", lambda o=xf, c=ct: nc.vector.tensor_copy(
                        xi_bf[:, c * TOK:(c + 1) * TOK], o[:]),
                        waits=[("fes", sr), xi_guard], inc="feb")
                    wslot[pp] = ("fei", S.v("fei"))
                    xf_slot[pp] = ((f"feu{pp}", S.v(f"feu{pp}")), ("feb", S.v("feb")))
                    S.mark(f"fe_{dir_}_ct{ct}")
                BF = S.v("feb")
                S.mark(f"fe_ct_{dir_}")
                for mi in range(NMC):
                    o = mi * MM
                    for c in range(16):
                        S.do("pe", lambda cc=c, oo=o, d=dir_: nc.tensor.matmul(
                            f2[0:80, 0:MM], wx_sb[d][:, cc * 96: cc * 96 + 80],
                            xi_bf[:, cc * TOK + oo: cc * TOK + oo + MM],
                            start=(cc == 0), stop=(cc == 15)),
                            waits=([("feb", BF), ("ldc2", LDC2),
                                    ("fxe", S.v("fxe"))] if c == 0 else []),
                            inc=("fxp" if c == 15 else None))
                    S.do("act", lambda oo=o: nc.scalar.activation(
                        dbc_ev[:, oo:oo + MM], f2[0:80, 0:MM], AF.Copy),
                        waits=[("fxp", S.v("fxp")), dbc_guard], inc="fxe")
                    for c in range(16):
                        S.do("pe", lambda cc=c, oo=o, d=dir_: nc.tensor.matmul(
                            f3[0:16, 0:MM], wx_sb[d][:, cc * 96 + 80: cc * 96 + 96],
                            xi_bf[:, cc * TOK + oo: cc * TOK + oo + MM],
                            start=(cc == 0), stop=(cc == 15)),
                            waits=([("fxe2", S.v("fxe2"))] if c == 0 else []),
                            inc=("fxp" if c == 15 else None))
                    S.do("act", lambda oo=o: nc.scalar.activation(
                        dbc_ev2[0:16, oo:oo + MM], f3[0:16, 0:MM], AF.Copy),
                        waits=[("fxp", S.v("fxp")), dbc_guard], inc="fxe2")
                xi_guard = ("fxp", S.v("fxp"))
                S.mark(f"fe_xp_{dir_}")
                S.do("sp", lambda d=dir_: nc.sync.dma_start(
                    out=dbc_in[d].ap()[0:80, :], in_=dbc_ev[:]),
                     waits=[("fxe", S.v("fxe")), ("fxe2", S.v("fxe2"))], inc="fxd", incn=16)
                S.do("sp", lambda d=dir_: nc.sync.dma_start(
                    out=dbc_in[d].ap()[80:96, :], in_=dbc_ev2[:]),
                     inc="fxd", incn=16)
                dbc_guard = ("fxd", S.v("fxd"))
                S.do("pool", lambda d=dir_: nc.gpsimd.collective_compute(
                    "AllGather", ALU.bypass, replica_groups=RG,
                    ins=[u_in[d].ap()], outs=[u_all[d].ap()]),
                    waits=[("feu0", S.v("feu0")), ("feu1", S.v("feu1"))], inc=f"agu{dir_}", cc=True)
                S.do("pool", lambda d=dir_: nc.gpsimd.collective_compute(
                    "AllGather", ALU.bypass, replica_groups=RG,
                    ins=[dbc_in[d].ap()], outs=[dbc_all[d].ap()]),
                    waits=[("fxd", S.v("fxd"))], inc=f"agd{dir_}", cc=True)
                S.mark(f"fe_ag_{dir_}")

    S.mark("fe")
    with ExitStack() as c3:
        sb3 = lambda n, s, dt=F32: c3.enter_context(nc.sbuf_tensor(n, list(s), dt))
        u_sc = sb3("u_sc", (128, NCT * TOK))
        delta = sb3("delta", (128, NCT * TOK))
        du = sb3("du", (128, NCT * TOK))
        spt = sb3("spt", (128, NCT * TOK))
        y_sc = sb3("y_sc", (128, NCT * TOK))
        dt_sb = sb3("dt_sb", (DTR, TOK), BF16)
        Brow = sb3("Brow", (16, TOK))
        Crow = sb3("Crow", (16, TOK))
        dA_t = sb3("dA_t", (128, SC))
        b_t = sb3("b_t", (128, SC))
        h_t = sb3("h_t", (128, SC))
        cm_t = sb3("cm_t", (128, SC))
        with nc.psum_tensor("scps0", [128, MM], F32) as s0, \
             nc.psum_tensor("scps1", [128, MM], F32) as s1, \
             nc.psum_tensor("scpB", [128, SC], F32) as psB, \
             nc.psum_tensor("scpC", [128, SC], F32) as psC:
            S.do("dve", lambda: nc.vector.memset(state[:], 0.0),
                 waits=[("ldc", LDC)], inc="scv")
            prev_consumed = ("scv", 0)
            prev_bc = ("scp", 0)
            for g in range(4):
                for jb, (dir_, b) in enumerate(JOBS):
                    gi = g if dir_ == "f" else 3 - g
                    p = 4 * b + gi
                    dA_gate = ("sce2", S.v("sce2"))
                    fe_gate = ([("aguf", 1), ("agub", 1), ("agdf", 1), ("agdb", 1)]
                               if (g == 0 and jb == 0) else [])
                    for ct in range(NCT):
                        icol = ((0 if dir_ == "f" else 1) * 2 + b) * 4 * NCT + gi * NCT + ct
                        S.do("pool", lambda c=ct, d=dir_, ic=icol: nc.gpsimd.indirect_dma_start(
                            u_sc[:, c * TOK:(c + 1) * TOK], None, u_all[d].ap(),
                            IndirectOffsetOnAxis(idxu_sb[:, ic:ic + 1], 0)),
                            waits=[(f"agu{d}", 1), ("ldc", LDC), prev_consumed, dA_gate] + fe_gate,
                            inc="sclP", incn=16)
                    S.do("pool", lambda d=dir_, pp_=p: nc.gpsimd.dma_start(
                        out=dt_sb[:], in_=dbc_all[d].ap()[96 * pp_: 96 * pp_ + 64, :]),
                        waits=[(f"agd{d}", 1), prev_bc] + fe_gate, inc="sclP", incn=16)
                    S.do("sp", lambda d=dir_, pp_=p: nc.sync.dma_start(
                        out=Brow[:], in_=dbc_all[d].ap()[96 * pp_ + 64: 96 * pp_ + 80, :]),
                        waits=[(f"agd{d}", 1), prev_bc] + fe_gate, inc="sclS", incn=16)
                    S.do("sp", lambda d=dir_, pp_=p: nc.sync.dma_start(
                        out=Crow[:], in_=dbc_all[d].ap()[96 * pp_ + 80: 96 * pp_ + 96, :]),
                        waits=[prev_bc] + fe_gate, inc="sclS", incn=16)
                    LD = None
                    LDP = S.v("sclP")
                    LDS = S.v("sclS")
                    for ct in range(NCT):
                        for mi in range(TOK // MM):
                            o = mi * MM
                            pb = [s0, s1][(ct * (TOK // MM) + mi) % 2]
                            S.do("pe", lambda c=ct, oo=o, d=dir_, p_=pb: nc.tensor.matmul(
                                p_[:, 0:MM], wdt_sb[d][:, c * 128:(c + 1) * 128],
                                dt_sb[:, oo:oo + MM], start=True, stop=True),
                                waits=[("sclP", LDP), ("sce", max(0, S.v("sce") - 2))],
                                inc="scp")
                            S.do("act", lambda c=ct, oo=o, d=dir_, p_=pb: nc.scalar.activation(
                                spt[:, c * TOK + oo: c * TOK + oo + MM], p_[:, 0:MM],
                                AF.Abs, bias=dtb_sb[d][:, c:c + 1]),
                                waits=[("scp", S.v("scp")), prev_consumed], inc="sce")
                            S.do("act", lambda c=ct, oo=o, d=dir_, p_=pb: nc.scalar.activation(
                                delta[:, c * TOK + oo: c * TOK + oo + MM], p_[:, 0:MM],
                                AF.Relu, bias=dtb_sb[d][:, c:c + 1]), inc="sce")
                    SPE = S.v("sce")
                    for ct in range(NCT):
                        S.do("act", lambda c=ct: nc.scalar.activation(
                            spt[:, c * TOK:(c + 1) * TOK], spt[:, c * TOK:(c + 1) * TOK],
                            AF.Exp, scale=cst[:, 2:3]), waits=[("sce", SPE)], inc="sce")
                        S.do("act", lambda c=ct: nc.scalar.activation(
                            spt[:, c * TOK:(c + 1) * TOK], spt[:, c * TOK:(c + 1) * TOK],
                            AF.Ln, bias=cst[:, 3:4]), inc="sce")
                        S.do("dve", lambda c=ct: nc.vector.tensor_tensor(
                            delta[:, c * TOK:(c + 1) * TOK], delta[:, c * TOK:(c + 1) * TOK],
                            spt[:, c * TOK:(c + 1) * TOK], ALU.add),
                            waits=[("sce", S.v("sce"))], inc="scv")
                        S.do("dve", lambda c=ct: nc.vector.tensor_tensor(
                            du[:, c * TOK:(c + 1) * TOK], delta[:, c * TOK:(c + 1) * TOK],
                            u_sc[:, c * TOK:(c + 1) * TOK], ALU.mult),
                            waits=[("sclP", LDP)], inc="scv")
                        S.do("dve", lambda c=ct, d=dir_: nc.vector.tensor_scalar(
                            y_sc[:, c * TOK:(c + 1) * TOK], u_sc[:, c * TOK:(c + 1) * TOK],
                            D_sb[d][:, c:c + 1], None, ALU.mult),
                            waits=[("scy", S.v("scy"))], inc="scv")
                    DLT = S.v("scv")
                    subs = list(range(NSUB)) if dir_ == "f" else list(range(NSUB - 1, -1, -1))
                    for sub in subs:
                        so = sub * SC
                        for n in range(NST):
                            for q in range(SC // MM):
                                S.do("pe", lambda nn=n, qq=q, oo=so: nc.tensor.matmul(
                                    psB[:, qq * MM:(qq + 1) * MM],
                                    sel_sb[:, nn * 128:(nn + 1) * 128],
                                    Brow[:, oo + qq * MM: oo + (qq + 1) * MM],
                                    start=True, stop=True),
                                    waits=[("sclS", LDS), ("scv2", max(0, S.v("scv2") - 4))],
                                    inc="scp")
                                S.do("pe", lambda nn=n, qq=q, oo=so: nc.tensor.matmul(
                                    psC[:, qq * MM:(qq + 1) * MM],
                                    sel_sb[:, nn * 128:(nn + 1) * 128],
                                    Crow[:, oo + qq * MM: oo + (qq + 1) * MM],
                                    start=True, stop=True),
                                    waits=[("scv2", max(0, S.v("scv2") - 1))],
                                    inc="scp")
                            BC = S.v("scp")
                            for ct in range(NCT):
                                sidx = (jb * NST + n) * NCT + ct
                                S.do("act", lambda c=ct, nn=n, d=dir_, oo=so: nc.scalar.activation(
                                    dA_t[:], delta[:, c * TOK + oo: c * TOK + oo + SC],
                                    AF.Exp, scale=A_sb[d][:, c * NST + nn:c * NST + nn + 1]),
                                    waits=[("scv", DLT), ("scv2", max(0, S.v("scv2") - 3))],
                                    inc="sce2")
                                S.do("dve", lambda c=ct, oo=so: nc.vector.tensor_tensor(
                                    b_t[:], du[:, c * TOK + oo: c * TOK + oo + SC],
                                    psB[:], ALU.mult),
                                    waits=[("scp", BC), ("sce2", S.v("sce2"))], inc="scv2")
                                if dir_ == "f":
                                    S.do("dve", lambda s_=sidx: nc.vector.tensor_tensor_scan(
                                        h_t[:], dA_t[:], b_t[:], state[:, s_:s_ + 1],
                                        ALU.mult, ALU.add), inc="scv2")
                                    S.do("dve", lambda s_=sidx: nc.vector.tensor_copy(
                                        state[:, s_:s_ + 1], h_t[:, SC - 1:SC]), inc="scv2")
                                else:
                                    S.do("dve", lambda s_=sidx: nc.vector.tensor_tensor_scan(
                                        h_t[:, ::-1], dA_t[:, ::-1], b_t[:, ::-1],
                                        state[:, s_:s_ + 1], ALU.mult, ALU.add), inc="scv2")
                                    S.do("dve", lambda s_=sidx: nc.vector.tensor_copy(
                                        state[:, s_:s_ + 1], h_t[:, 0:1]), inc="scv2")
                                S.do("dve", lambda: nc.vector.tensor_tensor(
                                    cm_t[:], h_t[:], psC[:], ALU.mult), inc="scv2")
                                S.do("dve", lambda c=ct, oo=so: nc.vector.tensor_tensor(
                                    y_sc[:, c * TOK + oo: c * TOK + oo + SC],
                                    y_sc[:, c * TOK + oo: c * TOK + oo + SC],
                                    cm_t[:], ALU.add), inc="scv2")
                    YV = S.v("scv2")
                    prev_consumed = ("scv2", YV)
                    prev_bc = ("scp", S.v("scp"))
                    for ct in range(NCT):
                        S.do("sp", lambda c=ct, g_=g, j_=jb: nc.sync.dma_start(
                            out=y_in[g_].ap()[j_ * CSL + c * 128: j_ * CSL + (c + 1) * 128, :],
                            in_=y_sc[:, c * TOK:(c + 1) * TOK]),
                            waits=[("scv2", YV)], inc="scy", incn=16)
                S.do("pool", lambda g_=g: nc.gpsimd.collective_compute(
                    "AllGather", ALU.bypass, replica_groups=RG,
                    ins=[y_in[g_].ap()],
                    outs=[y_allb.ap()[g_ * NC * 4 * CSL:(g_ + 1) * NC * 4 * CSL, :]]),
                    waits=[("scy", S.v("scy"))], inc=f"agy{g}", cc=True)

    S.mark("scan")
    YGATE = [(f"agy{g}", 1) for g in range(4)]

    with ExitStack() as c4:
        sb4 = lambda n, s, dt=F32: c4.enter_context(nc.sbuf_tensor(n, list(s), dt))
        w_st = [sb4(f"bwt{i}", (128, 8 * 128), BF16) for i in range(2)]
        sz_t = [sb4(f"szt{i}", (128, TOK)) for i in range(2)]
        ytl = [sb4(f"ytl{i}", (128, TOK)) for i in range(2)]
        yg = sb4("yg", (128, 16 * TOK), BF16)
        wko = sb4("wko", (128, 16 * D), BF16)
        ofev = [sb4(f"ofev{i}", (128, MM), BF16) for i in range(2)]
        with nc.psum_tensor("beps0", [128, MM], F32) as e0, \
             nc.psum_tensor("beps1", [128, MM], F32) as e1, \
             nc.psum_tensor("beps2", [128, MM], F32) as e2, \
             nc.psum_tensor("beps3", [128, MM], F32) as e3:
            wslot = [("bzp", 0), ("bzp", 0)]
            szslot = [("bgt", 0), ("bgt", 0)]
            ytslot = [("bgt", 0), ("bgt", 0)]
            for dir_ in "fb":
                for ct in range(16):
                    pp = ct % 2
                    wt, sz, yt = w_st[pp], sz_t[pp], ytl[pp]
                    S.do("pool", lambda d=dir_, c=ct, w=wt: nc.gpsimd.dma_start(
                        out=w[:].rearrange("p (kt c) -> p kt c", kt=8),
                        in_=w_in_z[d][:, c * 128:(c + 1) * 128]
                        .rearrange("(kt p) c -> p kt c", p=128)),
                        waits=[wslot[pp]] + YGATE, inc=f"bzw{pp}", incn=16)
                    wr = (f"bzw{pp}", S.v(f"bzw{pp}"))
                    S.do("pool", lambda d=dir_, c=ct, yt_=yt: nc.gpsimd.indirect_dma_start(
                        yt_[:], None, y_allb.ap(),
                        IndirectOffsetOnAxis(
                            idxy_sb[:, (0 if d == "f" else 1) * 16 + c:
                                    (0 if d == "f" else 1) * 16 + c + 1], 0)),
                        waits=YGATE + [ytslot[pp]], inc=f"bzl{pp}", incn=16)
                    for mi in range(NMC):
                        o = mi * MM
                        pb = [e0, e1][(ct * NMC + mi) % 2]
                        for k in range(8):
                            S.do("pe", lambda kk=k, oo=o, p_=pb, w=wt: nc.tensor.matmul(
                                p_[:, 0:MM], w[:, kk * 128:(kk + 1) * 128],
                                xn2_t[:, kk * TOKP + PAD + oo: kk * TOKP + PAD + oo + MM],
                                start=(kk == 0), stop=(kk == 7)),
                                waits=([wr, ("bze", max(0, S.v("bze") - 1))]
                                       if k == 0 else []),
                                inc=("bzp" if k == 7 else None))
                        S.do("act", lambda oo=o, p_=pb, s_=sz: nc.scalar.activation(
                            s_[:, oo:oo + MM], p_[:, 0:MM], AF.Silu),
                            waits=[("bzp", S.v("bzp"))] + ([szslot[pp]] if mi == 0 else []),
                            inc="bze")
                    S.do("dve", lambda c=ct, s_=sz, yt_=yt: nc.vector.tensor_tensor(
                        yg[:, c * TOK:(c + 1) * TOK], yt_[:], s_[:], ALU.mult),
                        waits=[("bze", S.v("bze")), (f"bzl{pp}", S.v(f"bzl{pp}")),
                               ("bop", S.v("bop"))], inc="bgt")
                    wslot[pp] = ("bzp", S.v("bzp"))
                    szslot[pp] = ("bgt", S.v("bgt"))
                    ytslot[pp] = ("bgt", S.v("bgt"))
                BGT = S.v("bgt")
                S.do("pool", lambda d=dir_: nc.gpsimd.dma_start(out=wko[:], in_=w_out[d]),
                     waits=[("bop", S.v("bop"))], inc="ffw", incn=16)
                BOW = S.v("ffw")
                for mi in range(NMC):
                    o = mi * MM
                    for dm in range(8):
                        pb = [e2, e3][dm % 2]
                        for k in range(16):
                            S.do("pe", lambda kk=k, oo=o, p_=pb, d_=dm: nc.tensor.matmul(
                                p_[:, 0:MM], wko[:, kk * D + d_ * 128: kk * D + (d_ + 1) * 128],
                                yg[:, kk * TOK + oo: kk * TOK + oo + MM],
                                start=(kk == 0), stop=(kk == 15)),
                                waits=([("bgt", BGT), ("ffw", BOW),
                                        ("boe", max(0, S.v("boe") - 1))] if k == 0 else []),
                                inc=("bop" if k == 15 else None))
                        osl = (mi * 8 + dm) % 2
                        oe = ofev[osl]
                        S.do("dve", lambda p_=pb, e_=oe: nc.vector.tensor_copy(e_[:], p_[:, 0:MM]),
                             waits=[("bop", S.v("bop")), (f"bod{osl}", S.v(f"bod{osl}"))],
                             inc="boe")
                        S.do("sp", lambda oo=o, d_=dm, dd=dir_, e_=oe: nc.sync.dma_start(
                            out=of_sp[dd].ap()[d_ * 128:(d_ + 1) * 128, oo:oo + MM], in_=e_[:]),
                            waits=[("boe", S.v("boe"))], inc=f"bod{osl}", incn=16)
    OFD2 = [("bod0", S.v("bod0")), ("bod1", S.v("bod1"))]
    mid.close()
    with ExitStack() as c4b:
        sb4b = lambda n, s, dt=F32: c4b.enter_context(nc.sbuf_tensor(n, list(s), dt))
        wkoc = sb4b("wko2", (128, 16 * D), BF16)
        ofr = [sb4b(f"ofr{i}", (128, MM), BF16) for i in range(4)]
        x1c = sb4b("x1c", (128, 8 * MM))
        sqc = sb4b("sqc", (128, MM))
        st3 = sb4b("st3", (1, 4 * MM))
        xn3f = sb4b("xn3f", (128, MM))
        xn3b = [sb4b(f"xn3b{i}", (128, MM), BF16) for i in range(2)]
        with nc.psum_tensor("cbps0", [128, MM], F32) as e0, \
             nc.psum_tensor("cbps1", [128, MM], F32) as e1, \
             nc.psum_tensor("cbps2", [128, MM], F32) as e2, \
             nc.psum_tensor("cbps3", [128, MM], F32) as e3:
            S.do("pool", lambda: nc.gpsimd.dma_start(out=wkoc[:], in_=w_comb),
                 waits=[("bop", S.v("bop")), ("bgt", S.v("bgt")), ("bze", S.v("bze")),
                        ("boe", S.v("boe"))] + OFD2 +
                       ([("dbgs", S.v("dbgs"))] if debug else []),
                 inc="ffw", incn=16)
            CBW = S.v("ffw")
            S.do("dve", lambda: nc.vector.memset(colsum[:], 0.0), inc="cls")
            stat_gate = ("l3p", 0)
            x1_guard = ("x1d", 0)
            for mi in range(NMC):
                o = mi * MM
                S.do("sp", lambda oo=o: nc.sync.dma_start(
                    out=x1c[:].rearrange("p (dm c) -> p dm c", dm=8),
                    in_=x1_sp.ap()[:, oo:oo + MM].rearrange("(dm p) c -> p dm c", p=128)),
                    waits=X1Q + [("cbe", max(0, S.v("cbe") - 8)), x1_guard],
                    inc="cbx", incn=16)
                CBX = S.v("cbx")
                for dm in range(8):
                    pb = [e0, e1][dm % 2]
                    for k in range(16):
                        dd = "f" if k < 8 else "b"
                        kk = k % 8
                        sl = k % 4
                        buf = ofr[sl]
                        S.do("sp", lambda d_=dd, k_=kk, oo=o, b_=buf, sl_=sl: nc.sync.dma_start(
                            out=b_[:], in_=of_sp[d_].ap()[k_ * 128:(k_ + 1) * 128, oo:oo + MM]),
                            waits=OFD2 + [("cbp", max(0, S.v("cbp") - 3)),
                                          (f"cbl{sl}", S.v(f"cbl{sl}"))],
                            inc=f"cbl{sl}", incn=16)
                        S.do("pe", lambda k_=k, oo=o, p_=pb, d_=dm, b_=buf: nc.tensor.matmul(
                            p_[:, 0:MM], wkoc[:, k_ * D + d_ * 128: k_ * D + (d_ + 1) * 128],
                            b_[:], start=(k_ == 0), stop=(k_ == 15)),
                            waits=[(f"cbl{sl}", S.v(f"cbl{sl}"))] + ([("ffw", CBW),
                                   ("cbe", max(0, S.v("cbe") - 1))] if k == 0 else []),
                            inc="cbp")
                    S.do("dve", lambda p_=pb, d_=dm: nc.vector.scalar_tensor_tensor(
                        x1c[:, d_ * MM:(d_ + 1) * MM], p_[:, 0:MM], cb_sb[:, d_:d_ + 1],
                        x1c[:, d_ * MM:(d_ + 1) * MM], ALU.add, ALU.add),
                        waits=[("cbp", S.v("cbp")), ("cbx", CBX)], inc="cbe")
                X1R = S.v("cbe")
                for dm in range(8):
                    S.do("act", lambda d_=dm: nc.scalar.activation(
                        sqc[:], x1c[:, d_ * MM:(d_ + 1) * MM], AF.Square),
                        waits=[("cbe", X1R), ("l3q", S.v("l3q"))], inc="l3a")
                    S.do("pe", lambda d_=dm: nc.tensor.matmul(
                        e2[0:1, 0:MM], onescol_sb[:], x1c[:, d_ * MM:(d_ + 1) * MM],
                        start=(d_ == 0), stop=(d_ == 7)),
                        waits=([("cbe", X1R), stat_gate] if dm == 0 else []),
                        inc="l3p" if dm == 7 else None)
                    S.do("pe", lambda d_=dm: nc.tensor.matmul(
                        e3[0:1, 0:MM], onescol_sb[:], sqc[:],
                        start=(d_ == 0), stop=(d_ == 7)),
                        waits=[("l3a", S.v("l3a"))],
                        inc="l3q")
                L3P = [("l3p", S.v("l3p")), ("l3q", S.v("l3q"))]
                S.do("dve", lambda: nc.vector.tensor_scalar(st3[0:1, 0:MM], e2[0:1, 0:MM], INV_D, None, ALU.mult),
                     waits=L3P + [("l3p2", S.v("l3p2"))], inc="l3v")
                S.do("dve", lambda: nc.vector.tensor_scalar(st3[0:1, MM:2 * MM], e3[0:1, 0:MM], INV_D, None, ALU.mult), inc="l3v")
                S.do("dve", lambda: nc.vector.tensor_tensor(st3[0:1, 2 * MM:3 * MM], st3[0:1, 0:MM], st3[0:1, 0:MM], ALU.mult), inc="l3v")
                S.do("dve", lambda: nc.vector.tensor_tensor(st3[0:1, MM:2 * MM], st3[0:1, MM:2 * MM], st3[0:1, 2 * MM:3 * MM], ALU.subtract), inc="l3v")
                S.do("act", lambda: nc.scalar.activation(st3[0:1, MM:2 * MM], st3[0:1, MM:2 * MM], AF.Ln, bias=cst[0:1, 0:1]),
                     waits=[("l3v", S.v("l3v"))], inc="l3a")
                S.do("act", lambda: nc.scalar.activation(st3[0:1, MM:2 * MM], st3[0:1, MM:2 * MM], AF.Exp, scale=cst[0:1, 1:2]), inc="l3a")
                S.do("pe", lambda: nc.tensor.matmul(e2[:, 0:MM], ones1_sb[:], st3[0:1, 0:MM],
                                                    start=True, stop=True),
                     waits=[("l3a", S.v("l3a")), ("l3v", S.v("l3v"))], inc="l3p2")
                S.do("pe", lambda: nc.tensor.matmul(e3[:, 0:MM], ones1_sb[:], st3[0:1, MM:2 * MM],
                                                    start=True, stop=True), inc="l3p2")
                L3B = S.v("l3p2")
                for dm in range(8):
                    xsl = dm % 2
                    xb = xn3b[xsl]
                    S.do("dve", lambda d_=dm: nc.vector.tensor_tensor(
                        xn3f[:], x1c[:, d_ * MM:(d_ + 1) * MM], e2[:, 0:MM], ALU.subtract),
                        waits=[("l3p2", L3B)], inc="l3v")
                    S.do("dve", lambda: nc.vector.tensor_tensor(xn3f[:], xn3f[:], e3[:, 0:MM], ALU.mult), inc="l3v")
                    S.do("dve", lambda d_=dm: nc.vector.tensor_tensor(
                        x1c[:, d_ * MM:(d_ + 1) * MM], x1c[:, d_ * MM:(d_ + 1) * MM], xn3f[:], ALU.add), inc="l3v")
                    S.do("dve", lambda xb_=xb: nc.vector.tensor_copy(xb_[:], xn3f[:]),
                         waits=[(f"l3d{xsl}", S.v(f"l3d{xsl}"))], inc="l3v")
                    S.do("dve", lambda: nc.vector.reduce_sum(stat[:, 0:1], xn3f[:], AX.X), inc="l3v")
                    S.do("dve", lambda d_=dm: nc.vector.tensor_tensor(
                        colsum[:, d_:d_ + 1], colsum[:, d_:d_ + 1], stat[:, 0:1], ALU.add),
                        waits=[("cls", S.v("cls"))], inc="cls")
                    V3 = S.v("l3v")
                    S.do("sp", lambda oo=o, d_=dm, xb_=xb: nc.sync.dma_start(
                        out=xn3_in.ap()[d_ * 128:(d_ + 1) * 128, oo:oo + MM], in_=xb_[:]),
                        waits=[("l3v", V3)], inc=f"l3d{xsl}", incn=16)
                    S.do("sp", lambda oo=o, d_=dm: nc.sync.dma_start(
                        out=x1_sp.ap()[d_ * 128:(d_ + 1) * 128, oo:oo + MM],
                        in_=x1c[:, d_ * MM:(d_ + 1) * MM]),
                        waits=[("l3v", V3)], inc="x1d", incn=16)
                stat_gate = ("l3v", S.v("l3v"))
                x1_guard = ("x1d", S.v("x1d"))
            S.do("dve", lambda: nc.vector.tensor_scalar(
                colsum[:], colsum[:], 1.0 / L, None, ALU.mult),
                waits=[("cls", S.v("cls"))], inc="cls")
            for bb in range(2):
                S.do("dve", lambda b_=bb: nc.vector.tensor_scalar(
                    pooled_sb[:, b_ * 8:(b_ + 1) * 8], colsum[:], bmask_sb[:, b_:b_ + 1], None, ALU.mult),
                    waits=[("cls", S.v("cls"))], inc="cls")
            S.do("sp", lambda: nc.sync.dma_start(out=pooled_in.ap()[0:128, :], in_=pooled_sb[:, 0:8]),
                 waits=[("cls", S.v("cls"))], inc="rtl", incn=16)
            S.do("sp", lambda: nc.sync.dma_start(out=pooled_in.ap()[128:256, :], in_=pooled_sb[:, 8:16]),
                 inc="rtl", incn=16)
            S.do("pool", lambda: nc.gpsimd.collective_compute(
                "AllGather", ALU.bypass, replica_groups=RG,
                ins=[xn3_in.ap()], outs=[xn3_all.ap()]),
                waits=[("l3d0", S.v("l3d0")), ("l3d1", S.v("l3d1"))], inc="agxn", cc=True)
            S.do("pool", lambda: nc.gpsimd.collective_compute(
                "AllReduce", ALU.add, replica_groups=RG,
                ins=[pooled_in.ap()], outs=[pooled_out.ap()]),
                waits=[("rtl", S.v("rtl"))], inc="arpl", cc=True)

    S.mark("be")
    with nc.psum_tensor("rps0", [128, MM], F32) as r0, \
         nc.psum_tensor("rps1", [128, MM], F32) as r1:
        S.do("sp", lambda: nc.sync.dma_start(
            out=pooled_sb[:, 0:8], in_=pooled_out.ap()[0:128, :]),
            waits=[("arpl", 1)], inc="rtl", incn=16)
        S.do("sp", lambda: nc.sync.dma_start(
            out=pooled_sb[:, 8:16], in_=pooled_out.ap()[128:256, :]),
            inc="rtl", incn=16)
        S.do("dve", lambda: nc.vector.tensor_copy(pooled_bf[:], pooled_sb[:]),
             waits=[("rtl", S.v("rtl"))], inc="rtv")
        RTL = S.v("rtv")
        for bb in range(2):
            for cti in range(2):
                for k in range(8):
                    S.do("pe", lambda b_=bb, c=cti, kk=k: nc.tensor.matmul(
                        r0[:, (b_ * 2 + c):(b_ * 2 + c) + 1],
                        w1_sb[:, kk * RH + c * 128: kk * RH + (c + 1) * 128],
                        pooled_bf[:, b_ * 8 + kk: b_ * 8 + kk + 1],
                        start=(kk == 0), stop=(kk == 7)),
                        waits=([("rtv", RTL), ("ldc2", LDC2), ("agxn", 1)] if k == 0 else []),
                        inc=("rtp" if k == 7 else None))
        for col, bcol in [(0, 0), (1, 1), (2, 0), (3, 1)]:
            S.do("act", lambda c=col, b_=bcol: nc.scalar.activation(
                h_sb[:, c:c + 1], r0[:, c:c + 1], AF.Tanh, bias=b1_sb[:, b_:b_ + 1]),
                waits=[("rtp", S.v("rtp"))], inc="rta")
        S.do("dve", lambda: nc.vector.tensor_copy(h_bf[:], h_sb[:]),
             waits=[("rta", S.v("rta"))], inc="rtv")
        RTA = S.v("rtv")
        for bb in range(2):
            for k in range(2):
                S.do("pe", lambda b_=bb, kk=k: nc.tensor.matmul(
                    r1[0:NZ, b_:b_ + 1], w2_sb[:, kk * NZ:(kk + 1) * NZ],
                    h_bf[:, b_ * 2 + kk: b_ * 2 + kk + 1],
                    start=(kk == 0), stop=(kk == 1)),
                    waits=([("rtv", RTA)] if k == 0 else []),
                    inc=("rtp" if k == 1 else None))
        S.do("dve", lambda: nc.vector.tensor_scalar(
            lgsb[:], r1[0:NZ, 0:2], b2_sb[:], None, ALU.add),
             waits=[("rtp", S.v("rtp"))], inc="rta")
        for bb in range(2):
            S.do("sp", lambda b_=bb: nc.sync.dma_start(
                out=lg_bounce.ap()[b_:b_ + 1, :], in_=lgsb[:, b_:b_ + 1]),
                waits=[("rta", S.v("rta"))], inc="rtl", incn=16)
        S.do("sp", lambda: nc.sync.dma_start(
            out=lgr[:], in_=lg_bounce.ap().rearrange("b c -> (b c)").unsqueeze(0)),
             waits=[("rtl", S.v("rtl"))], inc="rtl", incn=16)
        RTL2 = S.v("rtl")
        for bb in range(2):
            S.do("dve", lambda b_=bb: nc.vector.reduce_max(
                rscr[0:1, 0:1], lgr[0:1, b_ * NZ:(b_ + 1) * NZ], AX.X),
                waits=[("rtl", RTL2), ("rta", S.v("rta"))], inc="rtv")
            S.do("dve", lambda: nc.vector.tensor_scalar(rscr[0:1, 1:2], rscr[0:1, 0:1], -1.0, None, ALU.mult), inc="rtv")
            S.do("act", lambda b_=bb: nc.scalar.activation(
                rscr[0:1, 8:16], lgr[0:1, b_ * NZ:(b_ + 1) * NZ], AF.Exp, bias=rscr[0:1, 1:2]),
                waits=[("rtv", S.v("rtv"))], inc="rta")
            S.do("dve", lambda: nc.vector.reduce_sum(rscr[0:1, 2:3], rscr[0:1, 8:16], AX.X),
                 waits=[("rta", S.v("rta"))], inc="rtv")
            S.do("dve", lambda: nc.vector.reciprocal(rscr[0:1, 3:4], rscr[0:1, 2:3]), inc="rtv")
            S.do("dve", lambda: nc.vector.tensor_single_scalar(
                rscr[0:1, 8:16], rscr[0:1, 8:16], rscr[0:1, 3:4], ALU.mult), inc="rtv")
            S.do("dve", lambda: nc.vector.max(out=rscr[0:1, 16:24], in_=rscr[0:1, 8:16]), inc="rtv")
            S.do("dve", lambda: nc.vector.tensor_single_scalar(
                rscr[0:1, 24:32], rscr[0:1, 8:16], rscr[0:1, 18:19], ALU.is_ge), inc="rtv")
            S.do("dve", lambda: nc.vector.tensor_tensor(
                rscr[0:1, 24:32], rscr[0:1, 24:32], rscr[0:1, 8:16], ALU.mult), inc="rtv")
            S.do("dve", lambda: nc.vector.reduce_sum(rscr[0:1, 4:5], rscr[0:1, 24:32], AX.X), inc="rtv")
            S.do("dve", lambda: nc.vector.reciprocal(rscr[0:1, 5:6], rscr[0:1, 4:5]), inc="rtv")
            S.do("dve", lambda b_=bb: nc.vector.tensor_single_scalar(
                wrow[0:1, b_ * NZ:(b_ + 1) * NZ], rscr[0:1, 24:32], rscr[0:1, 5:6], ALU.mult), inc="rtv")
        RWV = S.v("rtv")
        S.do("pe", lambda: nc.tensor.matmul(r0[:, 0:16], ones1_sb[:], wrow[0:1, 0:16],
                                            start=True, stop=True),
             waits=[("rtv", RWV)], inc="rtp")
        S.do("dve", lambda: nc.vector.tensor_copy(w_bc[:], r0[:, 0:16]),
             waits=[("rtp", S.v("rtp"))], inc="rtv")
        for bb in range(2):
            S.do("dve", lambda b_=bb: nc.vector.tensor_tensor(
                zscr[:], w_bc[:, b_ * 8:(b_ + 1) * 8], zmask_sb[:], ALU.mult),
                waits=[("rtv", S.v("rtv"))], inc="rtv")
            S.do("dve", lambda b_=bb: nc.vector.reduce_sum(wz_col[:, b_:b_ + 1], zscr[:], AX.X), inc="rtv")
        WZ = S.v("rtv")
        if debug:
            S.do("pool", lambda: nc.gpsimd.dma_start(out=dbg["w"], in_=wrow[:]),
                 waits=[("rtv", RWV)], inc="dbgs", incn=16)

    S.mark("router")
    with ExitStack() as c6:
        sb6 = lambda n, s, dt=F32: c6.enter_context(nc.sbuf_tensor(n, list(s), dt))
        wa_sb = sb6("wa_sb", (128, 8 * DFF), BF16)
        wb_sb = sb6("wb_sb", (128, 32 * D), BF16)
        xn3r = [sb6(f"xn3r{i}", (128, 8 * MM), BF16) for i in range(2)]
        G_sb = sb6("G_sb", (128, 32 * MM), BF16)
        z_ev = [sb6(f"zev{i}", (128, MM)) for i in range(2)]
        with nc.psum_tensor("fz0", [128, MM], F32) as z0, \
             nc.psum_tensor("fz1", [128, MM], F32) as z1, \
             nc.psum_tensor("fz2", [128, MM], F32) as z2, \
             nc.psum_tensor("fz3", [128, MM], F32) as z3:
            be_exit = [("agxn", 1), ("x1d", S.v("x1d")),
                       ("bod0", S.v("bod0")), ("bod1", S.v("bod1"))]
            if debug:
                be_exit.append(("dbgs", S.v("dbgs")))
            S.do("pool", lambda: nc.gpsimd.dma_start(
                out=wa_sb[:].rearrange("p (kt f) -> p kt f", kt=8),
                in_=wa_d.rearrange("(kt p) f -> p kt f", p=128)),
                waits=be_exit, inc="ffw", incn=16)
            S.do("pool", lambda: nc.gpsimd.dma_start(out=wb_sb[:], in_=wb_d),
                 inc="ffw", incn=16)
            FFW = S.v("ffw")
            prev_wb = ("ffq", 0)
            for fc in range(NFC):
                pdst = fc // NMC
                off = (fc % NMC) * MM
                pp = fc % 2
                xr, ze = xn3r[pp], z_ev[pp]
                xgate = ("ffp", max(0, S.v("ffp") - 32))
                S.do("sp", lambda p_=pdst, o=off, x=xr: nc.sync.dma_start(
                    out=x[:].rearrange("p (kt c) -> p kt c", kt=8),
                    in_=xn3_all.ap()[p_ * D: (p_ + 1) * D, o:o + MM]
                    .rearrange("(kt p) c -> p kt c", p=128)),
                    waits=[("agxn", 1), xgate, (f"ffl{pp}", S.v(f"ffl{pp}"))],
                    inc=f"ffl{pp}", incn=16)
                FFL = (f"ffl{pp}", S.v(f"ffl{pp}"))
                for ft in range(32):
                    pb = [z0, z1][ft % 2]
                    for k in range(8):
                        S.do("pe", lambda kk=k, f_=ft, p_=pb, x=xr: nc.tensor.matmul(
                            p_[:, 0:MM], wa_sb[:, kk * DFF + f_ * 128: kk * DFF + (f_ + 1) * 128],
                            x[:, kk * MM:(kk + 1) * MM],
                            start=(kk == 0), stop=(kk == 7)),
                            waits=([FFL, ("ffw", FFW),
                                    ("ffg", max(0, S.v("ffg") - 1))] +
                                   ([("rtv", WZ)] if fc == 0 and ft == 0 else [])
                                   if k == 0 else []),
                            inc=("ffp" if k == 7 else None))
                    S.do("act", lambda f_=ft, p_=pb: nc.scalar.activation(
                        G_sb[:, f_ * MM:(f_ + 1) * MM], p_[:, 0:MM], AF.Gelu_apprx_tanh),
                        waits=[("ffp", S.v("ffp"))] + ([prev_wb] if ft == 0 else []),
                        inc="ffg")
                GD = S.v("ffg")
                bb = fc // (NFC // 2)
                for dm in range(8):
                    pb = [z2, z3][dm % 2]
                    for k in range(32):
                        S.do("pe", lambda kk=k, d_=dm, p_=pb: nc.tensor.matmul(
                            p_[:, 0:MM], wb_sb[:, kk * D + d_ * 128: kk * D + (d_ + 1) * 128],
                            G_sb[:, kk * MM:(kk + 1) * MM],
                            start=(kk == 0), stop=(kk == 31)),
                            waits=([("ffg", GD), ("ffv", max(0, S.v("ffv") - 1))]
                                   if k == 0 else []),
                            inc=("ffq" if k == 31 else None))
                    S.do("dve", lambda p_=pb, b_=bb, z=ze: nc.vector.tensor_scalar(
                        z[:], p_[:, 0:MM], wz_col[:, b_:b_ + 1], None, ALU.mult),
                        waits=[("ffq", S.v("ffq")), ("rtv", WZ),
                               (f"ffd{pp}", S.v(f"ffd{pp}"))], inc="ffv")
                    S.do("sp", lambda p_=pdst, o=off, d_=dm, z=ze: nc.sync.dma_start(
                        out=rs_in[o // MM].ap()[p_ * D + d_ * 128: p_ * D + (d_ + 1) * 128, :],
                        in_=z[:]),
                        waits=[("ffv", S.v("ffv"))], inc=f"ffd{pp}", incn=16)
                prev_wb = ("ffq", S.v("ffq"))
            FFD2 = [("ffd0", S.v("ffd0")), ("ffd1", S.v("ffd1"))]
            for q in range(NMC):
                S.do("pool", lambda q_=q: nc.gpsimd.collective_compute(
                    "ReduceScatter", ALU.add, replica_groups=RG,
                    ins=[rs_in[q_].ap()], outs=[rs_out[q_].ap()]),
                    waits=FFD2, inc=f"rs{q}", cc=True)

    S.mark("ffn")
    with ExitStack() as c7:
        sb7 = lambda n, s, dt=F32: c7.enter_context(nc.sbuf_tensor(n, list(s), dt))
        xf1 = [sb7(f"xf1{i}", (128, 128)) for i in range(2)]
        xrs = [sb7(f"xrs{i}", (128, 128)) for i in range(2)]
        fin = [sb7(f"fin{i}", (128, D)) for i in range(2)]
        with nc.psum_tensor("fp0", [128, MM], F32) as fp0, \
             nc.psum_tensor("fp1", [128, MM], F32) as fp1:
            for tt in range(TOK // 128):
                q = (tt * 128) // MM
                fsl = tt % 2
                ft = fin[fsl]
                for dm in range(8):
                    sl = dm % 2
                    b1_, b2_ = xf1[sl], xrs[sl]
                    pb = [fp0, fp1][sl]
                    S.do("sp", lambda t=tt, d_=dm, b=b1_: nc.sync.dma_start(
                        out=b[:], in_=x1_sp.ap()[d_ * 128:(d_ + 1) * 128, t * 128:(t + 1) * 128]),
                        waits=[(f"rs{q}", 1), ("x1d", S.v("x1d")),
                               ("fnp", max(0, S.v("fnp") - 1))],
                        inc=f"fnlA{sl}", incn=16)
                    S.do("sp", lambda t=tt, d_=dm, b=b2_, q_=q: nc.sync.dma_start(
                        out=b[:], in_=rs_out[q_].ap()[d_ * 128:(d_ + 1) * 128,
                                                      (t * 128) % MM:(t * 128) % MM + 128]),
                        waits=[("fna", max(0, S.v("fna") - 1))],
                        inc=f"fnlA{sl}", incn=16)
                    S.do("dve", lambda b=b1_, c=b2_: nc.vector.tensor_tensor(b[:], b[:], c[:], ALU.add),
                         waits=[(f"fnlA{sl}", S.v(f"fnlA{sl}")), (f"fnlA{sl}", S.v(f"fnlA{sl}"))],
                         inc="fna")
                    S.do("pe", lambda b=b1_, p_=pb: nc.tensor.transpose(
                        p_[:, 0:128], b[:], ident_sb[:]),
                        waits=[("fna", S.v("fna")), ("fne", max(0, S.v("fne") - 1))], inc="fnp")
                    S.do("act", lambda d_=dm, f_=ft, p_=pb: nc.scalar.activation(
                        f_[:, d_ * 128:(d_ + 1) * 128], p_[:, 0:128], AF.Copy),
                        waits=[("fnp", S.v("fnp"))] + ([(f"fno{fsl}", S.v(f"fno{fsl}"))] if dm == 0 else []),
                        inc="fne")
                S.do("sp", lambda t=tt, f_=ft: nc.sync.dma_start(
                    out=out_d[t * 128:(t + 1) * 128, :], in_=f_[:]),
                    waits=[("fne", S.v("fne"))], inc=f"fno{fsl}", incn=16)
    FNO = [("fno0", S.v("fno0")), ("fno1", S.v("fno1"))]

    if debug:
        S.do("pool", lambda: nc.gpsimd.dma_start(out=dbg["u_f"], in_=u_all["f"].ap()[0:DI, :]),
             waits=[("aguf", 1)], inc="dbgs", incn=16)
        S.do("pool", lambda: nc.gpsimd.dma_start(out=dbg["dbc_f"], in_=dbc_all["f"].ap()[0:96, :]),
             waits=[("agdf", 1)], inc="dbgs", incn=16)
        for j in range(NC):
            S.do("pool", lambda j_=j: nc.gpsimd.dma_start(
                out=dbg["yf"][j_ * CSL:(j_ + 1) * CSL, :],
                in_=y_allb.ap()[j_ * 4 * CSL: j_ * 4 * CSL + CSL, :]),
                waits=[("agy0", 1)], inc="dbgs", incn=16)
        S.do("pool", lambda: nc.gpsimd.dma_start(out=dbg["x1"], in_=x1_sp.ap()),
             waits=[("l3d0", S.v("l3d0")), ("l3d1", S.v("l3d1")), ("x1d", S.v("x1d"))],
             inc="dbgs", incn=16)
        S.do("pool", lambda: nc.gpsimd.dma_start(out=dbg["xn3"], in_=xn3_all.ap()[0:D, :]),
             waits=[("agxn", 1)], inc="dbgs", incn=16)
        for q in range(NMC):
            S.do("pool", lambda q_=q: nc.gpsimd.dma_start(
                out=dbg["zs"][:, q_ * MM:(q_ + 1) * MM], in_=rs_out[q_].ap()),
                waits=[(f"rs{q_2}", 1) for q_2 in range(NMC)], inc="dbgs", incn=16)

    S.do("pool", lambda: nc.gpsimd.nop(),
         waits=FNO + ([("dbgs", S.v("dbgs"))] if debug else []), inc=None, cc=True)

    if phases is not None:
        S.truncate(phases)
        with nc.sbuf_tensor("stub", [128, D], F32) as stub:
            S.do("dve", lambda: nc.vector.memset(stub[:], 0.0), inc="stbv")
            for tt in range(TOK // 128):
                S.do("sp", lambda t=tt: nc.sync.dma_start(
                    out=out_d[t * 128:(t + 1) * 128, :], in_=stub[:]),
                    waits=[("stbv", 1)], inc="stbd", incn=16)
            S.do("pool", lambda: nc.gpsimd.nop(), waits=[("stbd", S.v("stbd"))], cc=True)

    with nc.Block() as block:
        @block.tensor
        def _(eng):
            S.emit("pe", eng)

        @block.vector
        def _(eng):
            S.emit("dve", eng)

        @block.scalar
        def _(eng):
            S.emit("act", eng)

        @block.sync
        def _(eng):
            S.emit("sp", eng)

        @block.gpsimd
        def _(eng):
            S.emit("pool", eng)

    top.close()
    return nc


def pack_cols(vec):
    v = np.ascontiguousarray(np.asarray(vec, np.float32)).reshape(-1)
    c = v.shape[0] // 128
    return np.ascontiguousarray(v.reshape(c, 128).T)


def pack_blocks(mat):
    m = np.ascontiguousarray(np.asarray(mat, np.float32))
    c, w = m.shape
    nt = c // 128
    return np.ascontiguousarray(m.reshape(nt, 128, w).transpose(1, 0, 2).reshape(128, nt * w))


def _prep_inputs(L, kw):
    f32 = np.float32
    TOK = L // 4
    TOKP = TOK + 2 * PAD
    x = np.asarray(kw["L0_attn_x"], f32)

    def wT(a):
        return np.ascontiguousarray(np.asarray(a, f32).T)

    base = {}
    for d, pre in (("f", "L0_attn_f_"), ("b", "L0_attn_b_")):
        in_w = np.asarray(kw[pre + "in_w"], f32)
        base[f"w_in_xi_{d}"] = wT(in_w[0:DI])
        base[f"w_in_z_{d}"] = wT(in_w[DI:2 * DI])
        base[f"w_x_{d}"] = pack_blocks(wT(np.asarray(kw[pre + "x_w"], f32)))
        base[f"w_out_{d}"] = pack_blocks(wT(np.asarray(kw[pre + "out_w"], f32)))
        base[f"conv_w_{d}"] = pack_blocks(np.asarray(kw[pre + "conv_w"], f32))
        base[f"conv_b_{d}"] = pack_cols(np.asarray(kw[pre + "conv_b"], f32))
    base["w_comb"] = pack_blocks(wT(np.asarray(kw["L0_attn_combine_w"], f32)))
    base["g_col"] = pack_cols(np.asarray(kw["L0_attn_ln_g"], f32))
    base["b_col"] = pack_cols(np.asarray(kw["L0_attn_ln_b"], f32))
    base["cb_col"] = pack_cols(np.asarray(kw["L0_attn_combine_b"], f32))
    base["w1"] = pack_blocks(np.asarray(kw["L0_ffn_W1"], f32))
    base["b1_col"] = pack_cols(np.asarray(kw["L0_ffn_b1"], f32))
    base["w2"] = pack_blocks(np.asarray(kw["L0_ffn_W2"], f32))
    base["b2_col"] = np.ascontiguousarray(np.asarray(kw["L0_ffn_b2"], f32).reshape(NZ, 1))
    sel = np.zeros((16, NST * 128), f32)
    for n in range(NST):
        sel[n, n * 128:(n + 1) * 128] = 1.0
    base["sel"] = sel
    base["ones1"] = np.ones((1, 128), f32)
    base["onescol"] = np.ones((128, 1), f32)
    base["ident"] = np.eye(128, dtype=f32)
    cstv = np.zeros((128, 4), f32)
    cstv[:, 0] = EPS
    cstv[:, 1] = -0.5
    cstv[:, 2] = -1.0
    cstv[:, 3] = 1.0
    base["cst_in"] = cstv

    Wa = np.asarray(kw["L0_ffn_Wa"], f32)
    Wb = np.asarray(kw["L0_ffn_Wb"], f32)
    CSLn = DI // NC
    NCT_ = CSLn // 128
    lane = np.arange(128, dtype=np.int32)

    per_core = []
    for rank in range(NC):
        m = dict(base)
        b = rank // 4
        q = rank % 4
        t0 = q * TOK
        xp = np.zeros((TOKP, D), f32)
        lo = max(0, t0 - PAD)
        hi = min(L, t0 + TOK + PAD)
        xp[lo - (t0 - PAD):hi - (t0 - PAD)] = x[b, lo:hi]
        m["x0"] = xp
        c0 = rank * CSLn
        for d, pre in (("f", "L0_attn_f_"), ("b", "L0_attn_b_")):
            dtw = np.asarray(kw[pre + "dt_w"], f32)
            m[f"w_dt_{d}"] = np.ascontiguousarray(dtw.T[:, c0:c0 + CSLn])
            m[f"dt_b_{d}"] = pack_cols(np.asarray(kw[pre + "dt_b"], f32)[c0:c0 + CSLn])
            m[f"A_{d}"] = pack_blocks(np.asarray(kw[pre + "A_log"], f32)[c0:c0 + CSLn])
            m[f"D_{d}"] = pack_cols(np.asarray(kw[pre + "D"], f32)[c0:c0 + CSLn])
        m["wa"] = np.ascontiguousarray(Wa[rank])
        m["wb"] = pack_blocks(np.ascontiguousarray(Wb[rank]))
        m["edge"] = np.stack([np.full(128, 0.0 if q == 0 else 1.0, f32),
                              np.full(128, 0.0 if q == 3 else 1.0, f32)], axis=1)
        m["bmask"] = np.stack([np.full(128, 1.0 if b == 0 else 0.0, f32),
                               np.full(128, 1.0 if b == 1 else 0.0, f32)], axis=1)
        zm = np.zeros((128, NZ), f32)
        zm[:, rank] = 1.0
        m["zmask"] = zm
        idxu = np.zeros((128, 2 * 2 * 4 * NCT_), np.int32)
        for di in range(2):
            for bb in range(2):
                for gi in range(4):
                    p = 4 * bb + gi
                    for ct in range(NCT_):
                        col = (di * 2 + bb) * 4 * NCT_ + gi * NCT_ + ct
                        idxu[:, col] = p * DI + c0 + ct * 128 + lane
        m["idxu"] = np.ascontiguousarray(idxu)
        idxy = np.zeros((128, 2 * 16), np.int32)
        for di, dch in enumerate("fb"):
            g = q if dch == "f" else 3 - q
            jb = 2 * b + (0 if dch == "f" else 1)
            for rt in range(16):
                j = rt // 2
                ct = rt % 2
                idxy[:, di * 16 + rt] = (g * NC * 4 * CSLn + j * 4 * CSLn +
                                         jb * CSLn + ct * 128 + lane)
        m["idxy"] = np.ascontiguousarray(idxy)
        per_core.append(m)
    return per_core


_CACHE = {}


def kernel(**inputs):
    L = inputs["L0_attn_x"].shape[1]
    TOK = L // 4
    if ("prog", L) not in _CACHE:
        _CACHE[("prog", L)] = build_program(L, debug=False, chain=True)
    nc = _CACHE[("prog", L)]
    in_maps = _prep_inputs(L, inputs)
    res = run_bass_kernel_spmd(nc, in_maps, core_ids=list(range(NC))).results
    out = np.zeros((BATCH, L, D), np.float32)
    for rank in range(NC):
        b, q = rank // 4, rank % 4
        out[b, q * TOK:(q + 1) * TOK, :] = res[rank]["out"]
    return out


# version 05
# speedup vs baseline: 1.1382x; 1.1382x over previous
import numpy as np
import ml_dtypes
import concourse.bass as bass
import concourse.mybir as mybir
from concourse.bass_core_rust import IndirectOffsetOnAxis
from concourse.bass_utils import run_bass_kernel_spmd
from contextlib import ExitStack

F32 = mybir.dt.float32
BF16 = mybir.dt.bfloat16
I32 = mybir.dt.int32
AF = mybir.ActivationFunctionType
ALU = mybir.AluOpType
AX = mybir.AxisListType

NC = 8
D = 1024
DI = 2048
NST = 16
DTR = 64
NZ = 8
DFF = 4096
RH = 256
BATCH = 2
PAD = 128
MM = 512
EPS = 1e-5
ENGINES = ("pe", "dve", "act", "sp", "pool")
JOBS = [("f", 0), ("b", 0), ("f", 1), ("b", 1)]


class Sched:
    def __init__(self, nc, chain=False):
        self.nc = nc
        self.chain = chain
        self.ops = {e: [] for e in ENGINES}
        self.cnt = {}
        self.sems = {}
        self._ctx = []
        self.last_inc = {e: None for e in ENGINES}
        self.marks = {}

    def mark(self, name):
        self.marks[name] = ({e: len(self.ops[e]) for e in ENGINES},
                            dict(self.cnt), dict(self.last_inc))

    def truncate(self, name):
        lens, cnt, last = self.marks[name]
        for e in ENGINES:
            self.ops[e] = self.ops[e][:lens[e]]
        self.cnt = dict(cnt)
        self.last_inc = dict(last)

    def sem(self, name):
        if name not in self.sems:
            c = self.nc.semaphore(name)
            self.sems[name] = c.__enter__()
            self._ctx.append(c)
            self.cnt[name] = 0
        return name

    def do(self, eng, fn, waits=(), inc=None, incn=1, cc=False):
        w = []
        for s, val in waits:
            self.sem(s)
            assert val <= self.cnt[s], f"wait {s}>={val} but only {self.cnt[s]} scheduled"
            if val > 0:
                w.append((s, val))
        if self.chain and eng in ("dve", "act") and self.last_inc[eng]:
            w.append(self.last_inc[eng])
        if inc is not None:
            self.sem(inc)
            self.cnt[inc] += incn
            self.last_inc[eng] = (inc, self.cnt[inc])
        self.ops[eng].append((tuple(w), fn, inc, incn))

    def v(self, name):
        self.sem(name)
        return self.cnt.get(name, 0)

    def emit(self, eng_name, eng):
        for w, fn, inc, incn in self.ops[eng_name]:
            seen = set()
            for s, val in w:
                if (s, val) in seen:
                    continue
                seen.add((s, val))
                eng.wait_ge(self.sems[s], val)
            inst = fn()
            if inc is not None and inst is not None:
                inst.then_inc(self.sems[inc], incn)


def build_program(L, debug=False, chain=False, phases=None):
    TOK = L // 4
    TOKP = TOK + 2 * PAD
    NTT = TOKP // 128
    CSL = DI // NC
    NCT = CSL // 128
    SC = min(1024, TOK)
    NSUB = TOK // SC
    NMC = TOK // MM
    NFC = NC * TOK // MM
    QT = TOK // 4

    nc = bass.Bass("TRN2", target_bir_lowering=False)
    S = Sched(nc, chain=chain)
    RG = [list(range(NC))]

    def inp(name, shape, dtype=F32):
        return nc.dram_tensor(name, list(shape), dtype, kind="ExternalInput").ap()

    x0_d = inp("x0", (TOKP, D))
    w_in_xi = {d: inp(f"w_in_xi_{d}", (D, DI)) for d in "fb"}
    w_in_z = {d: inp(f"w_in_z_{d}", (D, DI)) for d in "fb"}
    w_x = {d: inp(f"w_x_{d}", (128, 16 * 96)) for d in "fb"}
    w_dt = {d: inp(f"w_dt_{d}", (DTR, CSL)) for d in "fb"}
    w_out = {d: inp(f"w_out_{d}", (128, 16 * D)) for d in "fb"}
    w_comb = inp("w_comb", (128, 16 * D))
    conv_w = {d: inp(f"conv_w_{d}", (128, 16 * 4)) for d in "fb"}
    conv_b = {d: inp(f"conv_b_{d}", (128, 16)) for d in "fb"}
    dt_b = {d: inp(f"dt_b_{d}", (128, NCT)) for d in "fb"}
    A_in = {d: inp(f"A_{d}", (128, NCT * NST)) for d in "fb"}
    D_in = {d: inp(f"D_{d}", (128, NCT)) for d in "fb"}
    g_col = inp("g_col", (128, 8))
    b_col = inp("b_col", (128, 8))
    cb_col = inp("cb_col", (128, 8))
    wa_d = inp("wa", (D, DFF))
    wb_d = inp("wb", (128, 32 * D))
    w1_d = inp("w1", (128, 8 * RH))
    b1_d = inp("b1_col", (128, 2))
    w2_d = inp("w2", (128, 2 * NZ))
    b2_d = inp("b2_col", (NZ, 1))
    sel_d = inp("sel", (16, NST * 128))
    ones1_d = inp("ones1", (1, 128))
    onescol_d = inp("onescol", (128, 1))
    ident_d = inp("ident", (128, 128))
    edge_d = inp("edge", (128, 2))
    bmask_d = inp("bmask", (128, 2))
    zmask_d = inp("zmask", (128, 8))
    cst_d = inp("cst_in", (128, 4))
    idxu_d = inp("idxu", (128, 2 * 2 * 4 * NCT), I32)
    idxy_d = inp("idxy", (128, 2 * 16), I32)

    out_d = nc.dram_tensor("out", [TOK, D], F32, kind="ExternalOutput").ap()
    dbg = {}
    if debug:
        for nm, shp in [("xn2", (D, TOKP)), ("u_f", (DI, TOK)), ("dbc_f", (96, TOK)),
                        ("yf", (DI, TOK)), ("x1", (D, TOK)), ("xn3", (D, TOK)),
                        ("w", (1, 2 * NZ)), ("zs", (D, TOK))]:
            dbg[nm] = nc.dram_tensor("dbg_" + nm, list(shp), F32, kind="ExternalOutput").ap()

    dr = lambda name, shape, dtype=F32: nc.dram_tensor(name, list(shape), dtype)
    drs = lambda name, shape, dtype=F32: nc.dram_tensor(name, list(shape), dtype,
                                                        addr_space="Shared")
    u_in = {d: dr(f"u_in_{d}", (DI, TOK), F32) for d in "fb"}
    u_all = {d: drs(f"u_all_{d}", (NC * DI, TOK), F32) for d in "fb"}
    dbc_in = {d: dr(f"dbc_in_{d}", (96, TOK), F32) for d in "fb"}
    dbc_all = {d: drs(f"dbc_all_{d}", (NC * 96, TOK), F32) for d in "fb"}
    y_in = [dr(f"y_in_{g}", (4 * CSL, TOK), F32) for g in range(4)]
    y_allb = drs("y_allb", (4 * NC * 4 * CSL, TOK), F32)
    of_sp = {d: dr(f"of_{d}", (D, TOK), BF16) for d in "fb"}
    xn3_in = dr("xn3_in", (D, TOK), BF16)
    xn3_all = drs("xn3_all", (NC * D, TOK), BF16)
    pooled_in = dr("pooled_in", (BATCH * 128, 8), F32)
    pooled_out = drs("pooled_out", (BATCH * 128, 8), F32)
    x1_sp = dr("x1_sp", (D, TOK), F32)
    rs_in = [dr(f"rs_in{q}", (NC * D, MM), F32) for q in range(NMC)]
    rs_out = [dr(f"rs_out{q}", (D, MM), F32) for q in range(NMC)]
    lg_bounce = dr("lg_bounce", (BATCH, NZ), F32)

    top = ExitStack()
    sbT = lambda n, s, dt=F32: top.enter_context(nc.sbuf_tensor(n, list(s), dt))

    convw_sb = {d: sbT(f"convw_{d}", (128, 16 * 4)) for d in "fb"}
    convb_sb = {d: sbT(f"convb_{d}", (128, 16)) for d in "fb"}
    dtb_sb = {d: sbT(f"dtb_{d}", (128, NCT)) for d in "fb"}
    A_sb = {d: sbT(f"Am_{d}", (128, NCT * NST)) for d in "fb"}
    D_sb = {d: sbT(f"Dp_{d}", (128, NCT)) for d in "fb"}
    g_sb = sbT("g_sb", (128, 8))
    bcol_sb = sbT("bcol_sb", (128, 8))
    cb_sb = sbT("cb_sb", (128, 8))
    sel_sb = sbT("sel_sb", (16, NST * 128))
    ones1_sb = sbT("ones1_sb", (1, 128))
    onescol_sb = sbT("onescol_sb", (128, 1))
    ident_sb = sbT("ident_sb", (128, 128))
    edge_sb = sbT("edge_sb", (128, 2))
    bmask_sb = sbT("bmask_sb", (128, 2))
    zmask_sb = sbT("zmask_sb", (128, 8))
    cst = sbT("cst", (128, 4))
    idxu_sb = sbT("idxu_sb", (128, 2 * 2 * 4 * NCT), I32)
    idxy_sb = sbT("idxy_sb", (128, 2 * 16), I32)
    wdt_sb = {d: sbT(f"wdt_{d}", (DTR, CSL), BF16) for d in "fb"}
    wx_sb = {d: sbT(f"wx_{d}", (128, 16 * 96), BF16) for d in "fb"}
    w1_sb = sbT("w1_sb", (128, 8 * RH), BF16)
    b1_sb = sbT("b1_sb", (128, 2))
    b2_sb = sbT("b2_sb", (NZ, 1))
    w2_sb = sbT("w2_sb", (128, 2 * NZ), BF16)
    stat = sbT("stat", (128, 16))
    state = sbT("state", (128, 4 * NST * NCT))
    pooled_sb = sbT("pooled_sb", (128, 2 * 8))
    h_sb = sbT("h_sb", (128, 4))
    lgsb = sbT("lgsb", (NZ, 2))
    lgr = sbT("lgr", (1, 2 * NZ))
    rscr = sbT("rscr", (1, 4 * NZ))
    wrow = sbT("wrow", (1, 2 * NZ))
    w_bc = sbT("w_bc", (128, 2 * NZ))
    wz_col = sbT("wz_col", (128, 2))
    zscr = sbT("zscr", (128, 8))
    colsum = sbT("colsum", (128, 8))
    pooled_bf = sbT("pooled_bf", (128, 2 * 8), BF16)
    h_bf = sbT("h_bf", (128, 4), BF16)
    mid = ExitStack()
    xn2_t = mid.enter_context(nc.sbuf_tensor("xn2", [128, 8 * TOKP], BF16))

    for dst, src in [
        (convw_sb["f"][:], conv_w["f"]), (convw_sb["b"][:], conv_w["b"]),
        (convb_sb["f"][:], conv_b["f"]), (convb_sb["b"][:], conv_b["b"]),
        (dtb_sb["f"][:], dt_b["f"]), (dtb_sb["b"][:], dt_b["b"]),
        (D_sb["f"][:], D_in["f"]), (D_sb["b"][:], D_in["b"]),
        (A_sb["f"][:], A_in["f"]), (A_sb["b"][:], A_in["b"]),
        (g_sb[:], g_col), (bcol_sb[:], b_col), (cb_sb[:], cb_col),
        (sel_sb[:], sel_d), (ones1_sb[:], ones1_d), (onescol_sb[:], onescol_d),
        (ident_sb[:], ident_d), (edge_sb[:], edge_d), (bmask_sb[:], bmask_d),
        (zmask_sb[:], zmask_d), (idxu_sb[:], idxu_d), (idxy_sb[:], idxy_d),
        (cst[:], cst_d),
        (b1_sb[:], b1_d), (b2_sb[:], b2_d),
    ]:
        S.do("sp", lambda d=dst, s=src: nc.sync.dma_start(out=d, in_=s), inc="ldc", incn=16)
    for d in "fb":
        S.do("pool", lambda dd=d: nc.gpsimd.dma_start(out=wdt_sb[dd][:], in_=w_dt[dd]), inc="ldc2", incn=16)
        S.do("pool", lambda dd=d: nc.gpsimd.dma_start(out=wx_sb[dd][:], in_=w_x[dd]), inc="ldc2", incn=16)
    S.do("pool", lambda: nc.gpsimd.dma_start(out=w1_sb[:], in_=w1_d), inc="ldc2", incn=16)
    S.do("pool", lambda: nc.gpsimd.dma_start(out=w2_sb[:], in_=w2_d), inc="ldc2", incn=16)
    LDC = S.v("ldc")
    LDC2 = S.v("ldc2")
    for d in "fb":
        S.do("act", lambda dd=d: nc.scalar.activation(A_sb[dd][:], A_sb[dd][:], AF.Exp),
             waits=[("ldc", LDC)], inc="p1a")
        S.do("dve", lambda dd=d: nc.vector.tensor_scalar(
            A_sb[dd][:], A_sb[dd][:], -1.0, None, ALU.mult),
            waits=[("p1a", S.v("p1a"))], inc="p1v")
    AEXP = S.v("p1v")

    INV_D = 1.0 / D
    with ExitStack() as c1:
        sb1 = lambda n, s, dt=F32: c1.enter_context(nc.sbuf_tensor(n, list(s), dt))
        x0_tm = [sb1(f"x0tm{i}", (128, D)) for i in range(2)]
        sq_tm = [sb1(f"sqtm{i}", (128, D)) for i in range(2)]
        xn1_tm = [sb1(f"xn1tm{i}", (128, D)) for i in range(2)]
        x1ev = [sb1(f"x1ev{i}", (128, 128)) for i in range(4)]
        with nc.psum_tensor("p1ps0", [128, MM], F32) as pA, \
             nc.psum_tensor("p1ps1", [128, MM], F32) as pB:
            psb = [pA, pB]
            slot_free = [("p1tp", 0), ("p1tp", 0)]
            x1ev_free = [(f"p1q{i}", 0) for i in range(4)]
            bank_free = [[("p1ev", 0)], [("p1ev", 0)]]
            nev = 0
            for ti in range(NTT):
                pp = ti % 2
                x0t, sqt, xn1t = x0_tm[pp], sq_tm[pp], xn1_tm[pp]
                own = 1 <= ti <= NTT - 2
                S.do("sp", lambda t=ti, b=x0t: nc.sync.dma_start(
                    out=b[:], in_=x0_d[t * 128:(t + 1) * 128, :]),
                    waits=[slot_free[pp]], inc=f"p1x{pp}", incn=16)
                ld = [(f"p1x{pp}", S.v(f"p1x{pp}"))]
                S.do("act", lambda b=x0t, o=sqt: nc.scalar.activation(o[:], b[:], AF.Square),
                     waits=ld + [("p1v", S.v("p1v"))], inc="p1a")
                S.do("dve", lambda b=x0t: nc.vector.reduce_sum(stat[:, 0:1], b[:], AX.X),
                     waits=ld + [("p1a", S.v("p1a") - 1)], inc="p1v")
                S.do("dve", lambda o=sqt: nc.vector.reduce_sum(stat[:, 1:2], o[:], AX.X),
                     waits=[("p1a", S.v("p1a"))], inc="p1v")
                S.do("dve", lambda: nc.vector.tensor_scalar(stat[:, 2:3], stat[:, 0:1], INV_D, None, ALU.mult), inc="p1v")
                S.do("dve", lambda: nc.vector.tensor_scalar(stat[:, 3:4], stat[:, 1:2], INV_D, None, ALU.mult), inc="p1v")
                S.do("dve", lambda: nc.vector.tensor_tensor(stat[:, 4:5], stat[:, 2:3], stat[:, 2:3], ALU.mult), inc="p1v")
                S.do("dve", lambda: nc.vector.tensor_tensor(stat[:, 5:6], stat[:, 3:4], stat[:, 4:5], ALU.subtract), inc="p1v")
                S.do("act", lambda: nc.scalar.activation(stat[:, 6:7], stat[:, 5:6], AF.Ln, bias=cst[:, 0:1]),
                     waits=[("p1v", S.v("p1v"))], inc="p1a")
                S.do("act", lambda: nc.scalar.activation(stat[:, 7:8], stat[:, 6:7], AF.Exp, scale=cst[:, 1:2]), inc="p1a")
                S.do("dve", lambda b=x0t, o=xn1t: nc.vector.tensor_scalar(
                    o[:], b[:], stat[:, 2:3], stat[:, 7:8], ALU.subtract, ALU.mult),
                    waits=[("p1a", S.v("p1a"))], inc="p1v")
                S.do("act", lambda b=xn1t, o=sqt: nc.scalar.activation(o[:], b[:], AF.Square),
                     waits=[("p1v", S.v("p1v"))], inc="p1a")
                S.do("dve", lambda b=xn1t: nc.vector.reduce_sum(stat[:, 8:9], b[:], AX.X), inc="p1v")
                S.do("dve", lambda o=sqt: nc.vector.reduce_sum(stat[:, 9:10], o[:], AX.X),
                     waits=[("p1a", S.v("p1a"))], inc="p1v")
                S.do("dve", lambda: nc.vector.tensor_scalar(stat[:, 10:11], stat[:, 8:9], INV_D, None, ALU.mult), inc="p1v")
                S.do("dve", lambda: nc.vector.tensor_scalar(stat[:, 11:12], stat[:, 9:10], INV_D, None, ALU.mult), inc="p1v")
                S.do("dve", lambda: nc.vector.tensor_tensor(stat[:, 12:13], stat[:, 10:11], stat[:, 10:11], ALU.mult), inc="p1v")
                S.do("dve", lambda: nc.vector.tensor_tensor(stat[:, 13:14], stat[:, 11:12], stat[:, 12:13], ALU.subtract), inc="p1v")
                S.do("act", lambda: nc.scalar.activation(stat[:, 14:15], stat[:, 13:14], AF.Ln, bias=cst[:, 0:1]),
                     waits=[("p1v", S.v("p1v"))], inc="p1a")
                S.do("act", lambda: nc.scalar.activation(stat[:, 15:16], stat[:, 14:15], AF.Exp, scale=cst[:, 1:2]), inc="p1a")
                a2 = [("p1a", S.v("p1a"))]
                if own:
                    S.do("dve", lambda b=x0t, a=xn1t: nc.vector.tensor_tensor(b[:], b[:], a[:], ALU.add),
                         waits=a2, inc="p1v")
                S.do("dve", lambda a=xn1t: nc.vector.tensor_scalar(
                    a[:], a[:], stat[:, 10:11], stat[:, 15:16], ALU.subtract, ALU.mult),
                    waits=a2, inc="p1v")
                vr = [("p1v", S.v("p1v"))]
                for dt_ in range(8):
                    bk = dt_ % 2
                    pb = psb[bk]
                    S.do("pe", lambda a=xn1t, p=pb, d=dt_: nc.tensor.transpose(
                        p[:, 0:128], a[:, d * 128:(d + 1) * 128], ident_sb[:]),
                        waits=[("ldc", LDC)] + vr + bank_free[bk], inc="p1tp")
                    S.do("dve", lambda p=pb, d=dt_, t=ti: nc.vector.tensor_scalar(
                        xn2_t[:, d * TOKP + t * 128: d * TOKP + (t + 1) * 128], p[:, 0:128],
                        g_sb[:, d:d + 1], bcol_sb[:, d:d + 1], ALU.mult, ALU.add),
                        waits=[("p1tp", S.v("p1tp"))], inc="p1ev")
                    bank_free[bk] = [("p1ev", S.v("p1ev"))]
                    if own:
                        ev = x1ev[nev % 4]
                        S.do("pe", lambda b=x0t, p=pb, d=dt_: nc.tensor.transpose(
                            p[:, 128:256], b[:, d * 128:(d + 1) * 128], ident_sb[:]),
                            waits=[("p1ev", S.v("p1ev"))] + bank_free[bk][1:], inc="p1tp")
                        S.do("dve", lambda p=pb, e=ev: nc.vector.tensor_copy(e[:], p[:, 128:256]),
                             waits=[("p1tp", S.v("p1tp")), x1ev_free[nev % 4]], inc="p1ev2")
                        bank_free[bk] = [("p1ev", S.v("p1ev")), ("p1ev2", S.v("p1ev2"))]
                        S.do("sp", lambda e=ev, d=dt_, t=ti: nc.sync.dma_start(
                            out=x1_sp.ap()[d * 128:(d + 1) * 128, (t - 1) * 128:t * 128], in_=e[:]),
                            waits=[("p1ev2", S.v("p1ev2"))], inc=f"p1q{nev % 4}", incn=16)
                        x1ev_free[nev % 4] = (f"p1q{nev % 4}", S.v(f"p1q{nev % 4}"))
                        nev += 1
                slot_free[pp] = ("p1tp", S.v("p1tp"))
    LNEV = S.v("p1ev")
    X1Q = [(f"p1q{i}", S.v(f"p1q{i}")) for i in range(4)]
    P1TPF = S.v("p1tp")
    P1GATE = [("p1tp", P1TPF), ("p1ev", LNEV), ("p1ev2", S.v("p1ev2"))] + X1Q

    if debug:
        for dt_ in range(8):
            S.do("pool", lambda d=dt_: nc.gpsimd.dma_start(
                out=dbg["xn2"][d * 128:(d + 1) * 128, :], in_=xn2_t[:, d * TOKP:(d + 1) * TOKP]),
                waits=[("p1ev", LNEV)], inc="dbgs", incn=16)

    S.mark("p1")
    with ExitStack() as c2:
        sb2 = lambda n, s, dt=F32: c2.enter_context(nc.sbuf_tensor(n, list(s), dt))
        w_ct = [sb2(f"wct{i}", (128, 8 * 128), BF16) for i in range(2)]
        xipre = [sb2(f"xipre{i}", (128, TOKP)) for i in range(2)]
        xi_f32 = [sb2(f"xif{i}", (128, TOK)) for i in range(2)]
        xi_bf = sb2("xibf", (128, 16 * TOK), BF16)
        dbc_ev = sb2("dbcev", (80, TOK))
        dbc_ev2 = sb2("dbcev2", (16, TOK))
        with nc.psum_tensor("feps0", [128, MM], F32) as f0, \
             nc.psum_tensor("feps1", [128, MM], F32) as f1, \
             nc.psum_tensor("feps2", [128, MM], F32) as f2, \
             nc.psum_tensor("feps3", [128, MM], F32) as f3:
            col_chunks = []
            c0 = 0
            while c0 < TOKP:
                cw = min(MM, TOKP - c0)
                col_chunks.append((c0, cw))
                c0 += cw
            xi_guard = ("fxp", 0)
            dbc_guard = ("fxd", 0)
            wslot = [("fei", 0), ("fei", 0)]
            xpre_slot = [("fec", 0), ("fec", 0)]
            xf_slot = [(("feu0", 0), ("feb", 0)), (("feu1", 0), ("feb", 0))]
            for dir_ in "fb":
                for ct in range(16):
                    pp = ct % 2
                    wt, xp, xf = w_ct[pp], xipre[pp], xi_f32[pp]
                    S.do("pool", lambda d=dir_, c=ct, w=wt: nc.gpsimd.dma_start(
                        out=w[:].rearrange("p (kt c) -> p kt c", kt=8),
                        in_=w_in_xi[d][:, c * 128:(c + 1) * 128]
                        .rearrange("(kt p) c -> p kt c", p=128)),
                        waits=[wslot[pp]] + (P1GATE if (dir_ == "f" and ct == 0) else []),
                        inc=f"few{pp}", incn=16)
                    S.mark(f"fw_{dir_}_{ct}")
                    wr = (f"few{pp}", S.v(f"few{pp}"))
                    for ci, (cc0, cw) in enumerate(col_chunks):
                        pb = [f0, f1][(ct * len(col_chunks) + ci) % 2]
                        for k in range(8):
                            S.do("pe", lambda kk=k, o=cc0, w=cw, p=pb, wt_=wt: nc.tensor.matmul(
                                p[:, 0:w], wt_[:, kk * 128:(kk + 1) * 128],
                                xn2_t[:, kk * TOKP + o: kk * TOKP + o + w],
                                start=(kk == 0), stop=(kk == 7)),
                                waits=([wr, ("p1ev", LNEV),
                                        ("fee", max(0, S.v("fee") - 1))] if k == 0 else []),
                                inc=("fei" if k == 7 else None))
                        S.mark(f"fmm_{dir_}_{ct}_{ci}")
                        S.do("act", lambda o=cc0, w=cw, p=pb, t=xp: nc.scalar.activation(
                            t[:, o:o + w], p[:, 0:w], AF.Copy),
                            waits=[("fei", S.v("fei")), xpre_slot[pp]], inc="fee")
                        S.mark(f"fm_{dir_}_{ct}_{ci}")
                    ir = [("fee", S.v("fee")), ("p1v", AEXP)]
                    if dir_ == "f":
                        S.do("dve", lambda t=xp: nc.vector.tensor_scalar(
                            t[:, PAD - 3:PAD], t[:, PAD - 3:PAD], edge_sb[:, 0:1], None, ALU.mult),
                            waits=ir, inc="fec")
                    else:
                        S.do("dve", lambda t=xp: nc.vector.tensor_scalar(
                            t[:, PAD + TOK:PAD + TOK + 3], t[:, PAD + TOK:PAD + TOK + 3],
                            edge_sb[:, 1:2], None, ALU.mult),
                            waits=ir, inc="fec")
                    offs = [PAD - 3, PAD - 2, PAD - 1, PAD] if dir_ == "f" else [PAD, PAD + 1, PAD + 2, PAD + 3]
                    wsel = [0, 1, 2, 3] if dir_ == "f" else [3, 2, 1, 0]
                    S.do("dve", lambda t=xp, o=xf, c=ct, d=dir_, oj=offs[0], wj=wsel[0]: nc.vector.tensor_scalar(
                        o[:], t[:, oj:oj + TOK], convw_sb[d][:, c * 4 + wj:c * 4 + wj + 1], None, ALU.mult),
                        waits=[xf_slot[pp][0]], inc="fec")
                    for j in range(1, 4):
                        S.do("dve", lambda t=xp, o=xf, c=ct, d=dir_, oj=offs[j], wj=wsel[j]: nc.vector.scalar_tensor_tensor(
                            o[:], t[:, oj:oj + TOK], convw_sb[d][:, c * 4 + wj:c * 4 + wj + 1], o[:],
                            ALU.mult, ALU.add), inc="fec")
                    xpre_slot[pp] = ("fec", S.v("fec"))
                    S.mark(f"fc_{dir_}_{ct}")
                    S.do("act", lambda o=xf, c=ct, d=dir_: nc.scalar.activation(
                        o[:], o[:], AF.Silu, bias=convb_sb[d][:, c:c + 1]),
                        waits=[("fec", S.v("fec"))], inc="fes")
                    sr = S.v("fes")
                    S.mark(f"fs_{dir_}_{ct}")
                    S.do("sp", lambda o=xf, c=ct, d=dir_: nc.sync.dma_start(
                        out=u_in[d].ap()[c * 128:(c + 1) * 128, :], in_=o[:]),
                        waits=[("fes", sr)], inc=f"feu{pp}", incn=16)
                    S.do("dve", lambda o=xf, c=ct: nc.vector.tensor_copy(
                        xi_bf[:, c * TOK:(c + 1) * TOK], o[:]),
                        waits=[("fes", sr), xi_guard], inc="feb")
                    wslot[pp] = ("fei", S.v("fei"))
                    xf_slot[pp] = ((f"feu{pp}", S.v(f"feu{pp}")), ("feb", S.v("feb")))
                    S.mark(f"fe_{dir_}_ct{ct}")
                BF = S.v("feb")
                S.mark(f"fe_ct_{dir_}")
                for mi in range(NMC):
                    o = mi * MM
                    for c in range(16):
                        S.do("pe", lambda cc=c, oo=o, d=dir_: nc.tensor.matmul(
                            f2[0:80, 0:MM], wx_sb[d][:, cc * 96: cc * 96 + 80],
                            xi_bf[:, cc * TOK + oo: cc * TOK + oo + MM],
                            start=(cc == 0), stop=(cc == 15)),
                            waits=([("feb", BF), ("ldc2", LDC2),
                                    ("fxe", S.v("fxe"))] if c == 0 else []),
                            inc=("fxp" if c == 15 else None))
                    S.do("act", lambda oo=o: nc.scalar.activation(
                        dbc_ev[:, oo:oo + MM], f2[0:80, 0:MM], AF.Copy),
                        waits=[("fxp", S.v("fxp")), dbc_guard], inc="fxe")
                    for c in range(16):
                        S.do("pe", lambda cc=c, oo=o, d=dir_: nc.tensor.matmul(
                            f3[0:16, 0:MM], wx_sb[d][:, cc * 96 + 80: cc * 96 + 96],
                            xi_bf[:, cc * TOK + oo: cc * TOK + oo + MM],
                            start=(cc == 0), stop=(cc == 15)),
                            waits=([("fxe2", S.v("fxe2"))] if c == 0 else []),
                            inc=("fxp" if c == 15 else None))
                    S.do("act", lambda oo=o: nc.scalar.activation(
                        dbc_ev2[0:16, oo:oo + MM], f3[0:16, 0:MM], AF.Copy),
                        waits=[("fxp", S.v("fxp")), dbc_guard], inc="fxe2")
                xi_guard = ("fxp", S.v("fxp"))
                S.mark(f"fe_xp_{dir_}")
                S.do("sp", lambda d=dir_: nc.sync.dma_start(
                    out=dbc_in[d].ap()[0:80, :], in_=dbc_ev[:]),
                     waits=[("fxe", S.v("fxe")), ("fxe2", S.v("fxe2"))], inc="fxd", incn=16)
                S.do("sp", lambda d=dir_: nc.sync.dma_start(
                    out=dbc_in[d].ap()[80:96, :], in_=dbc_ev2[:]),
                     inc="fxd", incn=16)
                dbc_guard = ("fxd", S.v("fxd"))
                S.do("pool", lambda d=dir_: nc.gpsimd.collective_compute(
                    "AllGather", ALU.bypass, replica_groups=RG,
                    ins=[u_in[d].ap()], outs=[u_all[d].ap()]),
                    waits=[("feu0", S.v("feu0")), ("feu1", S.v("feu1"))], inc=f"agu{dir_}", cc=True)
                S.do("pool", lambda d=dir_: nc.gpsimd.collective_compute(
                    "AllGather", ALU.bypass, replica_groups=RG,
                    ins=[dbc_in[d].ap()], outs=[dbc_all[d].ap()]),
                    waits=[("fxd", S.v("fxd"))], inc=f"agd{dir_}", cc=True)
                S.mark(f"fe_ag_{dir_}")

    S.mark("fe")
    with ExitStack() as c3:
        sb3 = lambda n, s, dt=F32: c3.enter_context(nc.sbuf_tensor(n, list(s), dt))
        u_sc = sb3("u_sc", (128, NCT * TOK))
        delta = sb3("delta", (128, NCT * TOK))
        du = sb3("du", (128, NCT * TOK))
        spt = sb3("spt", (128, NCT * TOK))
        y_sc = sb3("y_sc", (128, NCT * TOK))
        dt_sb = sb3("dt_sb", (DTR, TOK), BF16)
        Brow = sb3("Brow", (16, TOK))
        Crow = sb3("Crow", (16, TOK))
        dA_t = sb3("dA_t", (128, SC))
        b_t = sb3("b_t", (128, SC))
        h_t = sb3("h_t", (128, SC))
        cm_t = sb3("cm_t", (128, SC))
        with nc.psum_tensor("scps0", [128, MM], F32) as s0, \
             nc.psum_tensor("scps1", [128, MM], F32) as s1, \
             nc.psum_tensor("scpB", [128, SC], F32) as psB, \
             nc.psum_tensor("scpC", [128, SC], F32) as psC:
            S.do("dve", lambda: nc.vector.memset(state[:], 0.0),
                 waits=[("ldc", LDC)], inc="scv")
            prev_consumed = ("scv", 0)
            prev_bc = ("scp", 0)
            for g in range(4):
                for jb, (dir_, b) in enumerate(JOBS):
                    gi = g if dir_ == "f" else 3 - g
                    p = 4 * b + gi
                    dA_gate = ("sce2", S.v("sce2"))
                    fe_gate = ([("aguf", 1), ("agub", 1), ("agdf", 1), ("agdb", 1)]
                               if (g == 0 and jb == 0) else [])
                    for ct in range(NCT):
                        icol = ((0 if dir_ == "f" else 1) * 2 + b) * 4 * NCT + gi * NCT + ct
                        S.do("pool", lambda c=ct, d=dir_, ic=icol: nc.gpsimd.indirect_dma_start(
                            u_sc[:, c * TOK:(c + 1) * TOK], None, u_all[d].ap(),
                            IndirectOffsetOnAxis(idxu_sb[:, ic:ic + 1], 0)),
                            waits=[(f"agu{d}", 1), ("ldc", LDC), prev_consumed, dA_gate] + fe_gate,
                            inc="sclP", incn=16)
                    S.do("pool", lambda d=dir_, pp_=p: nc.gpsimd.dma_start(
                        out=dt_sb[:], in_=dbc_all[d].ap()[96 * pp_: 96 * pp_ + 64, :]),
                        waits=[(f"agd{d}", 1), prev_bc] + fe_gate, inc="sclP", incn=16)
                    S.do("sp", lambda d=dir_, pp_=p: nc.sync.dma_start(
                        out=Brow[:], in_=dbc_all[d].ap()[96 * pp_ + 64: 96 * pp_ + 80, :]),
                        waits=[(f"agd{d}", 1), prev_bc] + fe_gate, inc="sclS", incn=16)
                    S.do("sp", lambda d=dir_, pp_=p: nc.sync.dma_start(
                        out=Crow[:], in_=dbc_all[d].ap()[96 * pp_ + 80: 96 * pp_ + 96, :]),
                        waits=[prev_bc] + fe_gate, inc="sclS", incn=16)
                    LD = None
                    LDP = S.v("sclP")
                    LDS = S.v("sclS")
                    for ct in range(NCT):
                        for mi in range(TOK // MM):
                            o = mi * MM
                            pb = [s0, s1][(ct * (TOK // MM) + mi) % 2]
                            S.do("pe", lambda c=ct, oo=o, d=dir_, p_=pb: nc.tensor.matmul(
                                p_[:, 0:MM], wdt_sb[d][:, c * 128:(c + 1) * 128],
                                dt_sb[:, oo:oo + MM], start=True, stop=True),
                                waits=[("sclP", LDP), ("sce", max(0, S.v("sce") - 2))],
                                inc="scp")
                            S.do("act", lambda c=ct, oo=o, d=dir_, p_=pb: nc.scalar.activation(
                                spt[:, c * TOK + oo: c * TOK + oo + MM], p_[:, 0:MM],
                                AF.Abs, bias=dtb_sb[d][:, c:c + 1]),
                                waits=[("scp", S.v("scp")), prev_consumed], inc="sce")
                            S.do("act", lambda c=ct, oo=o, d=dir_, p_=pb: nc.scalar.activation(
                                delta[:, c * TOK + oo: c * TOK + oo + MM], p_[:, 0:MM],
                                AF.Relu, bias=dtb_sb[d][:, c:c + 1]), inc="sce")
                    SPE = S.v("sce")
                    for ct in range(NCT):
                        S.do("act", lambda c=ct: nc.scalar.activation(
                            spt[:, c * TOK:(c + 1) * TOK], spt[:, c * TOK:(c + 1) * TOK],
                            AF.Exp, scale=cst[:, 2:3]), waits=[("sce", SPE)], inc="sce")
                        S.do("act", lambda c=ct: nc.scalar.activation(
                            spt[:, c * TOK:(c + 1) * TOK], spt[:, c * TOK:(c + 1) * TOK],
                            AF.Ln, bias=cst[:, 3:4]), inc="sce")
                        S.do("dve", lambda c=ct: nc.vector.tensor_tensor(
                            delta[:, c * TOK:(c + 1) * TOK], delta[:, c * TOK:(c + 1) * TOK],
                            spt[:, c * TOK:(c + 1) * TOK], ALU.add),
                            waits=[("sce", S.v("sce"))], inc="scv")
                        S.do("dve", lambda c=ct: nc.vector.tensor_tensor(
                            du[:, c * TOK:(c + 1) * TOK], delta[:, c * TOK:(c + 1) * TOK],
                            u_sc[:, c * TOK:(c + 1) * TOK], ALU.mult),
                            waits=[("sclP", LDP)], inc="scv")
                        S.do("dve", lambda c=ct, d=dir_: nc.vector.tensor_scalar(
                            y_sc[:, c * TOK:(c + 1) * TOK], u_sc[:, c * TOK:(c + 1) * TOK],
                            D_sb[d][:, c:c + 1], None, ALU.mult),
                            waits=[("scy", S.v("scy"))], inc="scv")
                    DLT = S.v("scv")
                    subs = list(range(NSUB)) if dir_ == "f" else list(range(NSUB - 1, -1, -1))
                    for sub in subs:
                        so = sub * SC
                        for n in range(NST):
                            for q in range(SC // MM):
                                S.do("pe", lambda nn=n, qq=q, oo=so: nc.tensor.matmul(
                                    psB[:, qq * MM:(qq + 1) * MM],
                                    sel_sb[:, nn * 128:(nn + 1) * 128],
                                    Brow[:, oo + qq * MM: oo + (qq + 1) * MM],
                                    start=True, stop=True),
                                    waits=[("sclS", LDS), ("scv2", max(0, S.v("scv2") - 4))],
                                    inc="scp")
                                S.do("pe", lambda nn=n, qq=q, oo=so: nc.tensor.matmul(
                                    psC[:, qq * MM:(qq + 1) * MM],
                                    sel_sb[:, nn * 128:(nn + 1) * 128],
                                    Crow[:, oo + qq * MM: oo + (qq + 1) * MM],
                                    start=True, stop=True),
                                    waits=[("scv2", max(0, S.v("scv2") - 1))],
                                    inc="scp")
                            BC = S.v("scp")
                            for ct in range(NCT):
                                sidx = (jb * NST + n) * NCT + ct
                                S.do("act", lambda c=ct, nn=n, d=dir_, oo=so: nc.scalar.activation(
                                    dA_t[:], delta[:, c * TOK + oo: c * TOK + oo + SC],
                                    AF.Exp, scale=A_sb[d][:, c * NST + nn:c * NST + nn + 1]),
                                    waits=[("scv", DLT), ("scv2", max(0, S.v("scv2") - 3))],
                                    inc="sce2")
                                S.do("dve", lambda c=ct, oo=so: nc.vector.tensor_tensor(
                                    b_t[:], du[:, c * TOK + oo: c * TOK + oo + SC],
                                    psB[:], ALU.mult),
                                    waits=[("scp", BC), ("sce2", S.v("sce2"))], inc="scv2")
                                if dir_ == "f":
                                    S.do("dve", lambda s_=sidx: nc.vector.tensor_tensor_scan(
                                        h_t[:], dA_t[:], b_t[:], state[:, s_:s_ + 1],
                                        ALU.mult, ALU.add), inc="scv2")
                                    S.do("dve", lambda s_=sidx: nc.vector.tensor_copy(
                                        state[:, s_:s_ + 1], h_t[:, SC - 1:SC]), inc="scv2")
                                else:
                                    S.do("dve", lambda s_=sidx: nc.vector.tensor_tensor_scan(
                                        h_t[:, ::-1], dA_t[:, ::-1], b_t[:, ::-1],
                                        state[:, s_:s_ + 1], ALU.mult, ALU.add), inc="scv2")
                                    S.do("dve", lambda s_=sidx: nc.vector.tensor_copy(
                                        state[:, s_:s_ + 1], h_t[:, 0:1]), inc="scv2")
                                S.do("dve", lambda: nc.vector.tensor_tensor(
                                    cm_t[:], h_t[:], psC[:], ALU.mult), inc="scv2")
                                S.do("dve", lambda c=ct, oo=so: nc.vector.tensor_tensor(
                                    y_sc[:, c * TOK + oo: c * TOK + oo + SC],
                                    y_sc[:, c * TOK + oo: c * TOK + oo + SC],
                                    cm_t[:], ALU.add), inc="scv2")
                    YV = S.v("scv2")
                    prev_consumed = ("scv2", YV)
                    prev_bc = ("scp", S.v("scp"))
                    for ct in range(NCT):
                        S.do("sp", lambda c=ct, g_=g, j_=jb: nc.sync.dma_start(
                            out=y_in[g_].ap()[j_ * CSL + c * 128: j_ * CSL + (c + 1) * 128, :],
                            in_=y_sc[:, c * TOK:(c + 1) * TOK]),
                            waits=[("scv2", YV)], inc="scy", incn=16)
                S.do("pool", lambda g_=g: nc.gpsimd.collective_compute(
                    "AllGather", ALU.bypass, replica_groups=RG,
                    ins=[y_in[g_].ap()],
                    outs=[y_allb.ap()[g_ * NC * 4 * CSL:(g_ + 1) * NC * 4 * CSL, :]]),
                    waits=[("scy", S.v("scy"))], inc=f"agy{g}", cc=True)

    S.mark("scan")
    YGATE = [(f"agy{g}", 1) for g in range(4)]

    with ExitStack() as c4:
        sb4 = lambda n, s, dt=F32: c4.enter_context(nc.sbuf_tensor(n, list(s), dt))
        w_st = [sb4(f"bwt{i}", (128, 8 * 128), BF16) for i in range(2)]
        sz_t = [sb4(f"szt{i}", (128, TOK)) for i in range(2)]
        ytl = [sb4(f"ytl{i}", (128, TOK)) for i in range(2)]
        yg = sb4("yg", (128, 16 * TOK), BF16)
        wko = sb4("wko", (128, 16 * D), BF16)
        ofev = [sb4(f"ofev{i}", (128, MM), BF16) for i in range(2)]
        with nc.psum_tensor("beps0", [128, MM], F32) as e0, \
             nc.psum_tensor("beps1", [128, MM], F32) as e1, \
             nc.psum_tensor("beps2", [128, MM], F32) as e2, \
             nc.psum_tensor("beps3", [128, MM], F32) as e3:
            wslot = [("bzp", 0), ("bzp", 0)]
            szslot = [("bgt", 0), ("bgt", 0)]
            ytslot = [("bgt", 0), ("bgt", 0)]
            for dir_ in "fb":
                for ct in range(16):
                    pp = ct % 2
                    wt, sz, yt = w_st[pp], sz_t[pp], ytl[pp]
                    S.do("pool", lambda d=dir_, c=ct, w=wt: nc.gpsimd.dma_start(
                        out=w[:].rearrange("p (kt c) -> p kt c", kt=8),
                        in_=w_in_z[d][:, c * 128:(c + 1) * 128]
                        .rearrange("(kt p) c -> p kt c", p=128)),
                        waits=[wslot[pp]] + YGATE, inc=f"bzw{pp}", incn=16)
                    wr = (f"bzw{pp}", S.v(f"bzw{pp}"))
                    S.do("pool", lambda d=dir_, c=ct, yt_=yt: nc.gpsimd.indirect_dma_start(
                        yt_[:], None, y_allb.ap(),
                        IndirectOffsetOnAxis(
                            idxy_sb[:, (0 if d == "f" else 1) * 16 + c:
                                    (0 if d == "f" else 1) * 16 + c + 1], 0)),
                        waits=YGATE + [ytslot[pp]], inc=f"bzl{pp}", incn=16)
                    for mi in range(NMC):
                        o = mi * MM
                        pb = [e0, e1][(ct * NMC + mi) % 2]
                        for k in range(8):
                            S.do("pe", lambda kk=k, oo=o, p_=pb, w=wt: nc.tensor.matmul(
                                p_[:, 0:MM], w[:, kk * 128:(kk + 1) * 128],
                                xn2_t[:, kk * TOKP + PAD + oo: kk * TOKP + PAD + oo + MM],
                                start=(kk == 0), stop=(kk == 7)),
                                waits=([wr, ("bze", max(0, S.v("bze") - 1))]
                                       if k == 0 else []),
                                inc=("bzp" if k == 7 else None))
                        S.do("act", lambda oo=o, p_=pb, s_=sz: nc.scalar.activation(
                            s_[:, oo:oo + MM], p_[:, 0:MM], AF.Silu),
                            waits=[("bzp", S.v("bzp"))] + ([szslot[pp]] if mi == 0 else []),
                            inc="bze")
                    S.do("dve", lambda c=ct, s_=sz, yt_=yt: nc.vector.tensor_tensor(
                        yg[:, c * TOK:(c + 1) * TOK], yt_[:], s_[:], ALU.mult),
                        waits=[("bze", S.v("bze")), (f"bzl{pp}", S.v(f"bzl{pp}")),
                               ("bop", S.v("bop"))], inc="bgt")
                    wslot[pp] = ("bzp", S.v("bzp"))
                    szslot[pp] = ("bgt", S.v("bgt"))
                    ytslot[pp] = ("bgt", S.v("bgt"))
                BGT = S.v("bgt")
                S.do("pool", lambda d=dir_: nc.gpsimd.dma_start(out=wko[:], in_=w_out[d]),
                     waits=[("bop", S.v("bop"))], inc="ffw", incn=16)
                BOW = S.v("ffw")
                for mi in range(NMC):
                    o = mi * MM
                    for dm in range(8):
                        pb = [e2, e3][dm % 2]
                        for k in range(16):
                            S.do("pe", lambda kk=k, oo=o, p_=pb, d_=dm: nc.tensor.matmul(
                                p_[:, 0:MM], wko[:, kk * D + d_ * 128: kk * D + (d_ + 1) * 128],
                                yg[:, kk * TOK + oo: kk * TOK + oo + MM],
                                start=(kk == 0), stop=(kk == 15)),
                                waits=([("bgt", BGT), ("ffw", BOW),
                                        ("boe", max(0, S.v("boe") - 1))] if k == 0 else []),
                                inc=("bop" if k == 15 else None))
                        osl = (mi * 8 + dm) % 2
                        oe = ofev[osl]
                        S.do("dve", lambda p_=pb, e_=oe: nc.vector.tensor_copy(e_[:], p_[:, 0:MM]),
                             waits=[("bop", S.v("bop")), (f"bod{osl}", S.v(f"bod{osl}"))],
                             inc="boe")
                        S.do("sp", lambda oo=o, d_=dm, dd=dir_, e_=oe: nc.sync.dma_start(
                            out=of_sp[dd].ap()[d_ * 128:(d_ + 1) * 128, oo:oo + MM], in_=e_[:]),
                            waits=[("boe", S.v("boe"))], inc=f"bod{osl}", incn=16)
    OFD2 = [("bod0", S.v("bod0")), ("bod1", S.v("bod1"))]
    mid.close()
    with ExitStack() as c4b:
        sb4b = lambda n, s, dt=F32: c4b.enter_context(nc.sbuf_tensor(n, list(s), dt))
        wkoc = sb4b("wko2", (128, 16 * D), BF16)
        ofr = [sb4b(f"ofr{i}", (128, MM), BF16) for i in range(4)]
        x1c = sb4b("x1c", (128, 8 * MM))
        sqc = sb4b("sqc", (128, MM))
        st3 = sb4b("st3", (1, 4 * MM))
        xn3f = sb4b("xn3f", (128, MM))
        xn3b = [sb4b(f"xn3b{i}", (128, MM), BF16) for i in range(2)]
        with nc.psum_tensor("cbps0", [128, MM], F32) as e0, \
             nc.psum_tensor("cbps1", [128, MM], F32) as e1, \
             nc.psum_tensor("cbps2", [128, MM], F32) as e2, \
             nc.psum_tensor("cbps3", [128, MM], F32) as e3:
            S.do("pool", lambda: nc.gpsimd.dma_start(out=wkoc[:], in_=w_comb),
                 waits=[("bop", S.v("bop")), ("bgt", S.v("bgt")), ("bze", S.v("bze")),
                        ("boe", S.v("boe"))] + OFD2 +
                       ([("dbgs", S.v("dbgs"))] if debug else []),
                 inc="ffw", incn=16)
            CBW = S.v("ffw")
            S.do("dve", lambda: nc.vector.memset(colsum[:], 0.0), inc="cls")
            stat_gate = ("l3p", 0)
            x1_guard = ("x1d", 0)
            for mi in range(NMC):
                o = mi * MM
                S.do("sp", lambda oo=o: nc.sync.dma_start(
                    out=x1c[:].rearrange("p (dm c) -> p dm c", dm=8),
                    in_=x1_sp.ap()[:, oo:oo + MM].rearrange("(dm p) c -> p dm c", p=128)),
                    waits=X1Q + [("cbe", max(0, S.v("cbe") - 8)), x1_guard],
                    inc="cbx", incn=16)
                CBX = S.v("cbx")
                for dm in range(8):
                    pb = [e0, e1][dm % 2]
                    for k in range(16):
                        dd = "f" if k < 8 else "b"
                        kk = k % 8
                        sl = k % 4
                        buf = ofr[sl]
                        S.do("sp", lambda d_=dd, k_=kk, oo=o, b_=buf, sl_=sl: nc.sync.dma_start(
                            out=b_[:], in_=of_sp[d_].ap()[k_ * 128:(k_ + 1) * 128, oo:oo + MM]),
                            waits=OFD2 + [("cbp", max(0, S.v("cbp") - 3)),
                                          (f"cbl{sl}", S.v(f"cbl{sl}"))],
                            inc=f"cbl{sl}", incn=16)
                        S.do("pe", lambda k_=k, oo=o, p_=pb, d_=dm, b_=buf: nc.tensor.matmul(
                            p_[:, 0:MM], wkoc[:, k_ * D + d_ * 128: k_ * D + (d_ + 1) * 128],
                            b_[:], start=(k_ == 0), stop=(k_ == 15)),
                            waits=[(f"cbl{sl}", S.v(f"cbl{sl}"))] + ([("ffw", CBW),
                                   ("cbe", max(0, S.v("cbe") - 1))] if k == 0 else []),
                            inc="cbp")
                    S.do("dve", lambda p_=pb, d_=dm: nc.vector.scalar_tensor_tensor(
                        x1c[:, d_ * MM:(d_ + 1) * MM], p_[:, 0:MM], cb_sb[:, d_:d_ + 1],
                        x1c[:, d_ * MM:(d_ + 1) * MM], ALU.add, ALU.add),
                        waits=[("cbp", S.v("cbp")), ("cbx", CBX)], inc="cbe")
                X1R = S.v("cbe")
                for dm in range(8):
                    S.do("act", lambda d_=dm: nc.scalar.activation(
                        sqc[:], x1c[:, d_ * MM:(d_ + 1) * MM], AF.Square),
                        waits=[("cbe", X1R), ("l3q", S.v("l3q"))], inc="l3a")
                    S.do("pe", lambda d_=dm: nc.tensor.matmul(
                        e2[0:1, 0:MM], onescol_sb[:], x1c[:, d_ * MM:(d_ + 1) * MM],
                        start=(d_ == 0), stop=(d_ == 7)),
                        waits=([("cbe", X1R), stat_gate] if dm == 0 else []),
                        inc="l3p" if dm == 7 else None)
                    S.do("pe", lambda d_=dm: nc.tensor.matmul(
                        e3[0:1, 0:MM], onescol_sb[:], sqc[:],
                        start=(d_ == 0), stop=(d_ == 7)),
                        waits=[("l3a", S.v("l3a"))],
                        inc="l3q")
                L3P = [("l3p", S.v("l3p")), ("l3q", S.v("l3q"))]
                S.do("dve", lambda: nc.vector.tensor_scalar(st3[0:1, 0:MM], e2[0:1, 0:MM], INV_D, None, ALU.mult),
                     waits=L3P + [("l3p2", S.v("l3p2"))], inc="l3v")
                S.do("dve", lambda: nc.vector.tensor_scalar(st3[0:1, MM:2 * MM], e3[0:1, 0:MM], INV_D, None, ALU.mult), inc="l3v")
                S.do("dve", lambda: nc.vector.tensor_tensor(st3[0:1, 2 * MM:3 * MM], st3[0:1, 0:MM], st3[0:1, 0:MM], ALU.mult), inc="l3v")
                S.do("dve", lambda: nc.vector.tensor_tensor(st3[0:1, MM:2 * MM], st3[0:1, MM:2 * MM], st3[0:1, 2 * MM:3 * MM], ALU.subtract), inc="l3v")
                S.do("act", lambda: nc.scalar.activation(st3[0:1, MM:2 * MM], st3[0:1, MM:2 * MM], AF.Ln, bias=cst[0:1, 0:1]),
                     waits=[("l3v", S.v("l3v"))], inc="l3a")
                S.do("act", lambda: nc.scalar.activation(st3[0:1, MM:2 * MM], st3[0:1, MM:2 * MM], AF.Exp, scale=cst[0:1, 1:2]), inc="l3a")
                S.do("pe", lambda: nc.tensor.matmul(e2[:, 0:MM], ones1_sb[:], st3[0:1, 0:MM],
                                                    start=True, stop=True),
                     waits=[("l3a", S.v("l3a")), ("l3v", S.v("l3v"))], inc="l3p2")
                S.do("pe", lambda: nc.tensor.matmul(e3[:, 0:MM], ones1_sb[:], st3[0:1, MM:2 * MM],
                                                    start=True, stop=True), inc="l3p2")
                L3B = S.v("l3p2")
                for dm in range(8):
                    xsl = dm % 2
                    xb = xn3b[xsl]
                    S.do("dve", lambda d_=dm: nc.vector.tensor_tensor(
                        xn3f[:], x1c[:, d_ * MM:(d_ + 1) * MM], e2[:, 0:MM], ALU.subtract),
                        waits=[("l3p2", L3B)], inc="l3v")
                    S.do("dve", lambda: nc.vector.tensor_tensor(xn3f[:], xn3f[:], e3[:, 0:MM], ALU.mult), inc="l3v")
                    S.do("dve", lambda d_=dm: nc.vector.tensor_tensor(
                        x1c[:, d_ * MM:(d_ + 1) * MM], x1c[:, d_ * MM:(d_ + 1) * MM], xn3f[:], ALU.add), inc="l3v")
                    S.do("dve", lambda xb_=xb: nc.vector.tensor_copy(xb_[:], xn3f[:]),
                         waits=[(f"l3d{xsl}", S.v(f"l3d{xsl}"))], inc="l3v")
                    S.do("dve", lambda: nc.vector.reduce_sum(stat[:, 0:1], xn3f[:], AX.X), inc="l3v")
                    S.do("dve", lambda d_=dm: nc.vector.tensor_tensor(
                        colsum[:, d_:d_ + 1], colsum[:, d_:d_ + 1], stat[:, 0:1], ALU.add),
                        waits=[("cls", S.v("cls"))], inc="cls")
                    V3 = S.v("l3v")
                    S.do("sp", lambda oo=o, d_=dm, xb_=xb: nc.sync.dma_start(
                        out=xn3_in.ap()[d_ * 128:(d_ + 1) * 128, oo:oo + MM], in_=xb_[:]),
                        waits=[("l3v", V3)], inc=f"l3d{xsl}", incn=16)
                    S.do("sp", lambda oo=o, d_=dm: nc.sync.dma_start(
                        out=x1_sp.ap()[d_ * 128:(d_ + 1) * 128, oo:oo + MM],
                        in_=x1c[:, d_ * MM:(d_ + 1) * MM]),
                        waits=[("l3v", V3)], inc="x1d", incn=16)
                stat_gate = ("l3v", S.v("l3v"))
                x1_guard = ("x1d", S.v("x1d"))
            S.do("dve", lambda: nc.vector.tensor_scalar(
                colsum[:], colsum[:], 1.0 / L, None, ALU.mult),
                waits=[("cls", S.v("cls"))], inc="cls")
            for bb in range(2):
                S.do("dve", lambda b_=bb: nc.vector.tensor_scalar(
                    pooled_sb[:, b_ * 8:(b_ + 1) * 8], colsum[:], bmask_sb[:, b_:b_ + 1], None, ALU.mult),
                    waits=[("cls", S.v("cls"))], inc="cls")
            S.do("sp", lambda: nc.sync.dma_start(out=pooled_in.ap()[0:128, :], in_=pooled_sb[:, 0:8]),
                 waits=[("cls", S.v("cls"))], inc="rtl", incn=16)
            S.do("sp", lambda: nc.sync.dma_start(out=pooled_in.ap()[128:256, :], in_=pooled_sb[:, 8:16]),
                 inc="rtl", incn=16)
            S.do("pool", lambda: nc.gpsimd.collective_compute(
                "AllGather", ALU.bypass, replica_groups=RG,
                ins=[xn3_in.ap()], outs=[xn3_all.ap()]),
                waits=[("l3d0", S.v("l3d0")), ("l3d1", S.v("l3d1"))], inc="agxn", cc=True)
            S.do("pool", lambda: nc.gpsimd.collective_compute(
                "AllReduce", ALU.add, replica_groups=RG,
                ins=[pooled_in.ap()], outs=[pooled_out.ap()]),
                waits=[("rtl", S.v("rtl"))], inc="arpl", cc=True)

    S.mark("be")
    with nc.psum_tensor("rps0", [128, MM], F32) as r0, \
         nc.psum_tensor("rps1", [128, MM], F32) as r1:
        S.do("sp", lambda: nc.sync.dma_start(
            out=pooled_sb[:, 0:8], in_=pooled_out.ap()[0:128, :]),
            waits=[("arpl", 1)], inc="rtl", incn=16)
        S.do("sp", lambda: nc.sync.dma_start(
            out=pooled_sb[:, 8:16], in_=pooled_out.ap()[128:256, :]),
            inc="rtl", incn=16)
        S.do("dve", lambda: nc.vector.tensor_copy(pooled_bf[:], pooled_sb[:]),
             waits=[("rtl", S.v("rtl"))], inc="rtv")
        RTL = S.v("rtv")
        for bb in range(2):
            for cti in range(2):
                for k in range(8):
                    S.do("pe", lambda b_=bb, c=cti, kk=k: nc.tensor.matmul(
                        r0[:, (b_ * 2 + c):(b_ * 2 + c) + 1],
                        w1_sb[:, kk * RH + c * 128: kk * RH + (c + 1) * 128],
                        pooled_bf[:, b_ * 8 + kk: b_ * 8 + kk + 1],
                        start=(kk == 0), stop=(kk == 7)),
                        waits=([("rtv", RTL), ("ldc2", LDC2), ("agxn", 1)] if k == 0 else []),
                        inc=("rtp" if k == 7 else None))
        for col, bcol in [(0, 0), (1, 1), (2, 0), (3, 1)]:
            S.do("act", lambda c=col, b_=bcol: nc.scalar.activation(
                h_sb[:, c:c + 1], r0[:, c:c + 1], AF.Tanh, bias=b1_sb[:, b_:b_ + 1]),
                waits=[("rtp", S.v("rtp"))], inc="rta")
        S.do("dve", lambda: nc.vector.tensor_copy(h_bf[:], h_sb[:]),
             waits=[("rta", S.v("rta"))], inc="rtv")
        RTA = S.v("rtv")
        for bb in range(2):
            for k in range(2):
                S.do("pe", lambda b_=bb, kk=k: nc.tensor.matmul(
                    r1[0:NZ, b_:b_ + 1], w2_sb[:, kk * NZ:(kk + 1) * NZ],
                    h_bf[:, b_ * 2 + kk: b_ * 2 + kk + 1],
                    start=(kk == 0), stop=(kk == 1)),
                    waits=([("rtv", RTA)] if k == 0 else []),
                    inc=("rtp" if k == 1 else None))
        S.do("dve", lambda: nc.vector.tensor_scalar(
            lgsb[:], r1[0:NZ, 0:2], b2_sb[:], None, ALU.add),
             waits=[("rtp", S.v("rtp"))], inc="rta")
        for bb in range(2):
            S.do("sp", lambda b_=bb: nc.sync.dma_start(
                out=lg_bounce.ap()[b_:b_ + 1, :], in_=lgsb[:, b_:b_ + 1]),
                waits=[("rta", S.v("rta"))], inc="rtl", incn=16)
        S.do("sp", lambda: nc.sync.dma_start(
            out=lgr[:], in_=lg_bounce.ap().rearrange("b c -> (b c)").unsqueeze(0)),
             waits=[("rtl", S.v("rtl"))], inc="rtl", incn=16)
        RTL2 = S.v("rtl")
        for bb in range(2):
            S.do("dve", lambda b_=bb: nc.vector.reduce_max(
                rscr[0:1, 0:1], lgr[0:1, b_ * NZ:(b_ + 1) * NZ], AX.X),
                waits=[("rtl", RTL2), ("rta", S.v("rta"))], inc="rtv")
            S.do("dve", lambda: nc.vector.tensor_scalar(rscr[0:1, 1:2], rscr[0:1, 0:1], -1.0, None, ALU.mult), inc="rtv")
            S.do("act", lambda b_=bb: nc.scalar.activation(
                rscr[0:1, 8:16], lgr[0:1, b_ * NZ:(b_ + 1) * NZ], AF.Exp, bias=rscr[0:1, 1:2]),
                waits=[("rtv", S.v("rtv"))], inc="rta")
            S.do("dve", lambda: nc.vector.reduce_sum(rscr[0:1, 2:3], rscr[0:1, 8:16], AX.X),
                 waits=[("rta", S.v("rta"))], inc="rtv")
            S.do("dve", lambda: nc.vector.reciprocal(rscr[0:1, 3:4], rscr[0:1, 2:3]), inc="rtv")
            S.do("dve", lambda: nc.vector.tensor_single_scalar(
                rscr[0:1, 8:16], rscr[0:1, 8:16], rscr[0:1, 3:4], ALU.mult), inc="rtv")
            S.do("dve", lambda: nc.vector.max(out=rscr[0:1, 16:24], in_=rscr[0:1, 8:16]), inc="rtv")
            S.do("dve", lambda: nc.vector.tensor_single_scalar(
                rscr[0:1, 24:32], rscr[0:1, 8:16], rscr[0:1, 18:19], ALU.is_ge), inc="rtv")
            S.do("dve", lambda: nc.vector.tensor_tensor(
                rscr[0:1, 24:32], rscr[0:1, 24:32], rscr[0:1, 8:16], ALU.mult), inc="rtv")
            S.do("dve", lambda: nc.vector.reduce_sum(rscr[0:1, 4:5], rscr[0:1, 24:32], AX.X), inc="rtv")
            S.do("dve", lambda: nc.vector.reciprocal(rscr[0:1, 5:6], rscr[0:1, 4:5]), inc="rtv")
            S.do("dve", lambda b_=bb: nc.vector.tensor_single_scalar(
                wrow[0:1, b_ * NZ:(b_ + 1) * NZ], rscr[0:1, 24:32], rscr[0:1, 5:6], ALU.mult), inc="rtv")
        RWV = S.v("rtv")
        S.do("pe", lambda: nc.tensor.matmul(r0[:, 0:16], ones1_sb[:], wrow[0:1, 0:16],
                                            start=True, stop=True),
             waits=[("rtv", RWV)], inc="rtp")
        S.do("dve", lambda: nc.vector.tensor_copy(w_bc[:], r0[:, 0:16]),
             waits=[("rtp", S.v("rtp"))], inc="rtv")
        for bb in range(2):
            S.do("dve", lambda b_=bb: nc.vector.tensor_tensor(
                zscr[:], w_bc[:, b_ * 8:(b_ + 1) * 8], zmask_sb[:], ALU.mult),
                waits=[("rtv", S.v("rtv"))], inc="rtv")
            S.do("dve", lambda b_=bb: nc.vector.reduce_sum(wz_col[:, b_:b_ + 1], zscr[:], AX.X), inc="rtv")
        WZ = S.v("rtv")
        if debug:
            S.do("pool", lambda: nc.gpsimd.dma_start(out=dbg["w"], in_=wrow[:]),
                 waits=[("rtv", RWV)], inc="dbgs", incn=16)

    S.mark("router")
    with ExitStack() as c6:
        sb6 = lambda n, s, dt=F32: c6.enter_context(nc.sbuf_tensor(n, list(s), dt))
        wa_sb = sb6("wa_sb", (128, 8 * DFF), BF16)
        wb_sb = sb6("wb_sb", (128, 32 * D), BF16)
        xn3r = [sb6(f"xn3r{i}", (128, 8 * MM), BF16) for i in range(2)]
        G_sb = sb6("G_sb", (128, 32 * MM), BF16)
        z_ev = [sb6(f"zev{i}", (128, MM)) for i in range(2)]
        with nc.psum_tensor("fz0", [128, MM], F32) as z0, \
             nc.psum_tensor("fz1", [128, MM], F32) as z1, \
             nc.psum_tensor("fz2", [128, MM], F32) as z2, \
             nc.psum_tensor("fz3", [128, MM], F32) as z3:
            be_exit = [("agxn", 1), ("x1d", S.v("x1d")),
                       ("bod0", S.v("bod0")), ("bod1", S.v("bod1"))]
            if debug:
                be_exit.append(("dbgs", S.v("dbgs")))
            S.do("pool", lambda: nc.gpsimd.dma_start(
                out=wa_sb[:].rearrange("p (kt f) -> p kt f", kt=8),
                in_=wa_d.rearrange("(kt p) f -> p kt f", p=128)),
                waits=be_exit, inc="ffw", incn=16)
            S.do("pool", lambda: nc.gpsimd.dma_start(out=wb_sb[:], in_=wb_d),
                 inc="ffw", incn=16)
            FFW = S.v("ffw")
            prev_wb = ("ffq", 0)
            for fc, (mi_, pdst) in enumerate((m, p) for m in range(NMC)
                                             for p in range(NC)):
                off = mi_ * MM
                pp = fc % 2
                xr, ze = xn3r[pp], z_ev[pp]
                xgate = ("ffp", max(0, S.v("ffp") - 32))
                S.do("sp", lambda p_=pdst, o=off, x=xr: nc.sync.dma_start(
                    out=x[:].rearrange("p (kt c) -> p kt c", kt=8),
                    in_=xn3_all.ap()[p_ * D: (p_ + 1) * D, o:o + MM]
                    .rearrange("(kt p) c -> p kt c", p=128)),
                    waits=[("agxn", 1), xgate, (f"ffl{pp}", S.v(f"ffl{pp}"))],
                    inc=f"ffl{pp}", incn=16)
                FFL = (f"ffl{pp}", S.v(f"ffl{pp}"))
                for ft in range(32):
                    pb = [z0, z1][ft % 2]
                    for k in range(8):
                        S.do("pe", lambda kk=k, f_=ft, p_=pb, x=xr: nc.tensor.matmul(
                            p_[:, 0:MM], wa_sb[:, kk * DFF + f_ * 128: kk * DFF + (f_ + 1) * 128],
                            x[:, kk * MM:(kk + 1) * MM],
                            start=(kk == 0), stop=(kk == 7)),
                            waits=([FFL, ("ffw", FFW),
                                    ("ffg", max(0, S.v("ffg") - 1))] +
                                   ([("rtv", WZ)] if fc == 0 and ft == 0 else [])
                                   if k == 0 else []),
                            inc=("ffp" if k == 7 else None))
                    S.do("act", lambda f_=ft, p_=pb: nc.scalar.activation(
                        G_sb[:, f_ * MM:(f_ + 1) * MM], p_[:, 0:MM], AF.Gelu_apprx_tanh),
                        waits=[("ffp", S.v("ffp"))] + ([prev_wb] if ft == 0 else []),
                        inc="ffg")
                GD = S.v("ffg")
                bb = pdst // 4
                for dm in range(8):
                    pb = [z2, z3][dm % 2]
                    for k in range(32):
                        S.do("pe", lambda kk=k, d_=dm, p_=pb: nc.tensor.matmul(
                            p_[:, 0:MM], wb_sb[:, kk * D + d_ * 128: kk * D + (d_ + 1) * 128],
                            G_sb[:, kk * MM:(kk + 1) * MM],
                            start=(kk == 0), stop=(kk == 31)),
                            waits=([("ffg", GD), ("ffv", max(0, S.v("ffv") - 1))]
                                   if k == 0 else []),
                            inc=("ffq" if k == 31 else None))
                    S.do("dve", lambda p_=pb, b_=bb, z=ze: nc.vector.tensor_scalar(
                        z[:], p_[:, 0:MM], wz_col[:, b_:b_ + 1], None, ALU.mult),
                        waits=[("ffq", S.v("ffq")), ("rtv", WZ),
                               (f"ffd{pp}", S.v(f"ffd{pp}"))], inc="ffv")
                    S.do("sp", lambda p_=pdst, o=off, d_=dm, z=ze: nc.sync.dma_start(
                        out=rs_in[o // MM].ap()[p_ * D + d_ * 128: p_ * D + (d_ + 1) * 128, :],
                        in_=z[:]),
                        waits=[("ffv", S.v("ffv"))], inc=f"ffd{pp}", incn=16)
                prev_wb = ("ffq", S.v("ffq"))
                if pdst == NC - 1:
                    S.do("pool", lambda q_=mi_: nc.gpsimd.collective_compute(
                        "ReduceScatter", ALU.add, replica_groups=RG,
                        ins=[rs_in[q_].ap()], outs=[rs_out[q_].ap()]),
                        waits=[("ffd0", S.v("ffd0")), ("ffd1", S.v("ffd1"))],
                        inc=f"rs{mi_}", cc=True)
        ffn_exit = [("ffp", S.v("ffp")), ("ffq", S.v("ffq")), ("ffg", S.v("ffg")),
                    ("ffv", S.v("ffv")), ("ffd0", S.v("ffd0")), ("ffd1", S.v("ffd1")),
                    ("ffl0", S.v("ffl0")), ("ffl1", S.v("ffl1"))]

    S.mark("ffn")
    with ExitStack() as c7:
        sb7 = lambda n, s, dt=F32: c7.enter_context(nc.sbuf_tensor(n, list(s), dt))
        xf1 = [sb7(f"xf1{i}", (128, 128)) for i in range(2)]
        xrs = [sb7(f"xrs{i}", (128, 128)) for i in range(2)]
        fin = [sb7(f"fin{i}", (128, D)) for i in range(2)]
        with nc.psum_tensor("fp0", [128, MM], F32) as fp0, \
             nc.psum_tensor("fp1", [128, MM], F32) as fp1:
            for tt in range(TOK // 128):
                q = (tt * 128) // MM
                fsl = tt % 2
                ft = fin[fsl]
                for dm in range(8):
                    sl = dm % 2
                    b1_, b2_ = xf1[sl], xrs[sl]
                    pb = [fp0, fp1][sl]
                    S.do("sp", lambda t=tt, d_=dm, b=b1_: nc.sync.dma_start(
                        out=b[:], in_=x1_sp.ap()[d_ * 128:(d_ + 1) * 128, t * 128:(t + 1) * 128]),
                        waits=[(f"rs{q}", 1), ("x1d", S.v("x1d")),
                               ("fnp", max(0, S.v("fnp") - 1))] +
                              (ffn_exit if (tt == 0 and dm == 0) else []),
                        inc=f"fnlA{sl}", incn=16)
                    S.do("sp", lambda t=tt, d_=dm, b=b2_, q_=q: nc.sync.dma_start(
                        out=b[:], in_=rs_out[q_].ap()[d_ * 128:(d_ + 1) * 128,
                                                      (t * 128) % MM:(t * 128) % MM + 128]),
                        waits=[("fna", max(0, S.v("fna") - 1))],
                        inc=f"fnlA{sl}", incn=16)
                    S.do("dve", lambda b=b1_, c=b2_: nc.vector.tensor_tensor(b[:], b[:], c[:], ALU.add),
                         waits=[(f"fnlA{sl}", S.v(f"fnlA{sl}")), (f"fnlA{sl}", S.v(f"fnlA{sl}"))],
                         inc="fna")
                    S.do("pe", lambda b=b1_, p_=pb: nc.tensor.transpose(
                        p_[:, 0:128], b[:], ident_sb[:]),
                        waits=[("fna", S.v("fna")), ("fne", max(0, S.v("fne") - 1))], inc="fnp")
                    S.do("act", lambda d_=dm, f_=ft, p_=pb: nc.scalar.activation(
                        f_[:, d_ * 128:(d_ + 1) * 128], p_[:, 0:128], AF.Copy),
                        waits=[("fnp", S.v("fnp"))] + ([(f"fno{fsl}", S.v(f"fno{fsl}"))] if dm == 0 else []),
                        inc="fne")
                S.do("sp", lambda t=tt, f_=ft: nc.sync.dma_start(
                    out=out_d[t * 128:(t + 1) * 128, :], in_=f_[:]),
                    waits=[("fne", S.v("fne"))], inc=f"fno{fsl}", incn=16)
    FNO = [("fno0", S.v("fno0")), ("fno1", S.v("fno1"))]

    if debug:
        S.do("pool", lambda: nc.gpsimd.dma_start(out=dbg["u_f"], in_=u_all["f"].ap()[0:DI, :]),
             waits=[("aguf", 1)], inc="dbgs", incn=16)
        S.do("pool", lambda: nc.gpsimd.dma_start(out=dbg["dbc_f"], in_=dbc_all["f"].ap()[0:96, :]),
             waits=[("agdf", 1)], inc="dbgs", incn=16)
        for j in range(NC):
            S.do("pool", lambda j_=j: nc.gpsimd.dma_start(
                out=dbg["yf"][j_ * CSL:(j_ + 1) * CSL, :],
                in_=y_allb.ap()[j_ * 4 * CSL: j_ * 4 * CSL + CSL, :]),
                waits=[("agy0", 1)], inc="dbgs", incn=16)
        S.do("pool", lambda: nc.gpsimd.dma_start(out=dbg["x1"], in_=x1_sp.ap()),
             waits=[("l3d0", S.v("l3d0")), ("l3d1", S.v("l3d1")), ("x1d", S.v("x1d"))],
             inc="dbgs", incn=16)
        S.do("pool", lambda: nc.gpsimd.dma_start(out=dbg["xn3"], in_=xn3_all.ap()[0:D, :]),
             waits=[("agxn", 1)], inc="dbgs", incn=16)
        for q in range(NMC):
            S.do("pool", lambda q_=q: nc.gpsimd.dma_start(
                out=dbg["zs"][:, q_ * MM:(q_ + 1) * MM], in_=rs_out[q_].ap()),
                waits=[(f"rs{q_2}", 1) for q_2 in range(NMC)], inc="dbgs", incn=16)

    S.do("pool", lambda: nc.gpsimd.nop(),
         waits=FNO + ([("dbgs", S.v("dbgs"))] if debug else []), inc=None, cc=True)

    if phases is not None:
        S.truncate(phases)
        with nc.sbuf_tensor("stub", [128, D], F32) as stub:
            S.do("dve", lambda: nc.vector.memset(stub[:], 0.0), inc="stbv")
            for tt in range(TOK // 128):
                S.do("sp", lambda t=tt: nc.sync.dma_start(
                    out=out_d[t * 128:(t + 1) * 128, :], in_=stub[:]),
                    waits=[("stbv", 1)], inc="stbd", incn=16)
            S.do("pool", lambda: nc.gpsimd.nop(), waits=[("stbd", S.v("stbd"))], cc=True)

    with nc.Block() as block:
        @block.tensor
        def _(eng):
            S.emit("pe", eng)

        @block.vector
        def _(eng):
            S.emit("dve", eng)

        @block.scalar
        def _(eng):
            S.emit("act", eng)

        @block.sync
        def _(eng):
            S.emit("sp", eng)

        @block.gpsimd
        def _(eng):
            S.emit("pool", eng)

    top.close()
    return nc


def pack_cols(vec):
    v = np.ascontiguousarray(np.asarray(vec, np.float32)).reshape(-1)
    c = v.shape[0] // 128
    return np.ascontiguousarray(v.reshape(c, 128).T)


def pack_blocks(mat):
    m = np.ascontiguousarray(np.asarray(mat, np.float32))
    c, w = m.shape
    nt = c // 128
    return np.ascontiguousarray(m.reshape(nt, 128, w).transpose(1, 0, 2).reshape(128, nt * w))


def _prep_inputs(L, kw):
    f32 = np.float32
    TOK = L // 4
    TOKP = TOK + 2 * PAD
    x = np.asarray(kw["L0_attn_x"], f32)

    def wT(a):
        return np.ascontiguousarray(np.asarray(a, f32).T)

    base = {}
    for d, pre in (("f", "L0_attn_f_"), ("b", "L0_attn_b_")):
        in_w = np.asarray(kw[pre + "in_w"], f32)
        base[f"w_in_xi_{d}"] = wT(in_w[0:DI])
        base[f"w_in_z_{d}"] = wT(in_w[DI:2 * DI])
        base[f"w_x_{d}"] = pack_blocks(wT(np.asarray(kw[pre + "x_w"], f32)))
        base[f"w_out_{d}"] = pack_blocks(wT(np.asarray(kw[pre + "out_w"], f32)))
        base[f"conv_w_{d}"] = pack_blocks(np.asarray(kw[pre + "conv_w"], f32))
        base[f"conv_b_{d}"] = pack_cols(np.asarray(kw[pre + "conv_b"], f32))
    base["w_comb"] = pack_blocks(wT(np.asarray(kw["L0_attn_combine_w"], f32)))
    base["g_col"] = pack_cols(np.asarray(kw["L0_attn_ln_g"], f32))
    base["b_col"] = pack_cols(np.asarray(kw["L0_attn_ln_b"], f32))
    base["cb_col"] = pack_cols(np.asarray(kw["L0_attn_combine_b"], f32))
    base["w1"] = pack_blocks(np.asarray(kw["L0_ffn_W1"], f32))
    base["b1_col"] = pack_cols(np.asarray(kw["L0_ffn_b1"], f32))
    base["w2"] = pack_blocks(np.asarray(kw["L0_ffn_W2"], f32))
    base["b2_col"] = np.ascontiguousarray(np.asarray(kw["L0_ffn_b2"], f32).reshape(NZ, 1))
    sel = np.zeros((16, NST * 128), f32)
    for n in range(NST):
        sel[n, n * 128:(n + 1) * 128] = 1.0
    base["sel"] = sel
    base["ones1"] = np.ones((1, 128), f32)
    base["onescol"] = np.ones((128, 1), f32)
    base["ident"] = np.eye(128, dtype=f32)
    cstv = np.zeros((128, 4), f32)
    cstv[:, 0] = EPS
    cstv[:, 1] = -0.5
    cstv[:, 2] = -1.0
    cstv[:, 3] = 1.0
    base["cst_in"] = cstv

    Wa = np.asarray(kw["L0_ffn_Wa"], f32)
    Wb = np.asarray(kw["L0_ffn_Wb"], f32)
    CSLn = DI // NC
    NCT_ = CSLn // 128
    lane = np.arange(128, dtype=np.int32)

    per_core = []
    for rank in range(NC):
        m = dict(base)
        b = rank // 4
        q = rank % 4
        t0 = q * TOK
        xp = np.zeros((TOKP, D), f32)
        lo = max(0, t0 - PAD)
        hi = min(L, t0 + TOK + PAD)
        xp[lo - (t0 - PAD):hi - (t0 - PAD)] = x[b, lo:hi]
        m["x0"] = xp
        c0 = rank * CSLn
        for d, pre in (("f", "L0_attn_f_"), ("b", "L0_attn_b_")):
            dtw = np.asarray(kw[pre + "dt_w"], f32)
            m[f"w_dt_{d}"] = np.ascontiguousarray(dtw.T[:, c0:c0 + CSLn])
            m[f"dt_b_{d}"] = pack_cols(np.asarray(kw[pre + "dt_b"], f32)[c0:c0 + CSLn])
            m[f"A_{d}"] = pack_blocks(np.asarray(kw[pre + "A_log"], f32)[c0:c0 + CSLn])
            m[f"D_{d}"] = pack_cols(np.asarray(kw[pre + "D"], f32)[c0:c0 + CSLn])
        m["wa"] = np.ascontiguousarray(Wa[rank])
        m["wb"] = pack_blocks(np.ascontiguousarray(Wb[rank]))
        m["edge"] = np.stack([np.full(128, 0.0 if q == 0 else 1.0, f32),
                              np.full(128, 0.0 if q == 3 else 1.0, f32)], axis=1)
        m["bmask"] = np.stack([np.full(128, 1.0 if b == 0 else 0.0, f32),
                               np.full(128, 1.0 if b == 1 else 0.0, f32)], axis=1)
        zm = np.zeros((128, NZ), f32)
        zm[:, rank] = 1.0
        m["zmask"] = zm
        idxu = np.zeros((128, 2 * 2 * 4 * NCT_), np.int32)
        for di in range(2):
            for bb in range(2):
                for gi in range(4):
                    p = 4 * bb + gi
                    for ct in range(NCT_):
                        col = (di * 2 + bb) * 4 * NCT_ + gi * NCT_ + ct
                        idxu[:, col] = p * DI + c0 + ct * 128 + lane
        m["idxu"] = np.ascontiguousarray(idxu)
        idxy = np.zeros((128, 2 * 16), np.int32)
        for di, dch in enumerate("fb"):
            g = q if dch == "f" else 3 - q
            jb = 2 * b + (0 if dch == "f" else 1)
            for rt in range(16):
                j = rt // 2
                ct = rt % 2
                idxy[:, di * 16 + rt] = (g * NC * 4 * CSLn + j * 4 * CSLn +
                                         jb * CSLn + ct * 128 + lane)
        m["idxy"] = np.ascontiguousarray(idxy)
        per_core.append(m)
    return per_core


_CACHE = {}


def kernel(**inputs):
    L = inputs["L0_attn_x"].shape[1]
    TOK = L // 4
    if ("prog", L) not in _CACHE:
        _CACHE[("prog", L)] = build_program(L, debug=False, chain=True)
    nc = _CACHE[("prog", L)]
    in_maps = _prep_inputs(L, inputs)
    res = run_bass_kernel_spmd(nc, in_maps, core_ids=list(range(NC))).results
    out = np.zeros((BATCH, L, D), np.float32)
    for rank in range(NC):
        b, q = rank // 4, rank % 4
        out[b, q * TOK:(q + 1) * TOK, :] = res[rank]["out"]
    return out


# version 06
# speedup vs baseline: 1.2752x; 1.1204x over previous
import numpy as np
import ml_dtypes
import concourse.bass as bass
import concourse.mybir as mybir
from concourse.bass_core_rust import IndirectOffsetOnAxis
from concourse.bass_utils import run_bass_kernel_spmd
from contextlib import ExitStack

F32 = mybir.dt.float32
BF16 = mybir.dt.bfloat16
I32 = mybir.dt.int32
AF = mybir.ActivationFunctionType
ALU = mybir.AluOpType
AX = mybir.AxisListType

NC = 8
D = 1024
DI = 2048
NST = 16
DTR = 64
NZ = 8
DFF = 4096
RH = 256
BATCH = 2
PAD = 128
MM = 512
EPS = 1e-5
ENGINES = ("pe", "dve", "act", "sp", "pool")
JOBS = [("f", 0), ("b", 0), ("f", 1), ("b", 1)]


class Sched:
    def __init__(self, nc, chain=False):
        self.nc = nc
        self.chain = chain
        self.ops = {e: [] for e in ENGINES}
        self.cnt = {}
        self.sems = {}
        self._ctx = []
        self.last_inc = {e: None for e in ENGINES}
        self.marks = {}

    def mark(self, name):
        self.marks[name] = ({e: len(self.ops[e]) for e in ENGINES},
                            dict(self.cnt), dict(self.last_inc))

    def truncate(self, name):
        lens, cnt, last = self.marks[name]
        for e in ENGINES:
            self.ops[e] = self.ops[e][:lens[e]]
        self.cnt = dict(cnt)
        self.last_inc = dict(last)

    def sem(self, name):
        if name not in self.sems:
            c = self.nc.semaphore(name)
            self.sems[name] = c.__enter__()
            self._ctx.append(c)
            self.cnt[name] = 0
        return name

    def do(self, eng, fn, waits=(), inc=None, incn=1, cc=False):
        w = []
        for s, val in waits:
            self.sem(s)
            assert val <= self.cnt[s], f"wait {s}>={val} but only {self.cnt[s]} scheduled"
            if val > 0:
                w.append((s, val))
        if self.chain and eng in ("dve", "act") and self.last_inc[eng]:
            w.append(self.last_inc[eng])
        if inc is not None:
            self.sem(inc)
            self.cnt[inc] += incn
            self.last_inc[eng] = (inc, self.cnt[inc])
        self.ops[eng].append((tuple(w), fn, inc, incn))

    def v(self, name):
        self.sem(name)
        return self.cnt.get(name, 0)

    def emit(self, eng_name, eng):
        for w, fn, inc, incn in self.ops[eng_name]:
            seen = set()
            for s, val in w:
                if (s, val) in seen:
                    continue
                seen.add((s, val))
                eng.wait_ge(self.sems[s], val)
            inst = fn()
            if inc is not None and inst is not None:
                inst.then_inc(self.sems[inc], incn)


def build_program(L, debug=False, chain=False, phases=None):
    TOK = L // 4
    TOKP = TOK + 2 * PAD
    NTT = TOKP // 128
    CSL = DI // NC
    NCT = CSL // 128
    SC = min(1024, TOK)
    NSUB = TOK // SC
    NMC = TOK // MM
    NFC = NC * TOK // MM
    QT = TOK // 4

    nc = bass.Bass("TRN2", target_bir_lowering=False)
    S = Sched(nc, chain=chain)
    RG = [list(range(NC))]

    def inp(name, shape, dtype=F32):
        return nc.dram_tensor(name, list(shape), dtype, kind="ExternalInput").ap()

    x0_d = inp("x0", (TOKP, D))
    w_in_xi = {d: inp(f"w_in_xi_{d}", (D, DI)) for d in "fb"}
    w_in_z = {d: inp(f"w_in_z_{d}", (D, DI)) for d in "fb"}
    w_x = {d: inp(f"w_x_{d}", (128, 16 * 96)) for d in "fb"}
    w_dt = {d: inp(f"w_dt_{d}", (DTR, CSL)) for d in "fb"}
    w_out = {d: inp(f"w_out_{d}", (128, 16 * D)) for d in "fb"}
    w_comb = inp("w_comb", (128, 16 * D))
    conv_w = {d: inp(f"conv_w_{d}", (128, 16 * 4)) for d in "fb"}
    conv_b = {d: inp(f"conv_b_{d}", (128, 16)) for d in "fb"}
    dt_b = {d: inp(f"dt_b_{d}", (128, NCT)) for d in "fb"}
    A_in = {d: inp(f"A_{d}", (128, NCT * NST)) for d in "fb"}
    D_in = {d: inp(f"D_{d}", (128, NCT)) for d in "fb"}
    g_col = inp("g_col", (128, 8))
    b_col = inp("b_col", (128, 8))
    cb_col = inp("cb_col", (128, 8))
    wa_d = inp("wa", (D, DFF))
    wb_d = inp("wb", (128, 32 * D))
    w1_d = inp("w1", (128, 8 * RH))
    b1_d = inp("b1_col", (128, 2))
    w2_d = inp("w2", (128, 2 * NZ))
    b2_d = inp("b2_col", (NZ, 1))
    sel_d = inp("sel", (16, NST * 128))
    ones1_d = inp("ones1", (1, 128))
    onescol_d = inp("onescol", (128, 1))
    ident_d = inp("ident", (128, 128))
    edge_d = inp("edge", (128, 2))
    bmask_d = inp("bmask", (128, 2))
    zmask_d = inp("zmask", (128, 8))
    cst_d = inp("cst_in", (128, 4))
    idxu_d = inp("idxu", (128, 2 * 2 * 4 * NCT), I32)
    idxy_d = inp("idxy", (128, 2 * 16), I32)

    out_d = nc.dram_tensor("out", [TOK, D], F32, kind="ExternalOutput").ap()
    dbg = {}
    if debug:
        for nm, shp in [("xn2", (D, TOKP)), ("u_f", (DI, TOK)), ("dbc_f", (96, TOK)),
                        ("yf", (DI, TOK)), ("x1", (D, TOK)), ("xn3", (D, TOK)),
                        ("w", (1, 2 * NZ)), ("zs", (D, TOK))]:
            dbg[nm] = nc.dram_tensor("dbg_" + nm, list(shp), F32, kind="ExternalOutput").ap()

    dr = lambda name, shape, dtype=F32: nc.dram_tensor(name, list(shape), dtype)
    drs = lambda name, shape, dtype=F32: nc.dram_tensor(name, list(shape), dtype,
                                                        addr_space="Shared")
    u_in = {d: dr(f"u_in_{d}", (DI, TOK), F32) for d in "fb"}
    u_all = {d: drs(f"u_all_{d}", (NC * DI, TOK), F32) for d in "fb"}
    dbc_in = {d: dr(f"dbc_in_{d}", (96, TOK), F32) for d in "fb"}
    dbc_all = {d: drs(f"dbc_all_{d}", (NC * 96, TOK), F32) for d in "fb"}
    y_in = [dr(f"y_in_{g}", (4 * CSL, TOK), F32) for g in range(4)]
    y_allb = drs("y_allb", (4 * NC * 4 * CSL, TOK), F32)
    of_sp = {d: dr(f"of_{d}", (D, TOK), BF16) for d in "fb"}
    xn3_in = dr("xn3_in", (D, TOK), BF16)
    xn3_all = drs("xn3_all", (NC * D, TOK), BF16)
    pooled_in = dr("pooled_in", (BATCH * 128, 8), F32)
    pooled_out = drs("pooled_out", (BATCH * 128, 8), F32)
    x1_sp = dr("x1_sp", (D, TOK), F32)
    xn2_sp = dr("xn2_sp", (D, TOKP), BF16)
    rs_in = [dr(f"rs_in{q}", (NC * D, MM), F32) for q in range(NMC)]
    rs_out = [dr(f"rs_out{q}", (D, MM), F32) for q in range(NMC)]
    lg_bounce = dr("lg_bounce", (BATCH, NZ), F32)

    top = ExitStack()
    sbT = lambda n, s, dt=F32: top.enter_context(nc.sbuf_tensor(n, list(s), dt))

    convw_sb = {d: sbT(f"convw_{d}", (128, 16 * 4)) for d in "fb"}
    convb_sb = {d: sbT(f"convb_{d}", (128, 16)) for d in "fb"}
    dtb_sb = {d: sbT(f"dtb_{d}", (128, NCT)) for d in "fb"}
    A_sb = {d: sbT(f"Am_{d}", (128, NCT * NST)) for d in "fb"}
    D_sb = {d: sbT(f"Dp_{d}", (128, NCT)) for d in "fb"}
    g_sb = sbT("g_sb", (128, 8))
    bcol_sb = sbT("bcol_sb", (128, 8))
    cb_sb = sbT("cb_sb", (128, 8))
    sel_sb = sbT("sel_sb", (16, NST * 128))
    ones1_sb = sbT("ones1_sb", (1, 128))
    onescol_sb = sbT("onescol_sb", (128, 1))
    ident_sb = sbT("ident_sb", (128, 128))
    edge_sb = sbT("edge_sb", (128, 2))
    bmask_sb = sbT("bmask_sb", (128, 2))
    zmask_sb = sbT("zmask_sb", (128, 8))
    cst = sbT("cst", (128, 4))
    idxu_sb = sbT("idxu_sb", (128, 2 * 2 * 4 * NCT), I32)
    idxy_sb = sbT("idxy_sb", (128, 2 * 16), I32)
    wdt_sb = {d: sbT(f"wdt_{d}", (DTR, CSL), BF16) for d in "fb"}
    wx_sb = {d: sbT(f"wx_{d}", (128, 16 * 96), BF16) for d in "fb"}
    w1_sb = sbT("w1_sb", (128, 8 * RH), BF16)
    b1_sb = sbT("b1_sb", (128, 2))
    b2_sb = sbT("b2_sb", (NZ, 1))
    w2_sb = sbT("w2_sb", (128, 2 * NZ), BF16)
    stat = sbT("stat", (128, 16))
    state = sbT("state", (128, 4 * NST * NCT))
    pooled_sb = sbT("pooled_sb", (128, 2 * 8))
    h_sb = sbT("h_sb", (128, 4))
    lgsb = sbT("lgsb", (NZ, 2))
    lgr = sbT("lgr", (1, 2 * NZ))
    rscr = sbT("rscr", (1, 4 * NZ))
    wrow = sbT("wrow", (1, 2 * NZ))
    w_bc = sbT("w_bc", (128, 2 * NZ))
    wz_col = sbT("wz_col", (128, 2))
    zscr = sbT("zscr", (128, 8))
    colsum = sbT("colsum", (128, 8))
    pooled_bf = sbT("pooled_bf", (128, 2 * 8), BF16)
    h_bf = sbT("h_bf", (128, 4), BF16)
    mid = ExitStack()
    xn2_t = mid.enter_context(nc.sbuf_tensor("xn2", [128, 8 * TOKP], BF16))

    for dst, src in [
        (convw_sb["f"][:], conv_w["f"]), (convw_sb["b"][:], conv_w["b"]),
        (convb_sb["f"][:], conv_b["f"]), (convb_sb["b"][:], conv_b["b"]),
        (dtb_sb["f"][:], dt_b["f"]), (dtb_sb["b"][:], dt_b["b"]),
        (D_sb["f"][:], D_in["f"]), (D_sb["b"][:], D_in["b"]),
        (A_sb["f"][:], A_in["f"]), (A_sb["b"][:], A_in["b"]),
        (g_sb[:], g_col), (bcol_sb[:], b_col), (cb_sb[:], cb_col),
        (sel_sb[:], sel_d), (ones1_sb[:], ones1_d), (onescol_sb[:], onescol_d),
        (ident_sb[:], ident_d), (edge_sb[:], edge_d), (bmask_sb[:], bmask_d),
        (zmask_sb[:], zmask_d), (idxu_sb[:], idxu_d), (idxy_sb[:], idxy_d),
        (cst[:], cst_d),
        (b1_sb[:], b1_d), (b2_sb[:], b2_d),
    ]:
        S.do("sp", lambda d=dst, s=src: nc.sync.dma_start(out=d, in_=s), inc="ldc", incn=16)
    for d in "fb":
        S.do("pool", lambda dd=d: nc.gpsimd.dma_start(out=wdt_sb[dd][:], in_=w_dt[dd]), inc="ldc2", incn=16)
        S.do("pool", lambda dd=d: nc.gpsimd.dma_start(out=wx_sb[dd][:], in_=w_x[dd]), inc="ldc2", incn=16)
    S.do("pool", lambda: nc.gpsimd.dma_start(out=w1_sb[:], in_=w1_d), inc="ldc2", incn=16)
    S.do("pool", lambda: nc.gpsimd.dma_start(out=w2_sb[:], in_=w2_d), inc="ldc2", incn=16)
    LDC = S.v("ldc")
    LDC2 = S.v("ldc2")
    for d in "fb":
        S.do("act", lambda dd=d: nc.scalar.activation(A_sb[dd][:], A_sb[dd][:], AF.Exp),
             waits=[("ldc", LDC)], inc="p1a")
        S.do("dve", lambda dd=d: nc.vector.tensor_scalar(
            A_sb[dd][:], A_sb[dd][:], -1.0, None, ALU.mult),
            waits=[("p1a", S.v("p1a"))], inc="p1v")
    AEXP = S.v("p1v")

    INV_D = 1.0 / D
    with ExitStack() as c1:
        sb1 = lambda n, s, dt=F32: c1.enter_context(nc.sbuf_tensor(n, list(s), dt))
        x0_tm = [sb1(f"x0tm{i}", (128, D)) for i in range(2)]
        sq_tm = [sb1(f"sqtm{i}", (128, D)) for i in range(2)]
        xn1_tm = [sb1(f"xn1tm{i}", (128, D)) for i in range(2)]
        x1ev = [sb1(f"x1ev{i}", (128, 128)) for i in range(4)]
        with nc.psum_tensor("p1ps0", [128, MM], F32) as pA, \
             nc.psum_tensor("p1ps1", [128, MM], F32) as pB:
            psb = [pA, pB]
            slot_free = [("p1tp", 0), ("p1tp", 0)]
            x1ev_free = [(f"p1q{i}", 0) for i in range(4)]
            bank_free = [[("p1ev", 0)], [("p1ev", 0)]]
            nev = 0
            for ti in range(NTT):
                pp = ti % 2
                x0t, sqt, xn1t = x0_tm[pp], sq_tm[pp], xn1_tm[pp]
                own = 1 <= ti <= NTT - 2
                S.do("sp", lambda t=ti, b=x0t: nc.sync.dma_start(
                    out=b[:], in_=x0_d[t * 128:(t + 1) * 128, :]),
                    waits=[slot_free[pp]], inc=f"p1x{pp}", incn=16)
                ld = [(f"p1x{pp}", S.v(f"p1x{pp}"))]
                S.do("act", lambda b=x0t, o=sqt: nc.scalar.activation(o[:], b[:], AF.Square),
                     waits=ld + [("p1v", S.v("p1v"))], inc="p1a")
                S.do("dve", lambda b=x0t: nc.vector.reduce_sum(stat[:, 0:1], b[:], AX.X),
                     waits=ld + [("p1a", S.v("p1a") - 1)], inc="p1v")
                S.do("dve", lambda o=sqt: nc.vector.reduce_sum(stat[:, 1:2], o[:], AX.X),
                     waits=[("p1a", S.v("p1a"))], inc="p1v")
                S.do("dve", lambda: nc.vector.tensor_scalar(stat[:, 2:3], stat[:, 0:1], INV_D, None, ALU.mult), inc="p1v")
                S.do("dve", lambda: nc.vector.tensor_scalar(stat[:, 3:4], stat[:, 1:2], INV_D, None, ALU.mult), inc="p1v")
                S.do("dve", lambda: nc.vector.tensor_tensor(stat[:, 4:5], stat[:, 2:3], stat[:, 2:3], ALU.mult), inc="p1v")
                S.do("dve", lambda: nc.vector.tensor_tensor(stat[:, 5:6], stat[:, 3:4], stat[:, 4:5], ALU.subtract), inc="p1v")
                S.do("act", lambda: nc.scalar.activation(stat[:, 6:7], stat[:, 5:6], AF.Ln, bias=cst[:, 0:1]),
                     waits=[("p1v", S.v("p1v"))], inc="p1a")
                S.do("act", lambda: nc.scalar.activation(stat[:, 7:8], stat[:, 6:7], AF.Exp, scale=cst[:, 1:2]), inc="p1a")
                S.do("dve", lambda b=x0t, o=xn1t: nc.vector.tensor_scalar(
                    o[:], b[:], stat[:, 2:3], stat[:, 7:8], ALU.subtract, ALU.mult),
                    waits=[("p1a", S.v("p1a"))], inc="p1v")
                S.do("act", lambda b=xn1t, o=sqt: nc.scalar.activation(o[:], b[:], AF.Square),
                     waits=[("p1v", S.v("p1v"))], inc="p1a")
                S.do("dve", lambda b=xn1t: nc.vector.reduce_sum(stat[:, 8:9], b[:], AX.X), inc="p1v")
                S.do("dve", lambda o=sqt: nc.vector.reduce_sum(stat[:, 9:10], o[:], AX.X),
                     waits=[("p1a", S.v("p1a"))], inc="p1v")
                S.do("dve", lambda: nc.vector.tensor_scalar(stat[:, 10:11], stat[:, 8:9], INV_D, None, ALU.mult), inc="p1v")
                S.do("dve", lambda: nc.vector.tensor_scalar(stat[:, 11:12], stat[:, 9:10], INV_D, None, ALU.mult), inc="p1v")
                S.do("dve", lambda: nc.vector.tensor_tensor(stat[:, 12:13], stat[:, 10:11], stat[:, 10:11], ALU.mult), inc="p1v")
                S.do("dve", lambda: nc.vector.tensor_tensor(stat[:, 13:14], stat[:, 11:12], stat[:, 12:13], ALU.subtract), inc="p1v")
                S.do("act", lambda: nc.scalar.activation(stat[:, 14:15], stat[:, 13:14], AF.Ln, bias=cst[:, 0:1]),
                     waits=[("p1v", S.v("p1v"))], inc="p1a")
                S.do("act", lambda: nc.scalar.activation(stat[:, 15:16], stat[:, 14:15], AF.Exp, scale=cst[:, 1:2]), inc="p1a")
                a2 = [("p1a", S.v("p1a"))]
                if own:
                    S.do("dve", lambda b=x0t, a=xn1t: nc.vector.tensor_tensor(b[:], b[:], a[:], ALU.add),
                         waits=a2, inc="p1v")
                S.do("dve", lambda a=xn1t: nc.vector.tensor_scalar(
                    a[:], a[:], stat[:, 10:11], stat[:, 15:16], ALU.subtract, ALU.mult),
                    waits=a2, inc="p1v")
                vr = [("p1v", S.v("p1v"))]
                for dt_ in range(8):
                    bk = dt_ % 2
                    pb = psb[bk]
                    S.do("pe", lambda a=xn1t, p=pb, d=dt_: nc.tensor.transpose(
                        p[:, 0:128], a[:, d * 128:(d + 1) * 128], ident_sb[:]),
                        waits=[("ldc", LDC)] + vr + bank_free[bk], inc="p1tp")
                    S.do("dve", lambda p=pb, d=dt_, t=ti: nc.vector.tensor_scalar(
                        xn2_t[:, d * TOKP + t * 128: d * TOKP + (t + 1) * 128], p[:, 0:128],
                        g_sb[:, d:d + 1], bcol_sb[:, d:d + 1], ALU.mult, ALU.add),
                        waits=[("p1tp", S.v("p1tp"))], inc="p1ev")
                    bank_free[bk] = [("p1ev", S.v("p1ev"))]
                    if own:
                        ev = x1ev[nev % 4]
                        S.do("pe", lambda b=x0t, p=pb, d=dt_: nc.tensor.transpose(
                            p[:, 128:256], b[:, d * 128:(d + 1) * 128], ident_sb[:]),
                            waits=[("p1ev", S.v("p1ev"))] + bank_free[bk][1:], inc="p1tp")
                        S.do("dve", lambda p=pb, e=ev: nc.vector.tensor_copy(e[:], p[:, 128:256]),
                             waits=[("p1tp", S.v("p1tp")), x1ev_free[nev % 4]], inc="p1ev2")
                        bank_free[bk] = [("p1ev", S.v("p1ev")), ("p1ev2", S.v("p1ev2"))]
                        S.do("sp", lambda e=ev, d=dt_, t=ti: nc.sync.dma_start(
                            out=x1_sp.ap()[d * 128:(d + 1) * 128, (t - 1) * 128:t * 128], in_=e[:]),
                            waits=[("p1ev2", S.v("p1ev2"))], inc=f"p1q{nev % 4}", incn=16)
                        x1ev_free[nev % 4] = (f"p1q{nev % 4}", S.v(f"p1q{nev % 4}"))
                        nev += 1
                slot_free[pp] = ("p1tp", S.v("p1tp"))
    LNEV = S.v("p1ev")
    for dm in range(8):
        S.do("sp", lambda d=dm: nc.sync.dma_start(
            out=xn2_sp.ap()[d * 128:(d + 1) * 128, :],
            in_=xn2_t[:, d * TOKP:(d + 1) * TOKP]),
            waits=[("p1ev", LNEV)], inc="x2d", incn=16)
    X2D = S.v("x2d")
    X1Q = [(f"p1q{i}", S.v(f"p1q{i}")) for i in range(4)]
    P1TPF = S.v("p1tp")
    P1GATE = [("p1tp", P1TPF), ("p1ev", LNEV), ("p1ev2", S.v("p1ev2"))] + X1Q

    if debug:
        for dt_ in range(8):
            S.do("pool", lambda d=dt_: nc.gpsimd.dma_start(
                out=dbg["xn2"][d * 128:(d + 1) * 128, :], in_=xn2_t[:, d * TOKP:(d + 1) * TOKP]),
                waits=[("p1ev", LNEV)], inc="dbgs", incn=16)

    S.mark("p1")
    with ExitStack() as c2:
        sb2 = lambda n, s, dt=F32: c2.enter_context(nc.sbuf_tensor(n, list(s), dt))
        w_ct = [sb2(f"wct{i}", (128, 8 * 128), BF16) for i in range(2)]
        xipre = [sb2(f"xipre{i}", (128, TOKP)) for i in range(2)]
        xi_f32 = [sb2(f"xif{i}", (128, TOK)) for i in range(2)]
        xi_bf = sb2("xibf", (128, 16 * TOK), BF16)
        dbc_ev = sb2("dbcev", (80, TOK))
        dbc_ev2 = sb2("dbcev2", (16, TOK))
        with nc.psum_tensor("feps0", [128, MM], F32) as f0, \
             nc.psum_tensor("feps1", [128, MM], F32) as f1, \
             nc.psum_tensor("feps2", [128, MM], F32) as f2, \
             nc.psum_tensor("feps3", [128, MM], F32) as f3:
            col_chunks = []
            c0 = 0
            while c0 < TOKP:
                cw = min(MM, TOKP - c0)
                col_chunks.append((c0, cw))
                c0 += cw
            xi_guard = ("fxp", 0)
            dbc_guard = ("fxd", 0)
            wslot = [("fei", 0), ("fei", 0)]
            xpre_slot = [("fec", 0), ("fec", 0)]
            xf_slot = [(("feu0", 0), ("feb", 0)), (("feu1", 0), ("feb", 0))]
            for dir_ in "fb":
                for ct in range(16):
                    pp = ct % 2
                    wt, xp, xf = w_ct[pp], xipre[pp], xi_f32[pp]
                    S.do("pool", lambda d=dir_, c=ct, w=wt: nc.gpsimd.dma_start(
                        out=w[:].rearrange("p (kt c) -> p kt c", kt=8),
                        in_=w_in_xi[d][:, c * 128:(c + 1) * 128]
                        .rearrange("(kt p) c -> p kt c", p=128)),
                        waits=[wslot[pp]] + (P1GATE if (dir_ == "f" and ct == 0) else []),
                        inc=f"few{pp}", incn=16)
                    S.mark(f"fw_{dir_}_{ct}")
                    wr = (f"few{pp}", S.v(f"few{pp}"))
                    for ci, (cc0, cw) in enumerate(col_chunks):
                        pb = [f0, f1][(ct * len(col_chunks) + ci) % 2]
                        for k in range(8):
                            S.do("pe", lambda kk=k, o=cc0, w=cw, p=pb, wt_=wt: nc.tensor.matmul(
                                p[:, 0:w], wt_[:, kk * 128:(kk + 1) * 128],
                                xn2_t[:, kk * TOKP + o: kk * TOKP + o + w],
                                start=(kk == 0), stop=(kk == 7)),
                                waits=([wr, ("p1ev", LNEV),
                                        ("fee", max(0, S.v("fee") - 1))] if k == 0 else []),
                                inc=("fei" if k == 7 else None))
                        S.mark(f"fmm_{dir_}_{ct}_{ci}")
                        S.do("act", lambda o=cc0, w=cw, p=pb, t=xp: nc.scalar.activation(
                            t[:, o:o + w], p[:, 0:w], AF.Copy),
                            waits=[("fei", S.v("fei")), xpre_slot[pp]], inc="fee")
                        S.mark(f"fm_{dir_}_{ct}_{ci}")
                    ir = [("fee", S.v("fee")), ("p1v", AEXP)]
                    if dir_ == "f":
                        S.do("dve", lambda t=xp: nc.vector.tensor_scalar(
                            t[:, PAD - 3:PAD], t[:, PAD - 3:PAD], edge_sb[:, 0:1], None, ALU.mult),
                            waits=ir, inc="fec")
                    else:
                        S.do("dve", lambda t=xp: nc.vector.tensor_scalar(
                            t[:, PAD + TOK:PAD + TOK + 3], t[:, PAD + TOK:PAD + TOK + 3],
                            edge_sb[:, 1:2], None, ALU.mult),
                            waits=ir, inc="fec")
                    offs = [PAD - 3, PAD - 2, PAD - 1, PAD] if dir_ == "f" else [PAD, PAD + 1, PAD + 2, PAD + 3]
                    wsel = [0, 1, 2, 3] if dir_ == "f" else [3, 2, 1, 0]
                    S.do("dve", lambda t=xp, o=xf, c=ct, d=dir_, oj=offs[0], wj=wsel[0]: nc.vector.tensor_scalar(
                        o[:], t[:, oj:oj + TOK], convw_sb[d][:, c * 4 + wj:c * 4 + wj + 1], None, ALU.mult),
                        waits=[xf_slot[pp][0]], inc="fec")
                    for j in range(1, 4):
                        S.do("dve", lambda t=xp, o=xf, c=ct, d=dir_, oj=offs[j], wj=wsel[j]: nc.vector.scalar_tensor_tensor(
                            o[:], t[:, oj:oj + TOK], convw_sb[d][:, c * 4 + wj:c * 4 + wj + 1], o[:],
                            ALU.mult, ALU.add), inc="fec")
                    xpre_slot[pp] = ("fec", S.v("fec"))
                    S.mark(f"fc_{dir_}_{ct}")
                    S.do("act", lambda o=xf, c=ct, d=dir_: nc.scalar.activation(
                        o[:], o[:], AF.Silu, bias=convb_sb[d][:, c:c + 1]),
                        waits=[("fec", S.v("fec"))], inc="fes")
                    sr = S.v("fes")
                    S.mark(f"fs_{dir_}_{ct}")
                    S.do("sp", lambda o=xf, c=ct, d=dir_: nc.sync.dma_start(
                        out=u_in[d].ap()[c * 128:(c + 1) * 128, :], in_=o[:]),
                        waits=[("fes", sr)], inc=f"feu{pp}", incn=16)
                    S.do("dve", lambda o=xf, c=ct: nc.vector.tensor_copy(
                        xi_bf[:, c * TOK:(c + 1) * TOK], o[:]),
                        waits=[("fes", sr), xi_guard], inc="feb")
                    wslot[pp] = ("fei", S.v("fei"))
                    xf_slot[pp] = ((f"feu{pp}", S.v(f"feu{pp}")), ("feb", S.v("feb")))
                    S.mark(f"fe_{dir_}_ct{ct}")
                BF = S.v("feb")
                S.mark(f"fe_ct_{dir_}")
                for mi in range(NMC):
                    o = mi * MM
                    for c in range(16):
                        S.do("pe", lambda cc=c, oo=o, d=dir_: nc.tensor.matmul(
                            f2[0:80, 0:MM], wx_sb[d][:, cc * 96: cc * 96 + 80],
                            xi_bf[:, cc * TOK + oo: cc * TOK + oo + MM],
                            start=(cc == 0), stop=(cc == 15)),
                            waits=([("feb", BF), ("ldc2", LDC2),
                                    ("fxe", S.v("fxe"))] if c == 0 else []),
                            inc=("fxp" if c == 15 else None))
                    S.do("act", lambda oo=o: nc.scalar.activation(
                        dbc_ev[:, oo:oo + MM], f2[0:80, 0:MM], AF.Copy),
                        waits=[("fxp", S.v("fxp")), dbc_guard], inc="fxe")
                    for c in range(16):
                        S.do("pe", lambda cc=c, oo=o, d=dir_: nc.tensor.matmul(
                            f3[0:16, 0:MM], wx_sb[d][:, cc * 96 + 80: cc * 96 + 96],
                            xi_bf[:, cc * TOK + oo: cc * TOK + oo + MM],
                            start=(cc == 0), stop=(cc == 15)),
                            waits=([("fxe2", S.v("fxe2"))] if c == 0 else []),
                            inc=("fxp" if c == 15 else None))
                    S.do("act", lambda oo=o: nc.scalar.activation(
                        dbc_ev2[0:16, oo:oo + MM], f3[0:16, 0:MM], AF.Copy),
                        waits=[("fxp", S.v("fxp")), dbc_guard], inc="fxe2")
                xi_guard = ("fxp", S.v("fxp"))
                S.mark(f"fe_xp_{dir_}")
                S.do("sp", lambda d=dir_: nc.sync.dma_start(
                    out=dbc_in[d].ap()[0:80, :], in_=dbc_ev[:]),
                     waits=[("fxe", S.v("fxe")), ("fxe2", S.v("fxe2"))], inc="fxd", incn=16)
                S.do("sp", lambda d=dir_: nc.sync.dma_start(
                    out=dbc_in[d].ap()[80:96, :], in_=dbc_ev2[:]),
                     inc="fxd", incn=16)
                dbc_guard = ("fxd", S.v("fxd"))
                S.do("pool", lambda d=dir_: nc.gpsimd.collective_compute(
                    "AllGather", ALU.bypass, replica_groups=RG,
                    ins=[u_in[d].ap()], outs=[u_all[d].ap()]),
                    waits=[("feu0", S.v("feu0")), ("feu1", S.v("feu1"))], inc=f"agu{dir_}", cc=True)
                S.do("pool", lambda d=dir_: nc.gpsimd.collective_compute(
                    "AllGather", ALU.bypass, replica_groups=RG,
                    ins=[dbc_in[d].ap()], outs=[dbc_all[d].ap()]),
                    waits=[("fxd", S.v("fxd"))], inc=f"agd{dir_}", cc=True)
                S.mark(f"fe_ag_{dir_}")

    S.mark("fe")
    mid.close()
    with ExitStack() as c3:
        sb3 = lambda n, s, dt=F32: c3.enter_context(nc.sbuf_tensor(n, list(s), dt))
        u_sc = sb3("u_sc", (128, NCT * TOK))
        delta = [sb3(f"delta{i}", (128, NCT * TOK)) for i in range(2)]
        du = sb3("du", (128, NCT * TOK))
        spt = sb3("spt", (128, NCT * TOK))
        y_sc = sb3("y_sc", (128, NCT * TOK))
        dt_sb = [sb3(f"dt_sb{i}", (DTR, TOK), BF16) for i in range(2)]
        Brow = [sb3(f"Brow{i}", (16, TOK)) for i in range(2)]
        Crow = [sb3(f"Crow{i}", (16, TOK)) for i in range(2)]
        dA_t = [sb3(f"dA_t{i}", (128, SC)) for i in range(2)]
        b_t = [sb3(f"b_t{i}", (128, SC)) for i in range(2)]
        cm_t = [sb3(f"cm_t{i}", (128, SC)) for i in range(2)]
        h_t = sb3("h_t", (128, SC))
        with nc.psum_tensor("scps0", [128, MM], F32) as s0, \
             nc.psum_tensor("scps1", [128, MM], F32) as s1, \
             nc.psum_tensor("scpB", [128, SC], F32) as psB, \
             nc.psum_tensor("scpC", [128, SC], F32) as psC:
            S.do("dve", lambda: nc.vector.memset(state[:], 0.0),
                 waits=[("ldc", LDC)], inc="scv")
            ITERS = [(g,) + JOBS[jb] + (jb,) for g in range(4) for jb in range(4)]
            NIT = len(ITERS)
            NQ = SC // MM
            carry = {
                "PSBF": 0, "PSCF": 0,
                "TTSD": {0: 0, 1: 0},
                "CMF": {0: 0, 1: 0},
                "BCREAD": [0, 0], "DTREAD": [0, 0],
                "UCONS": 0, "YEV": 0,
            }
            LDP_it = [0] * NIT
            LDS_it = [0] * NIT
            SPLN_it = [0] * NIT
            DLT_it = [0] * NIT

            def emit_loads(it):
                g, dir_, bb_, jb = ITERS[it]
                gi = g if dir_ == "f" else 3 - g
                p = 4 * bb_ + gi
                pp2 = it % 2
                fe_gate = ([("aguf", 1), ("agub", 1), ("agdf", 1), ("agdb", 1),
                            ("x2d", X2D)] if it == 0 else [])
                for ct in range(NCT):
                    icol = ((0 if dir_ == "f" else 1) * 2 + bb_) * 4 * NCT + gi * NCT + ct
                    S.do("pool", lambda c=ct, d=dir_, ic=icol: nc.gpsimd.indirect_dma_start(
                        u_sc[:, c * TOK:(c + 1) * TOK], None, u_all[d].ap(),
                        IndirectOffsetOnAxis(idxu_sb[:, ic:ic + 1], 0)),
                        waits=[(f"agu{d}", 1), ("ldc", LDC), ("scv", carry["UCONS"])] + fe_gate,
                        inc="sclP", incn=16)
                S.do("pool", lambda d=dir_, pp_=p, b_=dt_sb[pp2]: nc.gpsimd.dma_start(
                    out=b_[:], in_=dbc_all[d].ap()[96 * pp_: 96 * pp_ + 64, :]),
                    waits=[(f"agd{d}", 1), ("scp", carry["DTREAD"][pp2])] + fe_gate,
                    inc="sclP", incn=16)
                LDP_it[it] = S.v("sclP")
                S.do("sp", lambda d=dir_, pp_=p, b_=Brow[pp2]: nc.sync.dma_start(
                    out=b_[:], in_=dbc_all[d].ap()[96 * pp_ + 64: 96 * pp_ + 80, :]),
                    waits=[(f"agd{d}", 1), ("scp", carry["BCREAD"][pp2])] + fe_gate,
                    inc="sclS", incn=16)
                S.do("sp", lambda d=dir_, pp_=p, b_=Crow[pp2]: nc.sync.dma_start(
                    out=b_[:], in_=dbc_all[d].ap()[96 * pp_ + 80: 96 * pp_ + 96, :]),
                    waits=[("scp", carry["BCREAD"][pp2])] + fe_gate,
                    inc="sclS", incn=16)
                LDS_it[it] = S.v("sclS")

            def emit_dtproj(it, chunk):
                g, dir_, bb_, jb = ITERS[it]
                pp2 = it % 2
                dl = delta[pp2]
                if chunk < 2:
                    ct = chunk
                    for mi in range(TOK // MM):
                        o = mi * MM
                        pb = [s0, s1][(ct * (TOK // MM) + mi) % 2]
                        S.do("pe", lambda c=ct, oo=o, d=dir_, p_=pb, pq=pp2: nc.tensor.matmul(
                            p_[:, 0:MM], wdt_sb[d][:, c * 128:(c + 1) * 128],
                            dt_sb[pq][:, oo:oo + MM], start=True, stop=True),
                            waits=[("sclP", LDP_it[it]), ("sce", max(0, S.v("sce") - 2))],
                            inc="scp")
                        S.do("act", lambda c=ct, oo=o, d=dir_, p_=pb: nc.scalar.activation(
                            spt[:, c * TOK + oo: c * TOK + oo + MM], p_[:, 0:MM],
                            AF.Abs, bias=dtb_sb[d][:, c:c + 1]),
                            waits=[("scp", S.v("scp")), ("scv", DLT_it[it - 1] if it else 0)],
                            inc="sce")
                        S.do("act", lambda c=ct, oo=o, d=dir_, p_=pb, dl_=dl: nc.scalar.activation(
                            dl_[:, c * TOK + oo: c * TOK + oo + MM], p_[:, 0:MM],
                            AF.Relu, bias=dtb_sb[d][:, c:c + 1]), inc="sce")
                    if ct == 1:
                        carry["DTREAD"][pp2] = S.v("scp")
                else:
                    ct = chunk - 2
                    S.do("act", lambda c=ct: nc.scalar.activation(
                        spt[:, c * TOK:(c + 1) * TOK], spt[:, c * TOK:(c + 1) * TOK],
                        AF.Exp, scale=cst[:, 2:3]), inc="sce")
                    S.do("act", lambda c=ct: nc.scalar.activation(
                        spt[:, c * TOK:(c + 1) * TOK], spt[:, c * TOK:(c + 1) * TOK],
                        AF.Ln, bias=cst[:, 3:4]), inc="sce")
                    if ct == NCT - 1:
                        SPLN_it[it] = S.v("sce")

            def emit_boundary(it):
                g, dir_, bb_, jb = ITERS[it]
                pp2 = it % 2
                dl = delta[pp2]
                for ct in range(NCT):
                    S.do("dve", lambda c=ct, dl_=dl: nc.vector.tensor_tensor(
                        dl_[:, c * TOK:(c + 1) * TOK], dl_[:, c * TOK:(c + 1) * TOK],
                        spt[:, c * TOK:(c + 1) * TOK], ALU.add),
                        waits=[("sce", SPLN_it[it])], inc="scv")
                for ct in range(NCT):
                    S.do("dve", lambda c=ct, dl_=dl: nc.vector.tensor_tensor(
                        du[:, c * TOK:(c + 1) * TOK], dl_[:, c * TOK:(c + 1) * TOK],
                        u_sc[:, c * TOK:(c + 1) * TOK], ALU.mult),
                        waits=[("sclP", LDP_it[it])], inc="scv")
                for ct in range(NCT):
                    S.do("dve", lambda c=ct, d=dir_: nc.vector.tensor_scalar(
                        y_sc[:, c * TOK:(c + 1) * TOK], u_sc[:, c * TOK:(c + 1) * TOK],
                        D_sb[d][:, c:c + 1], None, ALU.mult),
                        waits=[("scy", carry["YEV"])], inc="scv")
                DLT_it[it] = S.v("scv")
                carry["UCONS"] = S.v("scv")

            def emit_inner(it):
                g, dir_, bb_, jb = ITERS[it]
                pp2 = it % 2
                dl = delta[pp2]
                subs = list(range(NSUB)) if dir_ == "f" else list(range(NSUB - 1, -1, -1))
                sn_list = [(sub, n) for sub in subs for n in range(NST)]
                ns = len(sn_list)
                slots = {}
                used = -1
                for w, kind in zip([(2 * ns) // 32, (8 * ns) // 32, (12 * ns) // 32,
                                    (16 * ns) // 32, (20 * ns) // 32],
                                   ["L", 0, 1, 2, 3]):
                    sl = max(w, used + 1)
                    assert sl < ns
                    slots.setdefault(sl, []).append(kind)
                    used = sl
                for si, (sub, n) in enumerate(sn_list):
                    so = sub * SC
                    for q in range(NQ):
                        S.do("pe", lambda nn=n, qq=q, oo=so, b_=Brow[pp2]: nc.tensor.matmul(
                            psB[:, qq * MM:(qq + 1) * MM],
                            sel_sb[:, nn * 128:(nn + 1) * 128],
                            b_[:, oo + qq * MM: oo + (qq + 1) * MM],
                            start=True, stop=True),
                            waits=[("sclS", LDS_it[it]), ("scv2", carry["PSBF"])],
                            inc="scp")
                    for q in range(NQ):
                        S.do("pe", lambda nn=n, qq=q, oo=so, b_=Crow[pp2]: nc.tensor.matmul(
                            psC[:, qq * MM:(qq + 1) * MM],
                            sel_sb[:, nn * 128:(nn + 1) * 128],
                            b_[:, oo + qq * MM: oo + (qq + 1) * MM],
                            start=True, stop=True),
                            waits=[("scv2", carry["PSCF"])], inc="scp")
                    BC = S.v("scp")
                    dav = {}
                    for ct in range(NCT):
                        S.do("act", lambda c=ct, nn=n, d=dir_, oo=so, dl_=dl: nc.scalar.activation(
                            dA_t[c][:], dl_[:, c * TOK + oo: c * TOK + oo + SC],
                            AF.Exp, scale=A_sb[d][:, c * NST + nn:c * NST + nn + 1]),
                            waits=[("scv", DLT_it[it]), ("scv2", carry["TTSD"][ct])],
                            inc="sce2")
                        dav[ct] = S.v("sce2")
                    for ct in range(NCT):
                        S.do("dve", lambda c=ct, oo=so: nc.vector.tensor_tensor(
                            b_t[c][:], du[:, c * TOK + oo: c * TOK + oo + SC],
                            psB[:], ALU.mult),
                            waits=[("scp", BC)], inc="scv2")
                    carry["PSBF"] = S.v("scv2")
                    for ct in range(NCT):
                        sidx = (jb * NST + n) * NCT + ct
                        if dir_ == "f":
                            S.do("dve", lambda s_=sidx, c=ct: nc.vector.tensor_tensor_scan(
                                h_t[:], dA_t[c][:], b_t[c][:], state[:, s_:s_ + 1],
                                ALU.mult, ALU.add),
                                waits=[("sce2", dav[ct])], inc="scv2")
                            carry["TTSD"][ct] = S.v("scv2")
                            S.do("dve", lambda s_=sidx: nc.vector.tensor_copy(
                                state[:, s_:s_ + 1], h_t[:, SC - 1:SC]), inc="scv2")
                        else:
                            S.do("dve", lambda s_=sidx, c=ct: nc.vector.tensor_tensor_scan(
                                h_t[:, ::-1], dA_t[c][:, ::-1], b_t[c][:, ::-1],
                                state[:, s_:s_ + 1], ALU.mult, ALU.add),
                                waits=[("sce2", dav[ct])], inc="scv2")
                            carry["TTSD"][ct] = S.v("scv2")
                            S.do("dve", lambda s_=sidx: nc.vector.tensor_copy(
                                state[:, s_:s_ + 1], h_t[:, 0:1]), inc="scv2")
                        S.do("dve", lambda c=ct: nc.vector.tensor_tensor(
                            cm_t[c][:], h_t[:], psC[:], ALU.mult),
                            waits=[("scg", carry["CMF"][ct])], inc="scv2")
                        S.do("pool", lambda c=ct, oo=so: nc.gpsimd.tensor_tensor(
                            y_sc[:, c * TOK + oo: c * TOK + oo + SC],
                            y_sc[:, c * TOK + oo: c * TOK + oo + SC],
                            cm_t[c][:], ALU.add),
                            waits=[("scv2", S.v("scv2")), ("scg", S.v("scg"))], inc="scg")
                        carry["CMF"][ct] = S.v("scg")
                    carry["PSCF"] = S.v("scv2")
                    if it + 1 < NIT:
                        for kind in slots.get(si, ()):
                            if kind == "L":
                                emit_loads(it + 1)
                            else:
                                emit_dtproj(it + 1, kind)
                carry["BCREAD"][pp2] = S.v("scp")
                SCG = S.v("scg")
                for ct in range(NCT):
                    S.do("sp", lambda c=ct, g_=g, j_=jb: nc.sync.dma_start(
                        out=y_in[g_].ap()[j_ * CSL + c * 128: j_ * CSL + (c + 1) * 128, :],
                        in_=y_sc[:, c * TOK:(c + 1) * TOK]),
                        waits=[("scg", SCG)], inc="scy", incn=16)
                carry["YEV"] = S.v("scy")
                if jb == 3:
                    S.do("pool", lambda g_=g: nc.gpsimd.collective_compute(
                        "AllGather", ALU.bypass, replica_groups=RG,
                        ins=[y_in[g_].ap()],
                        outs=[y_allb.ap()[g_ * NC * 4 * CSL:(g_ + 1) * NC * 4 * CSL, :]]),
                        waits=[("scy", S.v("scy"))], inc=f"agy{g}", cc=True)

            emit_loads(0)
            for ch in range(4):
                emit_dtproj(0, ch)
            for it in range(NIT):
                emit_boundary(it)
                emit_inner(it)
    S.mark("scan")
    YGATE = [(f"agy{g}", 1) for g in range(4)]

    with ExitStack() as c4:
        sb4 = lambda n, s, dt=F32: c4.enter_context(nc.sbuf_tensor(n, list(s), dt))
        w_st = [sb4(f"bwt{i}", (128, 8 * 128), BF16) for i in range(2)]
        xn2b = sb4("xn2b", (128, 8 * TOK), BF16)
        sz_t = [sb4(f"szt{i}", (128, TOK)) for i in range(2)]
        ytl = [sb4(f"ytl{i}", (128, TOK)) for i in range(2)]
        yg = sb4("yg", (128, 16 * TOK), BF16)
        wko = sb4("wko", (128, 16 * D), BF16)
        ofev = [sb4(f"ofev{i}", (128, MM), BF16) for i in range(2)]
        with nc.psum_tensor("beps0", [128, MM], F32) as e0, \
             nc.psum_tensor("beps1", [128, MM], F32) as e1, \
             nc.psum_tensor("beps2", [128, MM], F32) as e2, \
             nc.psum_tensor("beps3", [128, MM], F32) as e3:
            wslot = [("bzp", 0), ("bzp", 0)]
            szslot = [("bgt", 0), ("bgt", 0)]
            ytslot = [("bgt", 0), ("bgt", 0)]
            S.do("sp", lambda: nc.sync.dma_start(
                out=xn2b[:].rearrange("p (dm c) -> p dm c", dm=8),
                in_=xn2_sp.ap()[:, PAD:PAD + TOK]
                .rearrange("(dm p) c -> p dm c", p=128)),
                waits=[("x2d", X2D)] + YGATE, inc="x2l", incn=16)
            X2L = S.v("x2l")
            for dir_ in "fb":
                for ct in range(16):
                    pp = ct % 2
                    wt, sz, yt = w_st[pp], sz_t[pp], ytl[pp]
                    S.do("pool", lambda d=dir_, c=ct, w=wt: nc.gpsimd.dma_start(
                        out=w[:].rearrange("p (kt c) -> p kt c", kt=8),
                        in_=w_in_z[d][:, c * 128:(c + 1) * 128]
                        .rearrange("(kt p) c -> p kt c", p=128)),
                        waits=[wslot[pp]] + YGATE, inc=f"bzw{pp}", incn=16)
                    wr = (f"bzw{pp}", S.v(f"bzw{pp}"))
                    S.do("pool", lambda d=dir_, c=ct, yt_=yt: nc.gpsimd.indirect_dma_start(
                        yt_[:], None, y_allb.ap(),
                        IndirectOffsetOnAxis(
                            idxy_sb[:, (0 if d == "f" else 1) * 16 + c:
                                    (0 if d == "f" else 1) * 16 + c + 1], 0)),
                        waits=YGATE + [ytslot[pp]], inc=f"bzl{pp}", incn=16)
                    for mi in range(NMC):
                        o = mi * MM
                        pb = [e0, e1][(ct * NMC + mi) % 2]
                        for k in range(8):
                            S.do("pe", lambda kk=k, oo=o, p_=pb, w=wt: nc.tensor.matmul(
                                p_[:, 0:MM], w[:, kk * 128:(kk + 1) * 128],
                                xn2b[:, kk * TOK + oo: kk * TOK + oo + MM],
                                start=(kk == 0), stop=(kk == 7)),
                                waits=([wr, ("x2l", X2L),
                                        ("bze", max(0, S.v("bze") - 1))]
                                       if k == 0 else []),
                                inc=("bzp" if k == 7 else None))
                        S.do("act", lambda oo=o, p_=pb, s_=sz: nc.scalar.activation(
                            s_[:, oo:oo + MM], p_[:, 0:MM], AF.Silu),
                            waits=[("bzp", S.v("bzp"))] + ([szslot[pp]] if mi == 0 else []),
                            inc="bze")
                    S.do("dve", lambda c=ct, s_=sz, yt_=yt: nc.vector.tensor_tensor(
                        yg[:, c * TOK:(c + 1) * TOK], yt_[:], s_[:], ALU.mult),
                        waits=[("bze", S.v("bze")), (f"bzl{pp}", S.v(f"bzl{pp}")),
                               ("bop", S.v("bop"))], inc="bgt")
                    wslot[pp] = ("bzp", S.v("bzp"))
                    szslot[pp] = ("bgt", S.v("bgt"))
                    ytslot[pp] = ("bgt", S.v("bgt"))
                BGT = S.v("bgt")
                S.do("pool", lambda d=dir_: nc.gpsimd.dma_start(out=wko[:], in_=w_out[d]),
                     waits=[("bop", S.v("bop"))], inc="ffw", incn=16)
                BOW = S.v("ffw")
                for mi in range(NMC):
                    o = mi * MM
                    for dm in range(8):
                        pb = [e2, e3][dm % 2]
                        for k in range(16):
                            S.do("pe", lambda kk=k, oo=o, p_=pb, d_=dm: nc.tensor.matmul(
                                p_[:, 0:MM], wko[:, kk * D + d_ * 128: kk * D + (d_ + 1) * 128],
                                yg[:, kk * TOK + oo: kk * TOK + oo + MM],
                                start=(kk == 0), stop=(kk == 15)),
                                waits=([("bgt", BGT), ("ffw", BOW),
                                        ("boe", max(0, S.v("boe") - 1))] if k == 0 else []),
                                inc=("bop" if k == 15 else None))
                        osl = (mi * 8 + dm) % 2
                        oe = ofev[osl]
                        S.do("dve", lambda p_=pb, e_=oe: nc.vector.tensor_copy(e_[:], p_[:, 0:MM]),
                             waits=[("bop", S.v("bop")), (f"bod{osl}", S.v(f"bod{osl}"))],
                             inc="boe")
                        S.do("sp", lambda oo=o, d_=dm, dd=dir_, e_=oe: nc.sync.dma_start(
                            out=of_sp[dd].ap()[d_ * 128:(d_ + 1) * 128, oo:oo + MM], in_=e_[:]),
                            waits=[("boe", S.v("boe"))], inc=f"bod{osl}", incn=16)
    OFD2 = [("bod0", S.v("bod0")), ("bod1", S.v("bod1"))]
    with ExitStack() as c4b:
        sb4b = lambda n, s, dt=F32: c4b.enter_context(nc.sbuf_tensor(n, list(s), dt))
        wkoc = sb4b("wko2", (128, 16 * D), BF16)
        ofr = [sb4b(f"ofr{i}", (128, MM), BF16) for i in range(4)]
        x1c = sb4b("x1c", (128, 8 * MM))
        sqc = sb4b("sqc", (128, MM))
        st3 = sb4b("st3", (1, 4 * MM))
        xn3f = sb4b("xn3f", (128, MM))
        xn3b = [sb4b(f"xn3b{i}", (128, MM), BF16) for i in range(2)]
        with nc.psum_tensor("cbps0", [128, MM], F32) as e0, \
             nc.psum_tensor("cbps1", [128, MM], F32) as e1, \
             nc.psum_tensor("cbps2", [128, MM], F32) as e2, \
             nc.psum_tensor("cbps3", [128, MM], F32) as e3:
            S.do("pool", lambda: nc.gpsimd.dma_start(out=wkoc[:], in_=w_comb),
                 waits=[("bop", S.v("bop")), ("bgt", S.v("bgt")), ("bze", S.v("bze")),
                        ("boe", S.v("boe"))] + OFD2 +
                       ([("dbgs", S.v("dbgs"))] if debug else []),
                 inc="ffw", incn=16)
            CBW = S.v("ffw")
            S.do("dve", lambda: nc.vector.memset(colsum[:], 0.0), inc="cls")
            stat_gate = ("l3p", 0)
            x1_guard = ("x1d", 0)
            for mi in range(NMC):
                o = mi * MM
                S.do("sp", lambda oo=o: nc.sync.dma_start(
                    out=x1c[:].rearrange("p (dm c) -> p dm c", dm=8),
                    in_=x1_sp.ap()[:, oo:oo + MM].rearrange("(dm p) c -> p dm c", p=128)),
                    waits=X1Q + [("cbe", max(0, S.v("cbe") - 8)), x1_guard],
                    inc="cbx", incn=16)
                CBX = S.v("cbx")
                for dm in range(8):
                    pb = [e0, e1][dm % 2]
                    for k in range(16):
                        dd = "f" if k < 8 else "b"
                        kk = k % 8
                        sl = k % 4
                        buf = ofr[sl]
                        S.do("sp", lambda d_=dd, k_=kk, oo=o, b_=buf, sl_=sl: nc.sync.dma_start(
                            out=b_[:], in_=of_sp[d_].ap()[k_ * 128:(k_ + 1) * 128, oo:oo + MM]),
                            waits=OFD2 + [("cbp", max(0, S.v("cbp") - 3)),
                                          (f"cbl{sl}", S.v(f"cbl{sl}"))],
                            inc=f"cbl{sl}", incn=16)
                        S.do("pe", lambda k_=k, oo=o, p_=pb, d_=dm, b_=buf: nc.tensor.matmul(
                            p_[:, 0:MM], wkoc[:, k_ * D + d_ * 128: k_ * D + (d_ + 1) * 128],
                            b_[:], start=(k_ == 0), stop=(k_ == 15)),
                            waits=[(f"cbl{sl}", S.v(f"cbl{sl}"))] + ([("ffw", CBW),
                                   ("cbe", max(0, S.v("cbe") - 1))] if k == 0 else []),
                            inc="cbp")
                    S.do("dve", lambda p_=pb, d_=dm: nc.vector.scalar_tensor_tensor(
                        x1c[:, d_ * MM:(d_ + 1) * MM], p_[:, 0:MM], cb_sb[:, d_:d_ + 1],
                        x1c[:, d_ * MM:(d_ + 1) * MM], ALU.add, ALU.add),
                        waits=[("cbp", S.v("cbp")), ("cbx", CBX)], inc="cbe")
                X1R = S.v("cbe")
                for dm in range(8):
                    S.do("act", lambda d_=dm: nc.scalar.activation(
                        sqc[:], x1c[:, d_ * MM:(d_ + 1) * MM], AF.Square),
                        waits=[("cbe", X1R), ("l3q", S.v("l3q"))], inc="l3a")
                    S.do("pe", lambda d_=dm: nc.tensor.matmul(
                        e2[0:1, 0:MM], onescol_sb[:], x1c[:, d_ * MM:(d_ + 1) * MM],
                        start=(d_ == 0), stop=(d_ == 7)),
                        waits=([("cbe", X1R), stat_gate] if dm == 0 else []),
                        inc="l3p" if dm == 7 else None)
                    S.do("pe", lambda d_=dm: nc.tensor.matmul(
                        e3[0:1, 0:MM], onescol_sb[:], sqc[:],
                        start=(d_ == 0), stop=(d_ == 7)),
                        waits=[("l3a", S.v("l3a"))],
                        inc="l3q")
                L3P = [("l3p", S.v("l3p")), ("l3q", S.v("l3q"))]
                S.do("dve", lambda: nc.vector.tensor_scalar(st3[0:1, 0:MM], e2[0:1, 0:MM], INV_D, None, ALU.mult),
                     waits=L3P + [("l3p2", S.v("l3p2"))], inc="l3v")
                S.do("dve", lambda: nc.vector.tensor_scalar(st3[0:1, MM:2 * MM], e3[0:1, 0:MM], INV_D, None, ALU.mult), inc="l3v")
                S.do("dve", lambda: nc.vector.tensor_tensor(st3[0:1, 2 * MM:3 * MM], st3[0:1, 0:MM], st3[0:1, 0:MM], ALU.mult), inc="l3v")
                S.do("dve", lambda: nc.vector.tensor_tensor(st3[0:1, MM:2 * MM], st3[0:1, MM:2 * MM], st3[0:1, 2 * MM:3 * MM], ALU.subtract), inc="l3v")
                S.do("act", lambda: nc.scalar.activation(st3[0:1, MM:2 * MM], st3[0:1, MM:2 * MM], AF.Ln, bias=cst[0:1, 0:1]),
                     waits=[("l3v", S.v("l3v"))], inc="l3a")
                S.do("act", lambda: nc.scalar.activation(st3[0:1, MM:2 * MM], st3[0:1, MM:2 * MM], AF.Exp, scale=cst[0:1, 1:2]), inc="l3a")
                S.do("pe", lambda: nc.tensor.matmul(e2[:, 0:MM], ones1_sb[:], st3[0:1, 0:MM],
                                                    start=True, stop=True),
                     waits=[("l3a", S.v("l3a")), ("l3v", S.v("l3v"))], inc="l3p2")
                S.do("pe", lambda: nc.tensor.matmul(e3[:, 0:MM], ones1_sb[:], st3[0:1, MM:2 * MM],
                                                    start=True, stop=True), inc="l3p2")
                L3B = S.v("l3p2")
                for dm in range(8):
                    xsl = dm % 2
                    xb = xn3b[xsl]
                    S.do("dve", lambda d_=dm: nc.vector.tensor_tensor(
                        xn3f[:], x1c[:, d_ * MM:(d_ + 1) * MM], e2[:, 0:MM], ALU.subtract),
                        waits=[("l3p2", L3B)], inc="l3v")
                    S.do("dve", lambda: nc.vector.tensor_tensor(xn3f[:], xn3f[:], e3[:, 0:MM], ALU.mult), inc="l3v")
                    S.do("dve", lambda d_=dm: nc.vector.tensor_tensor(
                        x1c[:, d_ * MM:(d_ + 1) * MM], x1c[:, d_ * MM:(d_ + 1) * MM], xn3f[:], ALU.add), inc="l3v")
                    S.do("dve", lambda xb_=xb: nc.vector.tensor_copy(xb_[:], xn3f[:]),
                         waits=[(f"l3d{xsl}", S.v(f"l3d{xsl}"))], inc="l3v")
                    S.do("dve", lambda: nc.vector.reduce_sum(stat[:, 0:1], xn3f[:], AX.X), inc="l3v")
                    S.do("dve", lambda d_=dm: nc.vector.tensor_tensor(
                        colsum[:, d_:d_ + 1], colsum[:, d_:d_ + 1], stat[:, 0:1], ALU.add),
                        waits=[("cls", S.v("cls"))], inc="cls")
                    V3 = S.v("l3v")
                    S.do("sp", lambda oo=o, d_=dm, xb_=xb: nc.sync.dma_start(
                        out=xn3_in.ap()[d_ * 128:(d_ + 1) * 128, oo:oo + MM], in_=xb_[:]),
                        waits=[("l3v", V3)], inc=f"l3d{xsl}", incn=16)
                    S.do("sp", lambda oo=o, d_=dm: nc.sync.dma_start(
                        out=x1_sp.ap()[d_ * 128:(d_ + 1) * 128, oo:oo + MM],
                        in_=x1c[:, d_ * MM:(d_ + 1) * MM]),
                        waits=[("l3v", V3)], inc="x1d", incn=16)
                stat_gate = ("l3v", S.v("l3v"))
                x1_guard = ("x1d", S.v("x1d"))
            S.do("dve", lambda: nc.vector.tensor_scalar(
                colsum[:], colsum[:], 1.0 / L, None, ALU.mult),
                waits=[("cls", S.v("cls"))], inc="cls")
            for bb in range(2):
                S.do("dve", lambda b_=bb: nc.vector.tensor_scalar(
                    pooled_sb[:, b_ * 8:(b_ + 1) * 8], colsum[:], bmask_sb[:, b_:b_ + 1], None, ALU.mult),
                    waits=[("cls", S.v("cls"))], inc="cls")
            S.do("sp", lambda: nc.sync.dma_start(out=pooled_in.ap()[0:128, :], in_=pooled_sb[:, 0:8]),
                 waits=[("cls", S.v("cls"))], inc="rtl", incn=16)
            S.do("sp", lambda: nc.sync.dma_start(out=pooled_in.ap()[128:256, :], in_=pooled_sb[:, 8:16]),
                 inc="rtl", incn=16)
            S.do("pool", lambda: nc.gpsimd.collective_compute(
                "AllGather", ALU.bypass, replica_groups=RG,
                ins=[xn3_in.ap()], outs=[xn3_all.ap()]),
                waits=[("l3d0", S.v("l3d0")), ("l3d1", S.v("l3d1"))], inc="agxn", cc=True)
            S.do("pool", lambda: nc.gpsimd.collective_compute(
                "AllReduce", ALU.add, replica_groups=RG,
                ins=[pooled_in.ap()], outs=[pooled_out.ap()]),
                waits=[("rtl", S.v("rtl"))], inc="arpl", cc=True)

    S.mark("be")
    with nc.psum_tensor("rps0", [128, MM], F32) as r0, \
         nc.psum_tensor("rps1", [128, MM], F32) as r1:
        S.do("sp", lambda: nc.sync.dma_start(
            out=pooled_sb[:, 0:8], in_=pooled_out.ap()[0:128, :]),
            waits=[("arpl", 1)], inc="rtl", incn=16)
        S.do("sp", lambda: nc.sync.dma_start(
            out=pooled_sb[:, 8:16], in_=pooled_out.ap()[128:256, :]),
            inc="rtl", incn=16)
        S.do("dve", lambda: nc.vector.tensor_copy(pooled_bf[:], pooled_sb[:]),
             waits=[("rtl", S.v("rtl"))], inc="rtv")
        RTL = S.v("rtv")
        for bb in range(2):
            for cti in range(2):
                for k in range(8):
                    S.do("pe", lambda b_=bb, c=cti, kk=k: nc.tensor.matmul(
                        r0[:, (b_ * 2 + c):(b_ * 2 + c) + 1],
                        w1_sb[:, kk * RH + c * 128: kk * RH + (c + 1) * 128],
                        pooled_bf[:, b_ * 8 + kk: b_ * 8 + kk + 1],
                        start=(kk == 0), stop=(kk == 7)),
                        waits=([("rtv", RTL), ("ldc2", LDC2), ("agxn", 1)] if k == 0 else []),
                        inc=("rtp" if k == 7 else None))
        for col, bcol in [(0, 0), (1, 1), (2, 0), (3, 1)]:
            S.do("act", lambda c=col, b_=bcol: nc.scalar.activation(
                h_sb[:, c:c + 1], r0[:, c:c + 1], AF.Tanh, bias=b1_sb[:, b_:b_ + 1]),
                waits=[("rtp", S.v("rtp"))], inc="rta")
        S.do("dve", lambda: nc.vector.tensor_copy(h_bf[:], h_sb[:]),
             waits=[("rta", S.v("rta"))], inc="rtv")
        RTA = S.v("rtv")
        for bb in range(2):
            for k in range(2):
                S.do("pe", lambda b_=bb, kk=k: nc.tensor.matmul(
                    r1[0:NZ, b_:b_ + 1], w2_sb[:, kk * NZ:(kk + 1) * NZ],
                    h_bf[:, b_ * 2 + kk: b_ * 2 + kk + 1],
                    start=(kk == 0), stop=(kk == 1)),
                    waits=([("rtv", RTA)] if k == 0 else []),
                    inc=("rtp" if k == 1 else None))
        S.do("dve", lambda: nc.vector.tensor_scalar(
            lgsb[:], r1[0:NZ, 0:2], b2_sb[:], None, ALU.add),
             waits=[("rtp", S.v("rtp"))], inc="rta")
        for bb in range(2):
            S.do("sp", lambda b_=bb: nc.sync.dma_start(
                out=lg_bounce.ap()[b_:b_ + 1, :], in_=lgsb[:, b_:b_ + 1]),
                waits=[("rta", S.v("rta"))], inc="rtl", incn=16)
        S.do("sp", lambda: nc.sync.dma_start(
            out=lgr[:], in_=lg_bounce.ap().rearrange("b c -> (b c)").unsqueeze(0)),
             waits=[("rtl", S.v("rtl"))], inc="rtl", incn=16)
        RTL2 = S.v("rtl")
        for bb in range(2):
            S.do("dve", lambda b_=bb: nc.vector.reduce_max(
                rscr[0:1, 0:1], lgr[0:1, b_ * NZ:(b_ + 1) * NZ], AX.X),
                waits=[("rtl", RTL2), ("rta", S.v("rta"))], inc="rtv")
            S.do("dve", lambda: nc.vector.tensor_scalar(rscr[0:1, 1:2], rscr[0:1, 0:1], -1.0, None, ALU.mult), inc="rtv")
            S.do("act", lambda b_=bb: nc.scalar.activation(
                rscr[0:1, 8:16], lgr[0:1, b_ * NZ:(b_ + 1) * NZ], AF.Exp, bias=rscr[0:1, 1:2]),
                waits=[("rtv", S.v("rtv"))], inc="rta")
            S.do("dve", lambda: nc.vector.reduce_sum(rscr[0:1, 2:3], rscr[0:1, 8:16], AX.X),
                 waits=[("rta", S.v("rta"))], inc="rtv")
            S.do("dve", lambda: nc.vector.reciprocal(rscr[0:1, 3:4], rscr[0:1, 2:3]), inc="rtv")
            S.do("dve", lambda: nc.vector.tensor_single_scalar(
                rscr[0:1, 8:16], rscr[0:1, 8:16], rscr[0:1, 3:4], ALU.mult), inc="rtv")
            S.do("dve", lambda: nc.vector.max(out=rscr[0:1, 16:24], in_=rscr[0:1, 8:16]), inc="rtv")
            S.do("dve", lambda: nc.vector.tensor_single_scalar(
                rscr[0:1, 24:32], rscr[0:1, 8:16], rscr[0:1, 18:19], ALU.is_ge), inc="rtv")
            S.do("dve", lambda: nc.vector.tensor_tensor(
                rscr[0:1, 24:32], rscr[0:1, 24:32], rscr[0:1, 8:16], ALU.mult), inc="rtv")
            S.do("dve", lambda: nc.vector.reduce_sum(rscr[0:1, 4:5], rscr[0:1, 24:32], AX.X), inc="rtv")
            S.do("dve", lambda: nc.vector.reciprocal(rscr[0:1, 5:6], rscr[0:1, 4:5]), inc="rtv")
            S.do("dve", lambda b_=bb: nc.vector.tensor_single_scalar(
                wrow[0:1, b_ * NZ:(b_ + 1) * NZ], rscr[0:1, 24:32], rscr[0:1, 5:6], ALU.mult), inc="rtv")
        RWV = S.v("rtv")
        S.do("pe", lambda: nc.tensor.matmul(r0[:, 0:16], ones1_sb[:], wrow[0:1, 0:16],
                                            start=True, stop=True),
             waits=[("rtv", RWV)], inc="rtp")
        S.do("dve", lambda: nc.vector.tensor_copy(w_bc[:], r0[:, 0:16]),
             waits=[("rtp", S.v("rtp"))], inc="rtv")
        for bb in range(2):
            S.do("dve", lambda b_=bb: nc.vector.tensor_tensor(
                zscr[:], w_bc[:, b_ * 8:(b_ + 1) * 8], zmask_sb[:], ALU.mult),
                waits=[("rtv", S.v("rtv"))], inc="rtv")
            S.do("dve", lambda b_=bb: nc.vector.reduce_sum(wz_col[:, b_:b_ + 1], zscr[:], AX.X), inc="rtv")
        WZ = S.v("rtv")
        if debug:
            S.do("pool", lambda: nc.gpsimd.dma_start(out=dbg["w"], in_=wrow[:]),
                 waits=[("rtv", RWV)], inc="dbgs", incn=16)

    S.mark("router")
    with ExitStack() as c6:
        sb6 = lambda n, s, dt=F32: c6.enter_context(nc.sbuf_tensor(n, list(s), dt))
        wa_sb = sb6("wa_sb", (128, 8 * DFF), BF16)
        wb_sb = sb6("wb_sb", (128, 32 * D), BF16)
        xn3r = [sb6(f"xn3r{i}", (128, 8 * MM), BF16) for i in range(2)]
        G_sb = sb6("G_sb", (128, 32 * MM), BF16)
        z_ev = [sb6(f"zev{i}", (128, MM)) for i in range(2)]
        with nc.psum_tensor("fz0", [128, MM], F32) as z0, \
             nc.psum_tensor("fz1", [128, MM], F32) as z1, \
             nc.psum_tensor("fz2", [128, MM], F32) as z2, \
             nc.psum_tensor("fz3", [128, MM], F32) as z3:
            be_exit = [("agxn", 1), ("x1d", S.v("x1d")),
                       ("bod0", S.v("bod0")), ("bod1", S.v("bod1"))]
            if debug:
                be_exit.append(("dbgs", S.v("dbgs")))
            S.do("pool", lambda: nc.gpsimd.dma_start(
                out=wa_sb[:].rearrange("p (kt f) -> p kt f", kt=8),
                in_=wa_d.rearrange("(kt p) f -> p kt f", p=128)),
                waits=be_exit, inc="ffw", incn=16)
            S.do("pool", lambda: nc.gpsimd.dma_start(out=wb_sb[:], in_=wb_d),
                 inc="ffw", incn=16)
            FFW = S.v("ffw")
            prev_wb = ("ffq", 0)
            for fc, (mi_, pdst) in enumerate((m, p) for m in range(NMC)
                                             for p in range(NC)):
                off = mi_ * MM
                pp = fc % 2
                xr, ze = xn3r[pp], z_ev[pp]
                xgate = ("ffp", max(0, S.v("ffp") - 32))
                S.do("sp", lambda p_=pdst, o=off, x=xr: nc.sync.dma_start(
                    out=x[:].rearrange("p (kt c) -> p kt c", kt=8),
                    in_=xn3_all.ap()[p_ * D: (p_ + 1) * D, o:o + MM]
                    .rearrange("(kt p) c -> p kt c", p=128)),
                    waits=[("agxn", 1), xgate, (f"ffl{pp}", S.v(f"ffl{pp}"))],
                    inc=f"ffl{pp}", incn=16)
                FFL = (f"ffl{pp}", S.v(f"ffl{pp}"))
                for ft in range(32):
                    pb = [z0, z1][ft % 2]
                    for k in range(8):
                        S.do("pe", lambda kk=k, f_=ft, p_=pb, x=xr: nc.tensor.matmul(
                            p_[:, 0:MM], wa_sb[:, kk * DFF + f_ * 128: kk * DFF + (f_ + 1) * 128],
                            x[:, kk * MM:(kk + 1) * MM],
                            start=(kk == 0), stop=(kk == 7)),
                            waits=([FFL, ("ffw", FFW),
                                    ("ffg", max(0, S.v("ffg") - 1))] +
                                   ([("rtv", WZ)] if fc == 0 and ft == 0 else [])
                                   if k == 0 else []),
                            inc=("ffp" if k == 7 else None))
                    S.do("act", lambda f_=ft, p_=pb: nc.scalar.activation(
                        G_sb[:, f_ * MM:(f_ + 1) * MM], p_[:, 0:MM], AF.Gelu_apprx_tanh),
                        waits=[("ffp", S.v("ffp"))] + ([prev_wb] if ft == 0 else []),
                        inc="ffg")
                GD = S.v("ffg")
                bb = pdst // 4
                for dm in range(8):
                    pb = [z2, z3][dm % 2]
                    for k in range(32):
                        S.do("pe", lambda kk=k, d_=dm, p_=pb: nc.tensor.matmul(
                            p_[:, 0:MM], wb_sb[:, kk * D + d_ * 128: kk * D + (d_ + 1) * 128],
                            G_sb[:, kk * MM:(kk + 1) * MM],
                            start=(kk == 0), stop=(kk == 31)),
                            waits=([("ffg", GD), ("ffv", max(0, S.v("ffv") - 1))]
                                   if k == 0 else []),
                            inc=("ffq" if k == 31 else None))
                    S.do("dve", lambda p_=pb, b_=bb, z=ze: nc.vector.tensor_scalar(
                        z[:], p_[:, 0:MM], wz_col[:, b_:b_ + 1], None, ALU.mult),
                        waits=[("ffq", S.v("ffq")), ("rtv", WZ),
                               (f"ffd{pp}", S.v(f"ffd{pp}"))], inc="ffv")
                    S.do("sp", lambda p_=pdst, o=off, d_=dm, z=ze: nc.sync.dma_start(
                        out=rs_in[o // MM].ap()[p_ * D + d_ * 128: p_ * D + (d_ + 1) * 128, :],
                        in_=z[:]),
                        waits=[("ffv", S.v("ffv"))], inc=f"ffd{pp}", incn=16)
                prev_wb = ("ffq", S.v("ffq"))
                if pdst == NC - 1:
                    S.do("pool", lambda q_=mi_: nc.gpsimd.collective_compute(
                        "ReduceScatter", ALU.add, replica_groups=RG,
                        ins=[rs_in[q_].ap()], outs=[rs_out[q_].ap()]),
                        waits=[("ffd0", S.v("ffd0")), ("ffd1", S.v("ffd1"))],
                        inc=f"rs{mi_}", cc=True)
        ffn_exit = [("ffp", S.v("ffp")), ("ffq", S.v("ffq")), ("ffg", S.v("ffg")),
                    ("ffv", S.v("ffv")), ("ffd0", S.v("ffd0")), ("ffd1", S.v("ffd1")),
                    ("ffl0", S.v("ffl0")), ("ffl1", S.v("ffl1"))]

    S.mark("ffn")
    with ExitStack() as c7:
        sb7 = lambda n, s, dt=F32: c7.enter_context(nc.sbuf_tensor(n, list(s), dt))
        xf1 = [sb7(f"xf1{i}", (128, 128)) for i in range(2)]
        xrs = [sb7(f"xrs{i}", (128, 128)) for i in range(2)]
        fin = [sb7(f"fin{i}", (128, D)) for i in range(2)]
        with nc.psum_tensor("fp0", [128, MM], F32) as fp0, \
             nc.psum_tensor("fp1", [128, MM], F32) as fp1:
            for tt in range(TOK // 128):
                q = (tt * 128) // MM
                fsl = tt % 2
                ft = fin[fsl]
                for dm in range(8):
                    sl = dm % 2
                    b1_, b2_ = xf1[sl], xrs[sl]
                    pb = [fp0, fp1][sl]
                    S.do("sp", lambda t=tt, d_=dm, b=b1_: nc.sync.dma_start(
                        out=b[:], in_=x1_sp.ap()[d_ * 128:(d_ + 1) * 128, t * 128:(t + 1) * 128]),
                        waits=[(f"rs{q}", 1), ("x1d", S.v("x1d")),
                               ("fnp", max(0, S.v("fnp") - 1))] +
                              (ffn_exit if (tt == 0 and dm == 0) else []),
                        inc=f"fnlA{sl}", incn=16)
                    S.do("sp", lambda t=tt, d_=dm, b=b2_, q_=q: nc.sync.dma_start(
                        out=b[:], in_=rs_out[q_].ap()[d_ * 128:(d_ + 1) * 128,
                                                      (t * 128) % MM:(t * 128) % MM + 128]),
                        waits=[("fna", max(0, S.v("fna") - 1))],
                        inc=f"fnlA{sl}", incn=16)
                    S.do("dve", lambda b=b1_, c=b2_: nc.vector.tensor_tensor(b[:], b[:], c[:], ALU.add),
                         waits=[(f"fnlA{sl}", S.v(f"fnlA{sl}")), (f"fnlA{sl}", S.v(f"fnlA{sl}"))],
                         inc="fna")
                    S.do("pe", lambda b=b1_, p_=pb: nc.tensor.transpose(
                        p_[:, 0:128], b[:], ident_sb[:]),
                        waits=[("fna", S.v("fna")), ("fne", max(0, S.v("fne") - 1))], inc="fnp")
                    S.do("act", lambda d_=dm, f_=ft, p_=pb: nc.scalar.activation(
                        f_[:, d_ * 128:(d_ + 1) * 128], p_[:, 0:128], AF.Copy),
                        waits=[("fnp", S.v("fnp"))] + ([(f"fno{fsl}", S.v(f"fno{fsl}"))] if dm == 0 else []),
                        inc="fne")
                S.do("sp", lambda t=tt, f_=ft: nc.sync.dma_start(
                    out=out_d[t * 128:(t + 1) * 128, :], in_=f_[:]),
                    waits=[("fne", S.v("fne"))], inc=f"fno{fsl}", incn=16)
    FNO = [("fno0", S.v("fno0")), ("fno1", S.v("fno1"))]

    if debug:
        S.do("pool", lambda: nc.gpsimd.dma_start(out=dbg["u_f"], in_=u_all["f"].ap()[0:DI, :]),
             waits=[("aguf", 1)], inc="dbgs", incn=16)
        S.do("pool", lambda: nc.gpsimd.dma_start(out=dbg["dbc_f"], in_=dbc_all["f"].ap()[0:96, :]),
             waits=[("agdf", 1)], inc="dbgs", incn=16)
        for j in range(NC):
            S.do("pool", lambda j_=j: nc.gpsimd.dma_start(
                out=dbg["yf"][j_ * CSL:(j_ + 1) * CSL, :],
                in_=y_allb.ap()[j_ * 4 * CSL: j_ * 4 * CSL + CSL, :]),
                waits=[("agy0", 1)], inc="dbgs", incn=16)
        S.do("pool", lambda: nc.gpsimd.dma_start(out=dbg["x1"], in_=x1_sp.ap()),
             waits=[("l3d0", S.v("l3d0")), ("l3d1", S.v("l3d1")), ("x1d", S.v("x1d"))],
             inc="dbgs", incn=16)
        S.do("pool", lambda: nc.gpsimd.dma_start(out=dbg["xn3"], in_=xn3_all.ap()[0:D, :]),
             waits=[("agxn", 1)], inc="dbgs", incn=16)
        for q in range(NMC):
            S.do("pool", lambda q_=q: nc.gpsimd.dma_start(
                out=dbg["zs"][:, q_ * MM:(q_ + 1) * MM], in_=rs_out[q_].ap()),
                waits=[(f"rs{q_2}", 1) for q_2 in range(NMC)], inc="dbgs", incn=16)

    S.do("pool", lambda: nc.gpsimd.nop(),
         waits=FNO + ([("dbgs", S.v("dbgs"))] if debug else []), inc=None, cc=True)

    if phases is not None:
        S.truncate(phases)
        with nc.sbuf_tensor("stub", [128, D], F32) as stub:
            S.do("dve", lambda: nc.vector.memset(stub[:], 0.0), inc="stbv")
            for tt in range(TOK // 128):
                S.do("sp", lambda t=tt: nc.sync.dma_start(
                    out=out_d[t * 128:(t + 1) * 128, :], in_=stub[:]),
                    waits=[("stbv", 1)], inc="stbd", incn=16)
            S.do("pool", lambda: nc.gpsimd.nop(), waits=[("stbd", S.v("stbd"))], cc=True)

    with nc.Block() as block:
        @block.tensor
        def _(eng):
            S.emit("pe", eng)

        @block.vector
        def _(eng):
            S.emit("dve", eng)

        @block.scalar
        def _(eng):
            S.emit("act", eng)

        @block.sync
        def _(eng):
            S.emit("sp", eng)

        @block.gpsimd
        def _(eng):
            S.emit("pool", eng)

    top.close()
    return nc


def pack_cols(vec):
    v = np.ascontiguousarray(np.asarray(vec, np.float32)).reshape(-1)
    c = v.shape[0] // 128
    return np.ascontiguousarray(v.reshape(c, 128).T)


def pack_blocks(mat):
    m = np.ascontiguousarray(np.asarray(mat, np.float32))
    c, w = m.shape
    nt = c // 128
    return np.ascontiguousarray(m.reshape(nt, 128, w).transpose(1, 0, 2).reshape(128, nt * w))


def _prep_inputs(L, kw):
    f32 = np.float32
    TOK = L // 4
    TOKP = TOK + 2 * PAD
    x = np.asarray(kw["L0_attn_x"], f32)

    def wT(a):
        return np.ascontiguousarray(np.asarray(a, f32).T)

    base = {}
    for d, pre in (("f", "L0_attn_f_"), ("b", "L0_attn_b_")):
        in_w = np.asarray(kw[pre + "in_w"], f32)
        base[f"w_in_xi_{d}"] = wT(in_w[0:DI])
        base[f"w_in_z_{d}"] = wT(in_w[DI:2 * DI])
        base[f"w_x_{d}"] = pack_blocks(wT(np.asarray(kw[pre + "x_w"], f32)))
        base[f"w_out_{d}"] = pack_blocks(wT(np.asarray(kw[pre + "out_w"], f32)))
        base[f"conv_w_{d}"] = pack_blocks(np.asarray(kw[pre + "conv_w"], f32))
        base[f"conv_b_{d}"] = pack_cols(np.asarray(kw[pre + "conv_b"], f32))
    base["w_comb"] = pack_blocks(wT(np.asarray(kw["L0_attn_combine_w"], f32)))
    base["g_col"] = pack_cols(np.asarray(kw["L0_attn_ln_g"], f32))
    base["b_col"] = pack_cols(np.asarray(kw["L0_attn_ln_b"], f32))
    base["cb_col"] = pack_cols(np.asarray(kw["L0_attn_combine_b"], f32))
    base["w1"] = pack_blocks(np.asarray(kw["L0_ffn_W1"], f32))
    base["b1_col"] = pack_cols(np.asarray(kw["L0_ffn_b1"], f32))
    base["w2"] = pack_blocks(np.asarray(kw["L0_ffn_W2"], f32))
    base["b2_col"] = np.ascontiguousarray(np.asarray(kw["L0_ffn_b2"], f32).reshape(NZ, 1))
    sel = np.zeros((16, NST * 128), f32)
    for n in range(NST):
        sel[n, n * 128:(n + 1) * 128] = 1.0
    base["sel"] = sel
    base["ones1"] = np.ones((1, 128), f32)
    base["onescol"] = np.ones((128, 1), f32)
    base["ident"] = np.eye(128, dtype=f32)
    cstv = np.zeros((128, 4), f32)
    cstv[:, 0] = EPS
    cstv[:, 1] = -0.5
    cstv[:, 2] = -1.0
    cstv[:, 3] = 1.0
    base["cst_in"] = cstv

    Wa = np.asarray(kw["L0_ffn_Wa"], f32)
    Wb = np.asarray(kw["L0_ffn_Wb"], f32)
    CSLn = DI // NC
    NCT_ = CSLn // 128
    lane = np.arange(128, dtype=np.int32)

    per_core = []
    for rank in range(NC):
        m = dict(base)
        b = rank // 4
        q = rank % 4
        t0 = q * TOK
        xp = np.zeros((TOKP, D), f32)
        lo = max(0, t0 - PAD)
        hi = min(L, t0 + TOK + PAD)
        xp[lo - (t0 - PAD):hi - (t0 - PAD)] = x[b, lo:hi]
        m["x0"] = xp
        c0 = rank * CSLn
        for d, pre in (("f", "L0_attn_f_"), ("b", "L0_attn_b_")):
            dtw = np.asarray(kw[pre + "dt_w"], f32)
            m[f"w_dt_{d}"] = np.ascontiguousarray(dtw.T[:, c0:c0 + CSLn])
            m[f"dt_b_{d}"] = pack_cols(np.asarray(kw[pre + "dt_b"], f32)[c0:c0 + CSLn])
            m[f"A_{d}"] = pack_blocks(np.asarray(kw[pre + "A_log"], f32)[c0:c0 + CSLn])
            m[f"D_{d}"] = pack_cols(np.asarray(kw[pre + "D"], f32)[c0:c0 + CSLn])
        m["wa"] = np.ascontiguousarray(Wa[rank])
        m["wb"] = pack_blocks(np.ascontiguousarray(Wb[rank]))
        m["edge"] = np.stack([np.full(128, 0.0 if q == 0 else 1.0, f32),
                              np.full(128, 0.0 if q == 3 else 1.0, f32)], axis=1)
        m["bmask"] = np.stack([np.full(128, 1.0 if b == 0 else 0.0, f32),
                               np.full(128, 1.0 if b == 1 else 0.0, f32)], axis=1)
        zm = np.zeros((128, NZ), f32)
        zm[:, rank] = 1.0
        m["zmask"] = zm
        idxu = np.zeros((128, 2 * 2 * 4 * NCT_), np.int32)
        for di in range(2):
            for bb in range(2):
                for gi in range(4):
                    p = 4 * bb + gi
                    for ct in range(NCT_):
                        col = (di * 2 + bb) * 4 * NCT_ + gi * NCT_ + ct
                        idxu[:, col] = p * DI + c0 + ct * 128 + lane
        m["idxu"] = np.ascontiguousarray(idxu)
        idxy = np.zeros((128, 2 * 16), np.int32)
        for di, dch in enumerate("fb"):
            g = q if dch == "f" else 3 - q
            jb = 2 * b + (0 if dch == "f" else 1)
            for rt in range(16):
                j = rt // 2
                ct = rt % 2
                idxy[:, di * 16 + rt] = (g * NC * 4 * CSLn + j * 4 * CSLn +
                                         jb * CSLn + ct * 128 + lane)
        m["idxy"] = np.ascontiguousarray(idxy)
        per_core.append(m)
    return per_core


_CACHE = {}


def kernel(**inputs):
    L = inputs["L0_attn_x"].shape[1]
    TOK = L // 4
    if ("prog", L) not in _CACHE:
        _CACHE[("prog", L)] = build_program(L, debug=False, chain=True)
    nc = _CACHE[("prog", L)]
    in_maps = _prep_inputs(L, inputs)
    res = run_bass_kernel_spmd(nc, in_maps, core_ids=list(range(NC))).results
    out = np.zeros((BATCH, L, D), np.float32)
    for rank in range(NC):
        b, q = rank // 4, rank % 4
        out[b, q * TOK:(q + 1) * TOK, :] = res[rank]["out"]
    return out


# version 07
# speedup vs baseline: 1.4375x; 1.1272x over previous
import numpy as np
import ml_dtypes
import concourse.bass as bass
import concourse.mybir as mybir
from concourse.bass_core_rust import IndirectOffsetOnAxis
from concourse.bass_utils import run_bass_kernel_spmd
from contextlib import ExitStack

F32 = mybir.dt.float32
BF16 = mybir.dt.bfloat16
I32 = mybir.dt.int32
AF = mybir.ActivationFunctionType
ALU = mybir.AluOpType
AX = mybir.AxisListType

NC = 8
D = 1024
DI = 2048
NST = 16
DTR = 64
NZ = 8
DFF = 4096
RH = 256
BATCH = 2
PAD = 128
MM = 512
EPS = 1e-5
ENGINES = ("pe", "dve", "act", "sp", "pool")
JOBS = [("f", 0), ("b", 0), ("f", 1), ("b", 1)]


class Sched:
    def __init__(self, nc, chain=False):
        self.nc = nc
        self.chain = chain
        self.ops = {e: [] for e in ENGINES}
        self.cnt = {}
        self.sems = {}
        self._ctx = []
        self.last_inc = {e: None for e in ENGINES}
        self.marks = {}

    def mark(self, name):
        self.marks[name] = ({e: len(self.ops[e]) for e in ENGINES},
                            dict(self.cnt), dict(self.last_inc))

    def truncate(self, name):
        lens, cnt, last = self.marks[name]
        for e in ENGINES:
            self.ops[e] = self.ops[e][:lens[e]]
        self.cnt = dict(cnt)
        self.last_inc = dict(last)

    def sem(self, name):
        if name not in self.sems:
            c = self.nc.semaphore(name)
            self.sems[name] = c.__enter__()
            self._ctx.append(c)
            self.cnt[name] = 0
        return name

    def do(self, eng, fn, waits=(), inc=None, incn=1, cc=False):
        w = []
        for s, val in waits:
            self.sem(s)
            assert val <= self.cnt[s], f"wait {s}>={val} but only {self.cnt[s]} scheduled"
            if val > 0:
                w.append((s, val))
        if self.chain and eng in ("dve", "act") and self.last_inc[eng]:
            w.append(self.last_inc[eng])
        if inc is not None:
            self.sem(inc)
            self.cnt[inc] += incn
            self.last_inc[eng] = (inc, self.cnt[inc])
        self.ops[eng].append((tuple(w), fn, inc, incn))

    def v(self, name):
        self.sem(name)
        return self.cnt.get(name, 0)

    def emit(self, eng_name, eng):
        for w, fn, inc, incn in self.ops[eng_name]:
            seen = set()
            for s, val in w:
                if (s, val) in seen:
                    continue
                seen.add((s, val))
                eng.wait_ge(self.sems[s], val)
            inst = fn()
            if inc is not None and inst is not None:
                inst.then_inc(self.sems[inc], incn)


def build_program(L, debug=False, chain=False, phases=None):
    TOK = L // 4
    TOKP = TOK + 2 * PAD
    NTT = TOKP // 128
    CSL = DI // NC
    NCT = CSL // 128
    SC = min(1024, TOK)
    NSUB = TOK // SC
    NMC = TOK // MM
    NFC = NC * TOK // MM
    QT = TOK // 4

    nc = bass.Bass("TRN2", target_bir_lowering=False)
    S = Sched(nc, chain=chain)
    RG = [list(range(NC))]

    def inp(name, shape, dtype=F32):
        return nc.dram_tensor(name, list(shape), dtype, kind="ExternalInput").ap()

    x0_d = inp("x0", (TOKP, D))
    w_in_xi = {d: inp(f"w_in_xi_{d}", (D, DI)) for d in "fb"}
    w_in_z = {d: inp(f"w_in_z_{d}", (D, DI)) for d in "fb"}
    w_x = {d: inp(f"w_x_{d}", (128, 16 * 96)) for d in "fb"}
    w_dt = {d: inp(f"w_dt_{d}", (DTR, CSL)) for d in "fb"}
    w_out = {d: inp(f"w_out_{d}", (128, 16 * D)) for d in "fb"}
    w_comb = inp("w_comb", (128, 16 * D))
    conv_w = {d: inp(f"conv_w_{d}", (128, 16 * 4)) for d in "fb"}
    conv_b = {d: inp(f"conv_b_{d}", (128, 16)) for d in "fb"}
    dt_b = {d: inp(f"dt_b_{d}", (128, NCT)) for d in "fb"}
    A_in = {d: inp(f"A_{d}", (128, NCT * NST)) for d in "fb"}
    D_in = {d: inp(f"D_{d}", (128, NCT)) for d in "fb"}
    g_col = inp("g_col", (128, 8))
    b_col = inp("b_col", (128, 8))
    cb_col = inp("cb_col", (128, 8))
    wa_d = inp("wa", (D, DFF))
    wb_d = inp("wb", (128, 32 * D))
    w1_d = inp("w1", (128, 8 * RH))
    b1_d = inp("b1_col", (128, 2))
    w2_d = inp("w2", (128, 2 * NZ))
    b2_d = inp("b2_col", (NZ, 1))
    sel_d = inp("sel", (16, NST * 128))
    ones1_d = inp("ones1", (1, 128))
    onescol_d = inp("onescol", (128, 1))
    ident_d = inp("ident", (128, 128))
    edge_d = inp("edge", (128, 2))
    bmask_d = inp("bmask", (128, 2))
    zmask_d = inp("zmask", (128, 8))
    cst_d = inp("cst_in", (128, 4))
    idxu_d = inp("idxu", (128, 2 * 2 * 4 * NCT), I32)
    idxy_d = inp("idxy", (128, 2 * 16), I32)

    out_d = nc.dram_tensor("out", [TOK, D], F32, kind="ExternalOutput").ap()
    dbg = {}
    if debug:
        for nm, shp in [("xn2", (D, TOKP)), ("u_f", (DI, TOK)), ("dbc_f", (96, TOK)),
                        ("yf", (DI, TOK)), ("x1", (D, TOK)), ("xn3", (D, TOK)),
                        ("w", (1, 2 * NZ)), ("zs", (D, TOK))]:
            dbg[nm] = nc.dram_tensor("dbg_" + nm, list(shp), F32, kind="ExternalOutput").ap()

    dr = lambda name, shape, dtype=F32: nc.dram_tensor(name, list(shape), dtype)
    drs = lambda name, shape, dtype=F32: nc.dram_tensor(name, list(shape), dtype,
                                                        addr_space="Shared")
    u_in = {d: dr(f"u_in_{d}", (DI, TOK), F32) for d in "fb"}
    u_all = {d: drs(f"u_all_{d}", (NC * DI, TOK), F32) for d in "fb"}
    dbc_in = {d: dr(f"dbc_in_{d}", (96, TOK), F32) for d in "fb"}
    dbc_all = {d: drs(f"dbc_all_{d}", (NC * 96, TOK), F32) for d in "fb"}
    y_in = [dr(f"y_in_{g}", (4 * CSL, TOK), F32) for g in range(4)]
    y_allb = drs("y_allb", (4 * NC * 4 * CSL, TOK), F32)
    of_sp = {d: dr(f"of_{d}", (D, TOK), BF16) for d in "fb"}
    xn3_in = dr("xn3_in", (D, TOK), BF16)
    xn3_all = drs("xn3_all", (NC * D, TOK), BF16)
    pooled_in = dr("pooled_in", (BATCH * 128, 8), F32)
    pooled_out = drs("pooled_out", (BATCH * 128, 8), F32)
    x1_sp = dr("x1_sp", (D, TOK), F32)
    xn2_sp = dr("xn2_sp", (D, TOKP), BF16)
    rs_in = [dr(f"rs_in{q}", (NC * D, MM), F32) for q in range(NMC)]
    rs_out = [dr(f"rs_out{q}", (D, MM), F32) for q in range(NMC)]
    lg_bounce = dr("lg_bounce", (BATCH, NZ), F32)

    top = ExitStack()
    sbT = lambda n, s, dt=F32: top.enter_context(nc.sbuf_tensor(n, list(s), dt))

    convw_sb = {d: sbT(f"convw_{d}", (128, 16 * 4)) for d in "fb"}
    convb_sb = {d: sbT(f"convb_{d}", (128, 16)) for d in "fb"}
    dtb_sb = {d: sbT(f"dtb_{d}", (128, NCT)) for d in "fb"}
    A_sb = {d: sbT(f"Am_{d}", (128, NCT * NST)) for d in "fb"}
    D_sb = {d: sbT(f"Dp_{d}", (128, NCT)) for d in "fb"}
    g_sb = sbT("g_sb", (128, 8))
    bcol_sb = sbT("bcol_sb", (128, 8))
    cb_sb = sbT("cb_sb", (128, 8))
    sel_sb = sbT("sel_sb", (16, NST * 128))
    ones1_sb = sbT("ones1_sb", (1, 128))
    onescol_sb = sbT("onescol_sb", (128, 1))
    ident_sb = sbT("ident_sb", (128, 128))
    edge_sb = sbT("edge_sb", (128, 2))
    bmask_sb = sbT("bmask_sb", (128, 2))
    zmask_sb = sbT("zmask_sb", (128, 8))
    cst = sbT("cst", (128, 4))
    idxu_sb = sbT("idxu_sb", (128, 2 * 2 * 4 * NCT), I32)
    idxy_sb = sbT("idxy_sb", (128, 2 * 16), I32)
    wdt_sb = {d: sbT(f"wdt_{d}", (DTR, CSL), BF16) for d in "fb"}
    wx_sb = {d: sbT(f"wx_{d}", (128, 16 * 96), BF16) for d in "fb"}
    w1_sb = sbT("w1_sb", (128, 8 * RH), BF16)
    b1_sb = sbT("b1_sb", (128, 2))
    b2_sb = sbT("b2_sb", (NZ, 1))
    w2_sb = sbT("w2_sb", (128, 2 * NZ), BF16)
    stat = sbT("stat", (128, 16))
    state = sbT("state", (128, 4 * NST * NCT))
    pooled_sb = sbT("pooled_sb", (128, 2 * 8))
    h_sb = sbT("h_sb", (128, 4))
    lgsb = sbT("lgsb", (NZ, 2))
    lgr = sbT("lgr", (1, 2 * NZ))
    rscr = sbT("rscr", (1, 4 * NZ))
    wrow = sbT("wrow", (1, 2 * NZ))
    w_bc = sbT("w_bc", (128, 2 * NZ))
    wz_col = sbT("wz_col", (128, 2))
    zscr = sbT("zscr", (128, 8))
    colsum = sbT("colsum", (128, 8))
    pooled_bf = sbT("pooled_bf", (128, 2 * 8), BF16)
    h_bf = sbT("h_bf", (128, 4), BF16)
    mid = ExitStack()
    xn2_t = mid.enter_context(nc.sbuf_tensor("xn2", [128, 8 * TOKP], BF16))

    for dst, src in [
        (convw_sb["f"][:], conv_w["f"]), (convw_sb["b"][:], conv_w["b"]),
        (convb_sb["f"][:], conv_b["f"]), (convb_sb["b"][:], conv_b["b"]),
        (dtb_sb["f"][:], dt_b["f"]), (dtb_sb["b"][:], dt_b["b"]),
        (D_sb["f"][:], D_in["f"]), (D_sb["b"][:], D_in["b"]),
        (A_sb["f"][:], A_in["f"]), (A_sb["b"][:], A_in["b"]),
        (g_sb[:], g_col), (bcol_sb[:], b_col), (cb_sb[:], cb_col),
        (sel_sb[:], sel_d), (ones1_sb[:], ones1_d), (onescol_sb[:], onescol_d),
        (ident_sb[:], ident_d), (edge_sb[:], edge_d), (bmask_sb[:], bmask_d),
        (zmask_sb[:], zmask_d), (idxu_sb[:], idxu_d), (idxy_sb[:], idxy_d),
        (cst[:], cst_d),
        (b1_sb[:], b1_d), (b2_sb[:], b2_d),
    ]:
        S.do("sp", lambda d=dst, s=src: nc.sync.dma_start(out=d, in_=s), inc="ldc", incn=16)
    for d in "fb":
        S.do("pool", lambda dd=d: nc.gpsimd.dma_start(out=wdt_sb[dd][:], in_=w_dt[dd]), inc="ldc2", incn=16)
        S.do("pool", lambda dd=d: nc.gpsimd.dma_start(out=wx_sb[dd][:], in_=w_x[dd]), inc="ldc2", incn=16)
    S.do("pool", lambda: nc.gpsimd.dma_start(out=w1_sb[:], in_=w1_d), inc="ldc2", incn=16)
    S.do("pool", lambda: nc.gpsimd.dma_start(out=w2_sb[:], in_=w2_d), inc="ldc2", incn=16)
    LDC = S.v("ldc")
    LDC2 = S.v("ldc2")
    for d in "fb":
        S.do("act", lambda dd=d: nc.scalar.activation(A_sb[dd][:], A_sb[dd][:], AF.Exp),
             waits=[("ldc", LDC)], inc="p1a")
        S.do("dve", lambda dd=d: nc.vector.tensor_scalar(
            A_sb[dd][:], A_sb[dd][:], -1.0, None, ALU.mult),
            waits=[("p1a", S.v("p1a"))], inc="p1v")
    AEXP = S.v("p1v")

    INV_D = 1.0 / D
    with ExitStack() as c1:
        sb1 = lambda n, s, dt=F32: c1.enter_context(nc.sbuf_tensor(n, list(s), dt))
        x0_tm = [sb1(f"x0tm{i}", (128, D)) for i in range(2)]
        sq_tm = [sb1(f"sqtm{i}", (128, D)) for i in range(2)]
        xn1_tm = [sb1(f"xn1tm{i}", (128, D)) for i in range(2)]
        x1ev = [sb1(f"x1ev{i}", (128, 128)) for i in range(4)]
        with nc.psum_tensor("p1ps0", [128, MM], F32) as pA, \
             nc.psum_tensor("p1ps1", [128, MM], F32) as pB:
            psb = [pA, pB]
            slot_free = [("p1tp", 0), ("p1tp", 0)]
            x1ev_free = [(f"p1q{i}", 0) for i in range(4)]
            bank_free = [[("p1ev", 0)], [("p1ev", 0)]]
            nev = 0
            for ti in range(NTT):
                pp = ti % 2
                x0t, sqt, xn1t = x0_tm[pp], sq_tm[pp], xn1_tm[pp]
                own = 1 <= ti <= NTT - 2
                S.do("sp", lambda t=ti, b=x0t: nc.sync.dma_start(
                    out=b[:], in_=x0_d[t * 128:(t + 1) * 128, :]),
                    waits=[slot_free[pp]], inc=f"p1x{pp}", incn=16)
                ld = [(f"p1x{pp}", S.v(f"p1x{pp}"))]
                S.do("act", lambda b=x0t, o=sqt: nc.scalar.activation(o[:], b[:], AF.Square),
                     waits=ld + [("p1v", S.v("p1v"))], inc="p1a")
                S.do("dve", lambda b=x0t: nc.vector.reduce_sum(stat[:, 0:1], b[:], AX.X),
                     waits=ld + [("p1a", S.v("p1a") - 1)], inc="p1v")
                S.do("dve", lambda o=sqt: nc.vector.reduce_sum(stat[:, 1:2], o[:], AX.X),
                     waits=[("p1a", S.v("p1a"))], inc="p1v")
                S.do("dve", lambda: nc.vector.tensor_scalar(stat[:, 2:3], stat[:, 0:1], INV_D, None, ALU.mult), inc="p1v")
                S.do("dve", lambda: nc.vector.tensor_scalar(stat[:, 3:4], stat[:, 1:2], INV_D, None, ALU.mult), inc="p1v")
                S.do("dve", lambda: nc.vector.tensor_tensor(stat[:, 4:5], stat[:, 2:3], stat[:, 2:3], ALU.mult), inc="p1v")
                S.do("dve", lambda: nc.vector.tensor_tensor(stat[:, 5:6], stat[:, 3:4], stat[:, 4:5], ALU.subtract), inc="p1v")
                S.do("act", lambda: nc.scalar.activation(stat[:, 6:7], stat[:, 5:6], AF.Ln, bias=cst[:, 0:1]),
                     waits=[("p1v", S.v("p1v"))], inc="p1a")
                S.do("act", lambda: nc.scalar.activation(stat[:, 7:8], stat[:, 6:7], AF.Exp, scale=cst[:, 1:2]), inc="p1a")
                S.do("dve", lambda b=x0t, o=xn1t: nc.vector.tensor_scalar(
                    o[:], b[:], stat[:, 2:3], stat[:, 7:8], ALU.subtract, ALU.mult),
                    waits=[("p1a", S.v("p1a"))], inc="p1v")
                S.do("act", lambda b=xn1t, o=sqt: nc.scalar.activation(o[:], b[:], AF.Square),
                     waits=[("p1v", S.v("p1v"))], inc="p1a")
                S.do("dve", lambda b=xn1t: nc.vector.reduce_sum(stat[:, 8:9], b[:], AX.X), inc="p1v")
                S.do("dve", lambda o=sqt: nc.vector.reduce_sum(stat[:, 9:10], o[:], AX.X),
                     waits=[("p1a", S.v("p1a"))], inc="p1v")
                S.do("dve", lambda: nc.vector.tensor_scalar(stat[:, 10:11], stat[:, 8:9], INV_D, None, ALU.mult), inc="p1v")
                S.do("dve", lambda: nc.vector.tensor_scalar(stat[:, 11:12], stat[:, 9:10], INV_D, None, ALU.mult), inc="p1v")
                S.do("dve", lambda: nc.vector.tensor_tensor(stat[:, 12:13], stat[:, 10:11], stat[:, 10:11], ALU.mult), inc="p1v")
                S.do("dve", lambda: nc.vector.tensor_tensor(stat[:, 13:14], stat[:, 11:12], stat[:, 12:13], ALU.subtract), inc="p1v")
                S.do("act", lambda: nc.scalar.activation(stat[:, 14:15], stat[:, 13:14], AF.Ln, bias=cst[:, 0:1]),
                     waits=[("p1v", S.v("p1v"))], inc="p1a")
                S.do("act", lambda: nc.scalar.activation(stat[:, 15:16], stat[:, 14:15], AF.Exp, scale=cst[:, 1:2]), inc="p1a")
                a2 = [("p1a", S.v("p1a"))]
                if own:
                    S.do("dve", lambda b=x0t, a=xn1t: nc.vector.tensor_tensor(b[:], b[:], a[:], ALU.add),
                         waits=a2, inc="p1v")
                S.do("dve", lambda a=xn1t: nc.vector.tensor_scalar(
                    a[:], a[:], stat[:, 10:11], stat[:, 15:16], ALU.subtract, ALU.mult),
                    waits=a2, inc="p1v")
                vr = [("p1v", S.v("p1v"))]
                for dt_ in range(8):
                    bk = dt_ % 2
                    pb = psb[bk]
                    S.do("pe", lambda a=xn1t, p=pb, d=dt_: nc.tensor.transpose(
                        p[:, 0:128], a[:, d * 128:(d + 1) * 128], ident_sb[:]),
                        waits=[("ldc", LDC)] + vr + bank_free[bk], inc="p1tp")
                    S.do("dve", lambda p=pb, d=dt_, t=ti: nc.vector.tensor_scalar(
                        xn2_t[:, d * TOKP + t * 128: d * TOKP + (t + 1) * 128], p[:, 0:128],
                        g_sb[:, d:d + 1], bcol_sb[:, d:d + 1], ALU.mult, ALU.add),
                        waits=[("p1tp", S.v("p1tp"))], inc="p1ev")
                    bank_free[bk] = [("p1ev", S.v("p1ev"))]
                    if own:
                        ev = x1ev[nev % 4]
                        S.do("pe", lambda b=x0t, p=pb, d=dt_: nc.tensor.transpose(
                            p[:, 128:256], b[:, d * 128:(d + 1) * 128], ident_sb[:]),
                            waits=[("p1ev", S.v("p1ev"))] + bank_free[bk][1:], inc="p1tp")
                        S.do("dve", lambda p=pb, e=ev: nc.vector.tensor_copy(e[:], p[:, 128:256]),
                             waits=[("p1tp", S.v("p1tp")), x1ev_free[nev % 4]], inc="p1ev2")
                        bank_free[bk] = [("p1ev", S.v("p1ev")), ("p1ev2", S.v("p1ev2"))]
                        S.do("sp", lambda e=ev, d=dt_, t=ti: nc.sync.dma_start(
                            out=x1_sp.ap()[d * 128:(d + 1) * 128, (t - 1) * 128:t * 128], in_=e[:]),
                            waits=[("p1ev2", S.v("p1ev2"))], inc=f"p1q{nev % 4}", incn=16)
                        x1ev_free[nev % 4] = (f"p1q{nev % 4}", S.v(f"p1q{nev % 4}"))
                        nev += 1
                slot_free[pp] = ("p1tp", S.v("p1tp"))
    LNEV = S.v("p1ev")
    for dm in range(8):
        S.do("sp", lambda d=dm: nc.sync.dma_start(
            out=xn2_sp.ap()[d * 128:(d + 1) * 128, :],
            in_=xn2_t[:, d * TOKP:(d + 1) * TOKP]),
            waits=[("p1ev", LNEV)], inc="x2d", incn=16)
    X2D = S.v("x2d")
    X1Q = [(f"p1q{i}", S.v(f"p1q{i}")) for i in range(4)]
    P1TPF = S.v("p1tp")
    P1GATE = [("p1tp", P1TPF), ("p1ev", LNEV), ("p1ev2", S.v("p1ev2"))] + X1Q

    if debug:
        for dt_ in range(8):
            S.do("pool", lambda d=dt_: nc.gpsimd.dma_start(
                out=dbg["xn2"][d * 128:(d + 1) * 128, :], in_=xn2_t[:, d * TOKP:(d + 1) * TOKP]),
                waits=[("p1ev", LNEV)], inc="dbgs", incn=16)

    S.mark("p1")
    with ExitStack() as c2:
        sb2 = lambda n, s, dt=F32: c2.enter_context(nc.sbuf_tensor(n, list(s), dt))
        w_ct = [sb2(f"wct{i}", (128, 8 * 128), BF16) for i in range(2)]
        xipre = [sb2(f"xipre{i}", (128, TOKP)) for i in range(2)]
        xi_f32 = [sb2(f"xif{i}", (128, TOK)) for i in range(2)]
        xi_bf = sb2("xibf", (128, 16 * TOK), BF16)
        dbc_ev = sb2("dbcev", (80, TOK))
        dbc_ev2 = sb2("dbcev2", (16, TOK))
        with nc.psum_tensor("feps0", [128, MM], F32) as f0, \
             nc.psum_tensor("feps1", [128, MM], F32) as f1, \
             nc.psum_tensor("feps2", [128, MM], F32) as f2, \
             nc.psum_tensor("feps3", [128, MM], F32) as f3:
            col_chunks = []
            c0 = 0
            while c0 < TOKP:
                cw = min(MM, TOKP - c0)
                col_chunks.append((c0, cw))
                c0 += cw
            xi_guard = ("fxp", 0)
            dbc_guard = ("fxd", 0)
            wslot = [("fei", 0), ("fei", 0)]
            xpre_slot = [("fec", 0), ("fec", 0)]
            xf_slot = [(("feu0", 0), ("feb", 0)), (("feu1", 0), ("feb", 0))]
            for dir_ in "fb":
                for ct in range(16):
                    pp = ct % 2
                    wt, xp, xf = w_ct[pp], xipre[pp], xi_f32[pp]
                    S.do("pool", lambda d=dir_, c=ct, w=wt: nc.gpsimd.dma_start(
                        out=w[:].rearrange("p (kt c) -> p kt c", kt=8),
                        in_=w_in_xi[d][:, c * 128:(c + 1) * 128]
                        .rearrange("(kt p) c -> p kt c", p=128)),
                        waits=[wslot[pp]] + (P1GATE if (dir_ == "f" and ct == 0) else []),
                        inc=f"few{pp}", incn=16)
                    S.mark(f"fw_{dir_}_{ct}")
                    wr = (f"few{pp}", S.v(f"few{pp}"))
                    for ci, (cc0, cw) in enumerate(col_chunks):
                        pb = [f0, f1][(ct * len(col_chunks) + ci) % 2]
                        for k in range(8):
                            S.do("pe", lambda kk=k, o=cc0, w=cw, p=pb, wt_=wt: nc.tensor.matmul(
                                p[:, 0:w], wt_[:, kk * 128:(kk + 1) * 128],
                                xn2_t[:, kk * TOKP + o: kk * TOKP + o + w],
                                start=(kk == 0), stop=(kk == 7)),
                                waits=([wr, ("p1ev", LNEV),
                                        ("fee", max(0, S.v("fee") - 1))] if k == 0 else []),
                                inc=("fei" if k == 7 else None))
                        S.mark(f"fmm_{dir_}_{ct}_{ci}")
                        S.do("act", lambda o=cc0, w=cw, p=pb, t=xp: nc.scalar.activation(
                            t[:, o:o + w], p[:, 0:w], AF.Copy),
                            waits=[("fei", S.v("fei")), xpre_slot[pp]], inc="fee")
                        S.mark(f"fm_{dir_}_{ct}_{ci}")
                    ir = [("fee", S.v("fee")), ("p1v", AEXP)]
                    if dir_ == "f":
                        S.do("dve", lambda t=xp: nc.vector.tensor_scalar(
                            t[:, PAD - 3:PAD], t[:, PAD - 3:PAD], edge_sb[:, 0:1], None, ALU.mult),
                            waits=ir, inc="fec")
                    else:
                        S.do("dve", lambda t=xp: nc.vector.tensor_scalar(
                            t[:, PAD + TOK:PAD + TOK + 3], t[:, PAD + TOK:PAD + TOK + 3],
                            edge_sb[:, 1:2], None, ALU.mult),
                            waits=ir, inc="fec")
                    offs = [PAD - 3, PAD - 2, PAD - 1, PAD] if dir_ == "f" else [PAD, PAD + 1, PAD + 2, PAD + 3]
                    wsel = [0, 1, 2, 3] if dir_ == "f" else [3, 2, 1, 0]
                    S.do("dve", lambda t=xp, o=xf, c=ct, d=dir_, oj=offs[0], wj=wsel[0]: nc.vector.tensor_scalar(
                        o[:], t[:, oj:oj + TOK], convw_sb[d][:, c * 4 + wj:c * 4 + wj + 1], None, ALU.mult),
                        waits=[xf_slot[pp][0]], inc="fec")
                    for j in range(1, 4):
                        S.do("dve", lambda t=xp, o=xf, c=ct, d=dir_, oj=offs[j], wj=wsel[j]: nc.vector.scalar_tensor_tensor(
                            o[:], t[:, oj:oj + TOK], convw_sb[d][:, c * 4 + wj:c * 4 + wj + 1], o[:],
                            ALU.mult, ALU.add), inc="fec")
                    xpre_slot[pp] = ("fec", S.v("fec"))
                    S.mark(f"fc_{dir_}_{ct}")
                    S.do("act", lambda o=xf, c=ct, d=dir_: nc.scalar.activation(
                        o[:], o[:], AF.Silu, bias=convb_sb[d][:, c:c + 1]),
                        waits=[("fec", S.v("fec"))], inc="fes")
                    sr = S.v("fes")
                    S.mark(f"fs_{dir_}_{ct}")
                    S.do("sp", lambda o=xf, c=ct, d=dir_: nc.sync.dma_start(
                        out=u_in[d].ap()[c * 128:(c + 1) * 128, :], in_=o[:]),
                        waits=[("fes", sr)], inc=f"feu{pp}", incn=16)
                    S.do("dve", lambda o=xf, c=ct: nc.vector.tensor_copy(
                        xi_bf[:, c * TOK:(c + 1) * TOK], o[:]),
                        waits=[("fes", sr), xi_guard], inc="feb")
                    wslot[pp] = ("fei", S.v("fei"))
                    xf_slot[pp] = ((f"feu{pp}", S.v(f"feu{pp}")), ("feb", S.v("feb")))
                    S.mark(f"fe_{dir_}_ct{ct}")
                BF = S.v("feb")
                S.mark(f"fe_ct_{dir_}")
                for mi in range(NMC):
                    o = mi * MM
                    for c in range(16):
                        S.do("pe", lambda cc=c, oo=o, d=dir_: nc.tensor.matmul(
                            f2[0:80, 0:MM], wx_sb[d][:, cc * 96: cc * 96 + 80],
                            xi_bf[:, cc * TOK + oo: cc * TOK + oo + MM],
                            start=(cc == 0), stop=(cc == 15)),
                            waits=([("feb", BF), ("ldc2", LDC2),
                                    ("fxe", S.v("fxe"))] if c == 0 else []),
                            inc=("fxp" if c == 15 else None))
                    S.do("act", lambda oo=o: nc.scalar.activation(
                        dbc_ev[:, oo:oo + MM], f2[0:80, 0:MM], AF.Copy),
                        waits=[("fxp", S.v("fxp")), dbc_guard], inc="fxe")
                    for c in range(16):
                        S.do("pe", lambda cc=c, oo=o, d=dir_: nc.tensor.matmul(
                            f3[0:16, 0:MM], wx_sb[d][:, cc * 96 + 80: cc * 96 + 96],
                            xi_bf[:, cc * TOK + oo: cc * TOK + oo + MM],
                            start=(cc == 0), stop=(cc == 15)),
                            waits=([("fxe2", S.v("fxe2"))] if c == 0 else []),
                            inc=("fxp" if c == 15 else None))
                    S.do("act", lambda oo=o: nc.scalar.activation(
                        dbc_ev2[0:16, oo:oo + MM], f3[0:16, 0:MM], AF.Copy),
                        waits=[("fxp", S.v("fxp")), dbc_guard], inc="fxe2")
                xi_guard = ("fxp", S.v("fxp"))
                S.mark(f"fe_xp_{dir_}")
                S.do("sp", lambda d=dir_: nc.sync.dma_start(
                    out=dbc_in[d].ap()[0:80, :], in_=dbc_ev[:]),
                     waits=[("fxe", S.v("fxe")), ("fxe2", S.v("fxe2"))], inc="fxd", incn=16)
                S.do("sp", lambda d=dir_: nc.sync.dma_start(
                    out=dbc_in[d].ap()[80:96, :], in_=dbc_ev2[:]),
                     inc="fxd", incn=16)
                dbc_guard = ("fxd", S.v("fxd"))
                S.do("pool", lambda d=dir_: nc.gpsimd.collective_compute(
                    "AllGather", ALU.bypass, replica_groups=RG,
                    ins=[u_in[d].ap()], outs=[u_all[d].ap()]),
                    waits=[("feu0", S.v("feu0")), ("feu1", S.v("feu1"))], inc=f"agu{dir_}", cc=True)
                S.do("pool", lambda d=dir_: nc.gpsimd.collective_compute(
                    "AllGather", ALU.bypass, replica_groups=RG,
                    ins=[dbc_in[d].ap()], outs=[dbc_all[d].ap()]),
                    waits=[("fxd", S.v("fxd"))], inc=f"agd{dir_}", cc=True)
                S.mark(f"fe_ag_{dir_}")

    S.mark("fe")
    mid.close()
    with ExitStack() as c3:
        sb3 = lambda n, s, dt=F32: c3.enter_context(nc.sbuf_tensor(n, list(s), dt))
        u_sc = sb3("u_sc", (128, NCT * TOK))
        delta = [sb3(f"delta{i}", (128, NCT * TOK)) for i in range(2)]
        du = sb3("du", (128, NCT * TOK))
        spt = sb3("spt", (128, NCT * TOK))
        y_sc = sb3("y_sc", (128, NCT * TOK))
        dt_sb = [sb3(f"dt_sb{i}", (DTR, TOK), BF16) for i in range(2)]
        Brow = [sb3(f"Brow{i}", (16, TOK)) for i in range(2)]
        Crow = [sb3(f"Crow{i}", (16, TOK)) for i in range(2)]
        dA_t = [sb3(f"dA_t{i}", (128, SC)) for i in range(2)]
        b_t = [sb3(f"b_t{i}", (128, SC)) for i in range(2)]
        cm_t = [sb3(f"cm_t{i}", (128, SC)) for i in range(2)]
        h_t = sb3("h_t", (128, SC))
        with nc.psum_tensor("scps0", [128, MM], F32) as s0, \
             nc.psum_tensor("scps1", [128, MM], F32) as s1, \
             nc.psum_tensor("scpB", [128, SC], F32) as psB, \
             nc.psum_tensor("scpC", [128, SC], F32) as psC:
            S.do("dve", lambda: nc.vector.memset(state[:], 0.0),
                 waits=[("ldc", LDC)], inc="scv")
            ITERS = [(g,) + JOBS[jb] + (jb,) for g in range(4) for jb in range(4)]
            NIT = len(ITERS)
            NQ = SC // MM
            carry = {
                "PSBF": 0, "PSCF": 0,
                "TTSD": {0: 0, 1: 0},
                "CMF": {0: 0, 1: 0},
                "BCREAD": [0, 0], "DTREAD": [0, 0],
                "UCONS": 0, "YEV": 0,
            }
            LDP_it = [0] * NIT
            LDS_it = [0] * NIT
            SPLN_it = [0] * NIT
            DLT_it = [0] * NIT

            def emit_loads(it):
                g, dir_, bb_, jb = ITERS[it]
                gi = g if dir_ == "f" else 3 - g
                p = 4 * bb_ + gi
                pp2 = it % 2
                fe_gate = ([("aguf", 1), ("agub", 1), ("agdf", 1), ("agdb", 1),
                            ("x2d", X2D)] if it == 0 else [])
                for ct in range(NCT):
                    icol = ((0 if dir_ == "f" else 1) * 2 + bb_) * 4 * NCT + gi * NCT + ct
                    S.do("pool", lambda c=ct, d=dir_, ic=icol: nc.gpsimd.indirect_dma_start(
                        u_sc[:, c * TOK:(c + 1) * TOK], None, u_all[d].ap(),
                        IndirectOffsetOnAxis(idxu_sb[:, ic:ic + 1], 0)),
                        waits=[(f"agu{d}", 1), ("ldc", LDC), ("scv", carry["UCONS"])] + fe_gate,
                        inc="sclP", incn=16)
                S.do("pool", lambda d=dir_, pp_=p, b_=dt_sb[pp2]: nc.gpsimd.dma_start(
                    out=b_[:], in_=dbc_all[d].ap()[96 * pp_: 96 * pp_ + 64, :]),
                    waits=[(f"agd{d}", 1), ("scp", carry["DTREAD"][pp2])] + fe_gate,
                    inc="sclP", incn=16)
                LDP_it[it] = S.v("sclP")
                S.do("sp", lambda d=dir_, pp_=p, b_=Brow[pp2]: nc.sync.dma_start(
                    out=b_[:], in_=dbc_all[d].ap()[96 * pp_ + 64: 96 * pp_ + 80, :]),
                    waits=[(f"agd{d}", 1), ("scp", carry["BCREAD"][pp2])] + fe_gate,
                    inc="sclS", incn=16)
                S.do("sp", lambda d=dir_, pp_=p, b_=Crow[pp2]: nc.sync.dma_start(
                    out=b_[:], in_=dbc_all[d].ap()[96 * pp_ + 80: 96 * pp_ + 96, :]),
                    waits=[("scp", carry["BCREAD"][pp2])] + fe_gate,
                    inc="sclS", incn=16)
                LDS_it[it] = S.v("sclS")

            def emit_dtproj(it, chunk):
                g, dir_, bb_, jb = ITERS[it]
                pp2 = it % 2
                dl = delta[pp2]
                if chunk < 2:
                    ct = chunk
                    for mi in range(TOK // MM):
                        o = mi * MM
                        pb = [s0, s1][(ct * (TOK // MM) + mi) % 2]
                        S.do("pe", lambda c=ct, oo=o, d=dir_, p_=pb, pq=pp2: nc.tensor.matmul(
                            p_[:, 0:MM], wdt_sb[d][:, c * 128:(c + 1) * 128],
                            dt_sb[pq][:, oo:oo + MM], start=True, stop=True),
                            waits=[("sclP", LDP_it[it]), ("sce", max(0, S.v("sce") - 2))],
                            inc="scp")
                        S.do("act", lambda c=ct, oo=o, d=dir_, p_=pb: nc.scalar.activation(
                            spt[:, c * TOK + oo: c * TOK + oo + MM], p_[:, 0:MM],
                            AF.Abs, bias=dtb_sb[d][:, c:c + 1]),
                            waits=[("scp", S.v("scp")), ("scv", DLT_it[it - 1] if it else 0)],
                            inc="sce")
                        S.do("act", lambda c=ct, oo=o, d=dir_, p_=pb, dl_=dl: nc.scalar.activation(
                            dl_[:, c * TOK + oo: c * TOK + oo + MM], p_[:, 0:MM],
                            AF.Relu, bias=dtb_sb[d][:, c:c + 1]), inc="sce")
                    if ct == 1:
                        carry["DTREAD"][pp2] = S.v("scp")
                else:
                    ct = chunk - 2
                    S.do("act", lambda c=ct: nc.scalar.activation(
                        spt[:, c * TOK:(c + 1) * TOK], spt[:, c * TOK:(c + 1) * TOK],
                        AF.Exp, scale=cst[:, 2:3]), inc="sce")
                    S.do("act", lambda c=ct: nc.scalar.activation(
                        spt[:, c * TOK:(c + 1) * TOK], spt[:, c * TOK:(c + 1) * TOK],
                        AF.Ln, bias=cst[:, 3:4]), inc="sce")
                    if ct == NCT - 1:
                        SPLN_it[it] = S.v("sce")

            def emit_boundary(it):
                g, dir_, bb_, jb = ITERS[it]
                pp2 = it % 2
                dl = delta[pp2]
                for ct in range(NCT):
                    S.do("dve", lambda c=ct, dl_=dl: nc.vector.tensor_tensor(
                        dl_[:, c * TOK:(c + 1) * TOK], dl_[:, c * TOK:(c + 1) * TOK],
                        spt[:, c * TOK:(c + 1) * TOK], ALU.add),
                        waits=[("sce", SPLN_it[it])], inc="scv")
                for ct in range(NCT):
                    S.do("dve", lambda c=ct, dl_=dl: nc.vector.tensor_tensor(
                        du[:, c * TOK:(c + 1) * TOK], dl_[:, c * TOK:(c + 1) * TOK],
                        u_sc[:, c * TOK:(c + 1) * TOK], ALU.mult),
                        waits=[("sclP", LDP_it[it])], inc="scv")
                for ct in range(NCT):
                    S.do("dve", lambda c=ct, d=dir_: nc.vector.tensor_scalar(
                        y_sc[:, c * TOK:(c + 1) * TOK], u_sc[:, c * TOK:(c + 1) * TOK],
                        D_sb[d][:, c:c + 1], None, ALU.mult),
                        waits=[("scy", carry["YEV"])], inc="scv")
                DLT_it[it] = S.v("scv")
                carry["UCONS"] = S.v("scv")

            def emit_inner(it):
                g, dir_, bb_, jb = ITERS[it]
                pp2 = it % 2
                dl = delta[pp2]
                subs = list(range(NSUB)) if dir_ == "f" else list(range(NSUB - 1, -1, -1))
                sn_list = [(sub, n) for sub in subs for n in range(NST)]
                ns = len(sn_list)
                slots = {}
                used = -1
                for w, kind in zip([(2 * ns) // 32, (8 * ns) // 32, (12 * ns) // 32,
                                    (16 * ns) // 32, (20 * ns) // 32],
                                   ["L", 0, 1, 2, 3]):
                    sl = max(w, used + 1)
                    assert sl < ns
                    slots.setdefault(sl, []).append(kind)
                    used = sl
                for si, (sub, n) in enumerate(sn_list):
                    so = sub * SC
                    for q in range(NQ):
                        S.do("pe", lambda nn=n, qq=q, oo=so, b_=Brow[pp2]: nc.tensor.matmul(
                            psB[:, qq * MM:(qq + 1) * MM],
                            sel_sb[:, nn * 128:(nn + 1) * 128],
                            b_[:, oo + qq * MM: oo + (qq + 1) * MM],
                            start=True, stop=True),
                            waits=[("sclS", LDS_it[it]), ("scv2", carry["PSBF"])],
                            inc="scp")
                    for q in range(NQ):
                        S.do("pe", lambda nn=n, qq=q, oo=so, b_=Crow[pp2]: nc.tensor.matmul(
                            psC[:, qq * MM:(qq + 1) * MM],
                            sel_sb[:, nn * 128:(nn + 1) * 128],
                            b_[:, oo + qq * MM: oo + (qq + 1) * MM],
                            start=True, stop=True),
                            waits=[("scv2", carry["PSCF"])], inc="scp")
                    BCC = S.v("scp")
                    BCB = BCC - NQ
                    dav = {}
                    for ct in range(NCT):
                        S.do("act", lambda c=ct, nn=n, d=dir_, oo=so, dl_=dl: nc.scalar.activation(
                            dA_t[c][:], dl_[:, c * TOK + oo: c * TOK + oo + SC],
                            AF.Exp, scale=A_sb[d][:, c * NST + nn:c * NST + nn + 1]),
                            waits=[("scv", DLT_it[it]), ("scv2", carry["TTSD"][ct])],
                            inc="sce2")
                        dav[ct] = S.v("sce2")
                    for ct in range(NCT):
                        S.do("dve", lambda c=ct, oo=so: nc.vector.tensor_tensor(
                            b_t[c][:], du[:, c * TOK + oo: c * TOK + oo + SC],
                            psB[:], ALU.mult),
                            waits=[("scp", BCB)], inc="scv2")
                    carry["PSBF"] = S.v("scv2")
                    for ct in range(NCT):
                        sidx = (jb * NST + n) * NCT + ct
                        if dir_ == "f":
                            S.do("dve", lambda s_=sidx, c=ct: nc.vector.tensor_tensor_scan(
                                h_t[:], dA_t[c][:], b_t[c][:], state[:, s_:s_ + 1],
                                ALU.mult, ALU.add),
                                waits=[("sce2", dav[ct])], inc="scv2")
                            carry["TTSD"][ct] = S.v("scv2")
                            S.do("dve", lambda s_=sidx: nc.vector.tensor_copy(
                                state[:, s_:s_ + 1], h_t[:, SC - 1:SC]), inc="scv2")
                        else:
                            S.do("dve", lambda s_=sidx, c=ct: nc.vector.tensor_tensor_scan(
                                h_t[:, ::-1], dA_t[c][:, ::-1], b_t[c][:, ::-1],
                                state[:, s_:s_ + 1], ALU.mult, ALU.add),
                                waits=[("sce2", dav[ct])], inc="scv2")
                            carry["TTSD"][ct] = S.v("scv2")
                            S.do("dve", lambda s_=sidx: nc.vector.tensor_copy(
                                state[:, s_:s_ + 1], h_t[:, 0:1]), inc="scv2")
                        S.do("dve", lambda c=ct: nc.vector.tensor_tensor(
                            cm_t[c][:], h_t[:], psC[:], ALU.mult),
                            waits=[("scp", BCC), ("scg", carry["CMF"][ct])], inc="scv2")
                        S.do("pool", lambda c=ct, oo=so: nc.gpsimd.tensor_tensor(
                            y_sc[:, c * TOK + oo: c * TOK + oo + SC],
                            y_sc[:, c * TOK + oo: c * TOK + oo + SC],
                            cm_t[c][:], ALU.add),
                            waits=[("scv2", S.v("scv2")), ("scg", S.v("scg"))], inc="scg")
                        carry["CMF"][ct] = S.v("scg")
                    carry["PSCF"] = S.v("scv2")
                    if it + 1 < NIT:
                        for kind in slots.get(si, ()):
                            if kind == "L":
                                emit_loads(it + 1)
                            else:
                                emit_dtproj(it + 1, kind)
                carry["BCREAD"][pp2] = S.v("scp")
                SCG = S.v("scg")
                for ct in range(NCT):
                    S.do("sp", lambda c=ct, g_=g, j_=jb: nc.sync.dma_start(
                        out=y_in[g_].ap()[j_ * CSL + c * 128: j_ * CSL + (c + 1) * 128, :],
                        in_=y_sc[:, c * TOK:(c + 1) * TOK]),
                        waits=[("scg", SCG)], inc="scy", incn=16)
                carry["YEV"] = S.v("scy")
                if jb == 3:
                    S.do("pool", lambda g_=g: nc.gpsimd.collective_compute(
                        "AllGather", ALU.bypass, replica_groups=RG,
                        ins=[y_in[g_].ap()],
                        outs=[y_allb.ap()[g_ * NC * 4 * CSL:(g_ + 1) * NC * 4 * CSL, :]]),
                        waits=[("scy", S.v("scy"))], inc=f"agy{g}", cc=True)

            emit_loads(0)
            for ch in range(4):
                emit_dtproj(0, ch)
            for it in range(NIT):
                emit_boundary(it)
                emit_inner(it)
    S.mark("scan")
    YGATE = [(f"agy{g}", 1) for g in range(4)]

    with ExitStack() as c4:
        sb4 = lambda n, s, dt=F32: c4.enter_context(nc.sbuf_tensor(n, list(s), dt))
        w_st = [sb4(f"bwt{i}", (128, 8 * 128), BF16) for i in range(3)]
        xn2b = sb4("xn2b", (128, 8 * TOK), BF16)
        sz_t = [sb4(f"szt{i}", (128, TOK)) for i in range(3)]
        ytl = [sb4(f"ytl{i}", (128, TOK)) for i in range(3)]
        yg = sb4("yg", (128, 16 * TOK), BF16)
        wko = sb4("wko", (128, 16 * D), BF16)
        ofev = [sb4(f"ofev{i}", (128, MM), BF16) for i in range(2)]
        with nc.psum_tensor("beps0", [128, MM], F32) as e0, \
             nc.psum_tensor("beps1", [128, MM], F32) as e1, \
             nc.psum_tensor("beps2", [128, MM], F32) as e2, \
             nc.psum_tensor("beps3", [128, MM], F32) as e3:
            wslot = [("bzp", 0)] * 3
            szslot = [("bgt", 0)] * 3
            ytslot = [("bgt", 0)] * 3
            S.do("sp", lambda: nc.sync.dma_start(
                out=xn2b[:].rearrange("p (dm c) -> p dm c", dm=8),
                in_=xn2_sp.ap()[:, PAD:PAD + TOK]
                .rearrange("(dm p) c -> p dm c", p=128)),
                waits=[("x2d", X2D)] + YGATE, inc="x2l", incn=16)
            X2L = S.v("x2l")
            for dir_ in "fb":
                for ct in range(16):
                    pp = ct % 3
                    wt, sz, yt = w_st[pp], sz_t[pp], ytl[pp]
                    S.do("pool", lambda d=dir_, c=ct, w=wt: nc.gpsimd.dma_start(
                        out=w[:].rearrange("p (kt c) -> p kt c", kt=8),
                        in_=w_in_z[d][:, c * 128:(c + 1) * 128]
                        .rearrange("(kt p) c -> p kt c", p=128)),
                        waits=[wslot[pp]] + YGATE, inc=f"bzw{pp}", incn=16)
                    wr = (f"bzw{pp}", S.v(f"bzw{pp}"))
                    S.do("pool", lambda d=dir_, c=ct, yt_=yt: nc.gpsimd.indirect_dma_start(
                        yt_[:], None, y_allb.ap(),
                        IndirectOffsetOnAxis(
                            idxy_sb[:, (0 if d == "f" else 1) * 16 + c:
                                    (0 if d == "f" else 1) * 16 + c + 1], 0)),
                        waits=YGATE + [ytslot[pp]], inc=f"bzl{pp}", incn=16)
                    for mi in range(NMC):
                        o = mi * MM
                        pb = [e0, e1][(ct * NMC + mi) % 2]
                        for k in range(8):
                            S.do("pe", lambda kk=k, oo=o, p_=pb, w=wt: nc.tensor.matmul(
                                p_[:, 0:MM], w[:, kk * 128:(kk + 1) * 128],
                                xn2b[:, kk * TOK + oo: kk * TOK + oo + MM],
                                start=(kk == 0), stop=(kk == 7)),
                                waits=([wr, ("x2l", X2L),
                                        ("bze", max(0, S.v("bze") - 1))]
                                       if k == 0 else []),
                                inc=("bzp" if k == 7 else None))
                        S.do("act", lambda oo=o, p_=pb, s_=sz: nc.scalar.activation(
                            s_[:, oo:oo + MM], p_[:, 0:MM], AF.Silu),
                            waits=[("bzp", S.v("bzp"))] + ([szslot[pp]] if mi == 0 else []),
                            inc="bze")
                    S.do("dve", lambda c=ct, s_=sz, yt_=yt: nc.vector.tensor_tensor(
                        yg[:, c * TOK:(c + 1) * TOK], yt_[:], s_[:], ALU.mult),
                        waits=[("bze", S.v("bze")), (f"bzl{pp}", S.v(f"bzl{pp}")),
                               ("bop", S.v("bop"))], inc="bgt")
                    wslot[pp] = ("bzp", S.v("bzp"))
                    szslot[pp] = ("bgt", S.v("bgt"))
                    ytslot[pp] = ("bgt", S.v("bgt"))
                BGT = S.v("bgt")
                S.do("pool", lambda d=dir_: nc.gpsimd.dma_start(out=wko[:], in_=w_out[d]),
                     waits=[("bop", S.v("bop"))], inc="ffw", incn=16)
                BOW = S.v("ffw")
                for mi in range(NMC):
                    o = mi * MM
                    for dm in range(8):
                        pb = [e2, e3][dm % 2]
                        for k in range(16):
                            S.do("pe", lambda kk=k, oo=o, p_=pb, d_=dm: nc.tensor.matmul(
                                p_[:, 0:MM], wko[:, kk * D + d_ * 128: kk * D + (d_ + 1) * 128],
                                yg[:, kk * TOK + oo: kk * TOK + oo + MM],
                                start=(kk == 0), stop=(kk == 15)),
                                waits=([("bgt", BGT), ("ffw", BOW),
                                        ("boe", max(0, S.v("boe") - 1))] if k == 0 else []),
                                inc=("bop" if k == 15 else None))
                        osl = (mi * 8 + dm) % 2
                        oe = ofev[osl]
                        S.do("dve", lambda p_=pb, e_=oe: nc.vector.tensor_copy(e_[:], p_[:, 0:MM]),
                             waits=[("bop", S.v("bop")), (f"bod{osl}", S.v(f"bod{osl}"))],
                             inc="boe")
                        S.do("sp", lambda oo=o, d_=dm, dd=dir_, e_=oe: nc.sync.dma_start(
                            out=of_sp[dd].ap()[d_ * 128:(d_ + 1) * 128, oo:oo + MM], in_=e_[:]),
                            waits=[("boe", S.v("boe"))], inc=f"bod{osl}", incn=16)
    OFD2 = [("bod0", S.v("bod0")), ("bod1", S.v("bod1"))]
    with ExitStack() as c4b:
        sb4b = lambda n, s, dt=F32: c4b.enter_context(nc.sbuf_tensor(n, list(s), dt))
        wkoc = sb4b("wko2", (128, 16 * D), BF16)
        ofr = [sb4b(f"ofr{i}", (128, MM), BF16) for i in range(4)]
        x1c = sb4b("x1c", (128, 8 * MM))
        sqc = sb4b("sqc", (128, MM))
        st3 = sb4b("st3", (1, 4 * MM))
        xn3f = sb4b("xn3f", (128, MM))
        xn3b = [sb4b(f"xn3b{i}", (128, MM), BF16) for i in range(2)]
        with nc.psum_tensor("cbps0", [128, MM], F32) as e0, \
             nc.psum_tensor("cbps1", [128, MM], F32) as e1, \
             nc.psum_tensor("cbps2", [128, MM], F32) as e2, \
             nc.psum_tensor("cbps3", [128, MM], F32) as e3:
            S.do("pool", lambda: nc.gpsimd.dma_start(out=wkoc[:], in_=w_comb),
                 waits=[("bop", S.v("bop")), ("bgt", S.v("bgt")), ("bze", S.v("bze")),
                        ("boe", S.v("boe"))] + OFD2 +
                       ([("dbgs", S.v("dbgs"))] if debug else []),
                 inc="ffw", incn=16)
            CBW = S.v("ffw")
            S.do("dve", lambda: nc.vector.memset(colsum[:], 0.0), inc="cls")
            stat_gate = ("l3p", 0)
            x1_guard = ("x1d", 0)
            for mi in range(NMC):
                o = mi * MM
                S.do("sp", lambda oo=o: nc.sync.dma_start(
                    out=x1c[:].rearrange("p (dm c) -> p dm c", dm=8),
                    in_=x1_sp.ap()[:, oo:oo + MM].rearrange("(dm p) c -> p dm c", p=128)),
                    waits=X1Q + [("cbe", max(0, S.v("cbe") - 8)), x1_guard],
                    inc="cbx", incn=16)
                CBX = S.v("cbx")
                for dm in range(8):
                    pb = [e0, e1][dm % 2]
                    for k in range(16):
                        dd = "f" if k < 8 else "b"
                        kk = k % 8
                        sl = k % 4
                        buf = ofr[sl]
                        S.do("sp", lambda d_=dd, k_=kk, oo=o, b_=buf, sl_=sl: nc.sync.dma_start(
                            out=b_[:], in_=of_sp[d_].ap()[k_ * 128:(k_ + 1) * 128, oo:oo + MM]),
                            waits=OFD2 + [("cbp", max(0, S.v("cbp") - 3)),
                                          (f"cbl{sl}", S.v(f"cbl{sl}"))],
                            inc=f"cbl{sl}", incn=16)
                        S.do("pe", lambda k_=k, oo=o, p_=pb, d_=dm, b_=buf: nc.tensor.matmul(
                            p_[:, 0:MM], wkoc[:, k_ * D + d_ * 128: k_ * D + (d_ + 1) * 128],
                            b_[:], start=(k_ == 0), stop=(k_ == 15)),
                            waits=[(f"cbl{sl}", S.v(f"cbl{sl}"))] + ([("ffw", CBW),
                                   ("cbe", max(0, S.v("cbe") - 1))] if k == 0 else []),
                            inc="cbp")
                    S.do("dve", lambda p_=pb, d_=dm: nc.vector.scalar_tensor_tensor(
                        x1c[:, d_ * MM:(d_ + 1) * MM], p_[:, 0:MM], cb_sb[:, d_:d_ + 1],
                        x1c[:, d_ * MM:(d_ + 1) * MM], ALU.add, ALU.add),
                        waits=[("cbp", S.v("cbp")), ("cbx", CBX)], inc="cbe")
                X1R = S.v("cbe")
                for dm in range(8):
                    S.do("act", lambda d_=dm: nc.scalar.activation(
                        sqc[:], x1c[:, d_ * MM:(d_ + 1) * MM], AF.Square),
                        waits=[("cbe", X1R), ("l3q", S.v("l3q"))], inc="l3a")
                    S.do("pe", lambda d_=dm: nc.tensor.matmul(
                        e2[0:1, 0:MM], onescol_sb[:], x1c[:, d_ * MM:(d_ + 1) * MM],
                        start=(d_ == 0), stop=(d_ == 7)),
                        waits=([("cbe", X1R), stat_gate] if dm == 0 else []),
                        inc="l3p" if dm == 7 else None)
                    S.do("pe", lambda d_=dm: nc.tensor.matmul(
                        e3[0:1, 0:MM], onescol_sb[:], sqc[:],
                        start=(d_ == 0), stop=(d_ == 7)),
                        waits=[("l3a", S.v("l3a"))],
                        inc="l3q")
                L3P = [("l3p", S.v("l3p")), ("l3q", S.v("l3q"))]
                S.do("dve", lambda: nc.vector.tensor_scalar(st3[0:1, 0:MM], e2[0:1, 0:MM], INV_D, None, ALU.mult),
                     waits=L3P + [("l3p2", S.v("l3p2"))], inc="l3v")
                S.do("dve", lambda: nc.vector.tensor_scalar(st3[0:1, MM:2 * MM], e3[0:1, 0:MM], INV_D, None, ALU.mult), inc="l3v")
                S.do("dve", lambda: nc.vector.tensor_tensor(st3[0:1, 2 * MM:3 * MM], st3[0:1, 0:MM], st3[0:1, 0:MM], ALU.mult), inc="l3v")
                S.do("dve", lambda: nc.vector.tensor_tensor(st3[0:1, MM:2 * MM], st3[0:1, MM:2 * MM], st3[0:1, 2 * MM:3 * MM], ALU.subtract), inc="l3v")
                S.do("act", lambda: nc.scalar.activation(st3[0:1, MM:2 * MM], st3[0:1, MM:2 * MM], AF.Ln, bias=cst[0:1, 0:1]),
                     waits=[("l3v", S.v("l3v"))], inc="l3a")
                S.do("act", lambda: nc.scalar.activation(st3[0:1, MM:2 * MM], st3[0:1, MM:2 * MM], AF.Exp, scale=cst[0:1, 1:2]), inc="l3a")
                S.do("pe", lambda: nc.tensor.matmul(e2[:, 0:MM], ones1_sb[:], st3[0:1, 0:MM],
                                                    start=True, stop=True),
                     waits=[("l3a", S.v("l3a")), ("l3v", S.v("l3v"))], inc="l3p2")
                S.do("pe", lambda: nc.tensor.matmul(e3[:, 0:MM], ones1_sb[:], st3[0:1, MM:2 * MM],
                                                    start=True, stop=True), inc="l3p2")
                L3B = S.v("l3p2")
                for dm in range(8):
                    xsl = dm % 2
                    xb = xn3b[xsl]
                    S.do("dve", lambda d_=dm: nc.vector.tensor_tensor(
                        xn3f[:], x1c[:, d_ * MM:(d_ + 1) * MM], e2[:, 0:MM], ALU.subtract),
                        waits=[("l3p2", L3B)], inc="l3v")
                    S.do("dve", lambda: nc.vector.tensor_tensor(xn3f[:], xn3f[:], e3[:, 0:MM], ALU.mult), inc="l3v")
                    S.do("dve", lambda d_=dm: nc.vector.tensor_tensor(
                        x1c[:, d_ * MM:(d_ + 1) * MM], x1c[:, d_ * MM:(d_ + 1) * MM], xn3f[:], ALU.add), inc="l3v")
                    S.do("dve", lambda xb_=xb: nc.vector.tensor_copy(xb_[:], xn3f[:]),
                         waits=[(f"l3d{xsl}", S.v(f"l3d{xsl}"))], inc="l3v")
                    S.do("dve", lambda: nc.vector.reduce_sum(stat[:, 0:1], xn3f[:], AX.X), inc="l3v")
                    S.do("dve", lambda d_=dm: nc.vector.tensor_tensor(
                        colsum[:, d_:d_ + 1], colsum[:, d_:d_ + 1], stat[:, 0:1], ALU.add),
                        waits=[("cls", S.v("cls"))], inc="cls")
                    V3 = S.v("l3v")
                    S.do("sp", lambda oo=o, d_=dm, xb_=xb: nc.sync.dma_start(
                        out=xn3_in.ap()[d_ * 128:(d_ + 1) * 128, oo:oo + MM], in_=xb_[:]),
                        waits=[("l3v", V3)], inc=f"l3d{xsl}", incn=16)
                    S.do("sp", lambda oo=o, d_=dm: nc.sync.dma_start(
                        out=x1_sp.ap()[d_ * 128:(d_ + 1) * 128, oo:oo + MM],
                        in_=x1c[:, d_ * MM:(d_ + 1) * MM]),
                        waits=[("l3v", V3)], inc="x1d", incn=16)
                stat_gate = ("l3v", S.v("l3v"))
                x1_guard = ("x1d", S.v("x1d"))
            S.do("dve", lambda: nc.vector.tensor_scalar(
                colsum[:], colsum[:], 1.0 / L, None, ALU.mult),
                waits=[("cls", S.v("cls"))], inc="cls")
            for bb in range(2):
                S.do("dve", lambda b_=bb: nc.vector.tensor_scalar(
                    pooled_sb[:, b_ * 8:(b_ + 1) * 8], colsum[:], bmask_sb[:, b_:b_ + 1], None, ALU.mult),
                    waits=[("cls", S.v("cls"))], inc="cls")
            S.do("sp", lambda: nc.sync.dma_start(out=pooled_in.ap()[0:128, :], in_=pooled_sb[:, 0:8]),
                 waits=[("cls", S.v("cls"))], inc="rtl", incn=16)
            S.do("sp", lambda: nc.sync.dma_start(out=pooled_in.ap()[128:256, :], in_=pooled_sb[:, 8:16]),
                 inc="rtl", incn=16)
            S.do("pool", lambda: nc.gpsimd.collective_compute(
                "AllGather", ALU.bypass, replica_groups=RG,
                ins=[xn3_in.ap()], outs=[xn3_all.ap()]),
                waits=[("l3d0", S.v("l3d0")), ("l3d1", S.v("l3d1"))], inc="agxn", cc=True)
            S.do("pool", lambda: nc.gpsimd.collective_compute(
                "AllReduce", ALU.add, replica_groups=RG,
                ins=[pooled_in.ap()], outs=[pooled_out.ap()]),
                waits=[("rtl", S.v("rtl"))], inc="arpl", cc=True)

    S.mark("be")
    with nc.psum_tensor("rps0", [128, MM], F32) as r0, \
         nc.psum_tensor("rps1", [128, MM], F32) as r1:
        S.do("sp", lambda: nc.sync.dma_start(
            out=pooled_sb[:, 0:8], in_=pooled_out.ap()[0:128, :]),
            waits=[("arpl", 1)], inc="rtl", incn=16)
        S.do("sp", lambda: nc.sync.dma_start(
            out=pooled_sb[:, 8:16], in_=pooled_out.ap()[128:256, :]),
            inc="rtl", incn=16)
        S.do("dve", lambda: nc.vector.tensor_copy(pooled_bf[:], pooled_sb[:]),
             waits=[("rtl", S.v("rtl"))], inc="rtv")
        RTL = S.v("rtv")
        for bb in range(2):
            for cti in range(2):
                for k in range(8):
                    S.do("pe", lambda b_=bb, c=cti, kk=k: nc.tensor.matmul(
                        r0[:, (b_ * 2 + c):(b_ * 2 + c) + 1],
                        w1_sb[:, kk * RH + c * 128: kk * RH + (c + 1) * 128],
                        pooled_bf[:, b_ * 8 + kk: b_ * 8 + kk + 1],
                        start=(kk == 0), stop=(kk == 7)),
                        waits=([("rtv", RTL), ("ldc2", LDC2), ("agxn", 1)] if k == 0 else []),
                        inc=("rtp" if k == 7 else None))
        for col, bcol in [(0, 0), (1, 1), (2, 0), (3, 1)]:
            S.do("act", lambda c=col, b_=bcol: nc.scalar.activation(
                h_sb[:, c:c + 1], r0[:, c:c + 1], AF.Tanh, bias=b1_sb[:, b_:b_ + 1]),
                waits=[("rtp", S.v("rtp"))], inc="rta")
        S.do("dve", lambda: nc.vector.tensor_copy(h_bf[:], h_sb[:]),
             waits=[("rta", S.v("rta"))], inc="rtv")
        RTA = S.v("rtv")
        for bb in range(2):
            for k in range(2):
                S.do("pe", lambda b_=bb, kk=k: nc.tensor.matmul(
                    r1[0:NZ, b_:b_ + 1], w2_sb[:, kk * NZ:(kk + 1) * NZ],
                    h_bf[:, b_ * 2 + kk: b_ * 2 + kk + 1],
                    start=(kk == 0), stop=(kk == 1)),
                    waits=([("rtv", RTA)] if k == 0 else []),
                    inc=("rtp" if k == 1 else None))
        S.do("dve", lambda: nc.vector.tensor_scalar(
            lgsb[:], r1[0:NZ, 0:2], b2_sb[:], None, ALU.add),
             waits=[("rtp", S.v("rtp"))], inc="rta")
        for bb in range(2):
            S.do("sp", lambda b_=bb: nc.sync.dma_start(
                out=lg_bounce.ap()[b_:b_ + 1, :], in_=lgsb[:, b_:b_ + 1]),
                waits=[("rta", S.v("rta"))], inc="rtl", incn=16)
        S.do("sp", lambda: nc.sync.dma_start(
            out=lgr[:], in_=lg_bounce.ap().rearrange("b c -> (b c)").unsqueeze(0)),
             waits=[("rtl", S.v("rtl"))], inc="rtl", incn=16)
        RTL2 = S.v("rtl")
        for bb in range(2):
            S.do("dve", lambda b_=bb: nc.vector.reduce_max(
                rscr[0:1, 0:1], lgr[0:1, b_ * NZ:(b_ + 1) * NZ], AX.X),
                waits=[("rtl", RTL2), ("rta", S.v("rta"))], inc="rtv")
            S.do("dve", lambda: nc.vector.tensor_scalar(rscr[0:1, 1:2], rscr[0:1, 0:1], -1.0, None, ALU.mult), inc="rtv")
            S.do("act", lambda b_=bb: nc.scalar.activation(
                rscr[0:1, 8:16], lgr[0:1, b_ * NZ:(b_ + 1) * NZ], AF.Exp, bias=rscr[0:1, 1:2]),
                waits=[("rtv", S.v("rtv"))], inc="rta")
            S.do("dve", lambda: nc.vector.reduce_sum(rscr[0:1, 2:3], rscr[0:1, 8:16], AX.X),
                 waits=[("rta", S.v("rta"))], inc="rtv")
            S.do("dve", lambda: nc.vector.reciprocal(rscr[0:1, 3:4], rscr[0:1, 2:3]), inc="rtv")
            S.do("dve", lambda: nc.vector.tensor_single_scalar(
                rscr[0:1, 8:16], rscr[0:1, 8:16], rscr[0:1, 3:4], ALU.mult), inc="rtv")
            S.do("dve", lambda: nc.vector.max(out=rscr[0:1, 16:24], in_=rscr[0:1, 8:16]), inc="rtv")
            S.do("dve", lambda: nc.vector.tensor_single_scalar(
                rscr[0:1, 24:32], rscr[0:1, 8:16], rscr[0:1, 18:19], ALU.is_ge), inc="rtv")
            S.do("dve", lambda: nc.vector.tensor_tensor(
                rscr[0:1, 24:32], rscr[0:1, 24:32], rscr[0:1, 8:16], ALU.mult), inc="rtv")
            S.do("dve", lambda: nc.vector.reduce_sum(rscr[0:1, 4:5], rscr[0:1, 24:32], AX.X), inc="rtv")
            S.do("dve", lambda: nc.vector.reciprocal(rscr[0:1, 5:6], rscr[0:1, 4:5]), inc="rtv")
            S.do("dve", lambda b_=bb: nc.vector.tensor_single_scalar(
                wrow[0:1, b_ * NZ:(b_ + 1) * NZ], rscr[0:1, 24:32], rscr[0:1, 5:6], ALU.mult), inc="rtv")
        RWV = S.v("rtv")
        S.do("pe", lambda: nc.tensor.matmul(r0[:, 0:16], ones1_sb[:], wrow[0:1, 0:16],
                                            start=True, stop=True),
             waits=[("rtv", RWV)], inc="rtp")
        S.do("dve", lambda: nc.vector.tensor_copy(w_bc[:], r0[:, 0:16]),
             waits=[("rtp", S.v("rtp"))], inc="rtv")
        for bb in range(2):
            S.do("dve", lambda b_=bb: nc.vector.tensor_tensor(
                zscr[:], w_bc[:, b_ * 8:(b_ + 1) * 8], zmask_sb[:], ALU.mult),
                waits=[("rtv", S.v("rtv"))], inc="rtv")
            S.do("dve", lambda b_=bb: nc.vector.reduce_sum(wz_col[:, b_:b_ + 1], zscr[:], AX.X), inc="rtv")
        WZ = S.v("rtv")
        if debug:
            S.do("pool", lambda: nc.gpsimd.dma_start(out=dbg["w"], in_=wrow[:]),
                 waits=[("rtv", RWV)], inc="dbgs", incn=16)

    S.mark("router")
    with ExitStack() as c6:
        sb6 = lambda n, s, dt=F32: c6.enter_context(nc.sbuf_tensor(n, list(s), dt))
        wa_sb = sb6("wa_sb", (128, 8 * DFF), BF16)
        wb_sb = sb6("wb_sb", (128, 32 * D), BF16)
        xn3r = [sb6(f"xn3r{i}", (128, 8 * MM), BF16) for i in range(2)]
        G_sb = sb6("G_sb", (128, 32 * MM), BF16)
        z_ev = [sb6(f"zev{i}", (128, MM)) for i in range(2)]
        with nc.psum_tensor("fz0", [128, MM], F32) as z0, \
             nc.psum_tensor("fz1", [128, MM], F32) as z1, \
             nc.psum_tensor("fz2", [128, MM], F32) as z2, \
             nc.psum_tensor("fz3", [128, MM], F32) as z3:
            be_exit = [("agxn", 1), ("x1d", S.v("x1d")),
                       ("bod0", S.v("bod0")), ("bod1", S.v("bod1"))]
            if debug:
                be_exit.append(("dbgs", S.v("dbgs")))
            S.do("pool", lambda: nc.gpsimd.dma_start(
                out=wa_sb[:].rearrange("p (kt f) -> p kt f", kt=8),
                in_=wa_d.rearrange("(kt p) f -> p kt f", p=128)),
                waits=be_exit, inc="ffw", incn=16)
            S.do("pool", lambda: nc.gpsimd.dma_start(out=wb_sb[:], in_=wb_d),
                 inc="ffw", incn=16)
            FFW = S.v("ffw")
            prev_wb = ("ffq", 0)
            for fc, (mi_, pdst) in enumerate((m, p) for m in range(NMC)
                                             for p in range(NC)):
                off = mi_ * MM
                pp = fc % 2
                xr, ze = xn3r[pp], z_ev[pp]
                xgate = ("ffp", max(0, S.v("ffp") - 32))
                S.do("sp", lambda p_=pdst, o=off, x=xr: nc.sync.dma_start(
                    out=x[:].rearrange("p (kt c) -> p kt c", kt=8),
                    in_=xn3_all.ap()[p_ * D: (p_ + 1) * D, o:o + MM]
                    .rearrange("(kt p) c -> p kt c", p=128)),
                    waits=[("agxn", 1), xgate, (f"ffl{pp}", S.v(f"ffl{pp}"))],
                    inc=f"ffl{pp}", incn=16)
                FFL = (f"ffl{pp}", S.v(f"ffl{pp}"))
                for ft in range(32):
                    pb = [z0, z1][ft % 2]
                    for k in range(8):
                        S.do("pe", lambda kk=k, f_=ft, p_=pb, x=xr: nc.tensor.matmul(
                            p_[:, 0:MM], wa_sb[:, kk * DFF + f_ * 128: kk * DFF + (f_ + 1) * 128],
                            x[:, kk * MM:(kk + 1) * MM],
                            start=(kk == 0), stop=(kk == 7)),
                            waits=([FFL, ("ffw", FFW),
                                    ("ffg", max(0, S.v("ffg") - 1))] +
                                   ([("rtv", WZ)] if fc == 0 and ft == 0 else [])
                                   if k == 0 else []),
                            inc=("ffp" if k == 7 else None))
                    S.do("act", lambda f_=ft, p_=pb: nc.scalar.activation(
                        G_sb[:, f_ * MM:(f_ + 1) * MM], p_[:, 0:MM], AF.Gelu_apprx_tanh),
                        waits=[("ffp", S.v("ffp"))] + ([prev_wb] if ft == 0 else []),
                        inc="ffg")
                GD = S.v("ffg")
                bb = pdst // 4
                for dm in range(8):
                    pb = [z2, z3][dm % 2]
                    for k in range(32):
                        S.do("pe", lambda kk=k, d_=dm, p_=pb: nc.tensor.matmul(
                            p_[:, 0:MM], wb_sb[:, kk * D + d_ * 128: kk * D + (d_ + 1) * 128],
                            G_sb[:, kk * MM:(kk + 1) * MM],
                            start=(kk == 0), stop=(kk == 31)),
                            waits=([("ffg", GD), ("ffv", max(0, S.v("ffv") - 1))]
                                   if k == 0 else []),
                            inc=("ffq" if k == 31 else None))
                    S.do("dve", lambda p_=pb, b_=bb, z=ze: nc.vector.tensor_scalar(
                        z[:], p_[:, 0:MM], wz_col[:, b_:b_ + 1], None, ALU.mult),
                        waits=[("ffq", S.v("ffq")), ("rtv", WZ),
                               (f"ffd{pp}", S.v(f"ffd{pp}"))], inc="ffv")
                    S.do("sp", lambda p_=pdst, o=off, d_=dm, z=ze: nc.sync.dma_start(
                        out=rs_in[o // MM].ap()[p_ * D + d_ * 128: p_ * D + (d_ + 1) * 128, :],
                        in_=z[:]),
                        waits=[("ffv", S.v("ffv"))], inc=f"ffd{pp}", incn=16)
                prev_wb = ("ffq", S.v("ffq"))
                if pdst == NC - 1:
                    S.do("pool", lambda q_=mi_: nc.gpsimd.collective_compute(
                        "ReduceScatter", ALU.add, replica_groups=RG,
                        ins=[rs_in[q_].ap()], outs=[rs_out[q_].ap()]),
                        waits=[("ffd0", S.v("ffd0")), ("ffd1", S.v("ffd1"))],
                        inc=f"rs{mi_}", cc=True)
        ffn_exit = [("ffp", S.v("ffp")), ("ffq", S.v("ffq")), ("ffg", S.v("ffg")),
                    ("ffv", S.v("ffv")), ("ffd0", S.v("ffd0")), ("ffd1", S.v("ffd1")),
                    ("ffl0", S.v("ffl0")), ("ffl1", S.v("ffl1"))]

    S.mark("ffn")
    with ExitStack() as c7:
        sb7 = lambda n, s, dt=F32: c7.enter_context(nc.sbuf_tensor(n, list(s), dt))
        xf1 = [sb7(f"xf1{i}", (128, 128)) for i in range(2)]
        xrs = [sb7(f"xrs{i}", (128, 128)) for i in range(2)]
        fin = [sb7(f"fin{i}", (128, D)) for i in range(2)]
        with nc.psum_tensor("fp0", [128, MM], F32) as fp0, \
             nc.psum_tensor("fp1", [128, MM], F32) as fp1:
            for tt in range(TOK // 128):
                q = (tt * 128) // MM
                fsl = tt % 2
                ft = fin[fsl]
                for dm in range(8):
                    sl = dm % 2
                    b1_, b2_ = xf1[sl], xrs[sl]
                    pb = [fp0, fp1][sl]
                    S.do("sp", lambda t=tt, d_=dm, b=b1_: nc.sync.dma_start(
                        out=b[:], in_=x1_sp.ap()[d_ * 128:(d_ + 1) * 128, t * 128:(t + 1) * 128]),
                        waits=[(f"rs{q}", 1), ("x1d", S.v("x1d")),
                               ("fnp", max(0, S.v("fnp") - 1))] +
                              (ffn_exit if (tt == 0 and dm == 0) else []),
                        inc=f"fnlA{sl}", incn=16)
                    S.do("sp", lambda t=tt, d_=dm, b=b2_, q_=q: nc.sync.dma_start(
                        out=b[:], in_=rs_out[q_].ap()[d_ * 128:(d_ + 1) * 128,
                                                      (t * 128) % MM:(t * 128) % MM + 128]),
                        waits=[("fna", max(0, S.v("fna") - 1))],
                        inc=f"fnlA{sl}", incn=16)
                    S.do("dve", lambda b=b1_, c=b2_: nc.vector.tensor_tensor(b[:], b[:], c[:], ALU.add),
                         waits=[(f"fnlA{sl}", S.v(f"fnlA{sl}")), (f"fnlA{sl}", S.v(f"fnlA{sl}"))],
                         inc="fna")
                    S.do("pe", lambda b=b1_, p_=pb: nc.tensor.transpose(
                        p_[:, 0:128], b[:], ident_sb[:]),
                        waits=[("fna", S.v("fna")), ("fne", max(0, S.v("fne") - 1))], inc="fnp")
                    S.do("act", lambda d_=dm, f_=ft, p_=pb: nc.scalar.activation(
                        f_[:, d_ * 128:(d_ + 1) * 128], p_[:, 0:128], AF.Copy),
                        waits=[("fnp", S.v("fnp"))] + ([(f"fno{fsl}", S.v(f"fno{fsl}"))] if dm == 0 else []),
                        inc="fne")
                S.do("sp", lambda t=tt, f_=ft: nc.sync.dma_start(
                    out=out_d[t * 128:(t + 1) * 128, :], in_=f_[:]),
                    waits=[("fne", S.v("fne"))], inc=f"fno{fsl}", incn=16)
    FNO = [("fno0", S.v("fno0")), ("fno1", S.v("fno1"))]

    if debug:
        S.do("pool", lambda: nc.gpsimd.dma_start(out=dbg["u_f"], in_=u_all["f"].ap()[0:DI, :]),
             waits=[("aguf", 1)], inc="dbgs", incn=16)
        S.do("pool", lambda: nc.gpsimd.dma_start(out=dbg["dbc_f"], in_=dbc_all["f"].ap()[0:96, :]),
             waits=[("agdf", 1)], inc="dbgs", incn=16)
        for j in range(NC):
            S.do("pool", lambda j_=j: nc.gpsimd.dma_start(
                out=dbg["yf"][j_ * CSL:(j_ + 1) * CSL, :],
                in_=y_allb.ap()[j_ * 4 * CSL: j_ * 4 * CSL + CSL, :]),
                waits=[("agy0", 1)], inc="dbgs", incn=16)
        S.do("pool", lambda: nc.gpsimd.dma_start(out=dbg["x1"], in_=x1_sp.ap()),
             waits=[("l3d0", S.v("l3d0")), ("l3d1", S.v("l3d1")), ("x1d", S.v("x1d"))],
             inc="dbgs", incn=16)
        S.do("pool", lambda: nc.gpsimd.dma_start(out=dbg["xn3"], in_=xn3_all.ap()[0:D, :]),
             waits=[("agxn", 1)], inc="dbgs", incn=16)
        for q in range(NMC):
            S.do("pool", lambda q_=q: nc.gpsimd.dma_start(
                out=dbg["zs"][:, q_ * MM:(q_ + 1) * MM], in_=rs_out[q_].ap()),
                waits=[(f"rs{q_2}", 1) for q_2 in range(NMC)], inc="dbgs", incn=16)

    S.do("pool", lambda: nc.gpsimd.nop(),
         waits=FNO + ([("dbgs", S.v("dbgs"))] if debug else []), inc=None, cc=True)

    if phases is not None:
        S.truncate(phases)
        with nc.sbuf_tensor("stub", [128, D], F32) as stub:
            S.do("dve", lambda: nc.vector.memset(stub[:], 0.0), inc="stbv")
            for tt in range(TOK // 128):
                S.do("sp", lambda t=tt: nc.sync.dma_start(
                    out=out_d[t * 128:(t + 1) * 128, :], in_=stub[:]),
                    waits=[("stbv", 1)], inc="stbd", incn=16)
            S.do("pool", lambda: nc.gpsimd.nop(), waits=[("stbd", S.v("stbd"))], cc=True)

    with nc.Block() as block:
        @block.tensor
        def _(eng):
            S.emit("pe", eng)

        @block.vector
        def _(eng):
            S.emit("dve", eng)

        @block.scalar
        def _(eng):
            S.emit("act", eng)

        @block.sync
        def _(eng):
            S.emit("sp", eng)

        @block.gpsimd
        def _(eng):
            S.emit("pool", eng)

    top.close()
    return nc


def pack_cols(vec):
    v = np.ascontiguousarray(np.asarray(vec, np.float32)).reshape(-1)
    c = v.shape[0] // 128
    return np.ascontiguousarray(v.reshape(c, 128).T)


def pack_blocks(mat):
    m = np.ascontiguousarray(np.asarray(mat, np.float32))
    c, w = m.shape
    nt = c // 128
    return np.ascontiguousarray(m.reshape(nt, 128, w).transpose(1, 0, 2).reshape(128, nt * w))


def _prep_inputs(L, kw):
    f32 = np.float32
    TOK = L // 4
    TOKP = TOK + 2 * PAD
    x = np.asarray(kw["L0_attn_x"], f32)

    def wT(a):
        return np.ascontiguousarray(np.asarray(a, f32).T)

    base = {}
    for d, pre in (("f", "L0_attn_f_"), ("b", "L0_attn_b_")):
        in_w = np.asarray(kw[pre + "in_w"], f32)
        base[f"w_in_xi_{d}"] = wT(in_w[0:DI])
        base[f"w_in_z_{d}"] = wT(in_w[DI:2 * DI])
        base[f"w_x_{d}"] = pack_blocks(wT(np.asarray(kw[pre + "x_w"], f32)))
        base[f"w_out_{d}"] = pack_blocks(wT(np.asarray(kw[pre + "out_w"], f32)))
        base[f"conv_w_{d}"] = pack_blocks(np.asarray(kw[pre + "conv_w"], f32))
        base[f"conv_b_{d}"] = pack_cols(np.asarray(kw[pre + "conv_b"], f32))
    base["w_comb"] = pack_blocks(wT(np.asarray(kw["L0_attn_combine_w"], f32)))
    base["g_col"] = pack_cols(np.asarray(kw["L0_attn_ln_g"], f32))
    base["b_col"] = pack_cols(np.asarray(kw["L0_attn_ln_b"], f32))
    base["cb_col"] = pack_cols(np.asarray(kw["L0_attn_combine_b"], f32))
    base["w1"] = pack_blocks(np.asarray(kw["L0_ffn_W1"], f32))
    base["b1_col"] = pack_cols(np.asarray(kw["L0_ffn_b1"], f32))
    base["w2"] = pack_blocks(np.asarray(kw["L0_ffn_W2"], f32))
    base["b2_col"] = np.ascontiguousarray(np.asarray(kw["L0_ffn_b2"], f32).reshape(NZ, 1))
    sel = np.zeros((16, NST * 128), f32)
    for n in range(NST):
        sel[n, n * 128:(n + 1) * 128] = 1.0
    base["sel"] = sel
    base["ones1"] = np.ones((1, 128), f32)
    base["onescol"] = np.ones((128, 1), f32)
    base["ident"] = np.eye(128, dtype=f32)
    cstv = np.zeros((128, 4), f32)
    cstv[:, 0] = EPS
    cstv[:, 1] = -0.5
    cstv[:, 2] = -1.0
    cstv[:, 3] = 1.0
    base["cst_in"] = cstv

    Wa = np.asarray(kw["L0_ffn_Wa"], f32)
    Wb = np.asarray(kw["L0_ffn_Wb"], f32)
    CSLn = DI // NC
    NCT_ = CSLn // 128
    lane = np.arange(128, dtype=np.int32)

    per_core = []
    for rank in range(NC):
        m = dict(base)
        b = rank // 4
        q = rank % 4
        t0 = q * TOK
        xp = np.zeros((TOKP, D), f32)
        lo = max(0, t0 - PAD)
        hi = min(L, t0 + TOK + PAD)
        xp[lo - (t0 - PAD):hi - (t0 - PAD)] = x[b, lo:hi]
        m["x0"] = xp
        c0 = rank * CSLn
        for d, pre in (("f", "L0_attn_f_"), ("b", "L0_attn_b_")):
            dtw = np.asarray(kw[pre + "dt_w"], f32)
            m[f"w_dt_{d}"] = np.ascontiguousarray(dtw.T[:, c0:c0 + CSLn])
            m[f"dt_b_{d}"] = pack_cols(np.asarray(kw[pre + "dt_b"], f32)[c0:c0 + CSLn])
            m[f"A_{d}"] = pack_blocks(np.asarray(kw[pre + "A_log"], f32)[c0:c0 + CSLn])
            m[f"D_{d}"] = pack_cols(np.asarray(kw[pre + "D"], f32)[c0:c0 + CSLn])
        m["wa"] = np.ascontiguousarray(Wa[rank])
        m["wb"] = pack_blocks(np.ascontiguousarray(Wb[rank]))
        m["edge"] = np.stack([np.full(128, 0.0 if q == 0 else 1.0, f32),
                              np.full(128, 0.0 if q == 3 else 1.0, f32)], axis=1)
        m["bmask"] = np.stack([np.full(128, 1.0 if b == 0 else 0.0, f32),
                               np.full(128, 1.0 if b == 1 else 0.0, f32)], axis=1)
        zm = np.zeros((128, NZ), f32)
        zm[:, rank] = 1.0
        m["zmask"] = zm
        idxu = np.zeros((128, 2 * 2 * 4 * NCT_), np.int32)
        for di in range(2):
            for bb in range(2):
                for gi in range(4):
                    p = 4 * bb + gi
                    for ct in range(NCT_):
                        col = (di * 2 + bb) * 4 * NCT_ + gi * NCT_ + ct
                        idxu[:, col] = p * DI + c0 + ct * 128 + lane
        m["idxu"] = np.ascontiguousarray(idxu)
        idxy = np.zeros((128, 2 * 16), np.int32)
        for di, dch in enumerate("fb"):
            g = q if dch == "f" else 3 - q
            jb = 2 * b + (0 if dch == "f" else 1)
            for rt in range(16):
                j = rt // 2
                ct = rt % 2
                idxy[:, di * 16 + rt] = (g * NC * 4 * CSLn + j * 4 * CSLn +
                                         jb * CSLn + ct * 128 + lane)
        m["idxy"] = np.ascontiguousarray(idxy)
        per_core.append(m)
    return per_core


_CACHE = {}


def kernel(**inputs):
    L = inputs["L0_attn_x"].shape[1]
    TOK = L // 4
    if ("prog", L) not in _CACHE:
        _CACHE[("prog", L)] = build_program(L, debug=False, chain=True)
    nc = _CACHE[("prog", L)]
    in_maps = _prep_inputs(L, inputs)
    res = run_bass_kernel_spmd(nc, in_maps, core_ids=list(range(NC))).results
    out = np.zeros((BATCH, L, D), np.float32)
    for rank in range(NC):
        b, q = rank // 4, rank % 4
        out[b, q * TOK:(q + 1) * TOK, :] = res[rank]["out"]
    return out


# version 08
# speedup vs baseline: 1.4573x; 1.0138x over previous
import numpy as np
import ml_dtypes
import concourse.bass as bass
import concourse.mybir as mybir
from concourse.bass_core_rust import IndirectOffsetOnAxis
from concourse.bass_utils import run_bass_kernel_spmd
from contextlib import ExitStack

F32 = mybir.dt.float32
BF16 = mybir.dt.bfloat16
I32 = mybir.dt.int32
AF = mybir.ActivationFunctionType
ALU = mybir.AluOpType
AX = mybir.AxisListType

NC = 8
D = 1024
DI = 2048
NST = 16
DTR = 64
NZ = 8
DFF = 4096
RH = 256
BATCH = 2
PAD = 128
MM = 512
EPS = 1e-5
ENGINES = ("pe", "dve", "act", "sp", "pool")
JOBS = [("f", 0), ("b", 0), ("f", 1), ("b", 1)]


class Sched:
    def __init__(self, nc, chain=False):
        self.nc = nc
        self.chain = chain
        self.ops = {e: [] for e in ENGINES}
        self.cnt = {}
        self.sems = {}
        self._ctx = []
        self.last_inc = {e: None for e in ENGINES}
        self.marks = {}

    def mark(self, name):
        self.marks[name] = ({e: len(self.ops[e]) for e in ENGINES},
                            dict(self.cnt), dict(self.last_inc))

    def truncate(self, name):
        lens, cnt, last = self.marks[name]
        for e in ENGINES:
            self.ops[e] = self.ops[e][:lens[e]]
        self.cnt = dict(cnt)
        self.last_inc = dict(last)

    def sem(self, name):
        if name not in self.sems:
            c = self.nc.semaphore(name)
            self.sems[name] = c.__enter__()
            self._ctx.append(c)
            self.cnt[name] = 0
        return name

    def do(self, eng, fn, waits=(), inc=None, incn=1, cc=False):
        w = []
        for s, val in waits:
            self.sem(s)
            assert val <= self.cnt[s], f"wait {s}>={val} but only {self.cnt[s]} scheduled"
            if val > 0:
                w.append((s, val))
        if self.chain and eng in ("dve", "act") and self.last_inc[eng]:
            w.append(self.last_inc[eng])
        if inc is not None:
            self.sem(inc)
            self.cnt[inc] += incn
            self.last_inc[eng] = (inc, self.cnt[inc])
        self.ops[eng].append((tuple(w), fn, inc, incn))

    def v(self, name):
        self.sem(name)
        return self.cnt.get(name, 0)

    def emit(self, eng_name, eng):
        for w, fn, inc, incn in self.ops[eng_name]:
            seen = set()
            for s, val in w:
                if (s, val) in seen:
                    continue
                seen.add((s, val))
                eng.wait_ge(self.sems[s], val)
            inst = fn()
            if inc is not None and inst is not None:
                inst.then_inc(self.sems[inc], incn)


def build_program(L, debug=False, chain=False, phases=None):
    TOK = L // 4
    TOKP = TOK + 2 * PAD
    NTT = TOKP // 128
    CSL = DI // NC
    NCT = CSL // 128
    SC = min(1024, TOK)
    NSUB = TOK // SC
    NMC = TOK // MM
    NFC = NC * TOK // MM
    QT = TOK // 4

    nc = bass.Bass("TRN2", target_bir_lowering=False)
    S = Sched(nc, chain=chain)
    RG = [list(range(NC))]

    def inp(name, shape, dtype=F32):
        return nc.dram_tensor(name, list(shape), dtype, kind="ExternalInput").ap()

    x0_d = inp("x0", (TOKP, D))
    w_in_xi = {d: inp(f"w_in_xi_{d}", (D, DI)) for d in "fb"}
    w_in_z = {d: inp(f"w_in_z_{d}", (D, DI)) for d in "fb"}
    w_x = {d: inp(f"w_x_{d}", (128, 16 * 96)) for d in "fb"}
    w_dt = {d: inp(f"w_dt_{d}", (DTR, CSL)) for d in "fb"}
    w_out = {d: inp(f"w_out_{d}", (128, 16 * D)) for d in "fb"}
    w_comb = inp("w_comb", (128, 16 * D))
    conv_w = {d: inp(f"conv_w_{d}", (128, 16 * 4)) for d in "fb"}
    conv_b = {d: inp(f"conv_b_{d}", (128, 16)) for d in "fb"}
    dt_b = {d: inp(f"dt_b_{d}", (128, NCT)) for d in "fb"}
    A_in = {d: inp(f"A_{d}", (128, NCT * NST)) for d in "fb"}
    D_in = {d: inp(f"D_{d}", (128, NCT)) for d in "fb"}
    g_col = inp("g_col", (128, 8))
    b_col = inp("b_col", (128, 8))
    cb_col = inp("cb_col", (128, 8))
    wa_d = inp("wa", (D, DFF))
    wb_d = inp("wb", (128, 32 * D))
    w1_d = inp("w1", (128, 8 * RH))
    b1_d = inp("b1_col", (128, 2))
    w2_d = inp("w2", (128, 2 * NZ))
    b2_d = inp("b2_col", (NZ, 1))
    sel_d = inp("sel", (16, NST * 128))
    ones1_d = inp("ones1", (1, 128))
    onescol_d = inp("onescol", (128, 1))
    ident_d = inp("ident", (128, 128))
    edge_d = inp("edge", (128, 2))
    bmask_d = inp("bmask", (128, 2))
    zmask_d = inp("zmask", (128, 8))
    cst_d = inp("cst_in", (128, 4))
    idxu_d = inp("idxu", (128, 2 * 2 * 4 * NCT), I32)
    idxy_d = inp("idxy", (128, 2 * 16), I32)

    out_d = nc.dram_tensor("out", [D, TOK], F32, kind="ExternalOutput").ap()
    dbg = {}
    if debug:
        for nm, shp in [("xn2", (D, TOKP)), ("u_f", (DI, TOK)), ("dbc_f", (96, TOK)),
                        ("yf", (DI, TOK)), ("x1", (D, TOK)), ("xn3", (D, TOK)),
                        ("w", (1, 2 * NZ)), ("zs", (D, TOK))]:
            dbg[nm] = nc.dram_tensor("dbg_" + nm, list(shp), F32, kind="ExternalOutput").ap()

    dr = lambda name, shape, dtype=F32: nc.dram_tensor(name, list(shape), dtype)
    drs = lambda name, shape, dtype=F32: nc.dram_tensor(name, list(shape), dtype,
                                                        addr_space="Shared")
    u_in = {d: dr(f"u_in_{d}", (DI, TOK), F32) for d in "fb"}
    u_all = {d: drs(f"u_all_{d}", (NC * DI, TOK), F32) for d in "fb"}
    dbc_in = {d: dr(f"dbc_in_{d}", (96, TOK), F32) for d in "fb"}
    dbc_all = {d: drs(f"dbc_all_{d}", (NC * 96, TOK), F32) for d in "fb"}
    y_in = [dr(f"y_in_{g}", (4 * CSL, TOK), F32) for g in range(4)]
    y_allb = drs("y_allb", (4 * NC * 4 * CSL, TOK), F32)
    of_sp = {d: dr(f"of_{d}", (D, TOK), BF16) for d in "fb"}
    xn3_in = dr("xn3_in", (D, TOK), BF16)
    xn3_all = drs("xn3_all", (NC * D, TOK), BF16)
    pooled_in = dr("pooled_in", (BATCH * 128, 8), F32)
    pooled_out = drs("pooled_out", (BATCH * 128, 8), F32)
    x1_sp = dr("x1_sp", (D, TOK), F32)
    xn2_sp = dr("xn2_sp", (D, TOKP), BF16)
    rs_in = [dr(f"rs_in{q}", (NC * D, MM), F32) for q in range(NMC)]
    rs_out = [dr(f"rs_out{q}", (D, MM), F32) for q in range(NMC)]
    lg_bounce = dr("lg_bounce", (BATCH, NZ), F32)

    top = ExitStack()
    sbT = lambda n, s, dt=F32: top.enter_context(nc.sbuf_tensor(n, list(s), dt))

    convw_sb = {d: sbT(f"convw_{d}", (128, 16 * 4)) for d in "fb"}
    convb_sb = {d: sbT(f"convb_{d}", (128, 16)) for d in "fb"}
    dtb_sb = {d: sbT(f"dtb_{d}", (128, NCT)) for d in "fb"}
    A_sb = {d: sbT(f"Am_{d}", (128, NCT * NST)) for d in "fb"}
    D_sb = {d: sbT(f"Dp_{d}", (128, NCT)) for d in "fb"}
    g_sb = sbT("g_sb", (128, 8))
    bcol_sb = sbT("bcol_sb", (128, 8))
    cb_sb = sbT("cb_sb", (128, 8))
    sel_sb = sbT("sel_sb", (16, NST * 128))
    ones1_sb = sbT("ones1_sb", (1, 128))
    onescol_sb = sbT("onescol_sb", (128, 1))
    ident_sb = sbT("ident_sb", (128, 128))
    edge_sb = sbT("edge_sb", (128, 2))
    bmask_sb = sbT("bmask_sb", (128, 2))
    zmask_sb = sbT("zmask_sb", (128, 8))
    cst = sbT("cst", (128, 4))
    idxu_sb = sbT("idxu_sb", (128, 2 * 2 * 4 * NCT), I32)
    idxy_sb = sbT("idxy_sb", (128, 2 * 16), I32)
    wdt_sb = {d: sbT(f"wdt_{d}", (DTR, CSL), BF16) for d in "fb"}
    wx_sb = {d: sbT(f"wx_{d}", (128, 16 * 96), BF16) for d in "fb"}
    w1_sb = sbT("w1_sb", (128, 8 * RH), BF16)
    b1_sb = sbT("b1_sb", (128, 2))
    b2_sb = sbT("b2_sb", (NZ, 1))
    w2_sb = sbT("w2_sb", (128, 2 * NZ), BF16)
    stat = sbT("stat", (128, 16))
    state = sbT("state", (128, 4 * NST * NCT))
    pooled_sb = sbT("pooled_sb", (128, 2 * 8))
    h_sb = sbT("h_sb", (128, 4))
    lgsb = sbT("lgsb", (NZ, 2))
    lgr = sbT("lgr", (1, 2 * NZ))
    rscr = sbT("rscr", (1, 4 * NZ))
    wrow = sbT("wrow", (1, 2 * NZ))
    w_bc = sbT("w_bc", (128, 2 * NZ))
    wz_col = sbT("wz_col", (128, 2))
    zscr = sbT("zscr", (128, 8))
    colsum = sbT("colsum", (128, 8))
    pooled_bf = sbT("pooled_bf", (128, 2 * 8), BF16)
    h_bf = sbT("h_bf", (128, 4), BF16)
    mid = ExitStack()
    xn2_t = mid.enter_context(nc.sbuf_tensor("xn2", [128, 8 * TOKP], BF16))

    for dst, src in [
        (convw_sb["f"][:], conv_w["f"]), (convw_sb["b"][:], conv_w["b"]),
        (convb_sb["f"][:], conv_b["f"]), (convb_sb["b"][:], conv_b["b"]),
        (dtb_sb["f"][:], dt_b["f"]), (dtb_sb["b"][:], dt_b["b"]),
        (D_sb["f"][:], D_in["f"]), (D_sb["b"][:], D_in["b"]),
        (A_sb["f"][:], A_in["f"]), (A_sb["b"][:], A_in["b"]),
        (g_sb[:], g_col), (bcol_sb[:], b_col), (cb_sb[:], cb_col),
        (sel_sb[:], sel_d), (ones1_sb[:], ones1_d), (onescol_sb[:], onescol_d),
        (ident_sb[:], ident_d), (edge_sb[:], edge_d), (bmask_sb[:], bmask_d),
        (zmask_sb[:], zmask_d), (idxu_sb[:], idxu_d), (idxy_sb[:], idxy_d),
        (cst[:], cst_d),
        (b1_sb[:], b1_d), (b2_sb[:], b2_d),
    ]:
        S.do("sp", lambda d=dst, s=src: nc.sync.dma_start(out=d, in_=s), inc="ldc", incn=16)
    for d in "fb":
        S.do("pool", lambda dd=d: nc.gpsimd.dma_start(out=wdt_sb[dd][:], in_=w_dt[dd]), inc="ldc2", incn=16)
        S.do("pool", lambda dd=d: nc.gpsimd.dma_start(out=wx_sb[dd][:], in_=w_x[dd]), inc="ldc2", incn=16)
    S.do("pool", lambda: nc.gpsimd.dma_start(out=w1_sb[:], in_=w1_d), inc="ldc2", incn=16)
    S.do("pool", lambda: nc.gpsimd.dma_start(out=w2_sb[:], in_=w2_d), inc="ldc2", incn=16)
    LDC = S.v("ldc")
    LDC2 = S.v("ldc2")
    for d in "fb":
        S.do("act", lambda dd=d: nc.scalar.activation(A_sb[dd][:], A_sb[dd][:], AF.Exp),
             waits=[("ldc", LDC)], inc="p1a")
        S.do("dve", lambda dd=d: nc.vector.tensor_scalar(
            A_sb[dd][:], A_sb[dd][:], -1.0, None, ALU.mult),
            waits=[("p1a", S.v("p1a"))], inc="p1v")
    AEXP = S.v("p1v")

    INV_D = 1.0 / D
    with ExitStack() as c1:
        sb1 = lambda n, s, dt=F32: c1.enter_context(nc.sbuf_tensor(n, list(s), dt))
        x0_tm = [sb1(f"x0tm{i}", (128, D)) for i in range(2)]
        sq_tm = [sb1(f"sqtm{i}", (128, D)) for i in range(2)]
        xn1_tm = [sb1(f"xn1tm{i}", (128, D)) for i in range(2)]
        x1ev = [sb1(f"x1ev{i}", (128, 128)) for i in range(4)]
        with nc.psum_tensor("p1ps0", [128, MM], F32) as pA, \
             nc.psum_tensor("p1ps1", [128, MM], F32) as pB:
            psb = [pA, pB]
            slot_free = [("p1tp", 0), ("p1tp", 0)]
            x1ev_free = [(f"p1q{i}", 0) for i in range(4)]
            bank_free = [[("p1ev", 0)], [("p1ev", 0)]]
            nev = 0
            for ti in range(NTT):
                pp = ti % 2
                x0t, sqt, xn1t = x0_tm[pp], sq_tm[pp], xn1_tm[pp]
                own = 1 <= ti <= NTT - 2
                S.do("sp", lambda t=ti, b=x0t: nc.sync.dma_start(
                    out=b[:], in_=x0_d[t * 128:(t + 1) * 128, :]),
                    waits=[slot_free[pp]], inc=f"p1x{pp}", incn=16)
                ld = [(f"p1x{pp}", S.v(f"p1x{pp}"))]
                S.do("act", lambda b=x0t, o=sqt: nc.scalar.activation(o[:], b[:], AF.Square),
                     waits=ld + [("p1v", S.v("p1v"))], inc="p1a")
                S.do("dve", lambda b=x0t: nc.vector.reduce_sum(stat[:, 0:1], b[:], AX.X),
                     waits=ld + [("p1a", S.v("p1a") - 1)], inc="p1v")
                S.do("dve", lambda o=sqt: nc.vector.reduce_sum(stat[:, 1:2], o[:], AX.X),
                     waits=[("p1a", S.v("p1a"))], inc="p1v")
                S.do("dve", lambda: nc.vector.tensor_scalar(stat[:, 2:3], stat[:, 0:1], INV_D, None, ALU.mult), inc="p1v")
                S.do("dve", lambda: nc.vector.tensor_scalar(stat[:, 3:4], stat[:, 1:2], INV_D, None, ALU.mult), inc="p1v")
                S.do("dve", lambda: nc.vector.tensor_tensor(stat[:, 4:5], stat[:, 2:3], stat[:, 2:3], ALU.mult), inc="p1v")
                S.do("dve", lambda: nc.vector.tensor_tensor(stat[:, 5:6], stat[:, 3:4], stat[:, 4:5], ALU.subtract), inc="p1v")
                S.do("act", lambda: nc.scalar.activation(stat[:, 6:7], stat[:, 5:6], AF.Ln, bias=cst[:, 0:1]),
                     waits=[("p1v", S.v("p1v"))], inc="p1a")
                S.do("act", lambda: nc.scalar.activation(stat[:, 7:8], stat[:, 6:7], AF.Exp, scale=cst[:, 1:2]), inc="p1a")
                S.do("dve", lambda b=x0t, o=xn1t: nc.vector.tensor_scalar(
                    o[:], b[:], stat[:, 2:3], stat[:, 7:8], ALU.subtract, ALU.mult),
                    waits=[("p1a", S.v("p1a"))], inc="p1v")
                S.do("act", lambda b=xn1t, o=sqt: nc.scalar.activation(o[:], b[:], AF.Square),
                     waits=[("p1v", S.v("p1v"))], inc="p1a")
                S.do("dve", lambda b=xn1t: nc.vector.reduce_sum(stat[:, 8:9], b[:], AX.X), inc="p1v")
                S.do("dve", lambda o=sqt: nc.vector.reduce_sum(stat[:, 9:10], o[:], AX.X),
                     waits=[("p1a", S.v("p1a"))], inc="p1v")
                S.do("dve", lambda: nc.vector.tensor_scalar(stat[:, 10:11], stat[:, 8:9], INV_D, None, ALU.mult), inc="p1v")
                S.do("dve", lambda: nc.vector.tensor_scalar(stat[:, 11:12], stat[:, 9:10], INV_D, None, ALU.mult), inc="p1v")
                S.do("dve", lambda: nc.vector.tensor_tensor(stat[:, 12:13], stat[:, 10:11], stat[:, 10:11], ALU.mult), inc="p1v")
                S.do("dve", lambda: nc.vector.tensor_tensor(stat[:, 13:14], stat[:, 11:12], stat[:, 12:13], ALU.subtract), inc="p1v")
                S.do("act", lambda: nc.scalar.activation(stat[:, 14:15], stat[:, 13:14], AF.Ln, bias=cst[:, 0:1]),
                     waits=[("p1v", S.v("p1v"))], inc="p1a")
                S.do("act", lambda: nc.scalar.activation(stat[:, 15:16], stat[:, 14:15], AF.Exp, scale=cst[:, 1:2]), inc="p1a")
                a2 = [("p1a", S.v("p1a"))]
                if own:
                    S.do("dve", lambda b=x0t, a=xn1t: nc.vector.tensor_tensor(b[:], b[:], a[:], ALU.add),
                         waits=a2, inc="p1v")
                S.do("dve", lambda a=xn1t: nc.vector.tensor_scalar(
                    a[:], a[:], stat[:, 10:11], stat[:, 15:16], ALU.subtract, ALU.mult),
                    waits=a2, inc="p1v")
                vr = [("p1v", S.v("p1v"))]
                for dt_ in range(8):
                    bk = dt_ % 2
                    pb = psb[bk]
                    S.do("pe", lambda a=xn1t, p=pb, d=dt_: nc.tensor.transpose(
                        p[:, 0:128], a[:, d * 128:(d + 1) * 128], ident_sb[:]),
                        waits=[("ldc", LDC)] + vr + bank_free[bk], inc="p1tp")
                    S.do("dve", lambda p=pb, d=dt_, t=ti: nc.vector.tensor_scalar(
                        xn2_t[:, d * TOKP + t * 128: d * TOKP + (t + 1) * 128], p[:, 0:128],
                        g_sb[:, d:d + 1], bcol_sb[:, d:d + 1], ALU.mult, ALU.add),
                        waits=[("p1tp", S.v("p1tp"))], inc="p1ev")
                    bank_free[bk] = [("p1ev", S.v("p1ev"))]
                    if own:
                        ev = x1ev[nev % 4]
                        S.do("pe", lambda b=x0t, p=pb, d=dt_: nc.tensor.transpose(
                            p[:, 128:256], b[:, d * 128:(d + 1) * 128], ident_sb[:]),
                            waits=[("p1ev", S.v("p1ev"))] + bank_free[bk][1:], inc="p1tp")
                        S.do("dve", lambda p=pb, e=ev: nc.vector.tensor_copy(e[:], p[:, 128:256]),
                             waits=[("p1tp", S.v("p1tp")), x1ev_free[nev % 4]], inc="p1ev2")
                        bank_free[bk] = [("p1ev", S.v("p1ev")), ("p1ev2", S.v("p1ev2"))]
                        S.do("sp", lambda e=ev, d=dt_, t=ti: nc.sync.dma_start(
                            out=x1_sp.ap()[d * 128:(d + 1) * 128, (t - 1) * 128:t * 128], in_=e[:]),
                            waits=[("p1ev2", S.v("p1ev2"))], inc=f"p1q{nev % 4}", incn=16)
                        x1ev_free[nev % 4] = (f"p1q{nev % 4}", S.v(f"p1q{nev % 4}"))
                        nev += 1
                slot_free[pp] = ("p1tp", S.v("p1tp"))
    LNEV = S.v("p1ev")
    for dm in range(8):
        S.do("sp", lambda d=dm: nc.sync.dma_start(
            out=xn2_sp.ap()[d * 128:(d + 1) * 128, :],
            in_=xn2_t[:, d * TOKP:(d + 1) * TOKP]),
            waits=[("p1ev", LNEV)], inc="x2d", incn=16)
    X2D = S.v("x2d")
    X1Q = [(f"p1q{i}", S.v(f"p1q{i}")) for i in range(4)]
    P1TPF = S.v("p1tp")
    P1GATE = [("p1tp", P1TPF), ("p1ev", LNEV), ("p1ev2", S.v("p1ev2"))] + X1Q

    if debug:
        for dt_ in range(8):
            S.do("pool", lambda d=dt_: nc.gpsimd.dma_start(
                out=dbg["xn2"][d * 128:(d + 1) * 128, :], in_=xn2_t[:, d * TOKP:(d + 1) * TOKP]),
                waits=[("p1ev", LNEV)], inc="dbgs", incn=16)

    S.mark("p1")
    with ExitStack() as c2:
        sb2 = lambda n, s, dt=F32: c2.enter_context(nc.sbuf_tensor(n, list(s), dt))
        w_ct = [sb2(f"wct{i}", (128, 8 * 128), BF16) for i in range(2)]
        xipre = [sb2(f"xipre{i}", (128, TOKP)) for i in range(2)]
        xi_f32 = [sb2(f"xif{i}", (128, TOK)) for i in range(2)]
        xi_bf = sb2("xibf", (128, 16 * TOK), BF16)
        dbc_ev = sb2("dbcev", (80, TOK))
        dbc_ev2 = sb2("dbcev2", (16, TOK))
        with nc.psum_tensor("feps0", [128, MM], F32) as f0, \
             nc.psum_tensor("feps1", [128, MM], F32) as f1, \
             nc.psum_tensor("feps2", [128, MM], F32) as f2, \
             nc.psum_tensor("feps3", [128, MM], F32) as f3:
            col_chunks = []
            c0 = 0
            while c0 < TOKP:
                cw = min(MM, TOKP - c0)
                col_chunks.append((c0, cw))
                c0 += cw
            xi_guard = ("fxp", 0)
            dbc_guard = ("fxd", 0)
            wslot = [("fei", 0), ("fei", 0)]
            xpre_slot = [("fec", 0), ("fec", 0)]
            xf_slot = [(("feu0", 0), ("feb", 0)), (("feu1", 0), ("feb", 0))]
            for dir_ in "fb":
                for ct in range(16):
                    pp = ct % 2
                    wt, xp, xf = w_ct[pp], xipre[pp], xi_f32[pp]
                    S.do("pool", lambda d=dir_, c=ct, w=wt: nc.gpsimd.dma_start(
                        out=w[:].rearrange("p (kt c) -> p kt c", kt=8),
                        in_=w_in_xi[d][:, c * 128:(c + 1) * 128]
                        .rearrange("(kt p) c -> p kt c", p=128)),
                        waits=[wslot[pp]] + (P1GATE if (dir_ == "f" and ct == 0) else []),
                        inc=f"few{pp}", incn=16)
                    S.mark(f"fw_{dir_}_{ct}")
                    wr = (f"few{pp}", S.v(f"few{pp}"))
                    for ci, (cc0, cw) in enumerate(col_chunks):
                        pb = [f0, f1][(ct * len(col_chunks) + ci) % 2]
                        for k in range(8):
                            S.do("pe", lambda kk=k, o=cc0, w=cw, p=pb, wt_=wt: nc.tensor.matmul(
                                p[:, 0:w], wt_[:, kk * 128:(kk + 1) * 128],
                                xn2_t[:, kk * TOKP + o: kk * TOKP + o + w],
                                start=(kk == 0), stop=(kk == 7)),
                                waits=([wr, ("p1ev", LNEV),
                                        ("fee", max(0, S.v("fee") - 1))] if k == 0 else []),
                                inc=("fei" if k == 7 else None))
                        S.mark(f"fmm_{dir_}_{ct}_{ci}")
                        S.do("act", lambda o=cc0, w=cw, p=pb, t=xp: nc.scalar.activation(
                            t[:, o:o + w], p[:, 0:w], AF.Copy),
                            waits=[("fei", S.v("fei")), xpre_slot[pp]], inc="fee")
                        S.mark(f"fm_{dir_}_{ct}_{ci}")
                    ir = [("fee", S.v("fee")), ("p1v", AEXP)]
                    if dir_ == "f":
                        S.do("dve", lambda t=xp: nc.vector.tensor_scalar(
                            t[:, PAD - 3:PAD], t[:, PAD - 3:PAD], edge_sb[:, 0:1], None, ALU.mult),
                            waits=ir, inc="fec")
                    else:
                        S.do("dve", lambda t=xp: nc.vector.tensor_scalar(
                            t[:, PAD + TOK:PAD + TOK + 3], t[:, PAD + TOK:PAD + TOK + 3],
                            edge_sb[:, 1:2], None, ALU.mult),
                            waits=ir, inc="fec")
                    offs = [PAD - 3, PAD - 2, PAD - 1, PAD] if dir_ == "f" else [PAD, PAD + 1, PAD + 2, PAD + 3]
                    wsel = [0, 1, 2, 3] if dir_ == "f" else [3, 2, 1, 0]
                    S.do("dve", lambda t=xp, o=xf, c=ct, d=dir_, oj=offs[0], wj=wsel[0]: nc.vector.tensor_scalar(
                        o[:], t[:, oj:oj + TOK], convw_sb[d][:, c * 4 + wj:c * 4 + wj + 1], None, ALU.mult),
                        waits=[xf_slot[pp][0]], inc="fec")
                    for j in range(1, 4):
                        S.do("dve", lambda t=xp, o=xf, c=ct, d=dir_, oj=offs[j], wj=wsel[j]: nc.vector.scalar_tensor_tensor(
                            o[:], t[:, oj:oj + TOK], convw_sb[d][:, c * 4 + wj:c * 4 + wj + 1], o[:],
                            ALU.mult, ALU.add), inc="fec")
                    xpre_slot[pp] = ("fec", S.v("fec"))
                    S.mark(f"fc_{dir_}_{ct}")
                    S.do("act", lambda o=xf, c=ct, d=dir_: nc.scalar.activation(
                        o[:], o[:], AF.Silu, bias=convb_sb[d][:, c:c + 1]),
                        waits=[("fec", S.v("fec"))], inc="fes")
                    sr = S.v("fes")
                    S.mark(f"fs_{dir_}_{ct}")
                    S.do("sp", lambda o=xf, c=ct, d=dir_: nc.sync.dma_start(
                        out=u_in[d].ap()[c * 128:(c + 1) * 128, :], in_=o[:]),
                        waits=[("fes", sr)], inc=f"feu{pp}", incn=16)
                    S.do("dve", lambda o=xf, c=ct: nc.vector.tensor_copy(
                        xi_bf[:, c * TOK:(c + 1) * TOK], o[:]),
                        waits=[("fes", sr), xi_guard], inc="feb")
                    wslot[pp] = ("fei", S.v("fei"))
                    xf_slot[pp] = ((f"feu{pp}", S.v(f"feu{pp}")), ("feb", S.v("feb")))
                    S.mark(f"fe_{dir_}_ct{ct}")
                BF = S.v("feb")
                S.mark(f"fe_ct_{dir_}")
                for mi in range(NMC):
                    o = mi * MM
                    for c in range(16):
                        S.do("pe", lambda cc=c, oo=o, d=dir_: nc.tensor.matmul(
                            f2[0:80, 0:MM], wx_sb[d][:, cc * 96: cc * 96 + 80],
                            xi_bf[:, cc * TOK + oo: cc * TOK + oo + MM],
                            start=(cc == 0), stop=(cc == 15)),
                            waits=([("feb", BF), ("ldc2", LDC2),
                                    ("fxe", S.v("fxe"))] if c == 0 else []),
                            inc=("fxp" if c == 15 else None))
                    S.do("act", lambda oo=o: nc.scalar.activation(
                        dbc_ev[:, oo:oo + MM], f2[0:80, 0:MM], AF.Copy),
                        waits=[("fxp", S.v("fxp")), dbc_guard], inc="fxe")
                    for c in range(16):
                        S.do("pe", lambda cc=c, oo=o, d=dir_: nc.tensor.matmul(
                            f3[0:16, 0:MM], wx_sb[d][:, cc * 96 + 80: cc * 96 + 96],
                            xi_bf[:, cc * TOK + oo: cc * TOK + oo + MM],
                            start=(cc == 0), stop=(cc == 15)),
                            waits=([("fxe2", S.v("fxe2"))] if c == 0 else []),
                            inc=("fxp" if c == 15 else None))
                    S.do("act", lambda oo=o: nc.scalar.activation(
                        dbc_ev2[0:16, oo:oo + MM], f3[0:16, 0:MM], AF.Copy),
                        waits=[("fxp", S.v("fxp")), dbc_guard], inc="fxe2")
                xi_guard = ("fxp", S.v("fxp"))
                S.mark(f"fe_xp_{dir_}")
                S.do("sp", lambda d=dir_: nc.sync.dma_start(
                    out=dbc_in[d].ap()[0:80, :], in_=dbc_ev[:]),
                     waits=[("fxe", S.v("fxe")), ("fxe2", S.v("fxe2"))], inc="fxd", incn=16)
                S.do("sp", lambda d=dir_: nc.sync.dma_start(
                    out=dbc_in[d].ap()[80:96, :], in_=dbc_ev2[:]),
                     inc="fxd", incn=16)
                dbc_guard = ("fxd", S.v("fxd"))
                S.do("pool", lambda d=dir_: nc.gpsimd.collective_compute(
                    "AllGather", ALU.bypass, replica_groups=RG,
                    ins=[u_in[d].ap()], outs=[u_all[d].ap()]),
                    waits=[("feu0", S.v("feu0")), ("feu1", S.v("feu1"))], inc=f"agu{dir_}", cc=True)
                S.do("pool", lambda d=dir_: nc.gpsimd.collective_compute(
                    "AllGather", ALU.bypass, replica_groups=RG,
                    ins=[dbc_in[d].ap()], outs=[dbc_all[d].ap()]),
                    waits=[("fxd", S.v("fxd"))], inc=f"agd{dir_}", cc=True)
                S.mark(f"fe_ag_{dir_}")

    S.mark("fe")
    mid.close()
    with ExitStack() as c3:
        sb3 = lambda n, s, dt=F32: c3.enter_context(nc.sbuf_tensor(n, list(s), dt))
        u_sc = sb3("u_sc", (128, NCT * TOK))
        delta = [sb3(f"delta{i}", (128, NCT * TOK)) for i in range(2)]
        du = sb3("du", (128, NCT * TOK))
        spt = sb3("spt", (128, NCT * TOK))
        y_sc = sb3("y_sc", (128, NCT * TOK))
        dt_sb = [sb3(f"dt_sb{i}", (DTR, TOK), BF16) for i in range(2)]
        Brow = [sb3(f"Brow{i}", (16, TOK)) for i in range(2)]
        Crow = [sb3(f"Crow{i}", (16, TOK)) for i in range(2)]
        dA_t = [sb3(f"dA_t{i}", (128, SC)) for i in range(2)]
        b_t = [sb3(f"b_t{i}", (128, SC)) for i in range(2)]
        cm_t = [sb3(f"cm_t{i}", (128, SC)) for i in range(2)]
        h_t = sb3("h_t", (128, SC))
        with nc.psum_tensor("scps0", [128, MM], F32) as s0, \
             nc.psum_tensor("scps1", [128, MM], F32) as s1, \
             nc.psum_tensor("scpB", [128, SC], F32) as psB, \
             nc.psum_tensor("scpC", [128, SC], F32) as psC:
            S.do("dve", lambda: nc.vector.memset(state[:], 0.0),
                 waits=[("ldc", LDC)], inc="scv")
            ITERS = [(g,) + JOBS[jb] + (jb,) for g in range(4) for jb in range(4)]
            NIT = len(ITERS)
            NQ = SC // MM
            carry = {
                "PSBF": 0, "PSCF": 0,
                "TTSD": {0: 0, 1: 0},
                "CMF": {0: 0, 1: 0},
                "BCREAD": [0, 0], "DTREAD": [0, 0],
                "UCONS": 0, "YEV": 0,
            }
            LDP_it = [0] * NIT
            LDS_it = [0] * NIT
            SPLN_it = [0] * NIT
            DLT_it = [0] * NIT

            def emit_loads(it):
                g, dir_, bb_, jb = ITERS[it]
                gi = g if dir_ == "f" else 3 - g
                p = 4 * bb_ + gi
                pp2 = it % 2
                fe_gate = ([("aguf", 1), ("agub", 1), ("agdf", 1), ("agdb", 1),
                            ("x2d", X2D)] if it == 0 else [])
                for ct in range(NCT):
                    icol = ((0 if dir_ == "f" else 1) * 2 + bb_) * 4 * NCT + gi * NCT + ct
                    S.do("pool", lambda c=ct, d=dir_, ic=icol: nc.gpsimd.indirect_dma_start(
                        u_sc[:, c * TOK:(c + 1) * TOK], None, u_all[d].ap(),
                        IndirectOffsetOnAxis(idxu_sb[:, ic:ic + 1], 0)),
                        waits=[(f"agu{d}", 1), ("ldc", LDC), ("scv", carry["UCONS"])] + fe_gate,
                        inc="sclP", incn=16)
                S.do("pool", lambda d=dir_, pp_=p, b_=dt_sb[pp2]: nc.gpsimd.dma_start(
                    out=b_[:], in_=dbc_all[d].ap()[96 * pp_: 96 * pp_ + 64, :]),
                    waits=[(f"agd{d}", 1), ("scp", carry["DTREAD"][pp2])] + fe_gate,
                    inc="sclP", incn=16)
                LDP_it[it] = S.v("sclP")
                S.do("sp", lambda d=dir_, pp_=p, b_=Brow[pp2]: nc.sync.dma_start(
                    out=b_[:], in_=dbc_all[d].ap()[96 * pp_ + 64: 96 * pp_ + 80, :]),
                    waits=[(f"agd{d}", 1), ("scp", carry["BCREAD"][pp2])] + fe_gate,
                    inc="sclS", incn=16)
                S.do("sp", lambda d=dir_, pp_=p, b_=Crow[pp2]: nc.sync.dma_start(
                    out=b_[:], in_=dbc_all[d].ap()[96 * pp_ + 80: 96 * pp_ + 96, :]),
                    waits=[("scp", carry["BCREAD"][pp2])] + fe_gate,
                    inc="sclS", incn=16)
                LDS_it[it] = S.v("sclS")

            def emit_dtproj(it, chunk):
                g, dir_, bb_, jb = ITERS[it]
                pp2 = it % 2
                dl = delta[pp2]
                if chunk < 2:
                    ct = chunk
                    for mi in range(TOK // MM):
                        o = mi * MM
                        pb = [s0, s1][(ct * (TOK // MM) + mi) % 2]
                        S.do("pe", lambda c=ct, oo=o, d=dir_, p_=pb, pq=pp2: nc.tensor.matmul(
                            p_[:, 0:MM], wdt_sb[d][:, c * 128:(c + 1) * 128],
                            dt_sb[pq][:, oo:oo + MM], start=True, stop=True),
                            waits=[("sclP", LDP_it[it]), ("sce", max(0, S.v("sce") - 2))],
                            inc="scp")
                        S.do("act", lambda c=ct, oo=o, d=dir_, p_=pb: nc.scalar.activation(
                            spt[:, c * TOK + oo: c * TOK + oo + MM], p_[:, 0:MM],
                            AF.Abs, bias=dtb_sb[d][:, c:c + 1]),
                            waits=[("scp", S.v("scp")), ("scv", DLT_it[it - 1] if it else 0)],
                            inc="sce")
                        S.do("act", lambda c=ct, oo=o, d=dir_, p_=pb, dl_=dl: nc.scalar.activation(
                            dl_[:, c * TOK + oo: c * TOK + oo + MM], p_[:, 0:MM],
                            AF.Relu, bias=dtb_sb[d][:, c:c + 1]), inc="sce")
                    if ct == 1:
                        carry["DTREAD"][pp2] = S.v("scp")
                else:
                    ct = chunk - 2
                    S.do("act", lambda c=ct: nc.scalar.activation(
                        spt[:, c * TOK:(c + 1) * TOK], spt[:, c * TOK:(c + 1) * TOK],
                        AF.Exp, scale=cst[:, 2:3]), inc="sce")
                    S.do("act", lambda c=ct: nc.scalar.activation(
                        spt[:, c * TOK:(c + 1) * TOK], spt[:, c * TOK:(c + 1) * TOK],
                        AF.Ln, bias=cst[:, 3:4]), inc="sce")
                    if ct == NCT - 1:
                        SPLN_it[it] = S.v("sce")

            def emit_boundary(it):
                g, dir_, bb_, jb = ITERS[it]
                pp2 = it % 2
                dl = delta[pp2]
                for ct in range(NCT):
                    S.do("dve", lambda c=ct, dl_=dl: nc.vector.tensor_tensor(
                        dl_[:, c * TOK:(c + 1) * TOK], dl_[:, c * TOK:(c + 1) * TOK],
                        spt[:, c * TOK:(c + 1) * TOK], ALU.add),
                        waits=[("sce", SPLN_it[it])], inc="scv")
                for ct in range(NCT):
                    S.do("dve", lambda c=ct, dl_=dl: nc.vector.tensor_tensor(
                        du[:, c * TOK:(c + 1) * TOK], dl_[:, c * TOK:(c + 1) * TOK],
                        u_sc[:, c * TOK:(c + 1) * TOK], ALU.mult),
                        waits=[("sclP", LDP_it[it])], inc="scv")
                for ct in range(NCT):
                    S.do("dve", lambda c=ct, d=dir_: nc.vector.tensor_scalar(
                        y_sc[:, c * TOK:(c + 1) * TOK], u_sc[:, c * TOK:(c + 1) * TOK],
                        D_sb[d][:, c:c + 1], None, ALU.mult),
                        waits=[("scy", carry["YEV"])], inc="scv")
                DLT_it[it] = S.v("scv")
                carry["UCONS"] = S.v("scv")

            def emit_inner(it):
                g, dir_, bb_, jb = ITERS[it]
                pp2 = it % 2
                dl = delta[pp2]
                subs = list(range(NSUB)) if dir_ == "f" else list(range(NSUB - 1, -1, -1))
                sn_list = [(sub, n) for sub in subs for n in range(NST)]
                ns = len(sn_list)
                slots = {}
                used = -1
                for w, kind in zip([(2 * ns) // 32, (8 * ns) // 32, (12 * ns) // 32,
                                    (16 * ns) // 32, (20 * ns) // 32],
                                   ["L", 0, 1, 2, 3]):
                    sl = max(w, used + 1)
                    assert sl < ns
                    slots.setdefault(sl, []).append(kind)
                    used = sl
                for si, (sub, n) in enumerate(sn_list):
                    so = sub * SC
                    for q in range(NQ):
                        S.do("pe", lambda nn=n, qq=q, oo=so, b_=Brow[pp2]: nc.tensor.matmul(
                            psB[:, qq * MM:(qq + 1) * MM],
                            sel_sb[:, nn * 128:(nn + 1) * 128],
                            b_[:, oo + qq * MM: oo + (qq + 1) * MM],
                            start=True, stop=True),
                            waits=[("sclS", LDS_it[it]), ("scv2", carry["PSBF"])],
                            inc="scp")
                    for q in range(NQ):
                        S.do("pe", lambda nn=n, qq=q, oo=so, b_=Crow[pp2]: nc.tensor.matmul(
                            psC[:, qq * MM:(qq + 1) * MM],
                            sel_sb[:, nn * 128:(nn + 1) * 128],
                            b_[:, oo + qq * MM: oo + (qq + 1) * MM],
                            start=True, stop=True),
                            waits=[("scv2", carry["PSCF"])], inc="scp")
                    BCC = S.v("scp")
                    BCB = BCC - NQ
                    dav = {}
                    for ct in range(NCT):
                        S.do("act", lambda c=ct, nn=n, d=dir_, oo=so, dl_=dl: nc.scalar.activation(
                            dA_t[c][:], dl_[:, c * TOK + oo: c * TOK + oo + SC],
                            AF.Exp, scale=A_sb[d][:, c * NST + nn:c * NST + nn + 1]),
                            waits=[("scv", DLT_it[it]), ("scv2", carry["TTSD"][ct])],
                            inc="sce2")
                        dav[ct] = S.v("sce2")
                    for ct in range(NCT):
                        S.do("dve", lambda c=ct, oo=so: nc.vector.tensor_tensor(
                            b_t[c][:], du[:, c * TOK + oo: c * TOK + oo + SC],
                            psB[:], ALU.mult),
                            waits=[("scp", BCB)], inc="scv2")
                    carry["PSBF"] = S.v("scv2")
                    for ct in range(NCT):
                        sidx = (jb * NST + n) * NCT + ct
                        if dir_ == "f":
                            S.do("dve", lambda s_=sidx, c=ct: nc.vector.tensor_tensor_scan(
                                h_t[:], dA_t[c][:], b_t[c][:], state[:, s_:s_ + 1],
                                ALU.mult, ALU.add),
                                waits=[("sce2", dav[ct])], inc="scv2")
                            carry["TTSD"][ct] = S.v("scv2")
                            S.do("dve", lambda s_=sidx: nc.vector.tensor_copy(
                                state[:, s_:s_ + 1], h_t[:, SC - 1:SC]), inc="scv2")
                        else:
                            S.do("dve", lambda s_=sidx, c=ct: nc.vector.tensor_tensor_scan(
                                h_t[:, ::-1], dA_t[c][:, ::-1], b_t[c][:, ::-1],
                                state[:, s_:s_ + 1], ALU.mult, ALU.add),
                                waits=[("sce2", dav[ct])], inc="scv2")
                            carry["TTSD"][ct] = S.v("scv2")
                            S.do("dve", lambda s_=sidx: nc.vector.tensor_copy(
                                state[:, s_:s_ + 1], h_t[:, 0:1]), inc="scv2")
                        S.do("dve", lambda c=ct: nc.vector.tensor_tensor(
                            cm_t[c][:], h_t[:], psC[:], ALU.mult),
                            waits=[("scp", BCC), ("scg", carry["CMF"][ct])], inc="scv2")
                        S.do("pool", lambda c=ct, oo=so: nc.gpsimd.tensor_tensor(
                            y_sc[:, c * TOK + oo: c * TOK + oo + SC],
                            y_sc[:, c * TOK + oo: c * TOK + oo + SC],
                            cm_t[c][:], ALU.add),
                            waits=[("scv2", S.v("scv2")), ("scg", S.v("scg"))], inc="scg")
                        carry["CMF"][ct] = S.v("scg")
                    carry["PSCF"] = S.v("scv2")
                    if it + 1 < NIT:
                        for kind in slots.get(si, ()):
                            if kind == "L":
                                emit_loads(it + 1)
                            else:
                                emit_dtproj(it + 1, kind)
                carry["BCREAD"][pp2] = S.v("scp")
                SCG = S.v("scg")
                for ct in range(NCT):
                    S.do("sp", lambda c=ct, g_=g, j_=jb: nc.sync.dma_start(
                        out=y_in[g_].ap()[j_ * CSL + c * 128: j_ * CSL + (c + 1) * 128, :],
                        in_=y_sc[:, c * TOK:(c + 1) * TOK]),
                        waits=[("scg", SCG)], inc="scy", incn=16)
                carry["YEV"] = S.v("scy")
                if jb == 3:
                    S.do("pool", lambda g_=g: nc.gpsimd.collective_compute(
                        "AllGather", ALU.bypass, replica_groups=RG,
                        ins=[y_in[g_].ap()],
                        outs=[y_allb.ap()[g_ * NC * 4 * CSL:(g_ + 1) * NC * 4 * CSL, :]]),
                        waits=[("scy", S.v("scy"))], inc=f"agy{g}", cc=True)

            emit_loads(0)
            for ch in range(4):
                emit_dtproj(0, ch)
            for it in range(NIT):
                emit_boundary(it)
                emit_inner(it)
    S.mark("scan")
    YGATE = [(f"agy{g}", 1) for g in range(4)]

    with ExitStack() as c4:
        sb4 = lambda n, s, dt=F32: c4.enter_context(nc.sbuf_tensor(n, list(s), dt))
        w_st = [sb4(f"bwt{i}", (128, 8 * 128), BF16) for i in range(3)]
        xn2b = sb4("xn2b", (128, 8 * TOK), BF16)
        sz_t = [sb4(f"szt{i}", (128, TOK)) for i in range(3)]
        ytl = [sb4(f"ytl{i}", (128, TOK)) for i in range(3)]
        yg = sb4("yg", (128, 16 * TOK), BF16)
        wko = sb4("wko", (128, 16 * D), BF16)
        ofev = [sb4(f"ofev{i}", (128, MM), BF16) for i in range(2)]
        with nc.psum_tensor("beps0", [128, MM], F32) as e0, \
             nc.psum_tensor("beps1", [128, MM], F32) as e1, \
             nc.psum_tensor("beps2", [128, MM], F32) as e2, \
             nc.psum_tensor("beps3", [128, MM], F32) as e3:
            wslot = [("bzp", 0)] * 3
            szslot = [("bgt", 0)] * 3
            ytslot = [("bgt", 0)] * 3
            S.do("sp", lambda: nc.sync.dma_start(
                out=xn2b[:].rearrange("p (dm c) -> p dm c", dm=8),
                in_=xn2_sp.ap()[:, PAD:PAD + TOK]
                .rearrange("(dm p) c -> p dm c", p=128)),
                waits=[("x2d", X2D)] + YGATE, inc="x2l", incn=16)
            X2L = S.v("x2l")
            for dir_ in "fb":
                for ct in range(16):
                    pp = ct % 3
                    wt, sz, yt = w_st[pp], sz_t[pp], ytl[pp]
                    S.do("pool", lambda d=dir_, c=ct, w=wt: nc.gpsimd.dma_start(
                        out=w[:].rearrange("p (kt c) -> p kt c", kt=8),
                        in_=w_in_z[d][:, c * 128:(c + 1) * 128]
                        .rearrange("(kt p) c -> p kt c", p=128)),
                        waits=[wslot[pp]] + YGATE, inc=f"bzw{pp}", incn=16)
                    wr = (f"bzw{pp}", S.v(f"bzw{pp}"))
                    S.do("pool", lambda d=dir_, c=ct, yt_=yt: nc.gpsimd.indirect_dma_start(
                        yt_[:], None, y_allb.ap(),
                        IndirectOffsetOnAxis(
                            idxy_sb[:, (0 if d == "f" else 1) * 16 + c:
                                    (0 if d == "f" else 1) * 16 + c + 1], 0)),
                        waits=YGATE + [ytslot[pp]], inc=f"bzl{pp}", incn=16)
                    for mi in range(NMC):
                        o = mi * MM
                        pb = [e0, e1][(ct * NMC + mi) % 2]
                        for k in range(8):
                            S.do("pe", lambda kk=k, oo=o, p_=pb, w=wt: nc.tensor.matmul(
                                p_[:, 0:MM], w[:, kk * 128:(kk + 1) * 128],
                                xn2b[:, kk * TOK + oo: kk * TOK + oo + MM],
                                start=(kk == 0), stop=(kk == 7)),
                                waits=([wr, ("x2l", X2L),
                                        ("bze", max(0, S.v("bze") - 1))]
                                       if k == 0 else []),
                                inc=("bzp" if k == 7 else None))
                        S.do("act", lambda oo=o, p_=pb, s_=sz: nc.scalar.activation(
                            s_[:, oo:oo + MM], p_[:, 0:MM], AF.Silu),
                            waits=[("bzp", S.v("bzp"))] + ([szslot[pp]] if mi == 0 else []),
                            inc="bze")
                    S.do("dve", lambda c=ct, s_=sz, yt_=yt: nc.vector.tensor_tensor(
                        yg[:, c * TOK:(c + 1) * TOK], yt_[:], s_[:], ALU.mult),
                        waits=[("bze", S.v("bze")), (f"bzl{pp}", S.v(f"bzl{pp}")),
                               ("bop", S.v("bop"))], inc="bgt")
                    wslot[pp] = ("bzp", S.v("bzp"))
                    szslot[pp] = ("bgt", S.v("bgt"))
                    ytslot[pp] = ("bgt", S.v("bgt"))
                BGT = S.v("bgt")
                S.do("pool", lambda d=dir_: nc.gpsimd.dma_start(out=wko[:], in_=w_out[d]),
                     waits=[("bop", S.v("bop"))], inc="ffw", incn=16)
                BOW = S.v("ffw")
                for mi in range(NMC):
                    o = mi * MM
                    for dm in range(8):
                        pb = [e2, e3][dm % 2]
                        for k in range(16):
                            S.do("pe", lambda kk=k, oo=o, p_=pb, d_=dm: nc.tensor.matmul(
                                p_[:, 0:MM], wko[:, kk * D + d_ * 128: kk * D + (d_ + 1) * 128],
                                yg[:, kk * TOK + oo: kk * TOK + oo + MM],
                                start=(kk == 0), stop=(kk == 15)),
                                waits=([("bgt", BGT), ("ffw", BOW),
                                        ("boe", max(0, S.v("boe") - 1))] if k == 0 else []),
                                inc=("bop" if k == 15 else None))
                        osl = (mi * 8 + dm) % 2
                        oe = ofev[osl]
                        S.do("dve", lambda p_=pb, e_=oe: nc.vector.tensor_copy(e_[:], p_[:, 0:MM]),
                             waits=[("bop", S.v("bop")), (f"bod{osl}", S.v(f"bod{osl}"))],
                             inc="boe")
                        S.do("sp", lambda oo=o, d_=dm, dd=dir_, e_=oe: nc.sync.dma_start(
                            out=of_sp[dd].ap()[d_ * 128:(d_ + 1) * 128, oo:oo + MM], in_=e_[:]),
                            waits=[("boe", S.v("boe"))], inc=f"bod{osl}", incn=16)
    OFD2 = [("bod0", S.v("bod0")), ("bod1", S.v("bod1"))]
    with ExitStack() as c4b:
        sb4b = lambda n, s, dt=F32: c4b.enter_context(nc.sbuf_tensor(n, list(s), dt))
        wkoc = sb4b("wko2", (128, 16 * D), BF16)
        ofr = [sb4b(f"ofr{i}", (128, MM), BF16) for i in range(4)]
        x1c = sb4b("x1c", (128, 8 * MM))
        sqc = sb4b("sqc", (128, MM))
        st3 = sb4b("st3", (1, 4 * MM))
        xn3f = sb4b("xn3f", (128, MM))
        xn3b = [sb4b(f"xn3b{i}", (128, MM), BF16) for i in range(2)]
        with nc.psum_tensor("cbps0", [128, MM], F32) as e0, \
             nc.psum_tensor("cbps1", [128, MM], F32) as e1, \
             nc.psum_tensor("cbps2", [128, MM], F32) as e2, \
             nc.psum_tensor("cbps3", [128, MM], F32) as e3:
            S.do("pool", lambda: nc.gpsimd.dma_start(out=wkoc[:], in_=w_comb),
                 waits=[("bop", S.v("bop")), ("bgt", S.v("bgt")), ("bze", S.v("bze")),
                        ("boe", S.v("boe"))] + OFD2 +
                       ([("dbgs", S.v("dbgs"))] if debug else []),
                 inc="ffw", incn=16)
            CBW = S.v("ffw")
            S.do("dve", lambda: nc.vector.memset(colsum[:], 0.0), inc="cls")
            stat_gate = ("l3p", 0)
            x1_guard = ("x1d", 0)
            for mi in range(NMC):
                o = mi * MM
                S.do("sp", lambda oo=o: nc.sync.dma_start(
                    out=x1c[:].rearrange("p (dm c) -> p dm c", dm=8),
                    in_=x1_sp.ap()[:, oo:oo + MM].rearrange("(dm p) c -> p dm c", p=128)),
                    waits=X1Q + [("cbe", max(0, S.v("cbe") - 8)), x1_guard],
                    inc="cbx", incn=16)
                CBX = S.v("cbx")
                for dm in range(8):
                    pb = [e0, e1][dm % 2]
                    for k in range(16):
                        dd = "f" if k < 8 else "b"
                        kk = k % 8
                        sl = k % 4
                        buf = ofr[sl]
                        S.do("sp", lambda d_=dd, k_=kk, oo=o, b_=buf, sl_=sl: nc.sync.dma_start(
                            out=b_[:], in_=of_sp[d_].ap()[k_ * 128:(k_ + 1) * 128, oo:oo + MM]),
                            waits=OFD2 + [("cbp", max(0, S.v("cbp") - 3)),
                                          (f"cbl{sl}", S.v(f"cbl{sl}"))],
                            inc=f"cbl{sl}", incn=16)
                        S.do("pe", lambda k_=k, oo=o, p_=pb, d_=dm, b_=buf: nc.tensor.matmul(
                            p_[:, 0:MM], wkoc[:, k_ * D + d_ * 128: k_ * D + (d_ + 1) * 128],
                            b_[:], start=(k_ == 0), stop=(k_ == 15)),
                            waits=[(f"cbl{sl}", S.v(f"cbl{sl}"))] + ([("ffw", CBW),
                                   ("cbe", max(0, S.v("cbe") - 1))] if k == 0 else []),
                            inc="cbp")
                    S.do("dve", lambda p_=pb, d_=dm: nc.vector.scalar_tensor_tensor(
                        x1c[:, d_ * MM:(d_ + 1) * MM], p_[:, 0:MM], cb_sb[:, d_:d_ + 1],
                        x1c[:, d_ * MM:(d_ + 1) * MM], ALU.add, ALU.add),
                        waits=[("cbp", S.v("cbp")), ("cbx", CBX)], inc="cbe")
                X1R = S.v("cbe")
                for dm in range(8):
                    S.do("act", lambda d_=dm: nc.scalar.activation(
                        sqc[:], x1c[:, d_ * MM:(d_ + 1) * MM], AF.Square),
                        waits=[("cbe", X1R), ("l3q", S.v("l3q"))], inc="l3a")
                    S.do("pe", lambda d_=dm: nc.tensor.matmul(
                        e2[0:1, 0:MM], onescol_sb[:], x1c[:, d_ * MM:(d_ + 1) * MM],
                        start=(d_ == 0), stop=(d_ == 7)),
                        waits=([("cbe", X1R), stat_gate] if dm == 0 else []),
                        inc="l3p" if dm == 7 else None)
                    S.do("pe", lambda d_=dm: nc.tensor.matmul(
                        e3[0:1, 0:MM], onescol_sb[:], sqc[:],
                        start=(d_ == 0), stop=(d_ == 7)),
                        waits=[("l3a", S.v("l3a"))],
                        inc="l3q")
                L3P = [("l3p", S.v("l3p")), ("l3q", S.v("l3q"))]
                S.do("dve", lambda: nc.vector.tensor_scalar(st3[0:1, 0:MM], e2[0:1, 0:MM], INV_D, None, ALU.mult),
                     waits=L3P + [("l3p2", S.v("l3p2"))], inc="l3v")
                S.do("dve", lambda: nc.vector.tensor_scalar(st3[0:1, MM:2 * MM], e3[0:1, 0:MM], INV_D, None, ALU.mult), inc="l3v")
                S.do("dve", lambda: nc.vector.tensor_tensor(st3[0:1, 2 * MM:3 * MM], st3[0:1, 0:MM], st3[0:1, 0:MM], ALU.mult), inc="l3v")
                S.do("dve", lambda: nc.vector.tensor_tensor(st3[0:1, MM:2 * MM], st3[0:1, MM:2 * MM], st3[0:1, 2 * MM:3 * MM], ALU.subtract), inc="l3v")
                S.do("act", lambda: nc.scalar.activation(st3[0:1, MM:2 * MM], st3[0:1, MM:2 * MM], AF.Ln, bias=cst[0:1, 0:1]),
                     waits=[("l3v", S.v("l3v"))], inc="l3a")
                S.do("act", lambda: nc.scalar.activation(st3[0:1, MM:2 * MM], st3[0:1, MM:2 * MM], AF.Exp, scale=cst[0:1, 1:2]), inc="l3a")
                S.do("pe", lambda: nc.tensor.matmul(e2[:, 0:MM], ones1_sb[:], st3[0:1, 0:MM],
                                                    start=True, stop=True),
                     waits=[("l3a", S.v("l3a")), ("l3v", S.v("l3v"))], inc="l3p2")
                S.do("pe", lambda: nc.tensor.matmul(e3[:, 0:MM], ones1_sb[:], st3[0:1, MM:2 * MM],
                                                    start=True, stop=True), inc="l3p2")
                L3B = S.v("l3p2")
                for dm in range(8):
                    xsl = dm % 2
                    xb = xn3b[xsl]
                    S.do("dve", lambda d_=dm: nc.vector.tensor_tensor(
                        xn3f[:], x1c[:, d_ * MM:(d_ + 1) * MM], e2[:, 0:MM], ALU.subtract),
                        waits=[("l3p2", L3B)], inc="l3v")
                    S.do("dve", lambda: nc.vector.tensor_tensor(xn3f[:], xn3f[:], e3[:, 0:MM], ALU.mult), inc="l3v")
                    S.do("dve", lambda d_=dm: nc.vector.tensor_tensor(
                        x1c[:, d_ * MM:(d_ + 1) * MM], x1c[:, d_ * MM:(d_ + 1) * MM], xn3f[:], ALU.add), inc="l3v")
                    S.do("dve", lambda xb_=xb: nc.vector.tensor_copy(xb_[:], xn3f[:]),
                         waits=[(f"l3d{xsl}", S.v(f"l3d{xsl}"))], inc="l3v")
                    S.do("dve", lambda: nc.vector.reduce_sum(stat[:, 0:1], xn3f[:], AX.X), inc="l3v")
                    S.do("dve", lambda d_=dm: nc.vector.tensor_tensor(
                        colsum[:, d_:d_ + 1], colsum[:, d_:d_ + 1], stat[:, 0:1], ALU.add),
                        waits=[("cls", S.v("cls"))], inc="cls")
                    V3 = S.v("l3v")
                    S.do("sp", lambda oo=o, d_=dm, xb_=xb: nc.sync.dma_start(
                        out=xn3_in.ap()[d_ * 128:(d_ + 1) * 128, oo:oo + MM], in_=xb_[:]),
                        waits=[("l3v", V3)], inc=f"l3d{xsl}", incn=16)
                    S.do("sp", lambda oo=o, d_=dm: nc.sync.dma_start(
                        out=x1_sp.ap()[d_ * 128:(d_ + 1) * 128, oo:oo + MM],
                        in_=x1c[:, d_ * MM:(d_ + 1) * MM]),
                        waits=[("l3v", V3)], inc="x1d", incn=16)
                stat_gate = ("l3v", S.v("l3v"))
                x1_guard = ("x1d", S.v("x1d"))
            S.do("dve", lambda: nc.vector.tensor_scalar(
                colsum[:], colsum[:], 1.0 / L, None, ALU.mult),
                waits=[("cls", S.v("cls"))], inc="cls")
            for bb in range(2):
                S.do("dve", lambda b_=bb: nc.vector.tensor_scalar(
                    pooled_sb[:, b_ * 8:(b_ + 1) * 8], colsum[:], bmask_sb[:, b_:b_ + 1], None, ALU.mult),
                    waits=[("cls", S.v("cls"))], inc="cls")
            S.do("sp", lambda: nc.sync.dma_start(out=pooled_in.ap()[0:128, :], in_=pooled_sb[:, 0:8]),
                 waits=[("cls", S.v("cls"))], inc="rtl", incn=16)
            S.do("sp", lambda: nc.sync.dma_start(out=pooled_in.ap()[128:256, :], in_=pooled_sb[:, 8:16]),
                 inc="rtl", incn=16)
            S.do("pool", lambda: nc.gpsimd.collective_compute(
                "AllGather", ALU.bypass, replica_groups=RG,
                ins=[xn3_in.ap()], outs=[xn3_all.ap()]),
                waits=[("l3d0", S.v("l3d0")), ("l3d1", S.v("l3d1"))], inc="agxn", cc=True)
            S.do("pool", lambda: nc.gpsimd.collective_compute(
                "AllReduce", ALU.add, replica_groups=RG,
                ins=[pooled_in.ap()], outs=[pooled_out.ap()]),
                waits=[("rtl", S.v("rtl"))], inc="arpl", cc=True)

    S.mark("be")
    with nc.psum_tensor("rps0", [128, MM], F32) as r0, \
         nc.psum_tensor("rps1", [128, MM], F32) as r1:
        S.do("sp", lambda: nc.sync.dma_start(
            out=pooled_sb[:, 0:8], in_=pooled_out.ap()[0:128, :]),
            waits=[("arpl", 1)], inc="rtl", incn=16)
        S.do("sp", lambda: nc.sync.dma_start(
            out=pooled_sb[:, 8:16], in_=pooled_out.ap()[128:256, :]),
            inc="rtl", incn=16)
        S.do("dve", lambda: nc.vector.tensor_copy(pooled_bf[:], pooled_sb[:]),
             waits=[("rtl", S.v("rtl"))], inc="rtv")
        RTL = S.v("rtv")
        for bb in range(2):
            for cti in range(2):
                for k in range(8):
                    S.do("pe", lambda b_=bb, c=cti, kk=k: nc.tensor.matmul(
                        r0[:, (b_ * 2 + c):(b_ * 2 + c) + 1],
                        w1_sb[:, kk * RH + c * 128: kk * RH + (c + 1) * 128],
                        pooled_bf[:, b_ * 8 + kk: b_ * 8 + kk + 1],
                        start=(kk == 0), stop=(kk == 7)),
                        waits=([("rtv", RTL), ("ldc2", LDC2), ("agxn", 1)] if k == 0 else []),
                        inc=("rtp" if k == 7 else None))
        for col, bcol in [(0, 0), (1, 1), (2, 0), (3, 1)]:
            S.do("act", lambda c=col, b_=bcol: nc.scalar.activation(
                h_sb[:, c:c + 1], r0[:, c:c + 1], AF.Tanh, bias=b1_sb[:, b_:b_ + 1]),
                waits=[("rtp", S.v("rtp"))], inc="rta")
        S.do("dve", lambda: nc.vector.tensor_copy(h_bf[:], h_sb[:]),
             waits=[("rta", S.v("rta"))], inc="rtv")
        RTA = S.v("rtv")
        for bb in range(2):
            for k in range(2):
                S.do("pe", lambda b_=bb, kk=k: nc.tensor.matmul(
                    r1[0:NZ, b_:b_ + 1], w2_sb[:, kk * NZ:(kk + 1) * NZ],
                    h_bf[:, b_ * 2 + kk: b_ * 2 + kk + 1],
                    start=(kk == 0), stop=(kk == 1)),
                    waits=([("rtv", RTA)] if k == 0 else []),
                    inc=("rtp" if k == 1 else None))
        S.do("dve", lambda: nc.vector.tensor_scalar(
            lgsb[:], r1[0:NZ, 0:2], b2_sb[:], None, ALU.add),
             waits=[("rtp", S.v("rtp"))], inc="rta")
        for bb in range(2):
            S.do("sp", lambda b_=bb: nc.sync.dma_start(
                out=lg_bounce.ap()[b_:b_ + 1, :], in_=lgsb[:, b_:b_ + 1]),
                waits=[("rta", S.v("rta"))], inc="rtl", incn=16)
        S.do("sp", lambda: nc.sync.dma_start(
            out=lgr[:], in_=lg_bounce.ap().rearrange("b c -> (b c)").unsqueeze(0)),
             waits=[("rtl", S.v("rtl"))], inc="rtl", incn=16)
        RTL2 = S.v("rtl")
        for bb in range(2):
            S.do("dve", lambda b_=bb: nc.vector.reduce_max(
                rscr[0:1, 0:1], lgr[0:1, b_ * NZ:(b_ + 1) * NZ], AX.X),
                waits=[("rtl", RTL2), ("rta", S.v("rta"))], inc="rtv")
            S.do("dve", lambda: nc.vector.tensor_scalar(rscr[0:1, 1:2], rscr[0:1, 0:1], -1.0, None, ALU.mult), inc="rtv")
            S.do("act", lambda b_=bb: nc.scalar.activation(
                rscr[0:1, 8:16], lgr[0:1, b_ * NZ:(b_ + 1) * NZ], AF.Exp, bias=rscr[0:1, 1:2]),
                waits=[("rtv", S.v("rtv"))], inc="rta")
            S.do("dve", lambda: nc.vector.reduce_sum(rscr[0:1, 2:3], rscr[0:1, 8:16], AX.X),
                 waits=[("rta", S.v("rta"))], inc="rtv")
            S.do("dve", lambda: nc.vector.reciprocal(rscr[0:1, 3:4], rscr[0:1, 2:3]), inc="rtv")
            S.do("dve", lambda: nc.vector.tensor_single_scalar(
                rscr[0:1, 8:16], rscr[0:1, 8:16], rscr[0:1, 3:4], ALU.mult), inc="rtv")
            S.do("dve", lambda: nc.vector.max(out=rscr[0:1, 16:24], in_=rscr[0:1, 8:16]), inc="rtv")
            S.do("dve", lambda: nc.vector.tensor_single_scalar(
                rscr[0:1, 24:32], rscr[0:1, 8:16], rscr[0:1, 18:19], ALU.is_ge), inc="rtv")
            S.do("dve", lambda: nc.vector.tensor_tensor(
                rscr[0:1, 24:32], rscr[0:1, 24:32], rscr[0:1, 8:16], ALU.mult), inc="rtv")
            S.do("dve", lambda: nc.vector.reduce_sum(rscr[0:1, 4:5], rscr[0:1, 24:32], AX.X), inc="rtv")
            S.do("dve", lambda: nc.vector.reciprocal(rscr[0:1, 5:6], rscr[0:1, 4:5]), inc="rtv")
            S.do("dve", lambda b_=bb: nc.vector.tensor_single_scalar(
                wrow[0:1, b_ * NZ:(b_ + 1) * NZ], rscr[0:1, 24:32], rscr[0:1, 5:6], ALU.mult), inc="rtv")
        RWV = S.v("rtv")
        S.do("pe", lambda: nc.tensor.matmul(r0[:, 0:16], ones1_sb[:], wrow[0:1, 0:16],
                                            start=True, stop=True),
             waits=[("rtv", RWV)], inc="rtp")
        S.do("dve", lambda: nc.vector.tensor_copy(w_bc[:], r0[:, 0:16]),
             waits=[("rtp", S.v("rtp"))], inc="rtv")
        for bb in range(2):
            S.do("dve", lambda b_=bb: nc.vector.tensor_tensor(
                zscr[:], w_bc[:, b_ * 8:(b_ + 1) * 8], zmask_sb[:], ALU.mult),
                waits=[("rtv", S.v("rtv"))], inc="rtv")
            S.do("dve", lambda b_=bb: nc.vector.reduce_sum(wz_col[:, b_:b_ + 1], zscr[:], AX.X), inc="rtv")
        WZ = S.v("rtv")
        if debug:
            S.do("pool", lambda: nc.gpsimd.dma_start(out=dbg["w"], in_=wrow[:]),
                 waits=[("rtv", RWV)], inc="dbgs", incn=16)

    S.mark("router")
    with ExitStack() as c6:
        sb6 = lambda n, s, dt=F32: c6.enter_context(nc.sbuf_tensor(n, list(s), dt))
        wa_sb = sb6("wa_sb", (128, 8 * DFF), BF16)
        wb_sb = sb6("wb_sb", (128, 32 * D), BF16)
        xn3r = [sb6(f"xn3r{i}", (128, 8 * MM), BF16) for i in range(2)]
        G_sb = sb6("G_sb", (128, 32 * MM), BF16)
        z_ev = [sb6(f"zev{i}", (128, MM)) for i in range(2)]
        with nc.psum_tensor("fz0", [128, MM], F32) as z0, \
             nc.psum_tensor("fz1", [128, MM], F32) as z1, \
             nc.psum_tensor("fz2", [128, MM], F32) as z2, \
             nc.psum_tensor("fz3", [128, MM], F32) as z3:
            be_exit = [("agxn", 1), ("x1d", S.v("x1d")),
                       ("bod0", S.v("bod0")), ("bod1", S.v("bod1"))]
            if debug:
                be_exit.append(("dbgs", S.v("dbgs")))
            S.do("pool", lambda: nc.gpsimd.dma_start(
                out=wa_sb[:].rearrange("p (kt f) -> p kt f", kt=8),
                in_=wa_d.rearrange("(kt p) f -> p kt f", p=128)),
                waits=be_exit, inc="ffw", incn=16)
            S.do("pool", lambda: nc.gpsimd.dma_start(out=wb_sb[:], in_=wb_d),
                 inc="ffw", incn=16)
            FFW = S.v("ffw")
            prev_wb = ("ffq", 0)
            for fc, (mi_, pdst) in enumerate((m, p) for m in range(NMC)
                                             for p in range(NC)):
                off = mi_ * MM
                pp = fc % 2
                xr, ze = xn3r[pp], z_ev[pp]
                xgate = ("ffp", max(0, S.v("ffp") - 32))
                S.do("sp", lambda p_=pdst, o=off, x=xr: nc.sync.dma_start(
                    out=x[:].rearrange("p (kt c) -> p kt c", kt=8),
                    in_=xn3_all.ap()[p_ * D: (p_ + 1) * D, o:o + MM]
                    .rearrange("(kt p) c -> p kt c", p=128)),
                    waits=[("agxn", 1), xgate, (f"ffl{pp}", S.v(f"ffl{pp}"))],
                    inc=f"ffl{pp}", incn=16)
                FFL = (f"ffl{pp}", S.v(f"ffl{pp}"))
                for ft in range(32):
                    pb = [z0, z1][ft % 2]
                    for k in range(8):
                        S.do("pe", lambda kk=k, f_=ft, p_=pb, x=xr: nc.tensor.matmul(
                            p_[:, 0:MM], wa_sb[:, kk * DFF + f_ * 128: kk * DFF + (f_ + 1) * 128],
                            x[:, kk * MM:(kk + 1) * MM],
                            start=(kk == 0), stop=(kk == 7)),
                            waits=([FFL, ("ffw", FFW),
                                    ("ffg", max(0, S.v("ffg") - 1))] +
                                   ([("rtv", WZ)] if fc == 0 and ft == 0 else [])
                                   if k == 0 else []),
                            inc=("ffp" if k == 7 else None))
                    S.do("act", lambda f_=ft, p_=pb: nc.scalar.activation(
                        G_sb[:, f_ * MM:(f_ + 1) * MM], p_[:, 0:MM], AF.Gelu_apprx_tanh),
                        waits=[("ffp", S.v("ffp"))] + ([prev_wb] if ft == 0 else []),
                        inc="ffg")
                GD = S.v("ffg")
                bb = pdst // 4
                for dm in range(8):
                    pb = [z2, z3][dm % 2]
                    for k in range(32):
                        S.do("pe", lambda kk=k, d_=dm, p_=pb: nc.tensor.matmul(
                            p_[:, 0:MM], wb_sb[:, kk * D + d_ * 128: kk * D + (d_ + 1) * 128],
                            G_sb[:, kk * MM:(kk + 1) * MM],
                            start=(kk == 0), stop=(kk == 31)),
                            waits=([("ffg", GD), ("ffv", max(0, S.v("ffv") - 1))]
                                   if k == 0 else []),
                            inc=("ffq" if k == 31 else None))
                    S.do("dve", lambda p_=pb, b_=bb, z=ze: nc.vector.tensor_scalar(
                        z[:], p_[:, 0:MM], wz_col[:, b_:b_ + 1], None, ALU.mult),
                        waits=[("ffq", S.v("ffq")), ("rtv", WZ),
                               (f"ffd{pp}", S.v(f"ffd{pp}"))], inc="ffv")
                    S.do("sp", lambda p_=pdst, o=off, d_=dm, z=ze: nc.sync.dma_start(
                        out=rs_in[o // MM].ap()[p_ * D + d_ * 128: p_ * D + (d_ + 1) * 128, :],
                        in_=z[:]),
                        waits=[("ffv", S.v("ffv"))], inc=f"ffd{pp}", incn=16)
                prev_wb = ("ffq", S.v("ffq"))
                if pdst == NC - 1:
                    S.do("pool", lambda q_=mi_: nc.gpsimd.collective_compute(
                        "ReduceScatter", ALU.add, replica_groups=RG,
                        ins=[rs_in[q_].ap()], outs=[rs_out[q_].ap()]),
                        waits=[("ffd0", S.v("ffd0")), ("ffd1", S.v("ffd1"))],
                        inc=f"rs{mi_}", cc=True)
        ffn_exit = [("ffp", S.v("ffp")), ("ffq", S.v("ffq")), ("ffg", S.v("ffg")),
                    ("ffv", S.v("ffv")), ("ffd0", S.v("ffd0")), ("ffd1", S.v("ffd1")),
                    ("ffl0", S.v("ffl0")), ("ffl1", S.v("ffl1"))]

    S.mark("ffn")
    with ExitStack() as c7:
        sb7 = lambda n, s, dt=F32: c7.enter_context(nc.sbuf_tensor(n, list(s), dt))
        xf1 = [sb7(f"xf1{i}", (128, MM)) for i in range(2)]
        xrs = [sb7(f"xrs{i}", (128, MM)) for i in range(2)]
        for q in range(NMC):
            for dm in range(8):
                sl = (q * 8 + dm) % 2
                b1_, b2_ = xf1[sl], xrs[sl]
                S.do("sp", lambda q_=q, d_=dm, b=b1_: nc.sync.dma_start(
                    out=b[:], in_=x1_sp.ap()[d_ * 128:(d_ + 1) * 128,
                                             q_ * MM:(q_ + 1) * MM]),
                    waits=[(f"rs{q}", 1), ("x1d", S.v("x1d")),
                           ("fna", max(0, S.v("fna") - 1)),
                           (f"fno{sl}", S.v(f"fno{sl}"))] +
                          (ffn_exit if (q == 0 and dm == 0) else []),
                    inc=f"fnlA{sl}", incn=16)
                S.do("sp", lambda q_=q, d_=dm, b=b2_: nc.sync.dma_start(
                    out=b[:], in_=rs_out[q_].ap()[d_ * 128:(d_ + 1) * 128, :]),
                    inc=f"fnlA{sl}", incn=16)
                S.do("dve", lambda b=b1_, c=b2_: nc.vector.tensor_tensor(
                    b[:], b[:], c[:], ALU.add),
                    waits=[(f"fnlA{sl}", S.v(f"fnlA{sl}"))], inc="fna")
                S.do("pool", lambda q_=q, d_=dm, b=b1_: nc.gpsimd.dma_start(
                    out=out_d[d_ * 128:(d_ + 1) * 128, q_ * MM:(q_ + 1) * MM],
                    in_=b[:]),
                    waits=[("fna", S.v("fna"))], inc=f"fno{sl}", incn=16)
    FNO = [("fno0", S.v("fno0")), ("fno1", S.v("fno1"))]

    if debug:
        S.do("pool", lambda: nc.gpsimd.dma_start(out=dbg["u_f"], in_=u_all["f"].ap()[0:DI, :]),
             waits=[("aguf", 1)], inc="dbgs", incn=16)
        S.do("pool", lambda: nc.gpsimd.dma_start(out=dbg["dbc_f"], in_=dbc_all["f"].ap()[0:96, :]),
             waits=[("agdf", 1)], inc="dbgs", incn=16)
        for j in range(NC):
            S.do("pool", lambda j_=j: nc.gpsimd.dma_start(
                out=dbg["yf"][j_ * CSL:(j_ + 1) * CSL, :],
                in_=y_allb.ap()[j_ * 4 * CSL: j_ * 4 * CSL + CSL, :]),
                waits=[("agy0", 1)], inc="dbgs", incn=16)
        S.do("pool", lambda: nc.gpsimd.dma_start(out=dbg["x1"], in_=x1_sp.ap()),
             waits=[("l3d0", S.v("l3d0")), ("l3d1", S.v("l3d1")), ("x1d", S.v("x1d"))],
             inc="dbgs", incn=16)
        S.do("pool", lambda: nc.gpsimd.dma_start(out=dbg["xn3"], in_=xn3_all.ap()[0:D, :]),
             waits=[("agxn", 1)], inc="dbgs", incn=16)
        for q in range(NMC):
            S.do("pool", lambda q_=q: nc.gpsimd.dma_start(
                out=dbg["zs"][:, q_ * MM:(q_ + 1) * MM], in_=rs_out[q_].ap()),
                waits=[(f"rs{q_2}", 1) for q_2 in range(NMC)], inc="dbgs", incn=16)

    S.do("pool", lambda: nc.gpsimd.nop(),
         waits=FNO + ([("dbgs", S.v("dbgs"))] if debug else []), inc=None, cc=True)

    if phases is not None:
        S.truncate(phases)
        with nc.sbuf_tensor("stub", [128, D], F32) as stub:
            S.do("dve", lambda: nc.vector.memset(stub[:], 0.0), inc="stbv")
            for tt in range(TOK // 128):
                S.do("sp", lambda t=tt: nc.sync.dma_start(
                    out=out_d[t * 128:(t + 1) * 128, :], in_=stub[:]),
                    waits=[("stbv", 1)], inc="stbd", incn=16)
            S.do("pool", lambda: nc.gpsimd.nop(), waits=[("stbd", S.v("stbd"))], cc=True)

    with nc.Block() as block:
        @block.tensor
        def _(eng):
            S.emit("pe", eng)

        @block.vector
        def _(eng):
            S.emit("dve", eng)

        @block.scalar
        def _(eng):
            S.emit("act", eng)

        @block.sync
        def _(eng):
            S.emit("sp", eng)

        @block.gpsimd
        def _(eng):
            S.emit("pool", eng)

    top.close()
    return nc


def pack_cols(vec):
    v = np.ascontiguousarray(np.asarray(vec, np.float32)).reshape(-1)
    c = v.shape[0] // 128
    return np.ascontiguousarray(v.reshape(c, 128).T)


def pack_blocks(mat):
    m = np.ascontiguousarray(np.asarray(mat, np.float32))
    c, w = m.shape
    nt = c // 128
    return np.ascontiguousarray(m.reshape(nt, 128, w).transpose(1, 0, 2).reshape(128, nt * w))


def _prep_inputs(L, kw):
    f32 = np.float32
    TOK = L // 4
    TOKP = TOK + 2 * PAD
    x = np.asarray(kw["L0_attn_x"], f32)

    def wT(a):
        return np.ascontiguousarray(np.asarray(a, f32).T)

    base = {}
    for d, pre in (("f", "L0_attn_f_"), ("b", "L0_attn_b_")):
        in_w = np.asarray(kw[pre + "in_w"], f32)
        base[f"w_in_xi_{d}"] = wT(in_w[0:DI])
        base[f"w_in_z_{d}"] = wT(in_w[DI:2 * DI])
        base[f"w_x_{d}"] = pack_blocks(wT(np.asarray(kw[pre + "x_w"], f32)))
        base[f"w_out_{d}"] = pack_blocks(wT(np.asarray(kw[pre + "out_w"], f32)))
        base[f"conv_w_{d}"] = pack_blocks(np.asarray(kw[pre + "conv_w"], f32))
        base[f"conv_b_{d}"] = pack_cols(np.asarray(kw[pre + "conv_b"], f32))
    base["w_comb"] = pack_blocks(wT(np.asarray(kw["L0_attn_combine_w"], f32)))
    base["g_col"] = pack_cols(np.asarray(kw["L0_attn_ln_g"], f32))
    base["b_col"] = pack_cols(np.asarray(kw["L0_attn_ln_b"], f32))
    base["cb_col"] = pack_cols(np.asarray(kw["L0_attn_combine_b"], f32))
    base["w1"] = pack_blocks(np.asarray(kw["L0_ffn_W1"], f32))
    base["b1_col"] = pack_cols(np.asarray(kw["L0_ffn_b1"], f32))
    base["w2"] = pack_blocks(np.asarray(kw["L0_ffn_W2"], f32))
    base["b2_col"] = np.ascontiguousarray(np.asarray(kw["L0_ffn_b2"], f32).reshape(NZ, 1))
    sel = np.zeros((16, NST * 128), f32)
    for n in range(NST):
        sel[n, n * 128:(n + 1) * 128] = 1.0
    base["sel"] = sel
    base["ones1"] = np.ones((1, 128), f32)
    base["onescol"] = np.ones((128, 1), f32)
    base["ident"] = np.eye(128, dtype=f32)
    cstv = np.zeros((128, 4), f32)
    cstv[:, 0] = EPS
    cstv[:, 1] = -0.5
    cstv[:, 2] = -1.0
    cstv[:, 3] = 1.0
    base["cst_in"] = cstv

    Wa = np.asarray(kw["L0_ffn_Wa"], f32)
    Wb = np.asarray(kw["L0_ffn_Wb"], f32)
    CSLn = DI // NC
    NCT_ = CSLn // 128
    lane = np.arange(128, dtype=np.int32)

    per_core = []
    for rank in range(NC):
        m = dict(base)
        b = rank // 4
        q = rank % 4
        t0 = q * TOK
        xp = np.zeros((TOKP, D), f32)
        lo = max(0, t0 - PAD)
        hi = min(L, t0 + TOK + PAD)
        xp[lo - (t0 - PAD):hi - (t0 - PAD)] = x[b, lo:hi]
        m["x0"] = xp
        c0 = rank * CSLn
        for d, pre in (("f", "L0_attn_f_"), ("b", "L0_attn_b_")):
            dtw = np.asarray(kw[pre + "dt_w"], f32)
            m[f"w_dt_{d}"] = np.ascontiguousarray(dtw.T[:, c0:c0 + CSLn])
            m[f"dt_b_{d}"] = pack_cols(np.asarray(kw[pre + "dt_b"], f32)[c0:c0 + CSLn])
            m[f"A_{d}"] = pack_blocks(np.asarray(kw[pre + "A_log"], f32)[c0:c0 + CSLn])
            m[f"D_{d}"] = pack_cols(np.asarray(kw[pre + "D"], f32)[c0:c0 + CSLn])
        m["wa"] = np.ascontiguousarray(Wa[rank])
        m["wb"] = pack_blocks(np.ascontiguousarray(Wb[rank]))
        m["edge"] = np.stack([np.full(128, 0.0 if q == 0 else 1.0, f32),
                              np.full(128, 0.0 if q == 3 else 1.0, f32)], axis=1)
        m["bmask"] = np.stack([np.full(128, 1.0 if b == 0 else 0.0, f32),
                               np.full(128, 1.0 if b == 1 else 0.0, f32)], axis=1)
        zm = np.zeros((128, NZ), f32)
        zm[:, rank] = 1.0
        m["zmask"] = zm
        idxu = np.zeros((128, 2 * 2 * 4 * NCT_), np.int32)
        for di in range(2):
            for bb in range(2):
                for gi in range(4):
                    p = 4 * bb + gi
                    for ct in range(NCT_):
                        col = (di * 2 + bb) * 4 * NCT_ + gi * NCT_ + ct
                        idxu[:, col] = p * DI + c0 + ct * 128 + lane
        m["idxu"] = np.ascontiguousarray(idxu)
        idxy = np.zeros((128, 2 * 16), np.int32)
        for di, dch in enumerate("fb"):
            g = q if dch == "f" else 3 - q
            jb = 2 * b + (0 if dch == "f" else 1)
            for rt in range(16):
                j = rt // 2
                ct = rt % 2
                idxy[:, di * 16 + rt] = (g * NC * 4 * CSLn + j * 4 * CSLn +
                                         jb * CSLn + ct * 128 + lane)
        m["idxy"] = np.ascontiguousarray(idxy)
        per_core.append(m)
    return per_core


_CACHE = {}


def kernel(**inputs):
    L = inputs["L0_attn_x"].shape[1]
    TOK = L // 4
    if ("prog", L) not in _CACHE:
        _CACHE[("prog", L)] = build_program(L, debug=False, chain=True)
    nc = _CACHE[("prog", L)]
    in_maps = _prep_inputs(L, inputs)
    res = run_bass_kernel_spmd(nc, in_maps, core_ids=list(range(NC))).results
    out = np.zeros((BATCH, L, D), np.float32)
    for rank in range(NC):
        b, q = rank // 4, rank % 4
        out[b, q * TOK:(q + 1) * TOK, :] = res[rank]["out"].T
    return out
